# version 01
# baseline (speedup 1.0000x reference)
import math
from contextlib import ExitStack

import numpy as np
import concourse.bass as bass
import concourse.mybir as mybir
from concourse.bass_utils import run_bass_kernel_spmd

F32 = mybir.dt.float32
BF16 = mybir.dt.bfloat16
I32 = mybir.dt.int32
AF = mybir.ActivationFunctionType
ALU = mybir.AluOpType
AX = mybir.AxisListType

N = 4096
D = 1024
T = D // 128
NCH = N // 512
NT = N // 128
H = 16
DH = 64
F0 = 4096
F1 = 3072
E = 8
R = 16
EPS = 1e-5

ENGINES = ("tensor", "scalar", "vector", "gpsimd", "sync")
DBG = None


class Sem:
    __slots__ = ("h", "n", "name")

    def __init__(self, h, name):
        self.h = h
        self.n = 0
        self.name = name


class Prog:
    CHAINED = ("scalar", "vector", "gpsimd")

    def __init__(self, nc, es):
        self.nc = nc
        self.es = es
        self.q = {e: [] for e in ENGINES}
        self.sems = {}
        self.chain = {e: self.sem("chain_" + e) for e in self.CHAINED}

    def sem(self, name):
        if name not in self.sems:
            h = self.es.enter_context(self.nc.semaphore(name))
            self.sems[name] = Sem(h, name)
        return self.sems[name]

    def op(self, eng, fn, inc=None, inc_k=1, chain=True):
        cs = self.chain.get(eng) if chain else None
        if cs is not None:
            assert inc is None or inc is cs, \
                f"chained op on {eng} may only inc its chain sem"
            if cs.n > 0:
                self.q[eng].append(lambda e, s=cs, v=cs.n: e.wait_ge(s.h, v))
            cs.n += 1
            self.q[eng].append(lambda e, fn=fn, c=cs, v=cs.n - 1:
                               fn(e).then_inc(c.h, 1))
            return cs.n
        if inc is not None:
            inc.n += inc_k
            self.q[eng].append(
                lambda e, fn=fn, s=inc, k=inc_k: fn(e).then_inc(s.h, k))
            return inc.n
        self.q[eng].append(lambda e, fn=fn: fn(e))
        return None

    def wait(self, eng, sem, val=None):
        v = sem.n if val is None else val
        if v <= 0:
            return
        self.q[eng].append(lambda e, s=sem, v=v: e.wait_ge(s.h, v))

    def dma(self, eng, out, in_, inc=None, stream=None, serial=True, **kw):
        s = inc if inc is not None else self.sem("dma_" + stream)
        if serial and s.n > 0:
            self.q[eng].append(lambda e, ss=s, v=s.n: e.wait_ge(ss.h, v))
        return self.op(eng, lambda e: e.dma_start(out, in_, **kw), inc=s,
                       inc_k=16, chain=False)

    def dsem(self, stream):
        return self.sem("dma_" + stream)

    def barrier(self, tag):
        s = self.sem("bar")
        for eng in ENGINES:
            self.op(eng, lambda e: e.nop(), inc=s, inc_k=1, chain=False)
        tgt = s.n
        for eng in ENGINES:
            self.wait(eng, s, tgt)

    def build(self, block):
        for eng in ENGINES:
            items = self.q[eng]

            def section(e, items=items):
                for fn in items:
                    fn(e)

            getattr(block, eng)(section)


class PsumPool:

    def __init__(self, B):
        self.B = B
        self.free = {i: (0, "pe") for i in range(6)}

    def acquire(self, eng, i):
        cnt, sem = self.free[i]
        self.B["P"].wait(eng, self.B["s_" + sem], cnt)
        return self.B["PS"][i]

    def release(self, i, sem_name, cnt):
        self.free[i] = (cnt, sem_name)




def build_program(stages):
    nc = bass.Bass("TRN2", target_bir_lowering=False)
    es = ExitStack()
    P = Prog(nc, es)

    x_in = nc.dram_tensor("x_in", [N, D], F32, kind="ExternalInput").ap()
    x_out = nc.dram_tensor("x_out", [N, D], F32, kind="ExternalOutput").ap()
    iden_h = nc.dram_tensor("iden", [128, 128], F32, kind="ExternalInput").ap()

    W = {}

    def win(name, shape, dtype=F32):
        W[name] = nc.dram_tensor(name, shape, dtype, kind="ExternalInput").ap()

    if "s1" in stages:
        win("L0_attn_w_kqv", [3 * D, D])
        win("L0_attn_w_proj", [D, D])
        win("L0_attn_b_proj", [1, D])
    if "s2" in stages:
        win("L0_ffn_gate_w", [E, D])
        win("L0_ffn_w_gate", [F0, D])
        win("L0_ffn_w_up", [F0, D])
        win("L0_ffn_w_down", [D, F0])
        win("L0_ffn_A1", [E * R, D])
        win("L0_ffn_B1", [E * F0, R])
        win("L0_ffn_A3", [E * R, D])
        win("L0_ffn_B3", [E * F0, R])
        win("L0_ffn_A2", [E * R, F0])
        win("L0_ffn_B2", [E * D, R])
        W["pT_scr"] = nc.dram_tensor("pT_scr", [F0 // 128, NCH, 128, 512], BF16,
                                     kind="ExternalOutput" if DBG == "s2p"
                                     else "Internal").ap()
    if "s3" in stages:
        win("L1_attn_Wqkvu", [4 * D, D])
        win("L1_attn_bqkvu", [1, 4 * D])
        win("L1_attn_Wout", [D, D])
        win("L1_attn_bout", [1, D])
        win("L1_attn_theta", [H, DH])
        win("L1_attn_ln_w", [1, D])
        win("L1_attn_ln_b", [1, D])
    if "s4" in stages:
        win("L1_ffn_w_gate", [F1, D], I32)
        win("L1_ffn_w_up", [F1, D], I32)
        win("L1_ffn_w_down", [D, F1], I32)
        W["h_scr"] = nc.dram_tensor("h_scr", [F1 // 128, NCH, 128, 512], BF16,
                                    kind="Internal").ap()

    sbt = lambda name, shape, dt: es.enter_context(nc.sbuf_tensor(name, shape, dt))
    B = {}
    B["XBF"] = sbt("XBF", [128, T * N], BF16)
    B["BIGA"] = sbt("BIGA", [128, 32768], BF16)
    B["BIGB"] = sbt("BIGB", [128, 16384], BF16)
    B["SCR32"] = sbt("SCR32", [128, 5120], F32)
    B["IDENT"] = sbt("IDENT", [128, 128], BF16)
    B["IDENT32"] = sbt("IDENT32", [64, 64], F32)
    B["ONESC"] = sbt("ONESC", [128, 1], BF16)
    B["ONESR"] = sbt("ONESR", [1, 512], BF16)
    B["CONST"] = sbt("CONST", [128, 8], F32)
    B["STAT"] = sbt("STAT", [128, 64], F32)
    B["STG"] = sbt("STG", [128, 4096], BF16)
    B["MSK"] = sbt("MSK", [128, 2048], BF16)
    B["ROWS"] = sbt("ROWS", [1, 3584], BF16)
    B["KVS"] = sbt("KVS", [128, 128], BF16)
    B["KLS"] = sbt("KLS", [128, 256], BF16)
    B["THSB"] = sbt("THSB", [16, 64], F32)
    B["THT2"] = sbt("THT2", [128, 16], F32)
    B["BQN"] = sbt("BQN", [32, 128], F32)
    B["BQNB"] = sbt("BQNB", [1, 1024], BF16)
    B["BQT"] = sbt("BQT", [128, 32], F32)
    B["BIASC"] = sbt("BIASC", [128, 2], F32)
    B["INT2"] = sbt("INT2", [128, 512], I32)

    B["PS"] = [es.enter_context(nc.psum_tensor(f"PS{i}", [128, 512], F32))
               for i in range(6)]
    B["TP"] = [es.enter_context(nc.psum_tensor(f"TP{i}", [128, 1024], BF16))
               for i in range(2)]

    block = es.enter_context(nc.Block())
    for s in ("s_ld", "s_pe", "s_st", "s_sp"):
        B[s] = P.sem(s)
    B["s_se"] = P.chain["scalar"]
    B["s_ve"] = P.chain["vector"]
    B["s_pl"] = P.chain["gpsimd"]
    B.update(nc=nc, P=P, W=W, x_in=x_in, x_out=x_out, iden=iden_h)
    B["pp"] = PsumPool(B)

    emit_prologue(B)
    for st in stages:
        {"s1": emit_stage1, "s2": emit_stage2, "s3": emit_stage3,
         "s4": emit_stage4}[st](B)
        P.barrier(st)
    if DBG in ("biga", "s2m") and False:
        pass
    elif DBG == "bigb":
        s_st = B["s_st"]
        for j in range(16):
            P.dma("gpsimd",
                  B["x_out"].rearrange("(q p) d -> p q d", p=128)[:, j, :],
                  B["BIGB"][:, j * 1024:(j + 1) * 1024], inc=s_st)
        P.dma("gpsimd",
              B["x_out"].rearrange("(q p) d -> p q d", p=128)[:, 16, 0:64],
              B["STAT"][:, :], inc=s_st)
        P.wait("gpsimd", s_st, s_st.n)
    elif DBG in ("biga", "s2m"):
        s_st = B["s_st"]
        P.wait("gpsimd", B["s_se"], B["s_se"].n)
        P.wait("gpsimd", B["s_ve"], B["s_ve"].n)
        P.wait("gpsimd", B["s_pe"], B["s_pe"].n)
        for j in range(NT):
            P.dma("gpsimd",
                  B["x_out"].rearrange("(q p) d -> p q d", p=128)[:, j, :],
                  B["BIGA"][:, j * 1024:(j + 1) * 1024], inc=s_st)
        P.wait("gpsimd", s_st, s_st.n)
    else:
        emit_epilogue(B)

    P.build(block)
    return nc, es




def emit_prologue(B):
    P = B["P"]
    s_pe, s_se, s_ve = B["s_pe"], B["s_se"], B["s_ve"]
    STG, IDENT, TP = B["STG"], B["IDENT"], B["TP"]

    P.dma("gpsimd", IDENT[:, :], B["iden"], stream="wconst")
    P.dma("sync", B["IDENT32"][:, :], B["iden"][0:64, 0:64], stream="wc32")
    P.op("vector", lambda e: e.memset(B["ONESC"][:, :], 1.0))
    P.op("vector", lambda e: e.memset(B["ONESR"][:, :], 1.0))
    P.op("vector", lambda e: e.memset(B["CONST"][:, :], 0.0))
    P.op("vector", lambda e: e.memset(B["CONST"][:, 0:1], EPS), inc=s_ve)

    xr = B["x_in"].rearrange("(q p) d -> p q d", p=128)
    pe_after = {}
    evac_after = {}
    sx = P.dsem("xin")
    for g in range(NT // 4):
        if g > 0:
            P.wait("gpsimd", s_pe, pe_after[g - 1])
        P.dma("gpsimd", STG[:, :].rearrange("p (j d) -> p j d", j=4),
              xr[:, g * 4:(g + 1) * 4, :], stream="xin")
        P.wait("tensor", sx, sx.n)
        if g == 0:
            P.wait("tensor", P.dsem("wconst"), 16)
        for j in range(4):
            nt = g * 4 + j
            tp = TP[nt % 2]
            if nt >= 2:
                P.wait("tensor", s_se, evac_after[nt - 2])
            for t in range(T):
                P.op("tensor", lambda e, j=j, t=t, tp=tp: e.transpose(
                    tp[:, t * 128:(t + 1) * 128],
                    STG[:, j * 1024 + t * 128:j * 1024 + (t + 1) * 128],
                    IDENT[:, :]), inc=(s_pe if t == T - 1 else None))
            P.wait("scalar", s_pe, s_pe.n)
            P.op("scalar", lambda e, nt=nt, tp=tp: e.copy(
                B["XBF"][:, :].rearrange("p (t n) -> p t n", t=T)[
                    :, :, nt * 128:(nt + 1) * 128],
                tp[:, :].rearrange("p (t n) -> p t n", t=T)), inc=s_se)
            evac_after[nt] = s_se.n
        pe_after[g] = s_pe.n


def emit_epilogue(B):
    P = B["P"]
    s_pe, s_ve, s_st = B["s_pe"], B["s_ve"], B["s_st"]
    s_se = B["s_se"]
    TP = B["TP"]
    evac_after = {}
    st_after = {}
    ep_se = s_se.n
    ep_ve = B["s_ve"].n
    for nt in range(NT):
        tp = TP[nt % 2]
        if nt < 2:
            P.wait("tensor", s_se, ep_se)
            P.wait("tensor", s_ve, ep_ve)
        if nt >= 2:
            P.wait("tensor", s_ve, evac_after[nt - 2])
        for t in range(T):
            P.op("tensor", lambda e, nt=nt, t=t, tp=tp: e.transpose(
                tp[:, t * 128:(t + 1) * 128],
                B["XBF"][:, t * N + nt * 128: t * N + (nt + 1) * 128],
                B["IDENT"][:, :]), inc=(s_pe if t == T - 1 else None))
        P.wait("vector", s_pe, s_pe.n)
        half = (nt % 2) * 1024
        if nt >= 2:
            P.wait("vector", s_st, st_after[nt - 2])
        P.op("vector", lambda e, tp=tp, half=half: e.tensor_copy(
            B["SCR32"][:, half:half + 1024], tp[:, :]), inc=s_ve)
        evac_after[nt] = s_ve.n
        P.wait("sync", s_ve, s_ve.n)
        P.dma("sync", B["x_out"].rearrange("(q p) d -> q p d", p=128)[nt, :, :],
              B["SCR32"][:, half:half + 1024], inc=s_st)
        st_after[nt] = s_st.n
    P.wait("sync", s_st, s_st.n)




def emit_ln_chunk(B, ch, produce_td, sq_off=0, sq_buf=None, scale=1.0):
    P = B["P"]
    s_pe, s_se, s_ve = B["s_pe"], B["s_se"], B["s_ve"]
    STG, PS = B["STG"], B["PS"]
    if sq_buf is None:
        sq_buf = STG
    STG = sq_buf
    pp = B["pp"]

    def xsl(td):
        return B["XBF"][:, td * N + ch * 512: td * N + (ch + 1) * 512]

    for td in range(T):
        bank = td % 2
        ps = pp.acquire("tensor", bank)
        pe_done = produce_td(td, ps)
        P.wait("vector", s_pe, pe_done)
        P.op("vector", lambda e, ps=ps, td=td: e.scalar_tensor_tensor(
            xsl(td), ps[:, :], scale, xsl(td), op0=ALU.mult, op1=ALU.add),
            inc=s_ve)
        pp.release(bank, "ve", s_ve.n)
        P.op("vector", lambda e, td=td: e.tensor_mul(
            STG[:, sq_off + td * 512: sq_off + (td + 1) * 512], xsl(td), xsl(td)),
            inc=s_ve)
        sq_done = s_ve.n
        if td == 0:
            pp.acquire("tensor", 2)
            pp.acquire("tensor", 3)
        P.wait("tensor", s_ve, sq_done)
        P.op("tensor", lambda e, td=td: e.matmul(
            PS[2][0:1, :], B["ONESC"][:, :], xsl(td),
            start=(td == 0), stop=(td == T - 1)),
            inc=(s_pe if td == T - 1 else None))
        P.op("tensor", lambda e, td=td: e.matmul(
            PS[3][0:1, :], B["ONESC"][:, :],
            STG[:, sq_off + td * 512: sq_off + (td + 1) * 512],
            start=(td == 0), stop=(td == T - 1)),
            inc=(s_pe if td == T - 1 else None))
    stats_pe = s_pe.n
    SS = B["SCR32"]
    P.wait("vector", s_pe, stats_pe)
    P.op("vector", lambda e: e.tensor_scalar(SS[0:1, 4096:4608], PS[2][0:1, :],
                                             1.0 / D, None, op0=ALU.mult),
         inc=s_ve)
    P.wait("vector", s_ve, s_ve.n)
    P.op("vector", lambda e: e.tensor_mul(SS[0:1, 4608:5120], SS[0:1, 4096:4608],
                                          SS[0:1, 4096:4608]), inc=s_ve)
    P.wait("vector", s_ve, s_ve.n)
    P.op("vector", lambda e: e.scalar_tensor_tensor(
        SS[0:1, 4608:5120], PS[3][0:1, :], 1.0 / D, SS[0:1, 4608:5120],
        op0=ALU.mult, op1=ALU.subtract), inc=s_ve)
    pp.release(2, "ve", s_ve.n)
    pp.release(3, "ve", s_ve.n)
    P.wait("scalar", s_ve, s_ve.n)
    P.op("scalar", lambda e: e.activation(SS[0:1, 4608:5120], SS[0:1, 4608:5120],
                                          AF.Ln, bias=B["CONST"][0:1, 0:1]),
         inc=s_se)
    P.wait("scalar", s_se, s_se.n)
    P.op("scalar", lambda e: e.activation(SS[0:1, 4608:5120], SS[0:1, 4608:5120],
                                          AF.Exp, scale=-0.5), inc=s_se)
    P.wait("vector", s_se, s_se.n)
    P.op("vector", lambda e: e.scalar_tensor_tensor(
        SS[0:1, 4096:4608], SS[0:1, 4096:4608], -1.0, SS[0:1, 4608:5120],
        op0=ALU.mult, op1=ALU.mult), inc=s_ve)
    P.op("vector", lambda e: e.tensor_copy(B["ROWS"][0:1, 0:512],
                                           SS[0:1, 4608:5120]), inc=s_ve)
    P.wait("vector", s_ve, s_ve.n)
    P.op("vector", lambda e: e.tensor_copy(B["ROWS"][0:1, 512:1024],
                                           SS[0:1, 4096:4608]), inc=s_ve)
    pp.acquire("tensor", 4)
    pp.acquire("tensor", 5)
    P.wait("tensor", s_ve, s_ve.n)
    P.op("tensor", lambda e: e.matmul(PS[4][:, :], B["ONESR"][0:1, 0:128],
                                      B["ROWS"][0:1, 0:512], start=True, stop=True))
    P.op("tensor", lambda e: e.matmul(PS[5][:, :], B["ONESR"][0:1, 0:128],
                                      B["ROWS"][0:1, 512:1024], start=True,
                                      stop=True), inc=s_pe)
    P.wait("vector", s_pe, s_pe.n)
    for td in range(T):
        P.op("vector", lambda e, td=td: e.tensor_mul(
            STG[:, sq_off + td * 512: sq_off + (td + 1) * 512], xsl(td),
            PS[4][:, :]))
        P.op("vector", lambda e, td=td: e.tensor_add(
            xsl(td), STG[:, sq_off + td * 512: sq_off + (td + 1) * 512],
            PS[5][:, :]), inc=s_ve)
    pp.release(4, "ve", s_ve.n)
    pp.release(5, "ve", s_ve.n)
    return s_ve.n




def emit_stage1(B):
    P, W = B["P"], B["W"]
    s_pe, s_se, s_ve, s_pl = (B["s_pe"], B["s_se"], B["s_ve"], B["s_pl"])
    XBF, BIGA, BIGB, STG, PS, TP = (B["XBF"], B["BIGA"], B["BIGB"], B["STG"],
                                    B["PS"], B["TP"])
    IDENT, ONESC, ONESR, STAT, MSK = (B["IDENT"], B["ONESC"], B["ONESR"],
                                      B["STAT"], B["MSK"])
    pp = B["pp"]

    QT_O = 0
    KT_O = 4096
    AT_O = 8192
    VA_O = 9728
    WK_O = [11808, 13344]
    SQ_O = 14880
    WPT_O = 4096

    P.wait("vector", B["s_se"], B["s_se"].n)
    P.op("vector", lambda e: e.memset(BIGB[64:128, QT_O:QT_O + 4096], 0.0))
    P.op("vector", lambda e: e.memset(BIGB[64:128, KT_O:KT_O + 4096], 0.0))
    P.op("vector", lambda e: e.memset(BIGB[64:128, SQ_O:SQ_O + 512], 0.0))
    P.op("vector", lambda e: e.memset(
        BIGB[:, VA_O:VA_O + 2080].rearrange("p (k o) -> p k o", o=65)[:, :, 64:65],
        1.0), inc=s_ve)
    P.op("gpsimd", lambda e: e.iota(B["SCR32"][:, 0:512], pattern=[[1, 512]],
                                    base=0, channel_multiplier=0,
                                    allow_small_or_imprecise_dtypes=True))
    P.op("gpsimd", lambda e: e.iota(B["SCR32"][:, 512:513], pattern=[[0, 1]],
                                    base=0, channel_multiplier=1,
                                    allow_small_or_imprecise_dtypes=True), inc=s_pl)
    P.wait("vector", s_pl, s_pl.n)
    for r in range(4):
        P.op("vector", lambda e, r=r: e.tensor_scalar(
            B["SCR32"][:, 513:514], B["SCR32"][:, 512:513], float(128 * r), None,
            op0=ALU.add), inc=s_ve)
        P.wait("vector", s_ve, s_ve.n)
        P.op("vector", lambda e, r=r: e.tensor_scalar(
            MSK[:, r * 512:(r + 1) * 512], B["SCR32"][:, 0:512],
            B["SCR32"][:, 513:514], None, op0=ALU.is_ge), inc=s_ve)
    P.dma("gpsimd", B["ROWS"][0:1, 1024:2048], W["L0_attn_b_proj"],
          stream="wconst")
    s_bp = P.dsem("wconst")
    bp_ld = s_bp.n

    wkr = W["L0_attn_w_kqv"].rearrange("(h d c) k -> h d (c k)", h=H, d=DH, c=3)

    wk_lastread = [0, 0]
    qkv_lastread = 0
    at_lastread = [0, 0, 0]
    ctxn = BIGA
    init_ve = s_ve.n

    for h in range(H):
        if h == 0:
            P.wait("tensor", s_ve, init_ve)
        buf = h % 2
        wko = WK_O[buf]
        P.wait("gpsimd", s_pe, wk_lastread[buf])
        P.wait("gpsimd", s_se, s_se.n)
        P.dma("gpsimd", STG[0:64, 0:3072], wkr[h], stream="wstg")
        P.wait("tensor", P.dsem("wstg"), P.dsem("wstg").n)
        P.wait("tensor", s_se, s_se.n)
        for idx in range(24):
            t, c = divmod(idx, 3)
            tp = TP[0] if idx < 16 else TP[1]
            sl = idx if idx < 16 else idx - 16
            P.op("tensor", lambda e, t=t, c=c, tp=tp, sl=sl: e.transpose(
                tp[:, sl * 64:(sl + 1) * 64],
                STG[0:64, c * 1024 + t * 128: c * 1024 + (t + 1) * 128],
                IDENT[0:64, 0:64]), inc=(s_pe if idx in (15, 23) else None))
        P.wait("scalar", s_pe, s_pe.n)
        P.op("scalar", lambda e, wko=wko: e.copy(BIGB[:, wko:wko + 1024],
                                                 TP[0][:, :]))
        P.op("scalar", lambda e, wko=wko: e.copy(BIGB[:, wko + 1024:wko + 1536],
                                                 TP[1][:, 0:512]), inc=s_se)
        wk_ready = s_se.n

        P.op("vector", lambda e: e.memset(STAT[0:1, 0:2], -1.0e30), inc=s_ve)
        for ch in range(NCH):
            bank = ch % 2
            ps = pp.acquire("tensor", bank)
            P.wait("tensor", s_se, wk_ready)
            P.wait("scalar", s_pe, qkv_lastread)
            P.wait("vector", s_pe, qkv_lastread)
            for t in range(T):
                P.op("tensor", lambda e, t=t, ch=ch, ps=ps, wko=wko: e.matmul(
                    ps[:, :], BIGB[:, wko + t * 192: wko + t * 192 + 128],
                    XBF[:, t * N + ch * 512: t * N + (ch + 1) * 512],
                    start=(t == 0), stop=(t == T - 1)),
                    inc=(s_pe if t == T - 1 else None))
            mm_done = s_pe.n
            P.wait("vector", s_pe, mm_done)
            P.op("vector", lambda e, ch=ch, ps=ps: e.tensor_copy(
                BIGB[0:64, KT_O + ch * 512: KT_O + (ch + 1) * 512], ps[0:64, :]),
                inc=s_ve)
            k_ready = s_ve.n
            P.wait("scalar", s_pe, mm_done)
            P.op("scalar", lambda e, ch=ch, ps=ps: e.copy(
                BIGB[0:64, QT_O + ch * 512: QT_O + (ch + 1) * 512],
                ps[64:128, :]), inc=s_se)
            q_ready = s_se.n
            pp.release(bank, "se", s_se.n)
            for which, rdy, rsem, off in (("q", q_ready, s_se, QT_O),
                                          ("k", k_ready, s_ve, KT_O)):
                P.wait("gpsimd", rsem, rdy)
                P.wait("gpsimd", s_pe, s_pe.n)
                P.op("gpsimd", lambda e, off=off, ch=ch: e.tensor_tensor(
                    BIGB[0:64, SQ_O:SQ_O + 512],
                    BIGB[0:64, off + ch * 512: off + (ch + 1) * 512],
                    BIGB[0:64, off + ch * 512: off + (ch + 1) * 512],
                    op=ALU.mult), inc=s_pl)
                pp.acquire("tensor", 4)
                P.wait("tensor", s_pl, s_pl.n)
                P.op("tensor", lambda e: e.matmul(
                    PS[4][0:1, :], ONESC[:, :], BIGB[:, SQ_O:SQ_O + 512],
                    start=True, stop=True), inc=s_pe)
                P.wait("vector", s_pe, s_pe.n)
                P.op("vector", lambda e: e.reduce_max(
                    STAT[0:1, 2:3], PS[4][0:1, :], axis=AX.X), inc=s_ve)
                pp.release(4, "ve", s_ve.n)
                col = 0 if which == "q" else 1
                P.wait("vector", s_ve, s_ve.n)
                P.op("vector", lambda e, col=col: e.tensor_max(
                    STAT[0:1, col:col + 1], STAT[0:1, col:col + 1],
                    STAT[0:1, 2:3]), inc=s_ve)
        P.op("vector", lambda e: e.tensor_add(STAT[0:1, 3:4], STAT[0:1, 0:1],
                                              STAT[0:1, 1:2]), inc=s_ve)
        P.wait("vector", s_ve, s_ve.n)
        P.op("vector", lambda e: e.tensor_scalar(
            STAT[0:1, 3:4], STAT[0:1, 3:4], -1.0 / 16.0, None, op0=ALU.mult),
            inc=s_ve)
        P.wait("vector", s_ve, s_ve.n)
        P.op("vector", lambda e: e.tensor_copy(B["ROWS"][0:1, 2048:2049],
                                               STAT[0:1, 3:4]), inc=s_ve)
        pp.acquire("tensor", 4)
        P.wait("tensor", s_ve, s_ve.n)
        P.op("tensor", lambda e: e.matmul(PS[4][:, 0:1], ONESR[0:1, 0:128],
                                          B["ROWS"][0:1, 2048:2049],
                                          start=True, stop=True), inc=s_pe)
        P.wait("scalar", s_pe, s_pe.n)
        P.op("scalar", lambda e: e.copy(STAT[:, 5:6], PS[4][:, 0:1]), inc=s_se)
        pp.release(4, "se", s_se.n)
        cb_ready = s_se.n

        for kt in range(NT):
            psv = pp.acquire("tensor", 2)
            if kt == 0:
                P.wait("vector", s_pe, qkv_lastread)
            for t in range(T):
                P.op("tensor", lambda e, t=t, kt=kt, wko=wko, psv=psv: e.matmul(
                    psv[:, 0:64], XBF[:, t * N + kt * 128: t * N + (kt + 1) * 128],
                    BIGB[:, wko + t * 192 + 128: wko + t * 192 + 192],
                    start=(t == 0), stop=(t == T - 1)),
                    inc=(s_pe if t == T - 1 else None))
            P.wait("vector", s_pe, s_pe.n)
            P.op("vector", lambda e, kt=kt, psv=psv: e.tensor_copy(
                BIGB[:, VA_O + kt * 65: VA_O + kt * 65 + 64], psv[:, 0:64]),
                inc=s_ve)
            pp.release(2, "ve", s_ve.n)
        v_ready = s_ve.n

        P.wait("scalar", s_se, cb_ready)
        P.wait("tensor", s_ve, v_ready)
        P.wait("tensor", s_se, cb_ready)
        for qc in range(NCH):
            cps = [pp.acquire("tensor", 2 + qq) for qq in range(4)]
            nk = 4 * qc + 4
            for kt in range(nk):
                sbank = kt % 2
                ps_s = pp.acquire("tensor", sbank)
                atb = AT_O + (kt % 3) * 512
                P.op("tensor", lambda e, kt=kt, qc=qc, ps_s=ps_s: e.matmul(
                    ps_s[:, :], BIGB[:, KT_O + kt * 128: KT_O + (kt + 1) * 128],
                    BIGB[:, QT_O + qc * 512: QT_O + (qc + 1) * 512],
                    start=True, stop=True), inc=s_pe)
                st_done = s_pe.n
                P.wait("scalar", s_pe, st_done)
                P.wait("scalar", s_pe, at_lastread[kt % 3])
                P.op("scalar", lambda e, ps_s=ps_s, atb=atb: e.activation(
                    BIGB[:, atb:atb + 512], ps_s[:, :], AF.Exp, scale=0.125,
                    bias=STAT[:, 5:6]), inc=s_se)
                pp.release(sbank, "se", s_se.n)
                at_ready = (s_se.n, "se")
                if kt >= 4 * qc:
                    r = kt - 4 * qc
                    P.wait("vector", s_se, s_se.n)
                    P.op("vector", lambda e, atb=atb, r=r: e.tensor_mul(
                        BIGB[:, atb:atb + 512], BIGB[:, atb:atb + 512],
                        MSK[:, r * 512:(r + 1) * 512]), inc=s_ve)
                    at_ready = (s_ve.n, "ve")
                cntv, semv = at_ready
                P.wait("tensor", {"se": s_se, "ve": s_ve}[semv], cntv)
                for qq in range(4):
                    P.op("tensor", lambda e, qq=qq, kt=kt, atb=atb, cps=cps,
                         nk=nk: e.matmul(
                             cps[qq][:, 0:65],
                             BIGB[:, atb + qq * 128: atb + (qq + 1) * 128],
                             BIGB[:, VA_O + kt * 65: VA_O + (kt + 1) * 65],
                             start=(kt == 0), stop=(kt == nk - 1)),
                         inc=(s_pe if qq == 3 else None))
                at_lastread[kt % 3] = s_pe.n
            ctx_done = s_pe.n
            P.wait("vector", s_pe, ctx_done)
            for qq in range(4):
                j = qc * 4 + qq
                P.op("vector", lambda e, qq=qq, cps=cps: e.reciprocal(
                    STAT[:, 8:9], cps[qq][:, 64:65]), inc=s_ve)
                P.wait("vector", s_ve, s_ve.n)
                P.op("vector", lambda e, qq=qq, j=j, h=h, cps=cps:
                     e.tensor_scalar(
                         ctxn[:, j * 1024 + h * 64: j * 1024 + h * 64 + 64],
                         cps[qq][:, 0:64], STAT[:, 8:9], None,
                         op0=ALU.mult), inc=s_ve)
                pp.release(2 + qq, "ve", s_ve.n)
        qkv_lastread = s_pe.n
        wk_lastread[buf] = s_pe.n
    all_ctxn = s_ve.n

    wpr = W["L0_attn_w_proj"].rearrange("(q p) k -> q p k", p=128)
    first_wpt_evac = True
    for rt in range(T):
        P.wait("gpsimd", s_se, s_se.n)
        P.wait("gpsimd", s_pe, qkv_lastread)
        P.dma("gpsimd", STG[:, 0:1024], wpr[rt], stream="wstg")
        P.wait("tensor", P.dsem("wstg"), P.dsem("wstg").n)
        P.wait("tensor", s_se, s_se.n)
        for thd in range(T):
            P.op("tensor", lambda e, thd=thd: e.transpose(
                TP[0][:, thd * 128:(thd + 1) * 128],
                STG[:, thd * 128:(thd + 1) * 128], IDENT[:, :]),
                inc=(s_pe if thd == T - 1 else None))
        P.wait("scalar", s_pe, s_pe.n)
        if first_wpt_evac:
            P.wait("scalar", s_pe, qkv_lastread)
            first_wpt_evac = False
        P.op("scalar", lambda e, rt=rt: e.copy(
            BIGB[:, WPT_O:WPT_O + 8192].rearrange("p (thd c) -> p thd c", thd=T)[
                :, :, rt * 128:(rt + 1) * 128],
            TP[0][:, :].rearrange("p (thd c) -> p thd c", thd=T)), inc=s_se)
    wpt_ready = s_se.n

    for ch in range(NCH):
        P.wait("tensor", s_ve, all_ctxn)
        for thd in range(T):
            P.wait("tensor", s_se, s_se.n)
            for jj in range(4):
                j = ch * 4 + jj
                P.op("tensor", lambda e, j=j, jj=jj, thd=thd: e.transpose(
                    TP[0][:, jj * 128:(jj + 1) * 128],
                    ctxn[:, j * 1024 + thd * 128: j * 1024 + (thd + 1) * 128],
                    IDENT[:, :]), inc=(s_pe if jj == 3 else None))
            P.wait("scalar", s_pe, s_pe.n)
            if ch == 0 and thd == 0:
                P.wait("scalar", s_pe, qkv_lastread)
            else:
                P.wait("scalar", s_pe, s_pe.n)
            P.op("scalar", lambda e, thd=thd: e.copy(
                BIGB[:, QT_O + thd * 512: QT_O + (thd + 1) * 512],
                TP[0][:, 0:512]), inc=s_se)
        ctxT_ready = s_se.n

        def produce_td(td, ps, ctxT_ready=ctxT_ready):
            P.wait("tensor", s_se, ctxT_ready)
            P.wait("tensor", s_bp, bp_ld)
            for thd in range(T):
                P.op("tensor", lambda e, td=td, thd=thd, ps=ps: e.matmul(
                    ps[:, :], BIGB[:, WPT_O + thd * 1024 + td * 128:
                                   WPT_O + thd * 1024 + (td + 1) * 128],
                    BIGB[:, QT_O + thd * 512: QT_O + (thd + 1) * 512],
                    start=(thd == 0), stop=False))
            P.op("tensor", lambda e, td=td, ps=ps: e.matmul(
                ps[:, :], B["ROWS"][0:1, 1024 + td * 128:1024 + (td + 1) * 128],
                B["ONESR"][0:1, :], start=False, stop=True), inc=s_pe)
            return s_pe.n

        emit_ln_chunk(B, ch, produce_td)


def emit_stage2(B):
    P, W = B["P"], B["W"]
    s_pe, s_se, s_ve, s_pl = (B["s_pe"], B["s_se"], B["s_ve"], B["s_pl"])
    swstg = P.dsem("wstg")
    XBF, BIGA, BIGB, STG, PS, TP = (B["XBF"], B["BIGA"], B["BIGB"], B["STG"],
                                    B["PS"], B["TP"])
    IDENT, ONESC, ONESR, STAT, MSK, SCR = (B["IDENT"], B["ONESC"], B["ONESR"],
                                           B["STAT"], B["MSK"], B["SCR32"])
    pp = B["pp"]
    GW_O, MUB_O, EXP_O, ON8_O = 1024, 1088, 1600, 1728

    def sl(i, c0, c1):
        return BIGA[:, i * 4096 + c0: i * 4096 + c1]

    base_se, base_ve, base_pe = s_se.n, s_ve.n, s_pe.n
    P.wait("vector", s_se, base_se)
    P.op("vector", lambda e: e.memset(SCR[:, 0:4096], 0.0), inc=s_ve)

    P.op("gpsimd", lambda e: e.iota(SCR[0:8, 4608:4736], pattern=[[1, 128]],
                                    base=0, channel_multiplier=0,
                                    allow_small_or_imprecise_dtypes=True))
    P.op("gpsimd", lambda e: e.iota(SCR[0:8, 4736:4737], pattern=[[0, 1]],
                                    base=0, channel_multiplier=1,
                                    allow_small_or_imprecise_dtypes=True), inc=s_pl)
    P.wait("vector", s_pl, s_pl.n)
    P.op("vector", lambda e: e.tensor_scalar(
        SCR[0:8, 4737:4738], SCR[0:8, 4736:4737], 16.0, None, op0=ALU.mult),
        inc=s_ve)
    P.wait("vector", s_ve, s_ve.n)
    P.op("vector", lambda e: e.tensor_scalar(
        SCR[0:8, 4738:4739], SCR[0:8, 4737:4738], 16.0, None, op0=ALU.add),
        inc=s_ve)
    P.wait("vector", s_ve, s_ve.n)
    P.op("vector", lambda e: e.tensor_scalar(
        SCR[0:8, 4864:4992], SCR[0:8, 4608:4736], SCR[0:8, 4737:4738], None,
        op0=ALU.is_ge), inc=s_ve)
    P.wait("vector", s_ve, s_ve.n)
    P.op("vector", lambda e: e.tensor_scalar(
        SCR[0:8, 4992:5120], SCR[0:8, 4608:4736], SCR[0:8, 4738:4739], None,
        op0=ALU.is_ge), inc=s_ve)
    P.wait("vector", s_ve, s_ve.n)
    P.op("vector", lambda e: e.tensor_sub(
        SCR[0:8, 4864:4992], SCR[0:8, 4864:4992], SCR[0:8, 4992:5120]),
        inc=s_ve)
    P.wait("vector", s_ve, s_ve.n)
    P.op("vector", lambda e: e.tensor_scalar(
        MSK[0:8, EXP_O:EXP_O + 128], SCR[0:8, 4864:4992], 2.0, None,
        op0=ALU.mult), inc=s_ve)
    P.op("vector", lambda e: e.memset(MSK[0:8, ON8_O:ON8_O + 128], 1.0), inc=s_ve)

    P.wait("gpsimd", s_se, base_se)
    P.dma("gpsimd", STG[0:8, 0:1024], W["L0_ffn_gate_w"], stream="wstg")
    P.wait("tensor", swstg, swstg.n)
    P.wait("tensor", s_se, s_se.n)
    for t in range(T):
        P.op("tensor", lambda e, t=t: e.transpose(
            TP[0][:, t * 8:(t + 1) * 8], STG[0:8, t * 128:(t + 1) * 128],
            IDENT[0:8, 0:8]), inc=(s_pe if t == T - 1 else None))
    P.wait("scalar", s_pe, s_pe.n)
    P.op("scalar", lambda e: e.copy(MSK[:, GW_O:GW_O + 64], TP[0][:, 0:64]),
         inc=s_se)
    gw_ready = s_se.n

    P.wait("gpsimd", s_pe, s_pe.n)
    P.dma("gpsimd", STG[:, 0:1024], W["L0_ffn_A1"], stream="wstg")
    P.dma("gpsimd", STG[:, 2048:3072], W["L0_ffn_A3"], stream="wstg")
    P.wait("tensor", swstg, swstg.n)
    for t in range(T):
        P.op("tensor", lambda e, t=t: e.transpose(
            TP[1][:, t * 128:(t + 1) * 128],
            STG[:, t * 128:(t + 1) * 128], IDENT[:, :]),
            inc=(s_pe if t == T - 1 else None))
    P.wait("scalar", s_pe, s_pe.n)
    P.op("scalar", lambda e: e.copy(STG[:, 1024:2048], TP[1][:, :]), inc=s_se)
    P.wait("tensor", s_se, s_se.n)
    for t in range(T):
        P.op("tensor", lambda e, t=t: e.transpose(
            TP[1][:, t * 128:(t + 1) * 128],
            STG[:, 2048 + t * 128: 2048 + (t + 1) * 128], IDENT[:, :]),
            inc=(s_pe if t == T - 1 else None))
    P.wait("scalar", s_pe, s_pe.n)
    P.op("scalar", lambda e: e.copy(STG[:, 3072:4096], TP[1][:, :]), inc=s_se)
    at_ready = s_se.n

    for which, ao, slot in (("r1", 1024, 6), ("r3", 3072, 7)):
        for ch in range(NCH):
            ps = pp.acquire("tensor", ch % 2)
            P.wait("tensor", s_se, at_ready)
            for t in range(T):
                P.op("tensor", lambda e, t=t, ch=ch, ao=ao, ps=ps: e.matmul(
                    ps[:, :], STG[:, ao + t * 128: ao + (t + 1) * 128],
                    XBF[:, t * N + ch * 512: t * N + (ch + 1) * 512],
                    start=(t == 0), stop=(t == T - 1)),
                    inc=(s_pe if t == T - 1 else None))
            P.wait("scalar", s_pe, s_pe.n)
            P.op("scalar", lambda e, ch=ch, slot=slot, ps=ps: e.copy(
                sl(slot, ch * 512, (ch + 1) * 512), ps[:, :]), inc=s_se)
            pp.release(ch % 2, "se", s_se.n)
    r_ready = s_se.n

    P.wait("vector", s_se, r_ready)
    prev_tp = 0
    for nt in range(NT):
        psl = pp.acquire("tensor", 2)
        P.wait("tensor", s_se, gw_ready)
        P.wait("vector", s_pe, prev_tp)
        for t in range(T):
            P.op("tensor", lambda e, t=t, nt=nt, psl=psl: e.matmul(
                psl[:, 0:8], XBF[:, t * N + nt * 128: t * N + (nt + 1) * 128],
                MSK[:, GW_O + t * 8: GW_O + (t + 1) * 8],
                start=(t == 0), stop=(t == T - 1)),
                inc=(s_pe if t == T - 1 else None))
        lg_pe = s_pe.n
        P.wait("vector", s_pe, lg_pe)
        P.op("vector", lambda e, psl=psl: e.reduce_max(
            SCR[:, 4608:4609], psl[:, 0:8], axis=AX.X), inc=s_ve)
        P.wait("vector", s_ve, s_ve.n)
        P.op("vector", lambda e, psl=psl: e.tensor_scalar(
            STG[:, 0:8], psl[:, 0:8], SCR[:, 4608:4609], None, op0=ALU.is_ge),
            inc=s_ve)
        P.wait("vector", s_ve, s_ve.n)
        P.op("vector", lambda e, psl=psl: e.scalar_tensor_tensor(
            SCR[:, 4616:4624], STG[:, 0:8], -1.0e30, psl[:, 0:8],
            op0=ALU.mult, op1=ALU.add), inc=s_ve)
        pp.release(2, "ve", s_ve.n)
        P.wait("vector", s_ve, s_ve.n)
        P.op("vector", lambda e: e.reduce_max(
            SCR[:, 4609:4610], SCR[:, 4616:4624], axis=AX.X), inc=s_ve)
        P.wait("vector", s_ve, s_ve.n)
        P.op("vector", lambda e: e.tensor_scalar(
            STG[:, 8:16], SCR[:, 4616:4624], SCR[:, 4609:4610], None,
            op0=ALU.is_ge), inc=s_ve)
        P.op("vector", lambda e: e.tensor_sub(
            SCR[:, 4610:4611], SCR[:, 4608:4609], SCR[:, 4609:4610]), inc=s_ve)
        P.wait("scalar", s_ve, s_ve.n)
        P.op("scalar", lambda e: e.activation(SCR[:, 4611:4612],
                                              SCR[:, 4610:4611], AF.Sigmoid),
             inc=s_se)
        P.op("scalar", lambda e: e.activation(SCR[:, 4612:4613],
                                              SCR[:, 4610:4611], AF.Sigmoid,
                                              scale=-1.0), inc=s_se)
        P.wait("vector", s_se, s_se.n)
        P.op("vector", lambda e: e.tensor_scalar(
            STG[:, 16:24], STG[:, 0:8], SCR[:, 4611:4612], None, op0=ALU.mult),
            inc=s_ve)
        P.op("vector", lambda e: e.tensor_scalar(
            STG[:, 24:32], STG[:, 8:16], SCR[:, 4612:4613], None, op0=ALU.mult),
            inc=s_ve)
        sv_ready = s_ve.n
        P.wait("tensor", s_ve, sv_ready)
        P.wait("tensor", s_se, s_se.n)
        for j in range(4):
            P.op("tensor", lambda e, j=j: e.transpose(
                TP[0][0:8, j * 128:(j + 1) * 128], STG[:, j * 8:(j + 1) * 8],
                IDENT[:, :]), inc=(s_pe if j == 3 else None))
        prev_tp = s_pe.n
        P.wait("scalar", s_pe, s_pe.n)
        P.op("scalar", lambda e, nt=nt: e.copy(
            BIGB[0:8, :].rearrange("p (j c) -> p j c", j=4)[:, :, nt * 128:(nt + 1) * 128],
            TP[0][0:8, 0:512].rearrange("p (j c) -> p j c", j=4)), inc=s_se)
    sv_done = s_se.n

    for ch in range(NCH):
        ps1 = pp.acquire("tensor", 0)
        ps2 = pp.acquire("tensor", 1)
        P.wait("tensor", s_se, sv_done)
        P.op("tensor", lambda e, ch=ch, ps1=ps1: e.matmul(
            ps1[:, :], MSK[0:8, EXP_O:EXP_O + 128],
            BIGB[0:8, 0 * 4096 + ch * 512: 0 * 4096 + (ch + 1) * 512],
            start=True, stop=True), inc=s_pe)
        P.op("tensor", lambda e, ch=ch, ps2=ps2: e.matmul(
            ps2[:, :], MSK[0:8, EXP_O:EXP_O + 128],
            BIGB[0:8, 1 * 4096 + ch * 512: 1 * 4096 + (ch + 1) * 512],
            start=True, stop=True), inc=s_pe)
        mm_done = s_pe.n
        P.wait("vector", s_pe, mm_done)
        P.op("vector", lambda e, ch=ch, ps1=ps1: e.tensor_mul(
            sl(0, ch * 512, (ch + 1) * 512), sl(6, ch * 512, (ch + 1) * 512),
            ps1[:, :]))
        P.op("vector", lambda e, ch=ch, ps2=ps2: e.tensor_mul(
            sl(1, ch * 512, (ch + 1) * 512), sl(6, ch * 512, (ch + 1) * 512),
            ps2[:, :]))
        P.op("vector", lambda e, ch=ch, ps1=ps1: e.tensor_mul(
            sl(2, ch * 512, (ch + 1) * 512), sl(7, ch * 512, (ch + 1) * 512),
            ps1[:, :]))
        P.op("vector", lambda e, ch=ch, ps2=ps2: e.tensor_mul(
            sl(3, ch * 512, (ch + 1) * 512), sl(7, ch * 512, (ch + 1) * 512),
            ps2[:, :]), inc=s_ve)
        pp.release(0, "ve", s_ve.n)
        pp.release(1, "ve", s_ve.n)
    P.wait("vector", s_ve, s_ve.n)
    P.op("vector", lambda e: e.tensor_sub(sl(1, 0, 4096), sl(1, 0, 4096),
                                          sl(0, 0, 4096)))
    P.op("vector", lambda e: e.tensor_sub(sl(3, 0, 4096), sl(3, 0, 4096),
                                          sl(2, 0, 4096)), inc=s_ve)
    for ch in range(NCH):
        ps1 = pp.acquire("tensor", 0)
        ps2 = pp.acquire("tensor", 1)
        for slot, svj in ((4, 2), (5, 3)):
            ps = ps1 if slot == 4 else ps2
            P.op("tensor", lambda e, ch=ch, svj=svj, ps=ps: e.matmul(
                ps[:, :], MSK[0:8, ON8_O:ON8_O + 128],
                BIGB[0:8, svj * 4096 + ch * 512: svj * 4096 + (ch + 1) * 512],
                start=True, stop=True), inc=s_pe)
        P.wait("scalar", s_pe, s_pe.n)
        P.op("scalar", lambda e, ch=ch, ps1=ps1: e.copy(
            sl(4, ch * 512, (ch + 1) * 512), ps1[:, :]))
        P.op("scalar", lambda e, ch=ch, ps2=ps2: e.copy(
            sl(5, ch * 512, (ch + 1) * 512), ps2[:, :]), inc=s_se)
        pp.release(0, "se", s_se.n)
        pp.release(1, "se", s_se.n)
    rs_all_ve = s_ve.n
    for ch in range(NCH):
        ps1 = pp.acquire("tensor", 0)
        ps2 = pp.acquire("tensor", 1)
        P.op("tensor", lambda e, ch=ch, ps1=ps1: e.matmul(
            ps1[:, :], MSK[0:8, EXP_O:EXP_O + 128],
            BIGB[0:8, 0 * 4096 + ch * 512: 0 * 4096 + (ch + 1) * 512],
            start=True, stop=True), inc=s_pe)
        P.op("tensor", lambda e, ch=ch, ps2=ps2: e.matmul(
            ps2[:, :], MSK[0:8, EXP_O:EXP_O + 128],
            BIGB[0:8, 1 * 4096 + ch * 512: 1 * 4096 + (ch + 1) * 512],
            start=True, stop=True), inc=s_pe)
        P.wait("scalar", s_pe, s_pe.n)
        P.wait("scalar", s_ve, rs_all_ve)
        P.op("scalar", lambda e, ch=ch, ps1=ps1: e.copy(
            sl(6, ch * 512, (ch + 1) * 512), ps1[:, :]))
        P.op("scalar", lambda e, ch=ch, ps2=ps2: e.copy(
            sl(7, ch * 512, (ch + 1) * 512), ps2[:, :]), inc=s_se)
        pp.release(0, "se", s_se.n)
        pp.release(1, "se", s_se.n)
    masks_ready = s_se.n
    if DBG == "s2m":
        return

    wgr = W["L0_ffn_w_gate"].rearrange("(f p) k -> p f k", p=128)
    wur = W["L0_ffn_w_up"].rearrange("(f p) k -> p f k", p=128)
    b1g = W["L0_ffn_B1"].rearrange("(e grp g p) r -> p grp g e r", e=E,
                                   grp=NCH, p=128)
    b3g = W["L0_ffn_B3"].rearrange("(e grp g p) r -> p grp g e r", e=E,
                                   grp=NCH, p=128)
    a2r = W["L0_ffn_A2"]
    S1_O, S2_O, TM_O, T1_O, T2_O, P1_O, P2_O = (8192, 8704, 9216, 9728, 10240,
                                                10752, 14848)
    DS_O = 14336
    B1Q_O, B3Q_O, A2Q_O, NATB_O = 11264, 11776, 12288, 12800
    p_spill = {}
    pbuf_last = [0, 0]
    GF = 4
    for grp in range(NCH):
        P.wait("gpsimd", s_pe, s_pe.n)
        P.dma("gpsimd", STG[:, :].rearrange("p (f k) -> p f k", f=GF),
              wgr[:, grp * GF:(grp + 1) * GF, :], stream="wstg")
        P.wait("tensor", swstg, swstg.n)
        P.wait("tensor", s_pe, s_pe.n)
        P.wait("tensor", s_se, s_se.n)
        evg = {}
        for ftl in range(GF):
            tp = TP[ftl % 2]
            if ftl >= 2:
                P.wait("tensor", s_se, evg[ftl - 2])
            for t in range(T):
                P.op("tensor", lambda e, ftl=ftl, t=t, tp=tp: e.transpose(
                    tp[:, t * 128:(t + 1) * 128],
                    STG[:, ftl * 1024 + t * 128: ftl * 1024 + (t + 1) * 128],
                    IDENT[:, :]), inc=(s_pe if t == T - 1 else None))
            P.wait("scalar", s_pe, s_pe.n)
            P.op("scalar", lambda e, ftl=ftl, tp=tp: e.copy(
                BIGB[:, ftl * 1024:(ftl + 1) * 1024], tp[:, :]), inc=s_se)
            evg[ftl] = s_se.n
        P.wait("gpsimd", s_pe, s_pe.n)
        P.dma("gpsimd", STG[:, :].rearrange("p (f k) -> p f k", f=GF),
              wur[:, grp * GF:(grp + 1) * GF, :], stream="wstg")
        P.wait("tensor", swstg, swstg.n)
        P.wait("tensor", s_se, s_se.n)
        evg = {}
        for ftl in range(GF):
            tp = TP[ftl % 2]
            if ftl >= 2:
                P.wait("tensor", s_se, evg[ftl - 2])
            for t in range(T):
                P.op("tensor", lambda e, ftl=ftl, t=t, tp=tp: e.transpose(
                    tp[:, t * 128:(t + 1) * 128],
                    STG[:, ftl * 1024 + t * 128: ftl * 1024 + (t + 1) * 128],
                    IDENT[:, :]), inc=(s_pe if t == T - 1 else None))
            P.wait("scalar", s_pe, s_pe.n)
            P.op("scalar", lambda e, ftl=ftl, tp=tp: e.copy(
                BIGB[:, 4096 + ftl * 1024: 4096 + (ftl + 1) * 1024], tp[:, :]),
                inc=s_se)
            evg[ftl] = s_se.n
        for ftl in range(GF):
            P.dma("gpsimd", BIGB[:, NATB_O + ftl * 128: NATB_O + (ftl + 1) * 128]
                  .rearrange("p (e r) -> p e r", e=E),
                  b1g[:, grp, ftl, :, :], stream="b1s%d" % ftl)
            P.dma("gpsimd", BIGB[:, NATB_O + 512 + ftl * 128: NATB_O + 512 + (ftl + 1) * 128]
                  .rearrange("p (e r) -> p e r", e=E),
                  b3g[:, grp, ftl, :, :], stream="b3s%d" % ftl)
        P.dma("gpsimd", BIGB[:, NATB_O + 1024: NATB_O + 1536],
              a2r[:, grp * 512:(grp + 1) * 512], stream="a2s")
        for ftl in range(GF):
            P.wait("tensor", P.dsem("b1s%d" % ftl), P.dsem("b1s%d" % ftl).n)
            P.wait("tensor", P.dsem("b3s%d" % ftl), P.dsem("b3s%d" % ftl).n)
        P.wait("tensor", P.dsem("a2s"), P.dsem("a2s").n)
        P.wait("tensor", s_se, s_se.n)
        for ftl in range(GF):
            P.op("tensor", lambda e, ftl=ftl: e.transpose(
                TP[0][:, ftl * 128:(ftl + 1) * 128],
                BIGB[:, NATB_O + ftl * 128: NATB_O + (ftl + 1) * 128],
                IDENT[:, :]))
            P.op("tensor", lambda e, ftl=ftl: e.transpose(
                TP[1][:, ftl * 128:(ftl + 1) * 128],
                BIGB[:, NATB_O + 512 + ftl * 128: NATB_O + 512 + (ftl + 1) * 128],
                IDENT[:, :]), inc=(s_pe if ftl == GF - 1 else None))
        P.wait("scalar", s_pe, s_pe.n)
        P.op("scalar", lambda e: e.copy(BIGB[:, B1Q_O:B1Q_O + 512],
                                        TP[0][:, 0:512]))
        P.op("scalar", lambda e: e.copy(BIGB[:, B3Q_O:B3Q_O + 512],
                                        TP[1][:, 0:512]), inc=s_se)
        P.wait("tensor", s_se, s_se.n)
        for ftl in range(GF):
            P.op("tensor", lambda e, ftl=ftl: e.transpose(
                TP[0][:, ftl * 128:(ftl + 1) * 128],
                BIGB[:, NATB_O + 1024 + ftl * 128: NATB_O + 1024 + (ftl + 1) * 128],
                IDENT[:, :]), inc=(s_pe if ftl == GF - 1 else None))
        P.wait("scalar", s_pe, s_pe.n)
        P.op("scalar", lambda e: e.copy(BIGB[:, A2Q_O:A2Q_O + 512],
                                        TP[0][:, 0:512]), inc=s_se)
        grp_w_ready = s_se.n

        for ch in range(NCH):
            psu1 = pp.acquire("tensor", 4)
            psu2 = pp.acquire("tensor", 5)
            for ftl in range(GF):
                psa = pp.acquire("tensor", ftl % 2)
                psb = pp.acquire("tensor", 2 + (ftl % 2))
                psd = pp.acquire("tensor", 3 - (ftl % 2))
                P.wait("tensor", s_se, grp_w_ready)
                if grp == 0 and ch == 0 and ftl == 0:
                    P.wait("tensor", s_se, masks_ready)
                    P.wait("tensor", s_ve, s_ve.n)
                for t in range(T):
                    P.op("tensor", lambda e, t=t, ftl=ftl, ch=ch, psa=psa: e.matmul(
                        psa[:, :], BIGB[:, ftl * 1024 + t * 128: ftl * 1024 + (t + 1) * 128],
                        XBF[:, t * N + ch * 512: t * N + (ch + 1) * 512],
                        start=(t == 0), stop=False))
                P.op("tensor", lambda e, ftl=ftl, ch=ch, psa=psa: e.matmul(
                    psa[:, :], BIGB[:, B1Q_O + ftl * 128: B1Q_O + (ftl + 1) * 128],
                    sl(0, ch * 512, (ch + 1) * 512), start=False, stop=True),
                    inc=s_pe)
                a1_pe = s_pe.n
                P.op("tensor", lambda e, ftl=ftl, ch=ch, psd=psd: e.matmul(
                    psd[:, :], BIGB[:, B1Q_O + ftl * 128: B1Q_O + (ftl + 1) * 128],
                    sl(1, ch * 512, (ch + 1) * 512), start=True, stop=True),
                    inc=s_pe)
                d1_pe = s_pe.n
                for t in range(T):
                    P.op("tensor", lambda e, t=t, ftl=ftl, ch=ch, psb=psb: e.matmul(
                        psb[:, :], BIGB[:, 4096 + ftl * 1024 + t * 128: 4096 + ftl * 1024 + (t + 1) * 128],
                        XBF[:, t * N + ch * 512: t * N + (ch + 1) * 512],
                        start=(t == 0), stop=False))
                P.op("tensor", lambda e, ftl=ftl, ch=ch, psb=psb: e.matmul(
                    psb[:, :], BIGB[:, B3Q_O + ftl * 128: B3Q_O + (ftl + 1) * 128],
                    sl(2, ch * 512, (ch + 1) * 512), start=False, stop=True),
                    inc=s_pe)
                b1_pe = s_pe.n
                P.wait("scalar", s_pe, a1_pe)
                P.op("scalar", lambda e, psa=psa: e.activation(
                    BIGB[:, S1_O:S1_O + 512], psa[:, :], AF.Silu), inc=s_se)
                s1_se = s_se.n
                P.wait("scalar", s_pe, d1_pe)
                P.op("scalar", lambda e, psd=psd: e.copy(
                    BIGB[:, DS_O:DS_O + 512], psd[:, :]), inc=s_se)
                ds1_se = s_se.n
                P.wait("vector", s_se, ds1_se)
                P.op("vector", lambda e, psa=psa: e.tensor_add(
                    SCR[:, 4096:4608], psa[:, :], BIGB[:, DS_O:DS_O + 512]),
                    inc=s_ve)
                a2_ve = s_ve.n
                pp.release(ftl % 2, "ve", a2_ve)
                P.wait("scalar", s_ve, a2_ve)
                P.op("scalar", lambda e: e.activation(
                    BIGB[:, S2_O:S2_O + 512], SCR[:, 4096:4608], AF.Silu),
                    inc=s_se)
                s2_se = s_se.n
                P.wait("tensor", s_se, ds1_se)
                P.op("tensor", lambda e, ftl=ftl, ch=ch, psd=psd: e.matmul(
                    psd[:, :], BIGB[:, B3Q_O + ftl * 128: B3Q_O + (ftl + 1) * 128],
                    sl(3, ch * 512, (ch + 1) * 512), start=True, stop=True),
                    inc=s_pe)
                d3_pe = s_pe.n
                P.wait("scalar", s_pe, d3_pe)
                P.wait("scalar", s_ve, a2_ve)
                P.op("scalar", lambda e, psd=psd: e.copy(
                    BIGB[:, DS_O:DS_O + 512], psd[:, :]), inc=s_se)
                ds3_se = s_se.n
                pp.release(3 - (ftl % 2), "se", ds3_se)
                P.wait("vector", s_pe, b1_pe)
                P.wait("vector", s_se, ds3_se)
                P.op("vector", lambda e, psb=psb: e.tensor_add(
                    SCR[:, 4608:5120], psb[:, :], BIGB[:, DS_O:DS_O + 512]),
                    inc=s_ve)
                P.wait("vector", s_se, s1_se)
                P.op("vector", lambda e, psb=psb: e.tensor_mul(
                    BIGB[:, TM_O:TM_O + 512], BIGB[:, S1_O:S1_O + 512],
                    psb[:, :]), inc=s_ve)
                pp.release(2 + (ftl % 2), "ve", s_ve.n)
                P.op("vector", lambda e, ch=ch: e.tensor_mul(
                    BIGB[:, T1_O:T1_O + 512], BIGB[:, TM_O:TM_O + 512],
                    sl(4, ch * 512, (ch + 1) * 512)), inc=s_ve)
                t1_ve = s_ve.n
                P.wait("vector", s_se, s2_se)
                P.op("vector", lambda e: e.tensor_mul(
                    BIGB[:, TM_O:TM_O + 512], BIGB[:, S2_O:S2_O + 512],
                    SCR[:, 4608:5120]), inc=s_ve)
                P.op("vector", lambda e, ch=ch: e.tensor_mul(
                    BIGB[:, T2_O:T2_O + 512], BIGB[:, TM_O:TM_O + 512],
                    sl(5, ch * 512, (ch + 1) * 512)), inc=s_ve)
                t2_ve = s_ve.n
                ft = grp * GF + ftl
                pb = P1_O if (ft % 2 == 0) else P2_O
                ssp = P.dsem("sp%d" % (ft % 2))
                P.wait("vector", ssp, pbuf_last[ft % 2])
                P.op("vector", lambda e, pb=pb: e.tensor_add(
                    BIGB[:, pb:pb + 512], BIGB[:, T1_O:T1_O + 512],
                    BIGB[:, T2_O:T2_O + 512]), inc=s_ve)
                p_ve = s_ve.n
                P.wait("tensor", s_ve, t2_ve)
                P.op("tensor", lambda e, ftl=ftl, psu1=psu1: e.matmul(
                    psu1[:, :], BIGB[:, A2Q_O + ftl * 128: A2Q_O + (ftl + 1) * 128],
                    BIGB[:, T1_O:T1_O + 512], start=(ftl == 0), stop=(ftl == GF - 1)))
                P.op("tensor", lambda e, ftl=ftl, psu2=psu2: e.matmul(
                    psu2[:, :], BIGB[:, A2Q_O + ftl * 128: A2Q_O + (ftl + 1) * 128],
                    BIGB[:, T2_O:T2_O + 512], start=(ftl == 0), stop=(ftl == GF - 1)),
                    inc=s_pe)
                P.wait("sync", s_ve, p_ve)
                P.dma("sync", W["pT_scr"][ft, ch, :, :], BIGB[:, pb:pb + 512],
                      stream="sp%d" % (ft % 2))
                pbuf_last[ft % 2] = ssp.n
            u_pe = s_pe.n
            P.wait("vector", s_pe, u_pe)
            P.op("vector", lambda e, ch=ch, psu1=psu1: e.tensor_mul(
                SCR[:, 4608:5120], psu1[:, :], sl(6, ch * 512, (ch + 1) * 512)),
                inc=s_ve)
            P.op("vector", lambda e, ch=ch: e.tensor_add(
                SCR[:, ch * 512:(ch + 1) * 512], SCR[:, ch * 512:(ch + 1) * 512],
                SCR[:, 4608:5120]), inc=s_ve)
            P.op("vector", lambda e, ch=ch, psu2=psu2: e.tensor_mul(
                SCR[:, 4608:5120], psu2[:, :], sl(7, ch * 512, (ch + 1) * 512)),
                inc=s_ve)
            P.op("vector", lambda e, ch=ch: e.tensor_add(
                SCR[:, ch * 512:(ch + 1) * 512], SCR[:, ch * 512:(ch + 1) * 512],
                SCR[:, 4608:5120]), inc=s_ve)
            pp.release(4, "ve", s_ve.n)
            pp.release(5, "ve", s_ve.n)
    sweep_pe = s_pe.n

    b2rr = W["L0_ffn_B2"].rearrange("(e q p) r -> p q e r", e=E, p=128)
    P.wait("gpsimd", s_pe, sweep_pe)
    P.wait("gpsimd", s_ve, s_ve.n)
    for td in range(T):
        P.dma("gpsimd", STG[:, td * 128:(td + 1) * 128]
              .rearrange("p (e r) -> p e r", e=E), b2rr[:, td, :, :],
              stream="wstg")
    P.wait("tensor", swstg, swstg.n)
    P.wait("tensor", s_se, s_se.n)
    for td in range(T):
        P.op("tensor", lambda e, td=td: e.transpose(
            TP[0][:, td * 128:(td + 1) * 128], STG[:, td * 128:(td + 1) * 128],
            IDENT[:, :]), inc=(s_pe if td == T - 1 else None))
    P.wait("scalar", s_pe, s_pe.n)
    P.op("scalar", lambda e: e.copy(MSK[:, 0:1024], TP[0][:, :]), inc=s_se)
    wdr = W["L0_ffn_w_down"].rearrange("(q p) k -> p q k", p=128)
    for td in range(T):
        P.wait("gpsimd", s_pe, s_pe.n)
        P.wait("gpsimd", s_se, s_se.n)
        P.dma("gpsimd", STG[:, :], wdr[:, td, :], stream="wstg")
        P.wait("tensor", swstg, swstg.n)
        P.wait("tensor", s_se, s_se.n)
        evg = {}
        for fg in range(4):
            tp = TP[fg % 2]
            if fg >= 2:
                P.wait("tensor", s_se, evg[fg - 2])
            for i in range(8):
                P.op("tensor", lambda e, fg=fg, i=i, tp=tp: e.transpose(
                    tp[:, i * 128:(i + 1) * 128],
                    STG[:, (fg * 8 + i) * 128:(fg * 8 + i + 1) * 128],
                    IDENT[:, :]), inc=(s_pe if i == 7 else None))
            P.wait("scalar", s_pe, s_pe.n)
            P.op("scalar", lambda e, td=td, fg=fg, tp=tp: e.copy(
                BIGA[:, :].rearrange("p (f c) -> p f c", f=32)[
                    :, fg * 8:(fg + 1) * 8, td * 128:(td + 1) * 128],
                tp[:, :].rearrange("p (i c) -> p i c", i=8)), inc=s_se)
            evg[fg] = s_se.n
    wdt_ready = s_se.n

    for ch in range(NCH):
        P.wait("sync", P.dsem("sp0"), P.dsem("sp0").n)
        P.wait("sync", P.dsem("sp1"), P.dsem("sp1").n)
        P.wait("sync", s_pe, s_pe.n)
        P.dma("sync", BIGB[:, 0:16384].rearrange("p (f n) -> p f n", f=32),
              W["pT_scr"][:, ch, :, :].rearrange("f p n -> p f n"),
              stream="pchunk")
        p_ld = P.dsem("pchunk").n
        P.wait("vector", s_ve, s_ve.n)
        P.op("vector", lambda e, ch=ch: e.tensor_copy(
            MSK[:, MUB_O:MUB_O + 512], SCR[:, ch * 512:(ch + 1) * 512]), inc=s_ve)
        mub_ve = s_ve.n

        def produce_td(td, ps, p_ld=p_ld, mub_ve=mub_ve):
            P.wait("tensor", P.dsem("pchunk"), p_ld)
            P.wait("tensor", s_se, wdt_ready)
            P.wait("tensor", s_ve, mub_ve)
            for ft in range(32):
                P.op("tensor", lambda e, td=td, ft=ft, ps=ps: e.matmul(
                    ps[:, :], BIGA[:, ft * 1024 + td * 128: ft * 1024 + (td + 1) * 128],
                    BIGB[:, ft * 512:(ft + 1) * 512],
                    start=(ft == 0), stop=False))
            P.op("tensor", lambda e, td=td, ps=ps: e.matmul(
                ps[:, :], MSK[:, td * 128:(td + 1) * 128],
                MSK[:, MUB_O:MUB_O + 512], start=False, stop=True), inc=s_pe)
            return s_pe.n

        emit_ln_chunk(B, ch, produce_td, sq_off=0)


def emit_stage3(B):
    P, W = B["P"], B["W"]
    s_pe, s_se, s_ve, s_pl = B["s_pe"], B["s_se"], B["s_ve"], B["s_pl"]
    XBF, BIGA, BIGB, STG, PS, TP = (B["XBF"], B["BIGA"], B["BIGB"], B["STG"],
                                    B["PS"], B["TP"])
    IDENT, ONESC, ONESR, STAT, MSK, SCR = (B["IDENT"], B["ONESC"], B["ONESR"],
                                           B["STAT"], B["MSK"], B["SCR32"])
    pp = B["pp"]
    KVS = B["KVS"]
    V_O, CS_O, QL_O = 0, 4096, [8192, 12288]
    INV2PI = 1.0 / (2.0 * math.pi)
    TWOPI = 2.0 * math.pi

    P.dma("gpsimd", B["THSB"][:, :], W["L1_attn_theta"], stream="wconst")
    P.dma("gpsimd", B["BQN"][0:32, 0:128],
          W["L1_attn_bqkvu"].rearrange("o (q p) -> (o q) p", p=128),
          stream="wconst")
    P.dma("gpsimd", B["BQNB"][0:1, :], W["L1_attn_bqkvu"][0:1, 2048:3072],
          stream="wconst")
    P.op("gpsimd", lambda e: e.iota(SCR[0:64, 0:4096], pattern=[[1, 4096]],
                                    base=0, channel_multiplier=0,
                                    allow_small_or_imprecise_dtypes=True),
         inc=s_pl)
    P.op("gpsimd", lambda e: e.iota(SCR[64:128, 0:4096], pattern=[[1, 4096]],
                                    base=0, channel_multiplier=0,
                                    allow_small_or_imprecise_dtypes=True),
         inc=s_pl)
    swc = P.dsem("wconst")
    th_ld = swc.n
    P.wait("tensor", swc, th_ld)
    P.wait("tensor", P.dsem("wc32"), P.dsem("wc32").n)
    P.wait("tensor", s_se, s_se.n)
    ps_a = pp.acquire("tensor", 2)
    ps_b = pp.acquire("tensor", 3)
    P.op("tensor", lambda e, ps_a=ps_a: e.transpose(
        ps_a[0:64, 0:16], B["THSB"][:, :], B["IDENT32"][0:16, 0:16]))
    P.op("tensor", lambda e, ps_b=ps_b: e.transpose(
        ps_b[:, 0:32], B["BQN"][0:32, :], B["IDENT32"][0:32, 0:32]), inc=s_pe)
    P.wait("vector", s_pe, s_pe.n)
    P.op("vector", lambda e, ps_a=ps_a: e.tensor_scalar(
        B["THT2"][0:64, 0:16], ps_a[0:64, 0:16], INV2PI, None, op0=ALU.mult),
        inc=s_ve)
    tht_ve = s_ve.n
    pp.release(2, "ve", s_ve.n)
    P.wait("scalar", s_pe, s_pe.n)
    P.op("scalar", lambda e, ps_b=ps_b: e.copy(B["BQT"][:, 0:32],
                                               ps_b[:, 0:32]), inc=s_se)
    bqt_se = s_se.n
    pp.release(3, "se", s_se.n)
    P.wait("gpsimd", s_ve, tht_ve)
    P.dma("gpsimd", B["THT2"][64:128, 0:16], B["THT2"][0:64, 0:16],
          stream="thtd")
    tht_d = P.dsem("thtd").n

    wq = W["L1_attn_Wqkvu"]
    pair_lastread = [0, 0]
    lrpe_done = [0, 0]

    for hp in range(H // 2):
        P.wait("gpsimd", s_pe, s_pe.n)
        P.wait("gpsimd", s_se, s_se.n)
        P.dma("gpsimd", STG[:, 0:1024],
              wq[2048 + hp * 128: 2048 + (hp + 1) * 128, :], stream="wstg")
        P.wait("tensor", P.dsem("wstg"), P.dsem("wstg").n)
        P.wait("tensor", s_se, s_se.n)
        for t in range(T):
            P.op("tensor", lambda e, t=t: e.transpose(
                TP[0][:, t * 128:(t + 1) * 128], STG[:, t * 128:(t + 1) * 128],
                IDENT[:, :]), inc=(s_pe if t == T - 1 else None))
        P.wait("scalar", s_pe, s_pe.n)
        P.op("scalar", lambda e: e.copy(MSK[:, 1024:2048], TP[0][:, :]),
             inc=s_se)
        wvt_se = s_se.n
        for nt in range(NT):
            psv = pp.acquire("tensor", 2 + (nt % 2))
            P.wait("tensor", s_se, wvt_se)
            P.wait("tensor", swc, th_ld)
            if hp > 0 and nt == 0:
                P.wait("tensor", s_pe, pair_lastread[0])
                P.wait("tensor", s_ve, s_ve.n)
            for t in range(T):
                P.op("tensor", lambda e, t=t, nt=nt, psv=psv: e.matmul(
                    psv[:, 0:128],
                    XBF[:, t * N + nt * 128: t * N + (nt + 1) * 128],
                    MSK[:, 1024 + t * 128: 1024 + (t + 1) * 128],
                    start=(t == 0), stop=False))
            P.op("tensor", lambda e, hp=hp, psv=psv: e.matmul(
                psv[:, 0:128], ONESR[0:1, 0:128],
                B["BQNB"][0:1, hp * 128:(hp + 1) * 128],
                start=False, stop=True), inc=s_pe)
            P.wait("vector", s_pe, s_pe.n)
            P.op("vector", lambda e, nt=nt, psv=psv: e.tensor_copy(
                BIGB[:, V_O + nt * 128: V_O + (nt + 1) * 128], psv[:, 0:128]),
                inc=s_ve)
            pp.release(2 + (nt % 2), "ve", s_ve.n)
        v_ready = s_ve.n

        for hh in range(2):
            h = hp * 2 + hh
            qlo = QL_O[hh]
            P.wait("vector", s_ve, tht_ve)
            P.wait("vector", P.dsem("thtd"), tht_d)
            P.wait("vector", s_ve, lrpe_done[0])
            P.wait("vector", s_ve, lrpe_done[1])
            P.wait("vector", s_pl, s_pl.n)
            for cc in range(8):
                P.wait("vector", s_se, s_se.n)
                P.op("vector", lambda e, cc=cc, h=h: e.tensor_scalar(
                    SCR[:, 4096:4608], SCR[:, cc * 512:(cc + 1) * 512],
                    B["THT2"][:, h:h + 1], None, op0=ALU.mult), inc=s_ve)
                P.op("vector", lambda e: e.tensor_scalar(
                    SCR[0:64, 4096:4608], SCR[0:64, 4096:4608], 0.25, None,
                    op0=ALU.add), inc=s_ve)
                P.op("vector", lambda e: e.tensor_copy(B["INT2"][:, :],
                                                       SCR[:, 4096:4608]),
                     inc=s_ve)
                P.op("vector", lambda e: e.tensor_copy(SCR[:, 4608:5120],
                                                       B["INT2"][:, :]),
                     inc=s_ve)
                P.op("vector", lambda e: e.tensor_sub(
                    SCR[:, 4096:4608], SCR[:, 4096:4608], SCR[:, 4608:5120]),
                    inc=s_ve)
                P.op("vector", lambda e: e.tensor_scalar(
                    SCR[:, 4608:5120], SCR[:, 4096:4608], 0.5, None,
                    op0=ALU.is_ge), inc=s_ve)
                P.op("vector", lambda e: e.scalar_tensor_tensor(
                    SCR[:, 4096:4608], SCR[:, 4608:5120], -1.0,
                    SCR[:, 4096:4608], op0=ALU.mult, op1=ALU.add), inc=s_ve)
                P.wait("scalar", s_ve, s_ve.n)
                P.wait("scalar", s_ve, lrpe_done[hh])
                P.op("scalar", lambda e, cc=cc: e.activation(
                    BIGB[:, CS_O + cc * 512: CS_O + (cc + 1) * 512],
                    SCR[:, 4096:4608], AF.Sin, scale=TWOPI), inc=s_se)
            cs_se = s_se.n

            P.wait("gpsimd", s_pe, s_pe.n)
            P.wait("gpsimd", s_se, s_se.n)
            P.dma("gpsimd", STG[0:64, 0:1024], wq[h * 64:(h + 1) * 64, :],
                  stream="wstg")
            P.dma("gpsimd", STG[0:64, 1024:2048],
                  wq[1024 + h * 64: 1024 + (h + 1) * 64, :], stream="wstg")
            P.wait("tensor", P.dsem("wstg"), P.dsem("wstg").n)
            P.wait("tensor", s_se, s_se.n)
            for t in range(T):
                P.op("tensor", lambda e, t=t: e.transpose(
                    TP[0][:, t * 64:(t + 1) * 64],
                    STG[0:64, t * 128:(t + 1) * 128], IDENT[0:64, 0:64]))
                P.op("tensor", lambda e, t=t: e.transpose(
                    TP[1][:, t * 64:(t + 1) * 64],
                    STG[0:64, 1024 + t * 128: 1024 + (t + 1) * 128],
                    IDENT[0:64, 0:64]), inc=(s_pe if t == T - 1 else None))
            P.wait("scalar", s_pe, s_pe.n)
            for half in range(2):
                P.op("scalar", lambda e, half=half: e.copy(
                    MSK[:, 0:1024].rearrange("p (t c) -> p t c", t=T)[
                        :, :, half * 64:(half + 1) * 64],
                    TP[0][:, 0:512].rearrange("p (t c) -> p t c", t=T)))
                P.op("scalar", lambda e, half=half: e.copy(
                    MSK[:, 1024:2048].rearrange("p (t c) -> p t c", t=T)[
                        :, :, half * 64:(half + 1) * 64],
                    TP[1][:, 0:512].rearrange("p (t c) -> p t c", t=T)),
                    inc=(s_se if half == 1 else None))
            wqk_se = s_se.n
            P.wait("vector", s_se, bqt_se)
            for half in range(2):
                P.op("vector", lambda e, h=h, half=half: e.tensor_copy(
                    B["BIASC"][half * 64:(half + 1) * 64, 0:1],
                    B["BQT"][(h % 2) * 64:(h % 2) * 64 + 64,
                             h // 2: h // 2 + 1]), inc=s_ve)
                P.op("vector", lambda e, h=h, half=half: e.tensor_copy(
                    B["BIASC"][half * 64:(half + 1) * 64, 1:2],
                    B["BQT"][(h % 2) * 64:(h % 2) * 64 + 64,
                             8 + h // 2: 8 + h // 2 + 1]), inc=s_ve)
            bias_ve = s_ve.n

            for ch in range(NCH):
                psq = pp.acquire("tensor", 2 + (ch % 2))
                psk = pp.acquire("tensor", 4 + (ch % 2))
                P.wait("tensor", s_se, wqk_se)
                if ch == 0:
                    P.wait("tensor", s_pe, pair_lastread[hh])
                for t in range(T):
                    P.op("tensor", lambda e, t=t, ch=ch, psq=psq: e.matmul(
                        psq[:, :], MSK[:, t * 128:(t + 1) * 128],
                        XBF[:, t * N + ch * 512: t * N + (ch + 1) * 512],
                        start=(t == 0), stop=(t == T - 1)),
                        inc=(s_pe if t == T - 1 else None))
                q_pe = s_pe.n
                for t in range(T):
                    P.op("tensor", lambda e, t=t, ch=ch, psk=psk: e.matmul(
                        psk[:, :], MSK[:, 1024 + t * 128: 1024 + (t + 1) * 128],
                        XBF[:, t * N + ch * 512: t * N + (ch + 1) * 512],
                        start=(t == 0), stop=(t == T - 1)),
                        inc=(s_pe if t == T - 1 else None))
                k_pe = s_pe.n
                P.wait("scalar", s_pe, q_pe)
                P.wait("scalar", s_ve, bias_ve)
                P.op("scalar", lambda e, ch=ch, qlo=qlo, psq=psq: e.activation(
                    BIGB[:, qlo + ch * 512: qlo + (ch + 1) * 512], psq[:, :],
                    AF.Silu, bias=B["BIASC"][:, 0:1]), inc=s_se)
                pp.release(2 + (ch % 2), "se", s_se.n)
                P.wait("scalar", s_pe, k_pe)
                P.op("scalar", lambda e, ch=ch, psk=psk: e.activation(
                    STG[:, ch * 512:(ch + 1) * 512], psk[:, :],
                    AF.Silu, bias=B["BIASC"][:, 1:2]), inc=s_se)
                pp.release(4 + (ch % 2), "se", s_se.n)
                qk_se = s_se.n
                P.wait("vector", s_se, qk_se)
                P.wait("vector", s_se, cs_se)
                P.op("vector", lambda e, ch=ch, qlo=qlo: e.tensor_mul(
                    BIGB[:, qlo + ch * 512: qlo + (ch + 1) * 512],
                    BIGB[:, qlo + ch * 512: qlo + (ch + 1) * 512],
                    BIGB[:, CS_O + ch * 512: CS_O + (ch + 1) * 512]), inc=s_ve)
                P.op("vector", lambda e, ch=ch: e.tensor_mul(
                    STG[:, ch * 512:(ch + 1) * 512],
                    STG[:, ch * 512:(ch + 1) * 512],
                    BIGB[:, CS_O + ch * 512: CS_O + (ch + 1) * 512]), inc=s_ve)
            lrpe_done[hh] = s_ve.n

            kvbank = 4 + hh
            pskv = pp.acquire("tensor", kvbank)
            P.wait("tensor", s_ve, lrpe_done[hh])
            P.wait("tensor", s_ve, v_ready)
            kl_evac = {}
            kv_mm = {}
            for nt in range(NT):
                tpb = TP[nt % 2]
                if nt >= 2:
                    P.wait("tensor", s_se, kl_evac[nt - 2])
                P.op("tensor", lambda e, nt=nt, tpb=tpb: e.transpose(
                    tpb[:, 0:128], STG[:, nt * 128:(nt + 1) * 128],
                    IDENT[:, :]), inc=s_pe)
                tr_pe = s_pe.n
                P.wait("scalar", s_pe, tr_pe)
                if nt >= 2:
                    P.wait("scalar", s_pe, kv_mm[nt - 2])
                P.op("scalar", lambda e, nt=nt, tpb=tpb: e.copy(
                    B["KLS"][:, (nt % 2) * 128:(nt % 2) * 128 + 128],
                    tpb[:, 0:128]), inc=s_se)
                kl_evac[nt] = s_se.n
                P.wait("tensor", s_se, kl_evac[nt])
                P.op("tensor", lambda e, nt=nt, hh=hh, pskv=pskv: e.matmul(
                    pskv[:, 0:64],
                    B["KLS"][:, (nt % 2) * 128:(nt % 2) * 128 + 128],
                    BIGB[:, V_O + nt * 128 + hh * 64:
                         V_O + nt * 128 + hh * 64 + 64],
                    start=(nt == 0), stop=(nt == NT - 1)), inc=s_pe)
                kv_mm[nt] = s_pe.n
            P.wait("scalar", s_pe, s_pe.n)
            P.op("scalar", lambda e, hh=hh, pskv=pskv: e.copy(
                KVS[:, hh * 64:(hh + 1) * 64], pskv[:, 0:64]), inc=s_se)
            pp.release(kvbank, "se", s_se.n)
        kvs_se = s_se.n

        ctx_ev = {}
        for nt in range(NT):
            psc = pp.acquire("tensor", 2 + (nt % 2))
            P.wait("tensor", s_se, kvs_se)
            for hh in range(2):
                P.op("tensor", lambda e, nt=nt, hh=hh, psc=psc: e.matmul(
                    psc[:, hh * 64:(hh + 1) * 64],
                    BIGB[:, QL_O[hh] + nt * 128: QL_O[hh] + (nt + 1) * 128],
                    KVS[:, hh * 64:(hh + 1) * 64], start=True, stop=True),
                    inc=(s_pe if hh == 1 else None))
            P.wait("scalar", s_pe, s_pe.n)
            P.op("scalar", lambda e, nt=nt, hp=hp, psc=psc: e.copy(
                BIGA[:, nt * 1024 + hp * 128: nt * 1024 + (hp + 1) * 128],
                psc[:, 0:128]), inc=s_se)
            pp.release(2 + (nt % 2), "se", s_se.n)
        pair_lastread[0] = s_pe.n
        pair_lastread[1] = s_pe.n
    ctx_all_se = s_se.n

    wo = W["L1_attn_Wout"].rearrange("(q p) k -> q p k", p=128)
    P.wait("gpsimd", s_pe, s_pe.n)
    P.wait("gpsimd", s_se, s_se.n)
    for which, base in (("wout", 0), ("wu", 8192)):
        for rt in range(T):
            P.wait("gpsimd", s_se, s_se.n)
            P.wait("gpsimd", s_pe, s_pe.n)
            if which == "wout":
                P.dma("gpsimd", STG[:, 0:1024], wo[rt], stream="wstg")
            else:
                P.dma("gpsimd", STG[:, 0:1024],
                      wq[3072 + rt * 128: 3072 + (rt + 1) * 128, :],
                      stream="wstg")
            P.wait("tensor", P.dsem("wstg"), P.dsem("wstg").n)
            P.wait("tensor", s_se, s_se.n)
            for thd in range(T):
                P.op("tensor", lambda e, thd=thd: e.transpose(
                    TP[0][:, thd * 128:(thd + 1) * 128],
                    STG[:, thd * 128:(thd + 1) * 128], IDENT[:, :]),
                    inc=(s_pe if thd == T - 1 else None))
            P.wait("scalar", s_pe, s_pe.n)
            P.op("scalar", lambda e, rt=rt, base=base: e.copy(
                BIGB[:, base:base + 8192].rearrange(
                    "p (thd c) -> p thd c", thd=T)[:, :, rt * 128:(rt + 1) * 128],
                TP[0][:, :].rearrange("p (thd c) -> p thd c", thd=T)),
                inc=s_se)
    wouts_se = s_se.n
    for wname, dst in (("L1_attn_ln_w", 0), ("L1_attn_ln_b", 1024)):
        P.wait("gpsimd", s_ve, s_ve.n)
        P.dma("gpsimd", B["ROWS"][0:1, 2560:3584], W[wname], stream="wconst")
        P.wait("tensor", P.dsem("wconst"), P.dsem("wconst").n)
        ps1 = pp.acquire("tensor", 2)
        P.op("tensor", lambda e, ps1=ps1: e.matmul(
            ps1[:, :], ONESR[0:1, 0:128], B["ROWS"][0:1, 2560:3072],
            start=True, stop=True))
        ps2 = pp.acquire("tensor", 3)
        P.op("tensor", lambda e, ps2=ps2: e.matmul(
            ps2[:, :], ONESR[0:1, 0:128], B["ROWS"][0:1, 3072:3584],
            start=True, stop=True), inc=s_pe)
        P.wait("vector", s_pe, s_pe.n)
        P.op("vector", lambda e, dst=dst, ps1=ps1: e.tensor_copy(
            SCR[:, dst:dst + 512], ps1[:, :]), inc=s_ve)
        P.op("vector", lambda e, dst=dst, ps2=ps2: e.tensor_copy(
            SCR[:, dst + 512:dst + 1024], ps2[:, :]), inc=s_ve)
        pp.release(2, "ve", s_ve.n)
        pp.release(3, "ve", s_ve.n)
    reps_ve = s_ve.n
    P.wait("gpsimd", s_ve, reps_ve)
    P.dma("gpsimd", B["ROWS"][0:1, 1024:2048], W["L1_attn_bout"], stream="wconst")
    P.dma("gpsimd", B["ROWS"][0:1, 2560:3584],
          W["L1_attn_bqkvu"][0:1, 3072:4096], stream="wconst")
    bias_ld = P.dsem("wconst").n

    for ch in range(NCH):
        for jj in range(4):
            nt = ch * 4 + jj
            psu = [pp.acquire("tensor", 2 + i) for i in range(2)]
            P.wait("tensor", s_se, wouts_se)
            P.wait("tensor", P.dsem("wconst"), bias_ld)
            for i in range(2):
                for t in range(T):
                    P.op("tensor", lambda e, i=i, t=t, nt=nt, psu=psu: e.matmul(
                        psu[i][:, :],
                        XBF[:, t * N + nt * 128: t * N + (nt + 1) * 128],
                        BIGB[:, 8192 + t * 1024 + i * 512:
                             8192 + t * 1024 + (i + 1) * 512],
                        start=(t == 0), stop=False))
                P.op("tensor", lambda e, i=i, psu=psu: e.matmul(
                    psu[i][:, :], ONESR[0:1, 0:128],
                    B["ROWS"][0:1, 2560 + i * 512:2560 + (i + 1) * 512],
                    start=False, stop=True),
                    inc=(s_pe if i == 1 else None))
            u_pe = s_pe.n
            P.wait("vector", s_se, ctx_all_se)
            P.op("vector", lambda e, nt=nt: e.reduce_sum(
                STAT[:, 16:17], BIGA[:, nt * 1024:(nt + 1) * 1024], axis=AX.X),
                inc=s_ve)
            sum_ve = s_ve.n
            P.wait("scalar", s_ve, sum_ve)
            P.op("scalar", lambda e, nt=nt: e.activation(
                SCR[:, 2048:3072], BIGA[:, nt * 1024:(nt + 1) * 1024],
                AF.Square, accum_out=STAT[:, 17:18]), inc=s_se)
            P.wait("vector", s_se, s_se.n)
            P.op("vector", lambda e: e.tensor_scalar(
                STAT[:, 18:19], STAT[:, 16:17], 1.0 / D, None, op0=ALU.mult),
                inc=s_ve)
            P.wait("vector", s_ve, s_ve.n)
            P.op("vector", lambda e: e.tensor_mul(
                STAT[:, 19:20], STAT[:, 18:19], STAT[:, 18:19]), inc=s_ve)
            P.wait("vector", s_ve, s_ve.n)
            P.op("vector", lambda e: e.scalar_tensor_tensor(
                STAT[:, 20:21], STAT[:, 17:18], 1.0 / D, STAT[:, 19:20],
                op0=ALU.mult, op1=ALU.subtract), inc=s_ve)
            P.wait("scalar", s_ve, s_ve.n)
            P.op("scalar", lambda e: e.activation(
                STAT[:, 21:22], STAT[:, 20:21], AF.Ln,
                bias=B["CONST"][:, 0:1]), inc=s_se)
            P.wait("scalar", s_se, s_se.n)
            P.op("scalar", lambda e: e.activation(
                STAT[:, 21:22], STAT[:, 21:22], AF.Exp, scale=-0.5), inc=s_se)
            rstd_se = s_se.n
            P.wait("vector", s_se, rstd_se)
            P.op("vector", lambda e: e.tensor_mul(
                STAT[:, 22:23], STAT[:, 18:19], STAT[:, 21:22]), inc=s_ve)
            P.wait("vector", s_ve, s_ve.n)
            P.op("vector", lambda e: e.tensor_scalar(
                STAT[:, 22:23], STAT[:, 22:23], -1.0, None, op0=ALU.mult),
                inc=s_ve)
            P.wait("vector", s_ve, s_ve.n)
            P.op("vector", lambda e, nt=nt: e.tensor_scalar(
                SCR[:, 2048:3072], BIGA[:, nt * 1024:(nt + 1) * 1024],
                STAT[:, 21:22], STAT[:, 22:23], op0=ALU.mult, op1=ALU.add),
                inc=s_ve)
            P.wait("vector", s_ve, s_ve.n)
            P.op("vector", lambda e: e.tensor_mul(
                SCR[:, 2048:3072], SCR[:, 2048:3072], SCR[:, 0:1024]),
                inc=s_ve)
            P.wait("vector", s_ve, s_ve.n)
            P.op("vector", lambda e: e.tensor_add(
                SCR[:, 2048:3072], SCR[:, 2048:3072], SCR[:, 1024:2048]),
                inc=s_ve)
            P.wait("vector", s_pe, u_pe)
            P.wait("vector", s_ve, s_ve.n)
            for i in range(2):
                P.op("vector", lambda e, i=i, psu=psu: e.tensor_mul(
                    MSK[:, i * 512:(i + 1) * 512],
                    SCR[:, 2048 + i * 512:2048 + (i + 1) * 512], psu[i][:, :]),
                    inc=s_ve)
            nlu_ve = s_ve.n
            pp.release(2, "ve", nlu_ve)
            pp.release(3, "ve", nlu_ve)
            P.wait("tensor", s_ve, nlu_ve)
            P.wait("tensor", s_se, s_se.n)
            for thd in range(T):
                P.op("tensor", lambda e, thd=thd: e.transpose(
                    TP[0][:, thd * 128:(thd + 1) * 128],
                    MSK[:, thd * 128:(thd + 1) * 128], IDENT[:, :]),
                    inc=(s_pe if thd == T - 1 else None))
            P.wait("scalar", s_pe, s_pe.n)
            P.op("scalar", lambda e, jj=jj: e.copy(
                STG[:, :].rearrange("p (thd j c) -> p thd j c", thd=T, j=4)[
                    :, :, jj, :],
                TP[0][:, :].rearrange("p (thd c) -> p thd c", thd=T)),
                inc=s_se)
        nlut_se = s_se.n

        def produce_td(td, ps, nlut_se=nlut_se):
            P.wait("tensor", s_se, nlut_se)
            for thd in range(T):
                P.op("tensor", lambda e, td=td, thd=thd, ps=ps: e.matmul(
                    ps[:, :],
                    BIGB[:, thd * 1024 + td * 128: thd * 1024 + (td + 1) * 128],
                    STG[:, thd * 512:(thd + 1) * 512],
                    start=(thd == 0), stop=False))
            P.op("tensor", lambda e, td=td, ps=ps: e.matmul(
                ps[:, :], B["ROWS"][0:1, 1024 + td * 128:1024 + (td + 1) * 128],
                ONESR[0:1, :], start=False, stop=True), inc=s_pe)
            return s_pe.n

        emit_ln_chunk(B, ch, produce_td, sq_off=ch * 4 * 1024, sq_buf=BIGA)


def emit_stage4(B):
    P, W = B["P"], B["W"]
    s_pe, s_se, s_ve, s_pl = B["s_pe"], B["s_se"], B["s_ve"], B["s_pl"]
    XBF, BIGA, BIGB, STG, PS, TP = (B["XBF"], B["BIGA"], B["BIGB"], B["STG"],
                                    B["PS"], B["TP"])
    IDENT, ONESC, ONESR, MSK, SCR = (B["IDENT"], B["ONESC"], B["ONESR"],
                                     B["MSK"], B["SCR32"])
    pp = B["pp"]
    NF = F1 // 128
    Q = 1.0 / 127.0
    Q2 = Q * Q

    wgr = W["L1_ffn_w_gate"].rearrange("(f p) k -> p f k", p=128)
    wur = W["L1_ffn_w_up"].rearrange("(f p) k -> p f k", p=128)
    wdr = W["L1_ffn_w_down"].rearrange("(q p) k -> p q k", p=128)

    hbuf_last = [0, 0]
    prev_h_ve = 0
    for ft in range(NF):
        P.wait("gpsimd", s_ve, s_ve.n)
        P.dma("gpsimd", SCR[:, 0:1024], wgr[:, ft, :], stream="wstg")
        P.dma("gpsimd", SCR[:, 1024:2048], wur[:, ft, :], stream="wstg")
        P.wait("vector", P.dsem("wstg"), P.dsem("wstg").n)
        P.wait("vector", s_pe, s_pe.n)
        P.op("vector", lambda e: e.tensor_copy(STG[:, 0:1024], SCR[:, 0:1024]),
             inc=s_ve)
        P.op("vector", lambda e: e.tensor_copy(STG[:, 1024:2048],
                                               SCR[:, 1024:2048]), inc=s_ve)
        wnat_ve = s_ve.n
        P.wait("tensor", s_ve, wnat_ve)
        P.wait("tensor", s_se, s_se.n)
        for t in range(T):
            P.op("tensor", lambda e, t=t: e.transpose(
                TP[0][:, t * 128:(t + 1) * 128], STG[:, t * 128:(t + 1) * 128],
                IDENT[:, :]))
            P.op("tensor", lambda e, t=t: e.transpose(
                TP[1][:, t * 128:(t + 1) * 128],
                STG[:, 1024 + t * 128: 1024 + (t + 1) * 128],
                IDENT[:, :]), inc=(s_pe if t == T - 1 else None))
        P.wait("scalar", s_pe, s_pe.n)
        P.op("scalar", lambda e: e.copy(MSK[:, 0:1024], TP[0][:, :]))
        P.op("scalar", lambda e: e.copy(MSK[:, 1024:2048], TP[1][:, :]),
             inc=s_se)
        wt_se = s_se.n

        for ch in range(NCH):
            psg = pp.acquire("tensor", ch % 2)
            psu = pp.acquire("tensor", 2 + (ch % 2))
            P.wait("tensor", s_se, wt_se)
            for t in range(T):
                P.op("tensor", lambda e, t=t, ch=ch, psg=psg: e.matmul(
                    psg[:, :], MSK[:, t * 128:(t + 1) * 128],
                    XBF[:, t * N + ch * 512: t * N + (ch + 1) * 512],
                    start=(t == 0), stop=(t == T - 1)),
                    inc=(s_pe if t == T - 1 else None))
            g_pe = s_pe.n
            for t in range(T):
                P.op("tensor", lambda e, t=t, ch=ch, psu=psu: e.matmul(
                    psu[:, :], MSK[:, 1024 + t * 128: 1024 + (t + 1) * 128],
                    XBF[:, t * N + ch * 512: t * N + (ch + 1) * 512],
                    start=(t == 0), stop=(t == T - 1)),
                    inc=(s_pe if t == T - 1 else None))
            u_pe = s_pe.n
            P.wait("scalar", s_pe, g_pe)
            P.wait("scalar", s_ve, prev_h_ve)
            P.op("scalar", lambda e, psg=psg: e.activation(
                STG[:, 3072:3584], psg[:, :], AF.Silu, scale=Q), inc=s_se)
            sg_se = s_se.n
            pp.release(ch % 2, "se", sg_se)
            hb = 2048 + (ft % 2) * 512
            shs = P.dsem("hs%d" % (ft % 2))
            P.wait("vector", shs, hbuf_last[ft % 2])
            P.wait("vector", s_se, sg_se)
            P.wait("vector", s_pe, u_pe)
            P.op("vector", lambda e, hb=hb, psu=psu: e.tensor_mul(
                STG[:, hb:hb + 512], STG[:, 3072:3584], psu[:, :]), inc=s_ve)
            h_ve = s_ve.n
            prev_h_ve = h_ve
            pp.release(2 + (ch % 2), "ve", h_ve)
            P.wait("sync", s_ve, h_ve)
            P.dma("sync", W["h_scr"][ft, ch, :, :], STG[:, hb:hb + 512],
                  stream="hs%d" % (ft % 2))
            hbuf_last[ft % 2] = shs.n
    sweep_pe = s_pe.n

    def wdt_slice(ft, td):
        if td < 5:
            return BIGB[:, td * 3072 + ft * 128: td * 3072 + (ft + 1) * 128]
        return BIGA[:, 12288 + (td - 5) * 3072 + ft * 128:
                    12288 + (td - 5) * 3072 + (ft + 1) * 128]

    P.wait("gpsimd", s_pe, s_pe.n)
    for td in range(T):
        P.wait("gpsimd", s_ve, s_ve.n)
        P.dma("gpsimd", SCR[:, 0:3072], wdr[:, td, :], stream="wstg")
        P.wait("vector", P.dsem("wstg"), P.dsem("wstg").n)
        P.wait("vector", s_pe, s_pe.n)
        if td == 0:
            P.wait("vector", P.dsem("hs0"), P.dsem("hs0").n)
            P.wait("vector", P.dsem("hs1"), P.dsem("hs1").n)
        P.op("vector", lambda e: e.tensor_copy(STG[:, 0:3072], SCR[:, 0:3072]),
             inc=s_ve)
        P.wait("tensor", s_ve, s_ve.n)
        evg = {}
        for fg in range(3):
            tp = TP[fg % 2]
            if fg >= 2:
                P.wait("tensor", s_se, evg[fg - 2])
            else:
                P.wait("tensor", s_se, s_se.n)
            for i in range(8):
                P.op("tensor", lambda e, fg=fg, i=i, tp=tp: e.transpose(
                    tp[:, i * 128:(i + 1) * 128],
                    STG[:, (fg * 8 + i) * 128:(fg * 8 + i + 1) * 128],
                    IDENT[:, :]), inc=(s_pe if i == 7 else None))
            P.wait("scalar", s_pe, s_pe.n)
            for i in range(8):
                P.op("scalar", lambda e, td=td, fg=fg, i=i, tp=tp: e.copy(
                    wdt_slice(fg * 8 + i, td), tp[:, i * 128:(i + 1) * 128]),
                    inc=(s_se if i == 7 else None))
            evg[fg] = s_se.n
    wdt_se = s_se.n

    for ch in range(NCH):
        P.wait("sync", P.dsem("hs0"), P.dsem("hs0").n)
        P.wait("sync", P.dsem("hs1"), P.dsem("hs1").n)
        P.wait("sync", s_pe, s_pe.n)
        P.dma("sync", BIGA[:, 0:12288].rearrange("p (f n) -> p f n", f=NF),
              W["h_scr"][:, ch, :, :].rearrange("f p n -> p f n"),
              stream="pchunk")
        h_ld = P.dsem("pchunk").n

        def produce_td(td, ps, h_ld=h_ld):
            P.wait("tensor", P.dsem("pchunk"), h_ld)
            P.wait("tensor", s_se, wdt_se)
            for ft in range(NF):
                P.op("tensor", lambda e, td=td, ft=ft, ps=ps: e.matmul(
                    ps[:, :], wdt_slice(ft, td),
                    BIGA[:, ft * 512:(ft + 1) * 512],
                    start=(ft == 0), stop=(ft == NF - 1)), inc=s_pe)
            return s_pe.n

        emit_ln_chunk(B, ch, produce_td, sq_off=0, scale=Q2)




_IDEN = np.eye(128, dtype=np.float32)


def _weight_names(grp):
    names = []
    if "s1" in grp:
        names += ["L0_attn_w_kqv", "L0_attn_w_proj", "L0_attn_b_proj"]
    if "s2" in grp:
        names += ["L0_ffn_gate_w", "L0_ffn_w_gate", "L0_ffn_w_up", "L0_ffn_w_down",
                  "L0_ffn_A1", "L0_ffn_B1", "L0_ffn_A3", "L0_ffn_B3",
                  "L0_ffn_A2", "L0_ffn_B2"]
    if "s3" in grp:
        names += ["L1_attn_Wqkvu", "L1_attn_bqkvu", "L1_attn_Wout", "L1_attn_bout",
                  "L1_attn_theta", "L1_attn_ln_w", "L1_attn_ln_b"]
    if "s4" in grp:
        names += ["L1_ffn_w_gate", "L1_ffn_w_up", "L1_ffn_w_down"]
    return names


def _prep(name, arr):
    a = np.asarray(arr)
    if name in ("L0_attn_b_proj", "L1_attn_bqkvu", "L1_attn_bout",
                "L1_attn_ln_w", "L1_attn_ln_b"):
        a = a.reshape(1, -1)
    elif name == "L1_attn_theta":
        a = a.reshape(H, DH)
    elif name in ("L0_ffn_A1", "L0_ffn_A3"):
        a = a.reshape(E * R, D)
    elif name in ("L0_ffn_B1", "L0_ffn_B3"):
        a = a.reshape(E * F0, R)
    elif name == "L0_ffn_A2":
        a = a.reshape(E * R, F0)
    elif name == "L0_ffn_B2":
        a = a.reshape(E * D, R)
    return np.ascontiguousarray(a)


def _run_stages(stage_groups, inputs):
    x = np.ascontiguousarray(inputs["L0_attn_x"].astype(np.float32, copy=False))
    nb = x.shape[0]
    cur = [np.ascontiguousarray(x[b]) for b in range(nb)]
    for grp in stage_groups:
        nc, es = build_program(grp)
        in_maps = []
        for b in range(nb):
            m = {"x_in": cur[b], "iden": _IDEN}
            for name in _weight_names(grp):
                m[name] = _prep(name, inputs[name])
            in_maps.append(m)
        res = run_bass_kernel_spmd(nc, in_maps, core_ids=list(range(nb)))
        cur = [np.asarray(res.results[b]["x_out"]) for b in range(nb)]
        es.close()
    return np.stack(cur, axis=0)


def kernel(**inputs):
    return _run_stages([["s1"], ["s2"], ["s3"], ["s4"]], inputs)


# version 02
# speedup vs baseline: 1.0328x; 1.0328x over previous
import math
from contextlib import ExitStack

import numpy as np
import concourse.bass as bass
import concourse.mybir as mybir
from concourse.bass_utils import run_bass_kernel_spmd

F32 = mybir.dt.float32
BF16 = mybir.dt.bfloat16
I32 = mybir.dt.int32
AF = mybir.ActivationFunctionType
ALU = mybir.AluOpType
AX = mybir.AxisListType

N = 4096
D = 1024
T = D // 128
NCH = N // 512
NT = N // 128
H = 16
DH = 64
F0 = 4096
F1 = 3072
E = 8
R = 16
EPS = 1e-5

ENGINES = ("tensor", "scalar", "vector", "gpsimd", "sync")
DBG = None


class Sem:
    __slots__ = ("h", "n", "name")

    def __init__(self, h, name):
        self.h = h
        self.n = 0
        self.name = name


class Prog:
    CHAINED = ("scalar", "vector", "gpsimd")

    def __init__(self, nc, es):
        self.nc = nc
        self.es = es
        self.q = {e: [] for e in ENGINES}
        self.sems = {}
        self.chain = {e: self.sem("chain_" + e) for e in self.CHAINED}

    def sem(self, name):
        if name not in self.sems:
            h = self.es.enter_context(self.nc.semaphore(name))
            self.sems[name] = Sem(h, name)
        return self.sems[name]

    def op(self, eng, fn, inc=None, inc_k=1, chain=True):
        cs = self.chain.get(eng) if chain else None
        if cs is not None:
            assert inc is None or inc is cs, \
                f"chained op on {eng} may only inc its chain sem"
            if cs.n > 0:
                self.q[eng].append(lambda e, s=cs, v=cs.n: e.wait_ge(s.h, v))
            cs.n += 1
            self.q[eng].append(lambda e, fn=fn, c=cs, v=cs.n - 1:
                               fn(e).then_inc(c.h, 1))
            return cs.n
        if inc is not None:
            inc.n += inc_k
            self.q[eng].append(
                lambda e, fn=fn, s=inc, k=inc_k: fn(e).then_inc(s.h, k))
            return inc.n
        self.q[eng].append(lambda e, fn=fn: fn(e))
        return None

    def wait(self, eng, sem, val=None):
        v = sem.n if val is None else val
        if v <= 0:
            return
        self.q[eng].append(lambda e, s=sem, v=v: e.wait_ge(s.h, v))

    def dma(self, eng, out, in_, inc=None, stream=None, serial=True, **kw):
        s = inc if inc is not None else self.sem("dma_" + stream)
        if serial and s.n > 0:
            self.q[eng].append(lambda e, ss=s, v=s.n: e.wait_ge(ss.h, v))
        return self.op(eng, lambda e: e.dma_start(out, in_, **kw), inc=s,
                       inc_k=16, chain=False)

    def dsem(self, stream):
        return self.sem("dma_" + stream)

    def barrier(self, tag):
        s = self.sem("bar")
        for eng in ENGINES:
            self.op(eng, lambda e: e.nop(), inc=s, inc_k=1, chain=False)
        tgt = s.n
        for eng in ENGINES:
            self.wait(eng, s, tgt)

    def build(self, block):
        for eng in ENGINES:
            items = self.q[eng]

            def section(e, items=items):
                for fn in items:
                    fn(e)

            getattr(block, eng)(section)


class PsumPool:

    def __init__(self, B):
        self.B = B
        self.free = {i: (0, "pe") for i in range(6)}

    def acquire(self, eng, i):
        cnt, sem = self.free[i]
        self.B["P"].wait(eng, self.B["s_" + sem], cnt)
        return self.B["PS"][i]

    def release(self, i, sem_name, cnt):
        self.free[i] = (cnt, sem_name)




def build_program(stages):
    nc = bass.Bass("TRN2", target_bir_lowering=False)
    es = ExitStack()
    P = Prog(nc, es)

    x_in = nc.dram_tensor("x_in", [N, D], F32, kind="ExternalInput").ap()
    x_out = nc.dram_tensor("x_out", [N, D], F32, kind="ExternalOutput").ap()
    iden_h = nc.dram_tensor("iden", [128, 128], F32, kind="ExternalInput").ap()

    W = {}

    def win(name, shape, dtype=F32):
        W[name] = nc.dram_tensor(name, shape, dtype, kind="ExternalInput").ap()

    if "s1" in stages:
        win("L0_attn_w_kqv", [3 * D, D])
        win("L0_attn_w_proj", [D, D])
        win("L0_attn_b_proj", [1, D])
    if "s2" in stages:
        win("L0_ffn_gate_w", [E, D])
        win("L0_ffn_w_gate", [F0, D])
        win("L0_ffn_w_up", [F0, D])
        win("L0_ffn_w_down", [D, F0])
        win("L0_ffn_A1", [E * R, D])
        win("L0_ffn_B1", [E * F0, R])
        win("L0_ffn_A3", [E * R, D])
        win("L0_ffn_B3", [E * F0, R])
        win("L0_ffn_A2", [E * R, F0])
        win("L0_ffn_B2", [E * D, R])
        W["pT_scr"] = nc.dram_tensor("pT_scr", [F0 // 128, NCH, 128, 512], BF16,
                                     kind="ExternalOutput" if DBG == "s2p"
                                     else "Internal").ap()
    if "s3" in stages:
        win("L1_attn_Wqkvu", [4 * D, D])
        win("L1_attn_bqkvu", [1, 4 * D])
        win("L1_attn_Wout", [D, D])
        win("L1_attn_bout", [1, D])
        win("L1_attn_theta", [H, DH])
        win("L1_attn_ln_w", [1, D])
        win("L1_attn_ln_b", [1, D])
    if "s4" in stages:
        win("L1_ffn_w_gate", [F1, D], I32)
        win("L1_ffn_w_up", [F1, D], I32)
        win("L1_ffn_w_down", [D, F1], I32)
        W["h_scr"] = nc.dram_tensor("h_scr", [F1 // 128, NCH, 128, 512], BF16,
                                    kind="Internal").ap()

    sbt = lambda name, shape, dt: es.enter_context(nc.sbuf_tensor(name, shape, dt))
    B = {}
    B["XBF"] = sbt("XBF", [128, T * N], BF16)
    B["BIGA"] = sbt("BIGA", [128, 32768], BF16)
    B["BIGB"] = sbt("BIGB", [128, 16384], BF16)
    B["SCR32"] = sbt("SCR32", [128, 5120], F32)
    B["IDENT"] = sbt("IDENT", [128, 128], BF16)
    B["IDENT32"] = sbt("IDENT32", [64, 64], F32)
    B["ONESC"] = sbt("ONESC", [128, 1], BF16)
    B["ONESR"] = sbt("ONESR", [1, 512], BF16)
    B["CONST"] = sbt("CONST", [128, 8], F32)
    B["STAT"] = sbt("STAT", [128, 64], F32)
    B["STG"] = sbt("STG", [128, 4096], BF16)
    B["MSK"] = sbt("MSK", [128, 2048], BF16)
    B["ROWS"] = sbt("ROWS", [1, 3584], BF16)
    B["KVS"] = sbt("KVS", [128, 128], BF16)
    B["KLS"] = sbt("KLS", [128, 256], BF16)
    B["THSB"] = sbt("THSB", [16, 64], F32)
    B["THT2"] = sbt("THT2", [128, 16], F32)
    B["BQN"] = sbt("BQN", [32, 128], F32)
    B["BQNB"] = sbt("BQNB", [1, 1024], BF16)
    B["BQT"] = sbt("BQT", [128, 32], F32)
    B["BIASC"] = sbt("BIASC", [128, 2], F32)
    B["INT2"] = sbt("INT2", [128, 512], I32)

    B["PS"] = [es.enter_context(nc.psum_tensor(f"PS{i}", [128, 512], F32))
               for i in range(6)]
    B["TP"] = [es.enter_context(nc.psum_tensor(f"TP{i}", [128, 1024], BF16))
               for i in range(2)]

    block = es.enter_context(nc.Block())
    for s in ("s_ld", "s_pe", "s_st", "s_sp"):
        B[s] = P.sem(s)
    B["s_se"] = P.chain["scalar"]
    B["s_ve"] = P.chain["vector"]
    B["s_pl"] = P.chain["gpsimd"]
    B.update(nc=nc, P=P, W=W, x_in=x_in, x_out=x_out, iden=iden_h)
    B["pp"] = PsumPool(B)

    emit_prologue(B)
    for st in stages:
        {"s1": emit_stage1, "s2": emit_stage2, "s3": emit_stage3,
         "s4": emit_stage4}[st](B)
        P.barrier(st)
    if DBG in ("biga", "s2m") and False:
        pass
    elif DBG == "bigb":
        s_st = B["s_st"]
        for j in range(16):
            P.dma("gpsimd",
                  B["x_out"].rearrange("(q p) d -> p q d", p=128)[:, j, :],
                  B["BIGB"][:, j * 1024:(j + 1) * 1024], inc=s_st)
        P.dma("gpsimd",
              B["x_out"].rearrange("(q p) d -> p q d", p=128)[:, 16, 0:64],
              B["STAT"][:, :], inc=s_st)
        P.wait("gpsimd", s_st, s_st.n)
    elif DBG in ("biga", "s2m"):
        s_st = B["s_st"]
        P.wait("gpsimd", B["s_se"], B["s_se"].n)
        P.wait("gpsimd", B["s_ve"], B["s_ve"].n)
        P.wait("gpsimd", B["s_pe"], B["s_pe"].n)
        for j in range(NT):
            P.dma("gpsimd",
                  B["x_out"].rearrange("(q p) d -> p q d", p=128)[:, j, :],
                  B["BIGA"][:, j * 1024:(j + 1) * 1024], inc=s_st)
        P.wait("gpsimd", s_st, s_st.n)
    else:
        emit_epilogue(B)

    P.build(block)
    return nc, es




def emit_prologue(B):
    P = B["P"]
    s_pe, s_se, s_ve = B["s_pe"], B["s_se"], B["s_ve"]
    STG, IDENT, TP = B["STG"], B["IDENT"], B["TP"]

    P.dma("gpsimd", IDENT[:, :], B["iden"], stream="wconst")
    P.dma("sync", B["IDENT32"][:, :], B["iden"][0:64, 0:64], stream="wc32")
    P.op("vector", lambda e: e.memset(B["ONESC"][:, :], 1.0))
    P.op("vector", lambda e: e.memset(B["ONESR"][:, :], 1.0))
    P.op("vector", lambda e: e.memset(B["CONST"][:, :], 0.0))
    P.op("vector", lambda e: e.memset(B["CONST"][:, 0:1], EPS), inc=s_ve)

    xr = B["x_in"].rearrange("(q p) d -> p q d", p=128)
    pe_after = {}
    evac_after = {}
    sx = P.dsem("xin")
    for g in range(NT // 4):
        if g > 0:
            P.wait("gpsimd", s_pe, pe_after[g - 1])
        P.dma("gpsimd", STG[:, :].rearrange("p (j d) -> p j d", j=4),
              xr[:, g * 4:(g + 1) * 4, :], stream="xin")
        P.wait("tensor", sx, sx.n)
        if g == 0:
            P.wait("tensor", P.dsem("wconst"), 16)
        for j in range(4):
            nt = g * 4 + j
            tp = TP[nt % 2]
            if nt >= 2:
                P.wait("tensor", s_se, evac_after[nt - 2])
            for t in range(T):
                P.op("tensor", lambda e, j=j, t=t, tp=tp: e.transpose(
                    tp[:, t * 128:(t + 1) * 128],
                    STG[:, j * 1024 + t * 128:j * 1024 + (t + 1) * 128],
                    IDENT[:, :]), inc=(s_pe if t == T - 1 else None))
            P.wait("scalar", s_pe, s_pe.n)
            P.op("scalar", lambda e, nt=nt, tp=tp: e.copy(
                B["XBF"][:, :].rearrange("p (t n) -> p t n", t=T)[
                    :, :, nt * 128:(nt + 1) * 128],
                tp[:, :].rearrange("p (t n) -> p t n", t=T)), inc=s_se)
            evac_after[nt] = s_se.n
        pe_after[g] = s_pe.n


def emit_epilogue(B):
    P = B["P"]
    s_pe, s_ve, s_st = B["s_pe"], B["s_ve"], B["s_st"]
    s_se = B["s_se"]
    TP = B["TP"]
    evac_after = {}
    st_after = {}
    ep_se = s_se.n
    ep_ve = B["s_ve"].n
    for nt in range(NT):
        tp = TP[nt % 2]
        if nt < 2:
            P.wait("tensor", s_se, ep_se)
            P.wait("tensor", s_ve, ep_ve)
        if nt >= 2:
            P.wait("tensor", s_ve, evac_after[nt - 2])
        for t in range(T):
            P.op("tensor", lambda e, nt=nt, t=t, tp=tp: e.transpose(
                tp[:, t * 128:(t + 1) * 128],
                B["XBF"][:, t * N + nt * 128: t * N + (nt + 1) * 128],
                B["IDENT"][:, :]), inc=(s_pe if t == T - 1 else None))
        P.wait("vector", s_pe, s_pe.n)
        half = (nt % 2) * 1024
        if nt >= 2:
            P.wait("vector", s_st, st_after[nt - 2])
        P.op("vector", lambda e, tp=tp, half=half: e.tensor_copy(
            B["SCR32"][:, half:half + 1024], tp[:, :]), inc=s_ve)
        evac_after[nt] = s_ve.n
        P.wait("sync", s_ve, s_ve.n)
        P.dma("sync", B["x_out"].rearrange("(q p) d -> q p d", p=128)[nt, :, :],
              B["SCR32"][:, half:half + 1024], inc=s_st)
        st_after[nt] = s_st.n
    P.wait("sync", s_st, s_st.n)




def emit_ln_chunk(B, ch, produce_td, sq_off=0, sq_buf=None, scale=1.0):
    P = B["P"]
    s_pe, s_se, s_ve = B["s_pe"], B["s_se"], B["s_ve"]
    STG, PS = B["STG"], B["PS"]
    if sq_buf is None:
        sq_buf = STG
    STG = sq_buf
    pp = B["pp"]

    def xsl(td):
        return B["XBF"][:, td * N + ch * 512: td * N + (ch + 1) * 512]

    for td in range(T):
        bank = td % 2
        ps = pp.acquire("tensor", bank)
        pe_done = produce_td(td, ps)
        P.wait("vector", s_pe, pe_done)
        P.op("vector", lambda e, ps=ps, td=td: e.scalar_tensor_tensor(
            xsl(td), ps[:, :], scale, xsl(td), op0=ALU.mult, op1=ALU.add),
            inc=s_ve)
        pp.release(bank, "ve", s_ve.n)
        P.op("vector", lambda e, td=td: e.tensor_mul(
            STG[:, sq_off + td * 512: sq_off + (td + 1) * 512], xsl(td), xsl(td)),
            inc=s_ve)
        sq_done = s_ve.n
        if td == 0:
            pp.acquire("tensor", 2)
            pp.acquire("tensor", 3)
        P.wait("tensor", s_ve, sq_done)
        P.op("tensor", lambda e, td=td: e.matmul(
            PS[2][0:1, :], B["ONESC"][:, :], xsl(td),
            start=(td == 0), stop=(td == T - 1)),
            inc=(s_pe if td == T - 1 else None))
        P.op("tensor", lambda e, td=td: e.matmul(
            PS[3][0:1, :], B["ONESC"][:, :],
            STG[:, sq_off + td * 512: sq_off + (td + 1) * 512],
            start=(td == 0), stop=(td == T - 1)),
            inc=(s_pe if td == T - 1 else None))
    stats_pe = s_pe.n
    SS = B["SCR32"]
    P.wait("vector", s_pe, stats_pe)
    P.op("vector", lambda e: e.tensor_scalar(SS[0:1, 4096:4608], PS[2][0:1, :],
                                             1.0 / D, None, op0=ALU.mult),
         inc=s_ve)
    P.wait("vector", s_ve, s_ve.n)
    P.op("vector", lambda e: e.tensor_mul(SS[0:1, 4608:5120], SS[0:1, 4096:4608],
                                          SS[0:1, 4096:4608]), inc=s_ve)
    P.wait("vector", s_ve, s_ve.n)
    P.op("vector", lambda e: e.scalar_tensor_tensor(
        SS[0:1, 4608:5120], PS[3][0:1, :], 1.0 / D, SS[0:1, 4608:5120],
        op0=ALU.mult, op1=ALU.subtract), inc=s_ve)
    pp.release(2, "ve", s_ve.n)
    pp.release(3, "ve", s_ve.n)
    P.wait("scalar", s_ve, s_ve.n)
    P.op("scalar", lambda e: e.activation(SS[0:1, 4608:5120], SS[0:1, 4608:5120],
                                          AF.Ln, bias=B["CONST"][0:1, 0:1]),
         inc=s_se)
    P.wait("scalar", s_se, s_se.n)
    P.op("scalar", lambda e: e.activation(SS[0:1, 4608:5120], SS[0:1, 4608:5120],
                                          AF.Exp, scale=-0.5), inc=s_se)
    P.wait("vector", s_se, s_se.n)
    P.op("vector", lambda e: e.scalar_tensor_tensor(
        SS[0:1, 4096:4608], SS[0:1, 4096:4608], -1.0, SS[0:1, 4608:5120],
        op0=ALU.mult, op1=ALU.mult), inc=s_ve)
    P.op("vector", lambda e: e.tensor_copy(B["ROWS"][0:1, 0:512],
                                           SS[0:1, 4608:5120]), inc=s_ve)
    P.wait("vector", s_ve, s_ve.n)
    P.op("vector", lambda e: e.tensor_copy(B["ROWS"][0:1, 512:1024],
                                           SS[0:1, 4096:4608]), inc=s_ve)
    pp.acquire("tensor", 4)
    pp.acquire("tensor", 5)
    P.wait("tensor", s_ve, s_ve.n)
    P.op("tensor", lambda e: e.matmul(PS[4][:, :], B["ONESR"][0:1, 0:128],
                                      B["ROWS"][0:1, 0:512], start=True, stop=True))
    P.op("tensor", lambda e: e.matmul(PS[5][:, :], B["ONESR"][0:1, 0:128],
                                      B["ROWS"][0:1, 512:1024], start=True,
                                      stop=True), inc=s_pe)
    P.wait("vector", s_pe, s_pe.n)
    for td in range(T):
        P.op("vector", lambda e, td=td: e.tensor_mul(
            STG[:, sq_off + td * 512: sq_off + (td + 1) * 512], xsl(td),
            PS[4][:, :]))
        P.op("vector", lambda e, td=td: e.tensor_add(
            xsl(td), STG[:, sq_off + td * 512: sq_off + (td + 1) * 512],
            PS[5][:, :]), inc=s_ve)
    pp.release(4, "ve", s_ve.n)
    pp.release(5, "ve", s_ve.n)
    return s_ve.n




def emit_stage1(B):
    P, W = B["P"], B["W"]
    s_pe, s_se, s_ve, s_pl = (B["s_pe"], B["s_se"], B["s_ve"], B["s_pl"])
    XBF, BIGA, BIGB, STG, PS, TP = (B["XBF"], B["BIGA"], B["BIGB"], B["STG"],
                                    B["PS"], B["TP"])
    IDENT, ONESC, ONESR, STAT, MSK = (B["IDENT"], B["ONESC"], B["ONESR"],
                                      B["STAT"], B["MSK"])
    pp = B["pp"]

    QT_O = 0
    KT_O = 4096
    AT_O = 8192
    VA_O = 9728
    WK_O = [11808, 13344]
    SQ_O = 14880
    WPT_O = 4096

    P.wait("vector", B["s_se"], B["s_se"].n)
    P.op("vector", lambda e: e.memset(BIGB[64:128, QT_O:QT_O + 4096], 0.0))
    P.op("vector", lambda e: e.memset(BIGB[64:128, KT_O:KT_O + 4096], 0.0))
    P.op("vector", lambda e: e.memset(BIGB[64:128, SQ_O:SQ_O + 512], 0.0))
    P.op("vector", lambda e: e.memset(
        BIGB[:, VA_O:VA_O + 2080].rearrange("p (k o) -> p k o", o=65)[:, :, 64:65],
        1.0), inc=s_ve)
    P.op("gpsimd", lambda e: e.iota(B["SCR32"][:, 0:512], pattern=[[1, 512]],
                                    base=0, channel_multiplier=0,
                                    allow_small_or_imprecise_dtypes=True))
    P.op("gpsimd", lambda e: e.iota(B["SCR32"][:, 512:513], pattern=[[0, 1]],
                                    base=0, channel_multiplier=1,
                                    allow_small_or_imprecise_dtypes=True), inc=s_pl)
    P.wait("vector", s_pl, s_pl.n)
    for r in range(4):
        P.op("vector", lambda e, r=r: e.tensor_scalar(
            B["SCR32"][:, 513:514], B["SCR32"][:, 512:513], float(128 * r), None,
            op0=ALU.add), inc=s_ve)
        P.wait("vector", s_ve, s_ve.n)
        P.op("vector", lambda e, r=r: e.tensor_scalar(
            MSK[:, r * 512:(r + 1) * 512], B["SCR32"][:, 0:512],
            B["SCR32"][:, 513:514], None, op0=ALU.is_ge), inc=s_ve)
    P.dma("gpsimd", B["ROWS"][0:1, 1024:2048], W["L0_attn_b_proj"],
          stream="wconst")
    s_bp = P.dsem("wconst")
    bp_ld = s_bp.n

    wkr = W["L0_attn_w_kqv"].rearrange("(h d c) k -> h d (c k)", h=H, d=DH, c=3)

    wk_lastread = [0, 0]
    qkv_lastread = 0
    at_lastread = [0, 0, 0]
    ctxn = BIGA
    init_ve = s_ve.n

    for h in range(H):
        if h == 0:
            P.wait("tensor", s_ve, init_ve)
        buf = h % 2
        wko = WK_O[buf]
        P.wait("gpsimd", s_pe, wk_lastread[buf])
        P.wait("gpsimd", s_se, s_se.n)
        P.dma("gpsimd", STG[0:64, 0:3072], wkr[h], stream="wstg")
        P.wait("tensor", P.dsem("wstg"), P.dsem("wstg").n)
        P.wait("tensor", s_se, s_se.n)
        for idx in range(24):
            t, c = divmod(idx, 3)
            tp = TP[0] if idx < 16 else TP[1]
            sl = idx if idx < 16 else idx - 16
            P.op("tensor", lambda e, t=t, c=c, tp=tp, sl=sl: e.transpose(
                tp[:, sl * 64:(sl + 1) * 64],
                STG[0:64, c * 1024 + t * 128: c * 1024 + (t + 1) * 128],
                IDENT[0:64, 0:64]), inc=(s_pe if idx in (15, 23) else None))
        P.wait("scalar", s_pe, s_pe.n)
        P.op("scalar", lambda e, wko=wko: e.copy(BIGB[:, wko:wko + 1024],
                                                 TP[0][:, :]))
        P.op("scalar", lambda e, wko=wko: e.copy(BIGB[:, wko + 1024:wko + 1536],
                                                 TP[1][:, 0:512]), inc=s_se)
        wk_ready = s_se.n

        P.op("vector", lambda e: e.memset(STAT[0:1, 0:2], -1.0e30), inc=s_ve)
        for ch in range(NCH):
            bank = ch % 2
            ps = pp.acquire("tensor", bank)
            P.wait("tensor", s_se, wk_ready)
            P.wait("scalar", s_pe, qkv_lastread)
            P.wait("vector", s_pe, qkv_lastread)
            for t in range(T):
                P.op("tensor", lambda e, t=t, ch=ch, ps=ps, wko=wko: e.matmul(
                    ps[:, :], BIGB[:, wko + t * 192: wko + t * 192 + 128],
                    XBF[:, t * N + ch * 512: t * N + (ch + 1) * 512],
                    start=(t == 0), stop=(t == T - 1)),
                    inc=(s_pe if t == T - 1 else None))
            mm_done = s_pe.n
            P.wait("vector", s_pe, mm_done)
            P.op("vector", lambda e, ch=ch, ps=ps: e.tensor_copy(
                BIGB[0:64, KT_O + ch * 512: KT_O + (ch + 1) * 512], ps[0:64, :]),
                inc=s_ve)
            k_ready = s_ve.n
            P.wait("scalar", s_pe, mm_done)
            P.op("scalar", lambda e, ch=ch, ps=ps: e.copy(
                BIGB[0:64, QT_O + ch * 512: QT_O + (ch + 1) * 512],
                ps[64:128, :]), inc=s_se)
            q_ready = s_se.n
            pp.release(bank, "se", s_se.n)
            for which, rdy, rsem, off in (("q", q_ready, s_se, QT_O),
                                          ("k", k_ready, s_ve, KT_O)):
                P.wait("gpsimd", rsem, rdy)
                P.wait("gpsimd", s_pe, s_pe.n)
                P.op("gpsimd", lambda e, off=off, ch=ch: e.tensor_tensor(
                    BIGB[0:64, SQ_O:SQ_O + 512],
                    BIGB[0:64, off + ch * 512: off + (ch + 1) * 512],
                    BIGB[0:64, off + ch * 512: off + (ch + 1) * 512],
                    op=ALU.mult), inc=s_pl)
                pp.acquire("tensor", 4)
                P.wait("tensor", s_pl, s_pl.n)
                P.op("tensor", lambda e: e.matmul(
                    PS[4][0:1, :], ONESC[:, :], BIGB[:, SQ_O:SQ_O + 512],
                    start=True, stop=True), inc=s_pe)
                P.wait("vector", s_pe, s_pe.n)
                P.op("vector", lambda e: e.reduce_max(
                    STAT[0:1, 2:3], PS[4][0:1, :], axis=AX.X), inc=s_ve)
                pp.release(4, "ve", s_ve.n)
                col = 0 if which == "q" else 1
                P.wait("vector", s_ve, s_ve.n)
                P.op("vector", lambda e, col=col: e.tensor_max(
                    STAT[0:1, col:col + 1], STAT[0:1, col:col + 1],
                    STAT[0:1, 2:3]), inc=s_ve)
        P.op("vector", lambda e: e.tensor_add(STAT[0:1, 3:4], STAT[0:1, 0:1],
                                              STAT[0:1, 1:2]), inc=s_ve)
        P.wait("vector", s_ve, s_ve.n)
        P.op("vector", lambda e: e.tensor_scalar(
            STAT[0:1, 3:4], STAT[0:1, 3:4], -1.0 / 16.0, None, op0=ALU.mult),
            inc=s_ve)
        P.wait("vector", s_ve, s_ve.n)
        P.op("vector", lambda e: e.tensor_copy(B["ROWS"][0:1, 2048:2049],
                                               STAT[0:1, 3:4]), inc=s_ve)
        pp.acquire("tensor", 4)
        P.wait("tensor", s_ve, s_ve.n)
        P.op("tensor", lambda e: e.matmul(PS[4][:, 0:1], ONESR[0:1, 0:128],
                                          B["ROWS"][0:1, 2048:2049],
                                          start=True, stop=True), inc=s_pe)
        P.wait("scalar", s_pe, s_pe.n)
        P.op("scalar", lambda e: e.copy(STAT[:, 5:6], PS[4][:, 0:1]), inc=s_se)
        pp.release(4, "se", s_se.n)
        cb_ready = s_se.n

        for kt in range(NT):
            psv = pp.acquire("tensor", 2)
            if kt == 0:
                P.wait("vector", s_pe, qkv_lastread)
            for t in range(T):
                P.op("tensor", lambda e, t=t, kt=kt, wko=wko, psv=psv: e.matmul(
                    psv[:, 0:64], XBF[:, t * N + kt * 128: t * N + (kt + 1) * 128],
                    BIGB[:, wko + t * 192 + 128: wko + t * 192 + 192],
                    start=(t == 0), stop=(t == T - 1)),
                    inc=(s_pe if t == T - 1 else None))
            P.wait("vector", s_pe, s_pe.n)
            P.op("vector", lambda e, kt=kt, psv=psv: e.tensor_copy(
                BIGB[:, VA_O + kt * 65: VA_O + kt * 65 + 64], psv[:, 0:64]),
                inc=s_ve)
            pp.release(2, "ve", s_ve.n)
        v_ready = s_ve.n

        P.wait("scalar", s_se, cb_ready)
        P.wait("tensor", s_ve, v_ready)
        P.wait("tensor", s_se, cb_ready)
        for qc in range(NCH):
            cps = [pp.acquire("tensor", 2 + qq) for qq in range(4)]
            nk = 4 * qc + 4
            for kt in range(nk):
                sbank = kt % 2
                ps_s = pp.acquire("tensor", sbank)
                atb = AT_O + (kt % 3) * 512
                P.op("tensor", lambda e, kt=kt, qc=qc, ps_s=ps_s: e.matmul(
                    ps_s[:, :], BIGB[:, KT_O + kt * 128: KT_O + (kt + 1) * 128],
                    BIGB[:, QT_O + qc * 512: QT_O + (qc + 1) * 512],
                    start=True, stop=True), inc=s_pe)
                st_done = s_pe.n
                P.wait("scalar", s_pe, st_done)
                P.wait("scalar", s_pe, at_lastread[kt % 3])
                P.op("scalar", lambda e, ps_s=ps_s, atb=atb: e.activation(
                    BIGB[:, atb:atb + 512], ps_s[:, :], AF.Exp, scale=0.125,
                    bias=STAT[:, 5:6]), inc=s_se)
                pp.release(sbank, "se", s_se.n)
                at_ready = (s_se.n, "se")
                if kt >= 4 * qc:
                    r = kt - 4 * qc
                    P.wait("vector", s_se, s_se.n)
                    P.op("vector", lambda e, atb=atb, r=r: e.tensor_mul(
                        BIGB[:, atb:atb + 512], BIGB[:, atb:atb + 512],
                        MSK[:, r * 512:(r + 1) * 512]), inc=s_ve)
                    at_ready = (s_ve.n, "ve")
                cntv, semv = at_ready
                P.wait("tensor", {"se": s_se, "ve": s_ve}[semv], cntv)
                for qq in range(4):
                    P.op("tensor", lambda e, qq=qq, kt=kt, atb=atb, cps=cps,
                         nk=nk: e.matmul(
                             cps[qq][:, 0:65],
                             BIGB[:, atb + qq * 128: atb + (qq + 1) * 128],
                             BIGB[:, VA_O + kt * 65: VA_O + (kt + 1) * 65],
                             start=(kt == 0), stop=(kt == nk - 1)),
                         inc=(s_pe if qq == 3 else None))
                at_lastread[kt % 3] = s_pe.n
            ctx_done = s_pe.n
            P.wait("vector", s_pe, ctx_done)
            for qq in range(4):
                j = qc * 4 + qq
                P.op("vector", lambda e, qq=qq, cps=cps: e.reciprocal(
                    STAT[:, 8:9], cps[qq][:, 64:65]), inc=s_ve)
                P.wait("vector", s_ve, s_ve.n)
                P.op("vector", lambda e, qq=qq, j=j, h=h, cps=cps:
                     e.tensor_scalar(
                         ctxn[:, j * 1024 + h * 64: j * 1024 + h * 64 + 64],
                         cps[qq][:, 0:64], STAT[:, 8:9], None,
                         op0=ALU.mult), inc=s_ve)
                pp.release(2 + qq, "ve", s_ve.n)
        qkv_lastread = s_pe.n
        wk_lastread[buf] = s_pe.n
    all_ctxn = s_ve.n

    wpr = W["L0_attn_w_proj"].rearrange("(q p) k -> q p k", p=128)
    first_wpt_evac = True
    for rt in range(T):
        P.wait("gpsimd", s_se, s_se.n)
        P.wait("gpsimd", s_pe, qkv_lastread)
        P.dma("gpsimd", STG[:, 0:1024], wpr[rt], stream="wstg")
        P.wait("tensor", P.dsem("wstg"), P.dsem("wstg").n)
        P.wait("tensor", s_se, s_se.n)
        for thd in range(T):
            P.op("tensor", lambda e, thd=thd: e.transpose(
                TP[0][:, thd * 128:(thd + 1) * 128],
                STG[:, thd * 128:(thd + 1) * 128], IDENT[:, :]),
                inc=(s_pe if thd == T - 1 else None))
        P.wait("scalar", s_pe, s_pe.n)
        if first_wpt_evac:
            P.wait("scalar", s_pe, qkv_lastread)
            first_wpt_evac = False
        P.op("scalar", lambda e, rt=rt: e.copy(
            BIGB[:, WPT_O:WPT_O + 8192].rearrange("p (thd c) -> p thd c", thd=T)[
                :, :, rt * 128:(rt + 1) * 128],
            TP[0][:, :].rearrange("p (thd c) -> p thd c", thd=T)), inc=s_se)
    wpt_ready = s_se.n

    for ch in range(NCH):
        P.wait("tensor", s_ve, all_ctxn)
        for thd in range(T):
            P.wait("tensor", s_se, s_se.n)
            for jj in range(4):
                j = ch * 4 + jj
                P.op("tensor", lambda e, j=j, jj=jj, thd=thd: e.transpose(
                    TP[0][:, jj * 128:(jj + 1) * 128],
                    ctxn[:, j * 1024 + thd * 128: j * 1024 + (thd + 1) * 128],
                    IDENT[:, :]), inc=(s_pe if jj == 3 else None))
            P.wait("scalar", s_pe, s_pe.n)
            if ch == 0 and thd == 0:
                P.wait("scalar", s_pe, qkv_lastread)
            else:
                P.wait("scalar", s_pe, s_pe.n)
            P.op("scalar", lambda e, thd=thd: e.copy(
                BIGB[:, QT_O + thd * 512: QT_O + (thd + 1) * 512],
                TP[0][:, 0:512]), inc=s_se)
        ctxT_ready = s_se.n

        def produce_td(td, ps, ctxT_ready=ctxT_ready):
            P.wait("tensor", s_se, ctxT_ready)
            P.wait("tensor", s_bp, bp_ld)
            for thd in range(T):
                P.op("tensor", lambda e, td=td, thd=thd, ps=ps: e.matmul(
                    ps[:, :], BIGB[:, WPT_O + thd * 1024 + td * 128:
                                   WPT_O + thd * 1024 + (td + 1) * 128],
                    BIGB[:, QT_O + thd * 512: QT_O + (thd + 1) * 512],
                    start=(thd == 0), stop=False))
            P.op("tensor", lambda e, td=td, ps=ps: e.matmul(
                ps[:, :], B["ROWS"][0:1, 1024 + td * 128:1024 + (td + 1) * 128],
                B["ONESR"][0:1, :], start=False, stop=True), inc=s_pe)
            return s_pe.n

        emit_ln_chunk(B, ch, produce_td)


def emit_stage2(B):
    P, W = B["P"], B["W"]
    s_pe, s_se, s_ve, s_pl = (B["s_pe"], B["s_se"], B["s_ve"], B["s_pl"])
    swstg = P.dsem("wstg")
    XBF, BIGA, BIGB, STG, PS, TP = (B["XBF"], B["BIGA"], B["BIGB"], B["STG"],
                                    B["PS"], B["TP"])
    IDENT, ONESC, ONESR, STAT, MSK, SCR = (B["IDENT"], B["ONESC"], B["ONESR"],
                                           B["STAT"], B["MSK"], B["SCR32"])
    pp = B["pp"]
    GW_O, MUB_O, EXP_O, ON8_O = 1024, 1088, 1600, 1728

    def sl(i, c0, c1):
        return BIGA[:, i * 4096 + c0: i * 4096 + c1]

    base_se, base_ve, base_pe = s_se.n, s_ve.n, s_pe.n
    P.wait("vector", s_se, base_se)
    P.op("vector", lambda e: e.memset(SCR[:, 0:4096], 0.0), inc=s_ve)

    P.op("gpsimd", lambda e: e.iota(SCR[0:8, 4608:4736], pattern=[[1, 128]],
                                    base=0, channel_multiplier=0,
                                    allow_small_or_imprecise_dtypes=True))
    P.op("gpsimd", lambda e: e.iota(SCR[0:8, 4736:4737], pattern=[[0, 1]],
                                    base=0, channel_multiplier=1,
                                    allow_small_or_imprecise_dtypes=True), inc=s_pl)
    P.wait("vector", s_pl, s_pl.n)
    P.op("vector", lambda e: e.tensor_scalar(
        SCR[0:8, 4737:4738], SCR[0:8, 4736:4737], 16.0, None, op0=ALU.mult),
        inc=s_ve)
    P.wait("vector", s_ve, s_ve.n)
    P.op("vector", lambda e: e.tensor_scalar(
        SCR[0:8, 4738:4739], SCR[0:8, 4737:4738], 16.0, None, op0=ALU.add),
        inc=s_ve)
    P.wait("vector", s_ve, s_ve.n)
    P.op("vector", lambda e: e.tensor_scalar(
        SCR[0:8, 4864:4992], SCR[0:8, 4608:4736], SCR[0:8, 4737:4738], None,
        op0=ALU.is_ge), inc=s_ve)
    P.wait("vector", s_ve, s_ve.n)
    P.op("vector", lambda e: e.tensor_scalar(
        SCR[0:8, 4992:5120], SCR[0:8, 4608:4736], SCR[0:8, 4738:4739], None,
        op0=ALU.is_ge), inc=s_ve)
    P.wait("vector", s_ve, s_ve.n)
    P.op("vector", lambda e: e.tensor_sub(
        SCR[0:8, 4864:4992], SCR[0:8, 4864:4992], SCR[0:8, 4992:5120]),
        inc=s_ve)
    P.wait("vector", s_ve, s_ve.n)
    P.op("vector", lambda e: e.tensor_scalar(
        MSK[0:8, EXP_O:EXP_O + 128], SCR[0:8, 4864:4992], 2.0, None,
        op0=ALU.mult), inc=s_ve)
    P.op("vector", lambda e: e.memset(MSK[0:8, ON8_O:ON8_O + 128], 1.0), inc=s_ve)

    P.wait("gpsimd", s_se, base_se)
    P.dma("gpsimd", STG[0:8, 0:1024], W["L0_ffn_gate_w"], stream="wstg")
    P.wait("tensor", swstg, swstg.n)
    P.wait("tensor", s_se, s_se.n)
    for t in range(T):
        P.op("tensor", lambda e, t=t: e.transpose(
            TP[0][:, t * 8:(t + 1) * 8], STG[0:8, t * 128:(t + 1) * 128],
            IDENT[0:8, 0:8]), inc=(s_pe if t == T - 1 else None))
    P.wait("scalar", s_pe, s_pe.n)
    P.op("scalar", lambda e: e.copy(MSK[:, GW_O:GW_O + 64], TP[0][:, 0:64]),
         inc=s_se)
    gw_ready = s_se.n

    P.wait("gpsimd", s_pe, s_pe.n)
    P.dma("gpsimd", STG[:, 0:1024], W["L0_ffn_A1"], stream="wstg")
    P.dma("gpsimd", STG[:, 2048:3072], W["L0_ffn_A3"], stream="wstg")
    P.wait("tensor", swstg, swstg.n)
    for t in range(T):
        P.op("tensor", lambda e, t=t: e.transpose(
            TP[1][:, t * 128:(t + 1) * 128],
            STG[:, t * 128:(t + 1) * 128], IDENT[:, :]),
            inc=(s_pe if t == T - 1 else None))
    P.wait("scalar", s_pe, s_pe.n)
    P.op("scalar", lambda e: e.copy(STG[:, 1024:2048], TP[1][:, :]), inc=s_se)
    P.wait("tensor", s_se, s_se.n)
    for t in range(T):
        P.op("tensor", lambda e, t=t: e.transpose(
            TP[1][:, t * 128:(t + 1) * 128],
            STG[:, 2048 + t * 128: 2048 + (t + 1) * 128], IDENT[:, :]),
            inc=(s_pe if t == T - 1 else None))
    P.wait("scalar", s_pe, s_pe.n)
    P.op("scalar", lambda e: e.copy(STG[:, 3072:4096], TP[1][:, :]), inc=s_se)
    at_ready = s_se.n

    for which, ao, slot in (("r1", 1024, 6), ("r3", 3072, 7)):
        for ch in range(NCH):
            ps = pp.acquire("tensor", ch % 2)
            P.wait("tensor", s_se, at_ready)
            for t in range(T):
                P.op("tensor", lambda e, t=t, ch=ch, ao=ao, ps=ps: e.matmul(
                    ps[:, :], STG[:, ao + t * 128: ao + (t + 1) * 128],
                    XBF[:, t * N + ch * 512: t * N + (ch + 1) * 512],
                    start=(t == 0), stop=(t == T - 1)),
                    inc=(s_pe if t == T - 1 else None))
            P.wait("scalar", s_pe, s_pe.n)
            P.op("scalar", lambda e, ch=ch, slot=slot, ps=ps: e.copy(
                sl(slot, ch * 512, (ch + 1) * 512), ps[:, :]), inc=s_se)
            pp.release(ch % 2, "se", s_se.n)
    r_ready = s_se.n

    P.wait("vector", s_se, r_ready)
    prev_tp = 0
    for nt in range(NT):
        psl = pp.acquire("tensor", 2)
        P.wait("tensor", s_se, gw_ready)
        P.wait("vector", s_pe, prev_tp)
        for t in range(T):
            P.op("tensor", lambda e, t=t, nt=nt, psl=psl: e.matmul(
                psl[:, 0:8], XBF[:, t * N + nt * 128: t * N + (nt + 1) * 128],
                MSK[:, GW_O + t * 8: GW_O + (t + 1) * 8],
                start=(t == 0), stop=(t == T - 1)),
                inc=(s_pe if t == T - 1 else None))
        lg_pe = s_pe.n
        P.wait("vector", s_pe, lg_pe)
        P.op("vector", lambda e, psl=psl: e.reduce_max(
            SCR[:, 4608:4609], psl[:, 0:8], axis=AX.X), inc=s_ve)
        P.wait("vector", s_ve, s_ve.n)
        P.op("vector", lambda e, psl=psl: e.tensor_scalar(
            STG[:, 0:8], psl[:, 0:8], SCR[:, 4608:4609], None, op0=ALU.is_ge),
            inc=s_ve)
        P.wait("vector", s_ve, s_ve.n)
        P.op("vector", lambda e, psl=psl: e.scalar_tensor_tensor(
            SCR[:, 4616:4624], STG[:, 0:8], -1.0e30, psl[:, 0:8],
            op0=ALU.mult, op1=ALU.add), inc=s_ve)
        pp.release(2, "ve", s_ve.n)
        P.wait("vector", s_ve, s_ve.n)
        P.op("vector", lambda e: e.reduce_max(
            SCR[:, 4609:4610], SCR[:, 4616:4624], axis=AX.X), inc=s_ve)
        P.wait("vector", s_ve, s_ve.n)
        P.op("vector", lambda e: e.tensor_scalar(
            STG[:, 8:16], SCR[:, 4616:4624], SCR[:, 4609:4610], None,
            op0=ALU.is_ge), inc=s_ve)
        P.op("vector", lambda e: e.tensor_sub(
            SCR[:, 4610:4611], SCR[:, 4608:4609], SCR[:, 4609:4610]), inc=s_ve)
        P.wait("scalar", s_ve, s_ve.n)
        P.op("scalar", lambda e: e.activation(SCR[:, 4611:4612],
                                              SCR[:, 4610:4611], AF.Sigmoid),
             inc=s_se)
        P.op("scalar", lambda e: e.activation(SCR[:, 4612:4613],
                                              SCR[:, 4610:4611], AF.Sigmoid,
                                              scale=-1.0), inc=s_se)
        P.wait("vector", s_se, s_se.n)
        P.op("vector", lambda e: e.tensor_scalar(
            STG[:, 16:24], STG[:, 0:8], SCR[:, 4611:4612], None, op0=ALU.mult),
            inc=s_ve)
        P.op("vector", lambda e: e.tensor_scalar(
            STG[:, 24:32], STG[:, 8:16], SCR[:, 4612:4613], None, op0=ALU.mult),
            inc=s_ve)
        sv_ready = s_ve.n
        P.wait("tensor", s_ve, sv_ready)
        P.wait("tensor", s_se, s_se.n)
        for j in range(4):
            P.op("tensor", lambda e, j=j: e.transpose(
                TP[0][0:8, j * 128:(j + 1) * 128], STG[:, j * 8:(j + 1) * 8],
                IDENT[:, :]), inc=(s_pe if j == 3 else None))
        prev_tp = s_pe.n
        P.wait("scalar", s_pe, s_pe.n)
        P.op("scalar", lambda e, nt=nt: e.copy(
            BIGB[0:8, :].rearrange("p (j c) -> p j c", j=4)[:, :, nt * 128:(nt + 1) * 128],
            TP[0][0:8, 0:512].rearrange("p (j c) -> p j c", j=4)), inc=s_se)
    sv_done = s_se.n

    for ch in range(NCH):
        ps1 = pp.acquire("tensor", 0)
        ps2 = pp.acquire("tensor", 1)
        P.wait("tensor", s_se, sv_done)
        P.op("tensor", lambda e, ch=ch, ps1=ps1: e.matmul(
            ps1[:, :], MSK[0:8, EXP_O:EXP_O + 128],
            BIGB[0:8, 0 * 4096 + ch * 512: 0 * 4096 + (ch + 1) * 512],
            start=True, stop=True), inc=s_pe)
        P.op("tensor", lambda e, ch=ch, ps2=ps2: e.matmul(
            ps2[:, :], MSK[0:8, EXP_O:EXP_O + 128],
            BIGB[0:8, 1 * 4096 + ch * 512: 1 * 4096 + (ch + 1) * 512],
            start=True, stop=True), inc=s_pe)
        mm_done = s_pe.n
        P.wait("vector", s_pe, mm_done)
        P.op("vector", lambda e, ch=ch, ps1=ps1: e.tensor_mul(
            sl(0, ch * 512, (ch + 1) * 512), sl(6, ch * 512, (ch + 1) * 512),
            ps1[:, :]))
        P.op("vector", lambda e, ch=ch, ps2=ps2: e.tensor_mul(
            sl(1, ch * 512, (ch + 1) * 512), sl(6, ch * 512, (ch + 1) * 512),
            ps2[:, :]))
        P.op("vector", lambda e, ch=ch, ps1=ps1: e.tensor_mul(
            sl(2, ch * 512, (ch + 1) * 512), sl(7, ch * 512, (ch + 1) * 512),
            ps1[:, :]))
        P.op("vector", lambda e, ch=ch, ps2=ps2: e.tensor_mul(
            sl(3, ch * 512, (ch + 1) * 512), sl(7, ch * 512, (ch + 1) * 512),
            ps2[:, :]), inc=s_ve)
        pp.release(0, "ve", s_ve.n)
        pp.release(1, "ve", s_ve.n)
    P.wait("vector", s_ve, s_ve.n)
    P.op("vector", lambda e: e.tensor_sub(sl(1, 0, 4096), sl(1, 0, 4096),
                                          sl(0, 0, 4096)))
    P.op("vector", lambda e: e.tensor_sub(sl(3, 0, 4096), sl(3, 0, 4096),
                                          sl(2, 0, 4096)), inc=s_ve)
    for ch in range(NCH):
        ps1 = pp.acquire("tensor", 0)
        ps2 = pp.acquire("tensor", 1)
        for slot, svj in ((4, 2), (5, 3)):
            ps = ps1 if slot == 4 else ps2
            P.op("tensor", lambda e, ch=ch, svj=svj, ps=ps: e.matmul(
                ps[:, :], MSK[0:8, ON8_O:ON8_O + 128],
                BIGB[0:8, svj * 4096 + ch * 512: svj * 4096 + (ch + 1) * 512],
                start=True, stop=True), inc=s_pe)
        P.wait("scalar", s_pe, s_pe.n)
        P.op("scalar", lambda e, ch=ch, ps1=ps1: e.copy(
            sl(4, ch * 512, (ch + 1) * 512), ps1[:, :]))
        P.op("scalar", lambda e, ch=ch, ps2=ps2: e.copy(
            sl(5, ch * 512, (ch + 1) * 512), ps2[:, :]), inc=s_se)
        pp.release(0, "se", s_se.n)
        pp.release(1, "se", s_se.n)
    rs_all_ve = s_ve.n
    for ch in range(NCH):
        ps1 = pp.acquire("tensor", 0)
        ps2 = pp.acquire("tensor", 1)
        P.op("tensor", lambda e, ch=ch, ps1=ps1: e.matmul(
            ps1[:, :], MSK[0:8, EXP_O:EXP_O + 128],
            BIGB[0:8, 0 * 4096 + ch * 512: 0 * 4096 + (ch + 1) * 512],
            start=True, stop=True), inc=s_pe)
        P.op("tensor", lambda e, ch=ch, ps2=ps2: e.matmul(
            ps2[:, :], MSK[0:8, EXP_O:EXP_O + 128],
            BIGB[0:8, 1 * 4096 + ch * 512: 1 * 4096 + (ch + 1) * 512],
            start=True, stop=True), inc=s_pe)
        P.wait("scalar", s_pe, s_pe.n)
        P.wait("scalar", s_ve, rs_all_ve)
        P.op("scalar", lambda e, ch=ch, ps1=ps1: e.copy(
            sl(6, ch * 512, (ch + 1) * 512), ps1[:, :]))
        P.op("scalar", lambda e, ch=ch, ps2=ps2: e.copy(
            sl(7, ch * 512, (ch + 1) * 512), ps2[:, :]), inc=s_se)
        pp.release(0, "se", s_se.n)
        pp.release(1, "se", s_se.n)
    masks_ready = s_se.n
    if DBG == "s2m":
        return

    wgr = W["L0_ffn_w_gate"].rearrange("(f p) k -> p f k", p=128)
    wur = W["L0_ffn_w_up"].rearrange("(f p) k -> p f k", p=128)
    b1g = W["L0_ffn_B1"].rearrange("(e grp g p) r -> p grp g e r", e=E,
                                   grp=NCH, p=128)
    b3g = W["L0_ffn_B3"].rearrange("(e grp g p) r -> p grp g e r", e=E,
                                   grp=NCH, p=128)
    a2r = W["L0_ffn_A2"]
    S1_O, S2_O, TM_O, T1_O, T2_O, P1_O, P2_O = (8192, 8704, 9216, 9728, 10240,
                                                10752, 14848)
    DS_O = 14336
    B1Q_O, B3Q_O, A2Q_O, NATB_O = 11264, 11776, 12288, 12800
    p_spill = {}
    pbuf_last = [0, 0]
    GF = 4
    for grp in range(NCH):
        P.wait("gpsimd", s_pe, s_pe.n)
        P.dma("gpsimd", STG[:, :].rearrange("p (f k) -> p f k", f=GF),
              wgr[:, grp * GF:(grp + 1) * GF, :], stream="wstg")
        P.wait("tensor", swstg, swstg.n)
        P.wait("tensor", s_pe, s_pe.n)
        P.wait("tensor", s_se, s_se.n)
        evg = {}
        for ftl in range(GF):
            tp = TP[ftl % 2]
            if ftl >= 2:
                P.wait("tensor", s_se, evg[ftl - 2])
            for t in range(T):
                P.op("tensor", lambda e, ftl=ftl, t=t, tp=tp: e.transpose(
                    tp[:, t * 128:(t + 1) * 128],
                    STG[:, ftl * 1024 + t * 128: ftl * 1024 + (t + 1) * 128],
                    IDENT[:, :]), inc=(s_pe if t == T - 1 else None))
            P.wait("scalar", s_pe, s_pe.n)
            P.op("scalar", lambda e, ftl=ftl, tp=tp: e.copy(
                BIGB[:, ftl * 1024:(ftl + 1) * 1024], tp[:, :]), inc=s_se)
            evg[ftl] = s_se.n
        P.wait("gpsimd", s_pe, s_pe.n)
        P.dma("gpsimd", STG[:, :].rearrange("p (f k) -> p f k", f=GF),
              wur[:, grp * GF:(grp + 1) * GF, :], stream="wstg")
        P.wait("tensor", swstg, swstg.n)
        P.wait("tensor", s_se, s_se.n)
        evg = {}
        for ftl in range(GF):
            tp = TP[ftl % 2]
            if ftl >= 2:
                P.wait("tensor", s_se, evg[ftl - 2])
            for t in range(T):
                P.op("tensor", lambda e, ftl=ftl, t=t, tp=tp: e.transpose(
                    tp[:, t * 128:(t + 1) * 128],
                    STG[:, ftl * 1024 + t * 128: ftl * 1024 + (t + 1) * 128],
                    IDENT[:, :]), inc=(s_pe if t == T - 1 else None))
            P.wait("scalar", s_pe, s_pe.n)
            P.op("scalar", lambda e, ftl=ftl, tp=tp: e.copy(
                BIGB[:, 4096 + ftl * 1024: 4096 + (ftl + 1) * 1024], tp[:, :]),
                inc=s_se)
            evg[ftl] = s_se.n
        for ftl in range(GF):
            P.dma("gpsimd", BIGB[:, NATB_O + ftl * 128: NATB_O + (ftl + 1) * 128]
                  .rearrange("p (e r) -> p e r", e=E),
                  b1g[:, grp, ftl, :, :], stream="b1s%d" % ftl)
            P.dma("gpsimd", BIGB[:, NATB_O + 512 + ftl * 128: NATB_O + 512 + (ftl + 1) * 128]
                  .rearrange("p (e r) -> p e r", e=E),
                  b3g[:, grp, ftl, :, :], stream="b3s%d" % ftl)
        P.dma("gpsimd", BIGB[:, NATB_O + 1024: NATB_O + 1536],
              a2r[:, grp * 512:(grp + 1) * 512], stream="a2s")
        for ftl in range(GF):
            P.wait("tensor", P.dsem("b1s%d" % ftl), P.dsem("b1s%d" % ftl).n)
            P.wait("tensor", P.dsem("b3s%d" % ftl), P.dsem("b3s%d" % ftl).n)
        P.wait("tensor", P.dsem("a2s"), P.dsem("a2s").n)
        P.wait("tensor", s_se, s_se.n)
        for ftl in range(GF):
            P.op("tensor", lambda e, ftl=ftl: e.transpose(
                TP[0][:, ftl * 128:(ftl + 1) * 128],
                BIGB[:, NATB_O + ftl * 128: NATB_O + (ftl + 1) * 128],
                IDENT[:, :]))
            P.op("tensor", lambda e, ftl=ftl: e.transpose(
                TP[1][:, ftl * 128:(ftl + 1) * 128],
                BIGB[:, NATB_O + 512 + ftl * 128: NATB_O + 512 + (ftl + 1) * 128],
                IDENT[:, :]), inc=(s_pe if ftl == GF - 1 else None))
        P.wait("scalar", s_pe, s_pe.n)
        P.op("scalar", lambda e: e.copy(BIGB[:, B1Q_O:B1Q_O + 512],
                                        TP[0][:, 0:512]))
        P.op("scalar", lambda e: e.copy(BIGB[:, B3Q_O:B3Q_O + 512],
                                        TP[1][:, 0:512]), inc=s_se)
        P.wait("tensor", s_se, s_se.n)
        for ftl in range(GF):
            P.op("tensor", lambda e, ftl=ftl: e.transpose(
                TP[0][:, ftl * 128:(ftl + 1) * 128],
                BIGB[:, NATB_O + 1024 + ftl * 128: NATB_O + 1024 + (ftl + 1) * 128],
                IDENT[:, :]), inc=(s_pe if ftl == GF - 1 else None))
        P.wait("scalar", s_pe, s_pe.n)
        P.op("scalar", lambda e: e.copy(BIGB[:, A2Q_O:A2Q_O + 512],
                                        TP[0][:, 0:512]), inc=s_se)
        grp_w_ready = s_se.n

        for ch in range(NCH):
            psu1 = pp.acquire("tensor", 4)
            psu2 = pp.acquire("tensor", 5)
            for ftl in range(GF):
                psa = pp.acquire("tensor", ftl % 2)
                psb = pp.acquire("tensor", 2 + (ftl % 2))
                psd = pp.acquire("tensor", 3 - (ftl % 2))
                P.wait("tensor", s_se, grp_w_ready)
                if grp == 0 and ch == 0 and ftl == 0:
                    P.wait("tensor", s_se, masks_ready)
                    P.wait("tensor", s_ve, s_ve.n)
                for t in range(T):
                    P.op("tensor", lambda e, t=t, ftl=ftl, ch=ch, psa=psa: e.matmul(
                        psa[:, :], BIGB[:, ftl * 1024 + t * 128: ftl * 1024 + (t + 1) * 128],
                        XBF[:, t * N + ch * 512: t * N + (ch + 1) * 512],
                        start=(t == 0), stop=False))
                P.op("tensor", lambda e, ftl=ftl, ch=ch, psa=psa: e.matmul(
                    psa[:, :], BIGB[:, B1Q_O + ftl * 128: B1Q_O + (ftl + 1) * 128],
                    sl(0, ch * 512, (ch + 1) * 512), start=False, stop=True),
                    inc=s_pe)
                a1_pe = s_pe.n
                P.op("tensor", lambda e, ftl=ftl, ch=ch, psd=psd: e.matmul(
                    psd[:, :], BIGB[:, B1Q_O + ftl * 128: B1Q_O + (ftl + 1) * 128],
                    sl(1, ch * 512, (ch + 1) * 512), start=True, stop=True),
                    inc=s_pe)
                d1_pe = s_pe.n
                for t in range(T):
                    P.op("tensor", lambda e, t=t, ftl=ftl, ch=ch, psb=psb: e.matmul(
                        psb[:, :], BIGB[:, 4096 + ftl * 1024 + t * 128: 4096 + ftl * 1024 + (t + 1) * 128],
                        XBF[:, t * N + ch * 512: t * N + (ch + 1) * 512],
                        start=(t == 0), stop=False))
                P.op("tensor", lambda e, ftl=ftl, ch=ch, psb=psb: e.matmul(
                    psb[:, :], BIGB[:, B3Q_O + ftl * 128: B3Q_O + (ftl + 1) * 128],
                    sl(2, ch * 512, (ch + 1) * 512), start=False, stop=True),
                    inc=s_pe)
                b1_pe = s_pe.n
                P.wait("scalar", s_pe, a1_pe)
                P.op("scalar", lambda e, psa=psa: e.activation(
                    BIGB[:, S1_O:S1_O + 512], psa[:, :], AF.Silu), inc=s_se)
                s1_se = s_se.n
                P.wait("scalar", s_pe, d1_pe)
                P.op("scalar", lambda e, psd=psd: e.copy(
                    BIGB[:, DS_O:DS_O + 512], psd[:, :]), inc=s_se)
                ds1_se = s_se.n
                P.wait("vector", s_se, ds1_se)
                P.op("vector", lambda e, psa=psa: e.tensor_add(
                    SCR[:, 4096:4608], psa[:, :], BIGB[:, DS_O:DS_O + 512]),
                    inc=s_ve)
                a2_ve = s_ve.n
                pp.release(ftl % 2, "ve", a2_ve)
                P.wait("scalar", s_ve, a2_ve)
                P.op("scalar", lambda e: e.activation(
                    BIGB[:, S2_O:S2_O + 512], SCR[:, 4096:4608], AF.Silu),
                    inc=s_se)
                s2_se = s_se.n
                P.wait("tensor", s_se, ds1_se)
                P.op("tensor", lambda e, ftl=ftl, ch=ch, psd=psd: e.matmul(
                    psd[:, :], BIGB[:, B3Q_O + ftl * 128: B3Q_O + (ftl + 1) * 128],
                    sl(3, ch * 512, (ch + 1) * 512), start=True, stop=True),
                    inc=s_pe)
                d3_pe = s_pe.n
                P.wait("scalar", s_pe, d3_pe)
                P.wait("scalar", s_ve, a2_ve)
                P.op("scalar", lambda e, psd=psd: e.copy(
                    BIGB[:, DS_O:DS_O + 512], psd[:, :]), inc=s_se)
                ds3_se = s_se.n
                pp.release(3 - (ftl % 2), "se", ds3_se)
                P.wait("vector", s_pe, b1_pe)
                P.wait("vector", s_se, ds3_se)
                P.op("vector", lambda e, psb=psb: e.tensor_add(
                    SCR[:, 4608:5120], psb[:, :], BIGB[:, DS_O:DS_O + 512]),
                    inc=s_ve)
                P.wait("vector", s_se, s1_se)
                P.op("vector", lambda e, psb=psb: e.tensor_mul(
                    BIGB[:, TM_O:TM_O + 512], BIGB[:, S1_O:S1_O + 512],
                    psb[:, :]), inc=s_ve)
                pp.release(2 + (ftl % 2), "ve", s_ve.n)
                P.op("vector", lambda e, ch=ch: e.tensor_mul(
                    BIGB[:, T1_O:T1_O + 512], BIGB[:, TM_O:TM_O + 512],
                    sl(4, ch * 512, (ch + 1) * 512)), inc=s_ve)
                t1_ve = s_ve.n
                P.wait("vector", s_se, s2_se)
                P.op("vector", lambda e: e.tensor_mul(
                    BIGB[:, TM_O:TM_O + 512], BIGB[:, S2_O:S2_O + 512],
                    SCR[:, 4608:5120]), inc=s_ve)
                P.op("vector", lambda e, ch=ch: e.tensor_mul(
                    BIGB[:, T2_O:T2_O + 512], BIGB[:, TM_O:TM_O + 512],
                    sl(5, ch * 512, (ch + 1) * 512)), inc=s_ve)
                t2_ve = s_ve.n
                ft = grp * GF + ftl
                pb = P1_O if (ft % 2 == 0) else P2_O
                ssp = P.dsem("sp%d" % (ft % 2))
                P.wait("vector", ssp, pbuf_last[ft % 2])
                P.op("vector", lambda e, pb=pb: e.tensor_add(
                    BIGB[:, pb:pb + 512], BIGB[:, T1_O:T1_O + 512],
                    BIGB[:, T2_O:T2_O + 512]), inc=s_ve)
                p_ve = s_ve.n
                P.wait("tensor", s_ve, t2_ve)
                P.op("tensor", lambda e, ftl=ftl, psu1=psu1: e.matmul(
                    psu1[:, :], BIGB[:, A2Q_O + ftl * 128: A2Q_O + (ftl + 1) * 128],
                    BIGB[:, T1_O:T1_O + 512], start=(ftl == 0), stop=(ftl == GF - 1)))
                P.op("tensor", lambda e, ftl=ftl, psu2=psu2: e.matmul(
                    psu2[:, :], BIGB[:, A2Q_O + ftl * 128: A2Q_O + (ftl + 1) * 128],
                    BIGB[:, T2_O:T2_O + 512], start=(ftl == 0), stop=(ftl == GF - 1)),
                    inc=s_pe)
                P.wait("sync", s_ve, p_ve)
                P.dma("sync", W["pT_scr"][ft, ch, :, :], BIGB[:, pb:pb + 512],
                      stream="sp%d" % (ft % 2))
                pbuf_last[ft % 2] = ssp.n
            u_pe = s_pe.n
            P.wait("vector", s_pe, u_pe)
            P.op("vector", lambda e, ch=ch, psu1=psu1: e.tensor_mul(
                SCR[:, 4608:5120], psu1[:, :], sl(6, ch * 512, (ch + 1) * 512)),
                inc=s_ve)
            P.op("vector", lambda e, ch=ch: e.tensor_add(
                SCR[:, ch * 512:(ch + 1) * 512], SCR[:, ch * 512:(ch + 1) * 512],
                SCR[:, 4608:5120]), inc=s_ve)
            P.op("vector", lambda e, ch=ch, psu2=psu2: e.tensor_mul(
                SCR[:, 4608:5120], psu2[:, :], sl(7, ch * 512, (ch + 1) * 512)),
                inc=s_ve)
            P.op("vector", lambda e, ch=ch: e.tensor_add(
                SCR[:, ch * 512:(ch + 1) * 512], SCR[:, ch * 512:(ch + 1) * 512],
                SCR[:, 4608:5120]), inc=s_ve)
            pp.release(4, "ve", s_ve.n)
            pp.release(5, "ve", s_ve.n)
    sweep_pe = s_pe.n

    b2rr = W["L0_ffn_B2"].rearrange("(e q p) r -> p q e r", e=E, p=128)
    P.wait("gpsimd", s_pe, sweep_pe)
    P.wait("gpsimd", s_ve, s_ve.n)
    for td in range(T):
        P.dma("gpsimd", STG[:, td * 128:(td + 1) * 128]
              .rearrange("p (e r) -> p e r", e=E), b2rr[:, td, :, :],
              stream="wstg")
    P.wait("tensor", swstg, swstg.n)
    P.wait("tensor", s_se, s_se.n)
    for td in range(T):
        P.op("tensor", lambda e, td=td: e.transpose(
            TP[0][:, td * 128:(td + 1) * 128], STG[:, td * 128:(td + 1) * 128],
            IDENT[:, :]), inc=(s_pe if td == T - 1 else None))
    P.wait("scalar", s_pe, s_pe.n)
    P.op("scalar", lambda e: e.copy(MSK[:, 0:1024], TP[0][:, :]), inc=s_se)
    wdr = W["L0_ffn_w_down"].rearrange("(q p) k -> p q k", p=128)
    for td in range(T):
        P.wait("gpsimd", s_pe, s_pe.n)
        P.wait("gpsimd", s_se, s_se.n)
        P.dma("gpsimd", STG[:, :], wdr[:, td, :], stream="wstg")
        P.wait("tensor", swstg, swstg.n)
        P.wait("tensor", s_se, s_se.n)
        evg = {}
        for fg in range(4):
            tp = TP[fg % 2]
            if fg >= 2:
                P.wait("tensor", s_se, evg[fg - 2])
            for i in range(8):
                P.op("tensor", lambda e, fg=fg, i=i, tp=tp: e.transpose(
                    tp[:, i * 128:(i + 1) * 128],
                    STG[:, (fg * 8 + i) * 128:(fg * 8 + i + 1) * 128],
                    IDENT[:, :]), inc=(s_pe if i == 7 else None))
            P.wait("scalar", s_pe, s_pe.n)
            P.op("scalar", lambda e, td=td, fg=fg, tp=tp: e.copy(
                BIGA[:, :].rearrange("p (f c) -> p f c", f=32)[
                    :, fg * 8:(fg + 1) * 8, td * 128:(td + 1) * 128],
                tp[:, :].rearrange("p (i c) -> p i c", i=8)), inc=s_se)
            evg[fg] = s_se.n
    wdt_ready = s_se.n

    for ch in range(NCH):
        P.wait("sync", P.dsem("sp0"), P.dsem("sp0").n)
        P.wait("sync", P.dsem("sp1"), P.dsem("sp1").n)
        P.wait("sync", s_pe, s_pe.n)
        P.dma("sync", BIGB[:, 0:16384].rearrange("p (f n) -> p f n", f=32),
              W["pT_scr"][:, ch, :, :].rearrange("f p n -> p f n"),
              stream="pchunk")
        p_ld = P.dsem("pchunk").n
        P.wait("vector", s_ve, s_ve.n)
        P.op("vector", lambda e, ch=ch: e.tensor_copy(
            MSK[:, MUB_O:MUB_O + 512], SCR[:, ch * 512:(ch + 1) * 512]), inc=s_ve)
        mub_ve = s_ve.n

        def produce_td(td, ps, p_ld=p_ld, mub_ve=mub_ve):
            P.wait("tensor", P.dsem("pchunk"), p_ld)
            P.wait("tensor", s_se, wdt_ready)
            P.wait("tensor", s_ve, mub_ve)
            for ft in range(32):
                P.op("tensor", lambda e, td=td, ft=ft, ps=ps: e.matmul(
                    ps[:, :], BIGA[:, ft * 1024 + td * 128: ft * 1024 + (td + 1) * 128],
                    BIGB[:, ft * 512:(ft + 1) * 512],
                    start=(ft == 0), stop=False))
            P.op("tensor", lambda e, td=td, ps=ps: e.matmul(
                ps[:, :], MSK[:, td * 128:(td + 1) * 128],
                MSK[:, MUB_O:MUB_O + 512], start=False, stop=True), inc=s_pe)
            return s_pe.n

        emit_ln_chunk(B, ch, produce_td, sq_off=0)


def emit_stage3(B):
    P, W = B["P"], B["W"]
    s_pe, s_se, s_ve, s_pl = B["s_pe"], B["s_se"], B["s_ve"], B["s_pl"]
    XBF, BIGA, BIGB, STG, PS, TP = (B["XBF"], B["BIGA"], B["BIGB"], B["STG"],
                                    B["PS"], B["TP"])
    IDENT, ONESC, ONESR, STAT, MSK, SCR = (B["IDENT"], B["ONESC"], B["ONESR"],
                                           B["STAT"], B["MSK"], B["SCR32"])
    pp = B["pp"]
    KVS = B["KVS"]
    V_O, CS_O, QL_O = 0, 4096, [8192, 12288]
    INV2PI = 1.0 / (2.0 * math.pi)
    TWOPI = 2.0 * math.pi

    P.dma("gpsimd", B["THSB"][:, :], W["L1_attn_theta"], stream="wconst")
    P.dma("gpsimd", B["BQN"][0:32, 0:128],
          W["L1_attn_bqkvu"].rearrange("o (q p) -> (o q) p", p=128),
          stream="wconst")
    P.dma("gpsimd", B["BQNB"][0:1, :], W["L1_attn_bqkvu"][0:1, 2048:3072],
          stream="wconst")
    P.op("gpsimd", lambda e: e.iota(SCR[0:64, 0:4096], pattern=[[1, 4096]],
                                    base=0, channel_multiplier=0,
                                    allow_small_or_imprecise_dtypes=True),
         inc=s_pl)
    P.op("gpsimd", lambda e: e.iota(SCR[64:128, 0:4096], pattern=[[1, 4096]],
                                    base=0, channel_multiplier=0,
                                    allow_small_or_imprecise_dtypes=True),
         inc=s_pl)
    swc = P.dsem("wconst")
    th_ld = swc.n
    P.wait("tensor", swc, th_ld)
    P.wait("tensor", P.dsem("wc32"), P.dsem("wc32").n)
    P.wait("tensor", s_se, s_se.n)
    ps_a = pp.acquire("tensor", 2)
    ps_b = pp.acquire("tensor", 3)
    P.op("tensor", lambda e, ps_a=ps_a: e.transpose(
        ps_a[0:64, 0:16], B["THSB"][:, :], B["IDENT32"][0:16, 0:16]))
    P.op("tensor", lambda e, ps_b=ps_b: e.transpose(
        ps_b[:, 0:32], B["BQN"][0:32, :], B["IDENT32"][0:32, 0:32]), inc=s_pe)
    P.wait("vector", s_pe, s_pe.n)
    P.op("vector", lambda e, ps_a=ps_a: e.tensor_scalar(
        B["THT2"][0:64, 0:16], ps_a[0:64, 0:16], INV2PI, None, op0=ALU.mult),
        inc=s_ve)
    tht_ve = s_ve.n
    pp.release(2, "ve", s_ve.n)
    P.wait("scalar", s_pe, s_pe.n)
    P.op("scalar", lambda e, ps_b=ps_b: e.copy(B["BQT"][:, 0:32],
                                               ps_b[:, 0:32]), inc=s_se)
    bqt_se = s_se.n
    pp.release(3, "se", s_se.n)
    P.wait("gpsimd", s_ve, tht_ve)
    P.dma("gpsimd", B["THT2"][64:128, 0:16], B["THT2"][0:64, 0:16],
          stream="thtd")
    tht_d = P.dsem("thtd").n

    wq = W["L1_attn_Wqkvu"]
    pair_lastread = [0, 0]
    lrpe_done = [0, 0]

    for hp in range(H // 2):
        P.wait("gpsimd", s_pe, s_pe.n)
        P.wait("gpsimd", s_se, s_se.n)
        P.dma("gpsimd", STG[:, 0:1024],
              wq[2048 + hp * 128: 2048 + (hp + 1) * 128, :], stream="wstg")
        P.wait("tensor", P.dsem("wstg"), P.dsem("wstg").n)
        P.wait("tensor", s_se, s_se.n)
        for t in range(T):
            P.op("tensor", lambda e, t=t: e.transpose(
                TP[0][:, t * 128:(t + 1) * 128], STG[:, t * 128:(t + 1) * 128],
                IDENT[:, :]), inc=(s_pe if t == T - 1 else None))
        P.wait("scalar", s_pe, s_pe.n)
        P.op("scalar", lambda e: e.copy(MSK[:, 1024:2048], TP[0][:, :]),
             inc=s_se)
        wvt_se = s_se.n
        for nt in range(NT):
            psv = pp.acquire("tensor", 2 + (nt % 2))
            P.wait("tensor", s_se, wvt_se)
            P.wait("tensor", swc, th_ld)
            if hp > 0 and nt == 0:
                P.wait("tensor", s_pe, pair_lastread[0])
                P.wait("tensor", s_ve, s_ve.n)
            for t in range(T):
                P.op("tensor", lambda e, t=t, nt=nt, psv=psv: e.matmul(
                    psv[:, 0:128],
                    XBF[:, t * N + nt * 128: t * N + (nt + 1) * 128],
                    MSK[:, 1024 + t * 128: 1024 + (t + 1) * 128],
                    start=(t == 0), stop=False))
            P.op("tensor", lambda e, hp=hp, psv=psv: e.matmul(
                psv[:, 0:128], ONESR[0:1, 0:128],
                B["BQNB"][0:1, hp * 128:(hp + 1) * 128],
                start=False, stop=True), inc=s_pe)
            P.wait("vector", s_pe, s_pe.n)
            P.op("vector", lambda e, nt=nt, psv=psv: e.tensor_copy(
                BIGB[:, V_O + nt * 128: V_O + (nt + 1) * 128], psv[:, 0:128]),
                inc=s_ve)
            pp.release(2 + (nt % 2), "ve", s_ve.n)
        v_ready = s_ve.n

        for hh in range(2):
            h = hp * 2 + hh
            qlo = QL_O[hh]
            P.wait("vector", s_ve, tht_ve)
            P.wait("vector", P.dsem("thtd"), tht_d)
            P.wait("vector", s_ve, lrpe_done[0])
            P.wait("vector", s_ve, lrpe_done[1])
            P.wait("vector", s_pl, s_pl.n)
            for cc in range(8):
                P.wait("vector", s_se, s_se.n)
                P.op("vector", lambda e, cc=cc, h=h: e.tensor_scalar(
                    SCR[:, 4096:4608], SCR[:, cc * 512:(cc + 1) * 512],
                    B["THT2"][:, h:h + 1], None, op0=ALU.mult), inc=s_ve)
                P.op("vector", lambda e: e.tensor_scalar(
                    SCR[0:64, 4096:4608], SCR[0:64, 4096:4608], 0.25, None,
                    op0=ALU.add), inc=s_ve)
                P.op("vector", lambda e: e.tensor_copy(B["INT2"][:, :],
                                                       SCR[:, 4096:4608]),
                     inc=s_ve)
                P.op("vector", lambda e: e.tensor_copy(SCR[:, 4608:5120],
                                                       B["INT2"][:, :]),
                     inc=s_ve)
                P.op("vector", lambda e: e.tensor_sub(
                    SCR[:, 4096:4608], SCR[:, 4096:4608], SCR[:, 4608:5120]),
                    inc=s_ve)
                P.op("vector", lambda e: e.tensor_scalar(
                    SCR[:, 4608:5120], SCR[:, 4096:4608], 0.5, None,
                    op0=ALU.is_ge), inc=s_ve)
                P.op("vector", lambda e: e.scalar_tensor_tensor(
                    SCR[:, 4096:4608], SCR[:, 4608:5120], -1.0,
                    SCR[:, 4096:4608], op0=ALU.mult, op1=ALU.add), inc=s_ve)
                P.wait("scalar", s_ve, s_ve.n)
                P.wait("scalar", s_ve, lrpe_done[hh])
                P.op("scalar", lambda e, cc=cc: e.activation(
                    BIGB[:, CS_O + cc * 512: CS_O + (cc + 1) * 512],
                    SCR[:, 4096:4608], AF.Sin, scale=TWOPI), inc=s_se)
            cs_se = s_se.n

            P.wait("gpsimd", s_pe, s_pe.n)
            P.wait("gpsimd", s_se, s_se.n)
            P.dma("gpsimd", STG[0:64, 0:1024], wq[h * 64:(h + 1) * 64, :],
                  stream="wstg")
            P.dma("gpsimd", STG[0:64, 1024:2048],
                  wq[1024 + h * 64: 1024 + (h + 1) * 64, :], stream="wstg")
            P.wait("tensor", P.dsem("wstg"), P.dsem("wstg").n)
            P.wait("tensor", s_se, s_se.n)
            for t in range(T):
                P.op("tensor", lambda e, t=t: e.transpose(
                    TP[0][:, t * 64:(t + 1) * 64],
                    STG[0:64, t * 128:(t + 1) * 128], IDENT[0:64, 0:64]))
                P.op("tensor", lambda e, t=t: e.transpose(
                    TP[1][:, t * 64:(t + 1) * 64],
                    STG[0:64, 1024 + t * 128: 1024 + (t + 1) * 128],
                    IDENT[0:64, 0:64]), inc=(s_pe if t == T - 1 else None))
            P.wait("scalar", s_pe, s_pe.n)
            for half in range(2):
                P.op("scalar", lambda e, half=half: e.copy(
                    MSK[:, 0:1024].rearrange("p (t c) -> p t c", t=T)[
                        :, :, half * 64:(half + 1) * 64],
                    TP[0][:, 0:512].rearrange("p (t c) -> p t c", t=T)))
                P.op("scalar", lambda e, half=half: e.copy(
                    MSK[:, 1024:2048].rearrange("p (t c) -> p t c", t=T)[
                        :, :, half * 64:(half + 1) * 64],
                    TP[1][:, 0:512].rearrange("p (t c) -> p t c", t=T)),
                    inc=(s_se if half == 1 else None))
            wqk_se = s_se.n
            P.wait("vector", s_se, bqt_se)
            for half in range(2):
                P.op("vector", lambda e, h=h, half=half: e.tensor_copy(
                    B["BIASC"][half * 64:(half + 1) * 64, 0:1],
                    B["BQT"][(h % 2) * 64:(h % 2) * 64 + 64,
                             h // 2: h // 2 + 1]), inc=s_ve)
                P.op("vector", lambda e, h=h, half=half: e.tensor_copy(
                    B["BIASC"][half * 64:(half + 1) * 64, 1:2],
                    B["BQT"][(h % 2) * 64:(h % 2) * 64 + 64,
                             8 + h // 2: 8 + h // 2 + 1]), inc=s_ve)
            bias_ve = s_ve.n

            for ch in range(NCH):
                psq = pp.acquire("tensor", 2 + (ch % 2))
                psk = pp.acquire("tensor", 4 + (ch % 2))
                P.wait("tensor", s_se, wqk_se)
                if ch == 0:
                    P.wait("tensor", s_pe, pair_lastread[hh])
                for t in range(T):
                    P.op("tensor", lambda e, t=t, ch=ch, psq=psq: e.matmul(
                        psq[:, :], MSK[:, t * 128:(t + 1) * 128],
                        XBF[:, t * N + ch * 512: t * N + (ch + 1) * 512],
                        start=(t == 0), stop=(t == T - 1)),
                        inc=(s_pe if t == T - 1 else None))
                q_pe = s_pe.n
                for t in range(T):
                    P.op("tensor", lambda e, t=t, ch=ch, psk=psk: e.matmul(
                        psk[:, :], MSK[:, 1024 + t * 128: 1024 + (t + 1) * 128],
                        XBF[:, t * N + ch * 512: t * N + (ch + 1) * 512],
                        start=(t == 0), stop=(t == T - 1)),
                        inc=(s_pe if t == T - 1 else None))
                k_pe = s_pe.n
                P.wait("scalar", s_pe, q_pe)
                P.wait("scalar", s_ve, bias_ve)
                P.op("scalar", lambda e, ch=ch, qlo=qlo, psq=psq: e.activation(
                    BIGB[:, qlo + ch * 512: qlo + (ch + 1) * 512], psq[:, :],
                    AF.Silu, bias=B["BIASC"][:, 0:1]), inc=s_se)
                pp.release(2 + (ch % 2), "se", s_se.n)
                P.wait("scalar", s_pe, k_pe)
                P.op("scalar", lambda e, ch=ch, psk=psk: e.activation(
                    STG[:, ch * 512:(ch + 1) * 512], psk[:, :],
                    AF.Silu, bias=B["BIASC"][:, 1:2]), inc=s_se)
                pp.release(4 + (ch % 2), "se", s_se.n)
                qk_se = s_se.n
                P.wait("vector", s_se, qk_se)
                P.wait("vector", s_se, cs_se)
                P.op("vector", lambda e, ch=ch, qlo=qlo: e.tensor_mul(
                    BIGB[:, qlo + ch * 512: qlo + (ch + 1) * 512],
                    BIGB[:, qlo + ch * 512: qlo + (ch + 1) * 512],
                    BIGB[:, CS_O + ch * 512: CS_O + (ch + 1) * 512]), inc=s_ve)
                P.op("vector", lambda e, ch=ch: e.tensor_mul(
                    STG[:, ch * 512:(ch + 1) * 512],
                    STG[:, ch * 512:(ch + 1) * 512],
                    BIGB[:, CS_O + ch * 512: CS_O + (ch + 1) * 512]), inc=s_ve)
            lrpe_done[hh] = s_ve.n

            kvbank = 4 + hh
            pskv = pp.acquire("tensor", kvbank)
            P.wait("tensor", s_ve, lrpe_done[hh])
            P.wait("tensor", s_ve, v_ready)
            kl_evac = {}
            kv_mm = {}
            for nt in range(NT):
                tpb = TP[nt % 2]
                if nt >= 2:
                    P.wait("tensor", s_se, kl_evac[nt - 2])
                P.op("tensor", lambda e, nt=nt, tpb=tpb: e.transpose(
                    tpb[:, 0:128], STG[:, nt * 128:(nt + 1) * 128],
                    IDENT[:, :]), inc=s_pe)
                tr_pe = s_pe.n
                P.wait("scalar", s_pe, tr_pe)
                if nt >= 2:
                    P.wait("scalar", s_pe, kv_mm[nt - 2])
                P.op("scalar", lambda e, nt=nt, tpb=tpb: e.copy(
                    B["KLS"][:, (nt % 2) * 128:(nt % 2) * 128 + 128],
                    tpb[:, 0:128]), inc=s_se)
                kl_evac[nt] = s_se.n
                P.wait("tensor", s_se, kl_evac[nt])
                P.op("tensor", lambda e, nt=nt, hh=hh, pskv=pskv: e.matmul(
                    pskv[:, 0:64],
                    B["KLS"][:, (nt % 2) * 128:(nt % 2) * 128 + 128],
                    BIGB[:, V_O + nt * 128 + hh * 64:
                         V_O + nt * 128 + hh * 64 + 64],
                    start=(nt == 0), stop=(nt == NT - 1)), inc=s_pe)
                kv_mm[nt] = s_pe.n
            P.wait("scalar", s_pe, s_pe.n)
            P.op("scalar", lambda e, hh=hh, pskv=pskv: e.copy(
                KVS[:, hh * 64:(hh + 1) * 64], pskv[:, 0:64]), inc=s_se)
            pp.release(kvbank, "se", s_se.n)
        kvs_se = s_se.n

        ctx_ev = {}
        for nt in range(NT):
            psc = pp.acquire("tensor", 2 + (nt % 2))
            P.wait("tensor", s_se, kvs_se)
            for hh in range(2):
                P.op("tensor", lambda e, nt=nt, hh=hh, psc=psc: e.matmul(
                    psc[:, hh * 64:(hh + 1) * 64],
                    BIGB[:, QL_O[hh] + nt * 128: QL_O[hh] + (nt + 1) * 128],
                    KVS[:, hh * 64:(hh + 1) * 64], start=True, stop=True),
                    inc=(s_pe if hh == 1 else None))
            P.wait("scalar", s_pe, s_pe.n)
            P.op("scalar", lambda e, nt=nt, hp=hp, psc=psc: e.copy(
                BIGA[:, nt * 1024 + hp * 128: nt * 1024 + (hp + 1) * 128],
                psc[:, 0:128]), inc=s_se)
            pp.release(2 + (nt % 2), "se", s_se.n)
        pair_lastread[0] = s_pe.n
        pair_lastread[1] = s_pe.n
    ctx_all_se = s_se.n

    wo = W["L1_attn_Wout"].rearrange("(q p) k -> q p k", p=128)
    P.wait("gpsimd", s_pe, s_pe.n)
    P.wait("gpsimd", s_se, s_se.n)
    for which, base in (("wout", 0), ("wu", 8192)):
        for rt in range(T):
            P.wait("gpsimd", s_se, s_se.n)
            P.wait("gpsimd", s_pe, s_pe.n)
            if which == "wout":
                P.dma("gpsimd", STG[:, 0:1024], wo[rt], stream="wstg")
            else:
                P.dma("gpsimd", STG[:, 0:1024],
                      wq[3072 + rt * 128: 3072 + (rt + 1) * 128, :],
                      stream="wstg")
            P.wait("tensor", P.dsem("wstg"), P.dsem("wstg").n)
            P.wait("tensor", s_se, s_se.n)
            for thd in range(T):
                P.op("tensor", lambda e, thd=thd: e.transpose(
                    TP[0][:, thd * 128:(thd + 1) * 128],
                    STG[:, thd * 128:(thd + 1) * 128], IDENT[:, :]),
                    inc=(s_pe if thd == T - 1 else None))
            P.wait("scalar", s_pe, s_pe.n)
            P.op("scalar", lambda e, rt=rt, base=base: e.copy(
                BIGB[:, base:base + 8192].rearrange(
                    "p (thd c) -> p thd c", thd=T)[:, :, rt * 128:(rt + 1) * 128],
                TP[0][:, :].rearrange("p (thd c) -> p thd c", thd=T)),
                inc=s_se)
    wouts_se = s_se.n
    for wname, dst in (("L1_attn_ln_w", 0), ("L1_attn_ln_b", 1024)):
        P.wait("gpsimd", s_ve, s_ve.n)
        P.dma("gpsimd", B["ROWS"][0:1, 2560:3584], W[wname], stream="wconst")
        P.wait("tensor", P.dsem("wconst"), P.dsem("wconst").n)
        ps1 = pp.acquire("tensor", 2)
        P.op("tensor", lambda e, ps1=ps1: e.matmul(
            ps1[:, :], ONESR[0:1, 0:128], B["ROWS"][0:1, 2560:3072],
            start=True, stop=True))
        ps2 = pp.acquire("tensor", 3)
        P.op("tensor", lambda e, ps2=ps2: e.matmul(
            ps2[:, :], ONESR[0:1, 0:128], B["ROWS"][0:1, 3072:3584],
            start=True, stop=True), inc=s_pe)
        P.wait("vector", s_pe, s_pe.n)
        P.op("vector", lambda e, dst=dst, ps1=ps1: e.tensor_copy(
            SCR[:, dst:dst + 512], ps1[:, :]), inc=s_ve)
        P.op("vector", lambda e, dst=dst, ps2=ps2: e.tensor_copy(
            SCR[:, dst + 512:dst + 1024], ps2[:, :]), inc=s_ve)
        pp.release(2, "ve", s_ve.n)
        pp.release(3, "ve", s_ve.n)
    reps_ve = s_ve.n
    P.wait("gpsimd", s_ve, reps_ve)
    P.dma("gpsimd", B["ROWS"][0:1, 1024:2048], W["L1_attn_bout"], stream="wconst")
    P.dma("gpsimd", B["ROWS"][0:1, 2560:3584],
          W["L1_attn_bqkvu"][0:1, 3072:4096], stream="wconst")
    bias_ld = P.dsem("wconst").n

    for ch in range(NCH):
        for jj in range(4):
            nt = ch * 4 + jj
            psu = [pp.acquire("tensor", 2 + i) for i in range(2)]
            P.wait("tensor", s_se, wouts_se)
            P.wait("tensor", P.dsem("wconst"), bias_ld)
            for i in range(2):
                for t in range(T):
                    P.op("tensor", lambda e, i=i, t=t, nt=nt, psu=psu: e.matmul(
                        psu[i][:, :],
                        XBF[:, t * N + nt * 128: t * N + (nt + 1) * 128],
                        BIGB[:, 8192 + t * 1024 + i * 512:
                             8192 + t * 1024 + (i + 1) * 512],
                        start=(t == 0), stop=False))
                P.op("tensor", lambda e, i=i, psu=psu: e.matmul(
                    psu[i][:, :], ONESR[0:1, 0:128],
                    B["ROWS"][0:1, 2560 + i * 512:2560 + (i + 1) * 512],
                    start=False, stop=True),
                    inc=(s_pe if i == 1 else None))
            u_pe = s_pe.n
            P.wait("vector", s_se, ctx_all_se)
            P.op("vector", lambda e, nt=nt: e.reduce_sum(
                STAT[:, 16:17], BIGA[:, nt * 1024:(nt + 1) * 1024], axis=AX.X),
                inc=s_ve)
            sum_ve = s_ve.n
            P.wait("scalar", s_ve, sum_ve)
            P.op("scalar", lambda e, nt=nt: e.activation(
                SCR[:, 2048:3072], BIGA[:, nt * 1024:(nt + 1) * 1024],
                AF.Square, accum_out=STAT[:, 17:18]), inc=s_se)
            P.wait("vector", s_se, s_se.n)
            P.op("vector", lambda e: e.tensor_scalar(
                STAT[:, 18:19], STAT[:, 16:17], 1.0 / D, None, op0=ALU.mult),
                inc=s_ve)
            P.wait("vector", s_ve, s_ve.n)
            P.op("vector", lambda e: e.tensor_mul(
                STAT[:, 19:20], STAT[:, 18:19], STAT[:, 18:19]), inc=s_ve)
            P.wait("vector", s_ve, s_ve.n)
            P.op("vector", lambda e: e.scalar_tensor_tensor(
                STAT[:, 20:21], STAT[:, 17:18], 1.0 / D, STAT[:, 19:20],
                op0=ALU.mult, op1=ALU.subtract), inc=s_ve)
            P.wait("scalar", s_ve, s_ve.n)
            P.op("scalar", lambda e: e.activation(
                STAT[:, 21:22], STAT[:, 20:21], AF.Ln,
                bias=B["CONST"][:, 0:1]), inc=s_se)
            P.wait("scalar", s_se, s_se.n)
            P.op("scalar", lambda e: e.activation(
                STAT[:, 21:22], STAT[:, 21:22], AF.Exp, scale=-0.5), inc=s_se)
            rstd_se = s_se.n
            P.wait("vector", s_se, rstd_se)
            P.op("vector", lambda e: e.tensor_mul(
                STAT[:, 22:23], STAT[:, 18:19], STAT[:, 21:22]), inc=s_ve)
            P.wait("vector", s_ve, s_ve.n)
            P.op("vector", lambda e: e.tensor_scalar(
                STAT[:, 22:23], STAT[:, 22:23], -1.0, None, op0=ALU.mult),
                inc=s_ve)
            P.wait("vector", s_ve, s_ve.n)
            P.op("vector", lambda e, nt=nt: e.tensor_scalar(
                SCR[:, 2048:3072], BIGA[:, nt * 1024:(nt + 1) * 1024],
                STAT[:, 21:22], STAT[:, 22:23], op0=ALU.mult, op1=ALU.add),
                inc=s_ve)
            P.wait("vector", s_ve, s_ve.n)
            P.op("vector", lambda e: e.tensor_mul(
                SCR[:, 2048:3072], SCR[:, 2048:3072], SCR[:, 0:1024]),
                inc=s_ve)
            P.wait("vector", s_ve, s_ve.n)
            P.op("vector", lambda e: e.tensor_add(
                SCR[:, 2048:3072], SCR[:, 2048:3072], SCR[:, 1024:2048]),
                inc=s_ve)
            P.wait("vector", s_pe, u_pe)
            P.wait("vector", s_ve, s_ve.n)
            for i in range(2):
                P.op("vector", lambda e, i=i, psu=psu: e.tensor_mul(
                    MSK[:, i * 512:(i + 1) * 512],
                    SCR[:, 2048 + i * 512:2048 + (i + 1) * 512], psu[i][:, :]),
                    inc=s_ve)
            nlu_ve = s_ve.n
            pp.release(2, "ve", nlu_ve)
            pp.release(3, "ve", nlu_ve)
            P.wait("tensor", s_ve, nlu_ve)
            P.wait("tensor", s_se, s_se.n)
            for thd in range(T):
                P.op("tensor", lambda e, thd=thd: e.transpose(
                    TP[0][:, thd * 128:(thd + 1) * 128],
                    MSK[:, thd * 128:(thd + 1) * 128], IDENT[:, :]),
                    inc=(s_pe if thd == T - 1 else None))
            P.wait("scalar", s_pe, s_pe.n)
            P.op("scalar", lambda e, jj=jj: e.copy(
                STG[:, :].rearrange("p (thd j c) -> p thd j c", thd=T, j=4)[
                    :, :, jj, :],
                TP[0][:, :].rearrange("p (thd c) -> p thd c", thd=T)),
                inc=s_se)
        nlut_se = s_se.n

        def produce_td(td, ps, nlut_se=nlut_se):
            P.wait("tensor", s_se, nlut_se)
            for thd in range(T):
                P.op("tensor", lambda e, td=td, thd=thd, ps=ps: e.matmul(
                    ps[:, :],
                    BIGB[:, thd * 1024 + td * 128: thd * 1024 + (td + 1) * 128],
                    STG[:, thd * 512:(thd + 1) * 512],
                    start=(thd == 0), stop=False))
            P.op("tensor", lambda e, td=td, ps=ps: e.matmul(
                ps[:, :], B["ROWS"][0:1, 1024 + td * 128:1024 + (td + 1) * 128],
                ONESR[0:1, :], start=False, stop=True), inc=s_pe)
            return s_pe.n

        emit_ln_chunk(B, ch, produce_td, sq_off=ch * 4 * 1024, sq_buf=BIGA)


def emit_stage4(B):
    P, W = B["P"], B["W"]
    s_pe, s_se, s_ve, s_pl = B["s_pe"], B["s_se"], B["s_ve"], B["s_pl"]
    XBF, BIGA, BIGB, STG, PS, TP = (B["XBF"], B["BIGA"], B["BIGB"], B["STG"],
                                    B["PS"], B["TP"])
    IDENT, ONESC, ONESR, MSK, SCR = (B["IDENT"], B["ONESC"], B["ONESR"],
                                     B["MSK"], B["SCR32"])
    pp = B["pp"]
    NF = F1 // 128
    Q = 1.0 / 127.0
    Q2 = Q * Q

    wgr = W["L1_ffn_w_gate"].rearrange("(f p) k -> p f k", p=128)
    wur = W["L1_ffn_w_up"].rearrange("(f p) k -> p f k", p=128)
    wdr = W["L1_ffn_w_down"].rearrange("(q p) k -> p q k", p=128)

    hbuf_last = [0, 0]
    prev_h_ve = 0
    for ft in range(NF):
        P.wait("gpsimd", s_ve, s_ve.n)
        P.dma("gpsimd", SCR[:, 0:1024], wgr[:, ft, :], stream="wstg")
        P.dma("gpsimd", SCR[:, 1024:2048], wur[:, ft, :], stream="wstg")
        P.wait("vector", P.dsem("wstg"), P.dsem("wstg").n)
        P.wait("vector", s_pe, s_pe.n)
        P.op("vector", lambda e: e.tensor_copy(STG[:, 0:1024], SCR[:, 0:1024]),
             inc=s_ve)
        P.op("vector", lambda e: e.tensor_copy(STG[:, 1024:2048],
                                               SCR[:, 1024:2048]), inc=s_ve)
        wnat_ve = s_ve.n
        P.wait("tensor", s_ve, wnat_ve)
        P.wait("tensor", s_se, s_se.n)
        for t in range(T):
            P.op("tensor", lambda e, t=t: e.transpose(
                TP[0][:, t * 128:(t + 1) * 128], STG[:, t * 128:(t + 1) * 128],
                IDENT[:, :]))
            P.op("tensor", lambda e, t=t: e.transpose(
                TP[1][:, t * 128:(t + 1) * 128],
                STG[:, 1024 + t * 128: 1024 + (t + 1) * 128],
                IDENT[:, :]), inc=(s_pe if t == T - 1 else None))
        P.wait("scalar", s_pe, s_pe.n)
        P.op("scalar", lambda e: e.copy(MSK[:, 0:1024], TP[0][:, :]))
        P.op("scalar", lambda e: e.copy(MSK[:, 1024:2048], TP[1][:, :]),
             inc=s_se)
        wt_se = s_se.n

        for ch in range(NCH):
            psg = pp.acquire("tensor", ch % 2)
            psu = pp.acquire("tensor", 2 + (ch % 2))
            P.wait("tensor", s_se, wt_se)
            for t in range(T):
                P.op("tensor", lambda e, t=t, ch=ch, psg=psg: e.matmul(
                    psg[:, :], MSK[:, t * 128:(t + 1) * 128],
                    XBF[:, t * N + ch * 512: t * N + (ch + 1) * 512],
                    start=(t == 0), stop=(t == T - 1)),
                    inc=(s_pe if t == T - 1 else None))
            g_pe = s_pe.n
            for t in range(T):
                P.op("tensor", lambda e, t=t, ch=ch, psu=psu: e.matmul(
                    psu[:, :], MSK[:, 1024 + t * 128: 1024 + (t + 1) * 128],
                    XBF[:, t * N + ch * 512: t * N + (ch + 1) * 512],
                    start=(t == 0), stop=(t == T - 1)),
                    inc=(s_pe if t == T - 1 else None))
            u_pe = s_pe.n
            P.wait("scalar", s_pe, g_pe)
            P.wait("scalar", s_ve, prev_h_ve)
            P.op("scalar", lambda e, psg=psg: e.activation(
                STG[:, 3072:3584], psg[:, :], AF.Silu, scale=Q), inc=s_se)
            sg_se = s_se.n
            pp.release(ch % 2, "se", sg_se)
            hb = 2048 + (ft % 2) * 512
            shs = P.dsem("hs%d" % (ft % 2))
            P.wait("vector", shs, hbuf_last[ft % 2])
            P.wait("vector", s_se, sg_se)
            P.wait("vector", s_pe, u_pe)
            P.op("vector", lambda e, hb=hb, psu=psu: e.tensor_mul(
                STG[:, hb:hb + 512], STG[:, 3072:3584], psu[:, :]), inc=s_ve)
            h_ve = s_ve.n
            prev_h_ve = h_ve
            pp.release(2 + (ch % 2), "ve", h_ve)
            P.wait("sync", s_ve, h_ve)
            P.dma("sync", W["h_scr"][ft, ch, :, :], STG[:, hb:hb + 512],
                  stream="hs%d" % (ft % 2))
            hbuf_last[ft % 2] = shs.n
    sweep_pe = s_pe.n

    def wdt_slice(ft, td):
        if td < 5:
            return BIGB[:, td * 3072 + ft * 128: td * 3072 + (ft + 1) * 128]
        return BIGA[:, 12288 + (td - 5) * 3072 + ft * 128:
                    12288 + (td - 5) * 3072 + (ft + 1) * 128]

    P.wait("gpsimd", s_pe, s_pe.n)
    for td in range(T):
        P.wait("gpsimd", s_ve, s_ve.n)
        P.dma("gpsimd", SCR[:, 0:3072], wdr[:, td, :], stream="wstg")
        P.wait("vector", P.dsem("wstg"), P.dsem("wstg").n)
        P.wait("vector", s_pe, s_pe.n)
        if td == 0:
            P.wait("vector", P.dsem("hs0"), P.dsem("hs0").n)
            P.wait("vector", P.dsem("hs1"), P.dsem("hs1").n)
        P.op("vector", lambda e: e.tensor_copy(STG[:, 0:3072], SCR[:, 0:3072]),
             inc=s_ve)
        P.wait("tensor", s_ve, s_ve.n)
        evg = {}
        for fg in range(3):
            tp = TP[fg % 2]
            if fg >= 2:
                P.wait("tensor", s_se, evg[fg - 2])
            else:
                P.wait("tensor", s_se, s_se.n)
            for i in range(8):
                P.op("tensor", lambda e, fg=fg, i=i, tp=tp: e.transpose(
                    tp[:, i * 128:(i + 1) * 128],
                    STG[:, (fg * 8 + i) * 128:(fg * 8 + i + 1) * 128],
                    IDENT[:, :]), inc=(s_pe if i == 7 else None))
            P.wait("scalar", s_pe, s_pe.n)
            for i in range(8):
                P.op("scalar", lambda e, td=td, fg=fg, i=i, tp=tp: e.copy(
                    wdt_slice(fg * 8 + i, td), tp[:, i * 128:(i + 1) * 128]),
                    inc=(s_se if i == 7 else None))
            evg[fg] = s_se.n
    wdt_se = s_se.n

    for ch in range(NCH):
        P.wait("sync", P.dsem("hs0"), P.dsem("hs0").n)
        P.wait("sync", P.dsem("hs1"), P.dsem("hs1").n)
        P.wait("sync", s_pe, s_pe.n)
        P.dma("sync", BIGA[:, 0:12288].rearrange("p (f n) -> p f n", f=NF),
              W["h_scr"][:, ch, :, :].rearrange("f p n -> p f n"),
              stream="pchunk")
        h_ld = P.dsem("pchunk").n

        def produce_td(td, ps, h_ld=h_ld):
            P.wait("tensor", P.dsem("pchunk"), h_ld)
            P.wait("tensor", s_se, wdt_se)
            for ft in range(NF):
                P.op("tensor", lambda e, td=td, ft=ft, ps=ps: e.matmul(
                    ps[:, :], wdt_slice(ft, td),
                    BIGA[:, ft * 512:(ft + 1) * 512],
                    start=(ft == 0), stop=(ft == NF - 1)), inc=s_pe)
            return s_pe.n

        emit_ln_chunk(B, ch, produce_td, sq_off=0, scale=Q2)




_IDEN = np.eye(128, dtype=np.float32)


def _weight_names(grp):
    names = []
    if "s1" in grp:
        names += ["L0_attn_w_kqv", "L0_attn_w_proj", "L0_attn_b_proj"]
    if "s2" in grp:
        names += ["L0_ffn_gate_w", "L0_ffn_w_gate", "L0_ffn_w_up", "L0_ffn_w_down",
                  "L0_ffn_A1", "L0_ffn_B1", "L0_ffn_A3", "L0_ffn_B3",
                  "L0_ffn_A2", "L0_ffn_B2"]
    if "s3" in grp:
        names += ["L1_attn_Wqkvu", "L1_attn_bqkvu", "L1_attn_Wout", "L1_attn_bout",
                  "L1_attn_theta", "L1_attn_ln_w", "L1_attn_ln_b"]
    if "s4" in grp:
        names += ["L1_ffn_w_gate", "L1_ffn_w_up", "L1_ffn_w_down"]
    return names


def _prep(name, arr):
    a = np.asarray(arr)
    if name in ("L0_attn_b_proj", "L1_attn_bqkvu", "L1_attn_bout",
                "L1_attn_ln_w", "L1_attn_ln_b"):
        a = a.reshape(1, -1)
    elif name == "L1_attn_theta":
        a = a.reshape(H, DH)
    elif name in ("L0_ffn_A1", "L0_ffn_A3"):
        a = a.reshape(E * R, D)
    elif name in ("L0_ffn_B1", "L0_ffn_B3"):
        a = a.reshape(E * F0, R)
    elif name == "L0_ffn_A2":
        a = a.reshape(E * R, F0)
    elif name == "L0_ffn_B2":
        a = a.reshape(E * D, R)
    return np.ascontiguousarray(a)


def _run_stages(stage_groups, inputs):
    x = np.ascontiguousarray(inputs["L0_attn_x"].astype(np.float32, copy=False))
    nb = x.shape[0]
    cur = [np.ascontiguousarray(x[b]) for b in range(nb)]
    for grp in stage_groups:
        nc, es = build_program(grp)
        in_maps = []
        for b in range(nb):
            m = {"x_in": cur[b], "iden": _IDEN}
            for name in _weight_names(grp):
                m[name] = _prep(name, inputs[name])
            in_maps.append(m)
        res = run_bass_kernel_spmd(nc, in_maps, core_ids=list(range(nb)))
        cur = [np.asarray(res.results[b]["x_out"]) for b in range(nb)]
        es.close()
    return np.stack(cur, axis=0)


def kernel(**inputs):
    return _run_stages([["s1", "s2", "s3", "s4"]], inputs)


# version 03
# speedup vs baseline: 1.3880x; 1.3439x over previous
import math
from contextlib import ExitStack

import numpy as np
import concourse.bass as bass
import concourse.mybir as mybir
from concourse.bass_utils import run_bass_kernel_spmd

F32 = mybir.dt.float32
BF16 = mybir.dt.bfloat16
I32 = mybir.dt.int32
AF = mybir.ActivationFunctionType
ALU = mybir.AluOpType
AX = mybir.AxisListType

N = 4096
D = 1024
T = D // 128
NCH = N // 512
NT = N // 128
H = 16
DH = 64
F0 = 4096
F1 = 3072
E = 8
R = 16
EPS = 1e-5

ENGINES = ("tensor", "scalar", "vector", "gpsimd", "sync")
DBG = None


class Sem:
    __slots__ = ("h", "n", "name")

    def __init__(self, h, name):
        self.h = h
        self.n = 0
        self.name = name


class Prog:
    CHAINED = ("scalar", "vector", "gpsimd")

    def __init__(self, nc, es):
        self.nc = nc
        self.es = es
        self.q = {e: [] for e in ENGINES}
        self.sems = {}
        self.chain = {e: self.sem("chain_" + e) for e in self.CHAINED}

    def sem(self, name):
        if name not in self.sems:
            h = self.es.enter_context(self.nc.semaphore(name))
            self.sems[name] = Sem(h, name)
        return self.sems[name]

    def op(self, eng, fn, inc=None, inc_k=1, chain=True):
        cs = self.chain.get(eng) if chain else None
        if cs is not None:
            assert inc is None or inc is cs, \
                f"chained op on {eng} may only inc its chain sem"
            if cs.n > 0:
                self.q[eng].append(lambda e, s=cs, v=cs.n: e.wait_ge(s.h, v))
            cs.n += 1
            self.q[eng].append(lambda e, fn=fn, c=cs, v=cs.n - 1:
                               fn(e).then_inc(c.h, 1))
            return cs.n
        if inc is not None:
            inc.n += inc_k
            self.q[eng].append(
                lambda e, fn=fn, s=inc, k=inc_k: fn(e).then_inc(s.h, k))
            return inc.n
        self.q[eng].append(lambda e, fn=fn: fn(e))
        return None

    def wait(self, eng, sem, val=None):
        v = sem.n if val is None else val
        if v <= 0:
            return
        self.q[eng].append(lambda e, s=sem, v=v: e.wait_ge(s.h, v))

    def dma(self, eng, out, in_, inc=None, stream=None, serial=True, **kw):
        s = inc if inc is not None else self.sem("dma_" + stream)
        if serial and s.n > 0:
            self.q[eng].append(lambda e, ss=s, v=s.n: e.wait_ge(ss.h, v))
        return self.op(eng, lambda e: e.dma_start(out, in_, **kw), inc=s,
                       inc_k=16, chain=False)

    def dsem(self, stream):
        return self.sem("dma_" + stream)

    def barrier(self, tag):
        s = self.sem("bar")
        for eng in ENGINES:
            self.op(eng, lambda e: e.nop(), inc=s, inc_k=1, chain=False)
        tgt = s.n
        for eng in ENGINES:
            self.wait(eng, s, tgt)

    def build(self, block):
        for eng in ENGINES:
            items = self.q[eng]

            def section(e, items=items):
                for fn in items:
                    fn(e)

            getattr(block, eng)(section)


class PsumPool:

    def __init__(self, B):
        self.B = B
        self.free = {i: (0, "pe") for i in range(6)}

    def acquire(self, eng, i):
        cnt, sem = self.free[i]
        self.B["P"].wait(eng, self.B["s_" + sem], cnt)
        return self.B["PS"][i]

    def release(self, i, sem_name, cnt):
        self.free[i] = (cnt, sem_name)




def build_program(stages):
    nc = bass.Bass("TRN2", target_bir_lowering=False)
    es = ExitStack()
    P = Prog(nc, es)

    x_in = nc.dram_tensor("x_in", [N, D], F32, kind="ExternalInput").ap()
    x_out = nc.dram_tensor("x_out", [N, D], F32, kind="ExternalOutput").ap()
    iden_h = nc.dram_tensor("iden", [128, 128], F32, kind="ExternalInput").ap()

    W = {}

    def win(name, shape, dtype=F32):
        W[name] = nc.dram_tensor(name, shape, dtype, kind="ExternalInput").ap()

    if "s1" in stages:
        win("L0_attn_w_kqv", [3 * D, D])
        win("L0_attn_w_proj", [D, D])
        win("L0_attn_b_proj", [1, D])
    if "s2" in stages:
        win("L0_ffn_gate_w", [E, D])
        win("L0_ffn_w_gate", [F0, D])
        win("L0_ffn_w_up", [F0, D])
        win("L0_ffn_w_down", [D, F0])
        win("L0_ffn_A1", [E * R, D])
        win("L0_ffn_B1", [E * F0, R])
        win("L0_ffn_A3", [E * R, D])
        win("L0_ffn_B3", [E * F0, R])
        win("L0_ffn_A2", [E * R, F0])
        win("L0_ffn_B2", [E * D, R])
        W["pT_scr"] = nc.dram_tensor("pT_scr", [F0 // 128, NCH, 128, 512], BF16,
                                     kind="ExternalOutput" if DBG == "s2p"
                                     else "Internal").ap()
    if "s3" in stages:
        win("L1_attn_Wqkvu", [4 * D, D])
        win("L1_attn_bqkvu", [1, 4 * D])
        win("L1_attn_Wout", [D, D])
        win("L1_attn_bout", [1, D])
        win("L1_attn_theta", [H, DH])
        win("L1_attn_ln_w", [1, D])
        win("L1_attn_ln_b", [1, D])
    if "s4" in stages:
        win("L1_ffn_w_gate", [F1, D], I32)
        win("L1_ffn_w_up", [F1, D], I32)
        win("L1_ffn_w_down", [D, F1], I32)
        W["h_scr"] = nc.dram_tensor("h_scr", [F1 // 128, NCH, 128, 512], BF16,
                                    kind="Internal").ap()

    sbt = lambda name, shape, dt: es.enter_context(nc.sbuf_tensor(name, shape, dt))
    B = {}
    B["XBF"] = sbt("XBF", [128, T * N], BF16)
    B["BIGA"] = sbt("BIGA", [128, 32768], BF16)
    B["BIGB"] = sbt("BIGB", [128, 16384], BF16)
    B["SCR32"] = sbt("SCR32", [128, 5120], F32)
    B["IDENT"] = sbt("IDENT", [128, 128], BF16)
    B["IDENT32"] = sbt("IDENT32", [64, 64], F32)
    B["ONESC"] = sbt("ONESC", [128, 1], BF16)
    B["ONESR"] = sbt("ONESR", [1, 512], BF16)
    B["CONST"] = sbt("CONST", [128, 8], F32)
    B["STAT"] = sbt("STAT", [128, 64], F32)
    B["STG"] = sbt("STG", [128, 4096], BF16)
    B["MSK"] = sbt("MSK", [128, 2048], BF16)
    B["ROWS"] = sbt("ROWS", [1, 3584], BF16)
    B["KVS"] = sbt("KVS", [128, 128], BF16)
    B["KLS"] = sbt("KLS", [128, 256], BF16)
    B["THSB"] = sbt("THSB", [16, 64], F32)
    B["THT2"] = sbt("THT2", [128, 16], F32)
    B["BQN"] = sbt("BQN", [32, 128], F32)
    B["BQNB"] = sbt("BQNB", [1, 1024], BF16)
    B["BQT"] = sbt("BQT", [128, 32], F32)
    B["BIASC"] = sbt("BIASC", [128, 2], F32)
    B["INT2"] = sbt("INT2", [128, 512], I32)

    B["PS"] = [es.enter_context(nc.psum_tensor(f"PS{i}", [128, 512], F32))
               for i in range(6)]
    B["TP"] = [es.enter_context(nc.psum_tensor(f"TP{i}", [128, 1024], BF16))
               for i in range(2)]

    block = es.enter_context(nc.Block())
    for s in ("s_ld", "s_pe", "s_st", "s_sp"):
        B[s] = P.sem(s)
    B["s_se"] = P.chain["scalar"]
    B["s_ve"] = P.chain["vector"]
    B["s_pl"] = P.chain["gpsimd"]
    B.update(nc=nc, P=P, W=W, x_in=x_in, x_out=x_out, iden=iden_h)
    B["pp"] = PsumPool(B)

    emit_prologue(B)
    for st in stages:
        {"s1": emit_stage1, "s2": emit_stage2, "s3": emit_stage3,
         "s4": emit_stage4}[st](B)
        P.barrier(st)
    if DBG in ("biga", "s2m") and False:
        pass
    elif DBG == "bigb":
        s_st = B["s_st"]
        for j in range(16):
            P.dma("gpsimd",
                  B["x_out"].rearrange("(q p) d -> p q d", p=128)[:, j, :],
                  B["BIGB"][:, j * 1024:(j + 1) * 1024], inc=s_st)
        P.dma("gpsimd",
              B["x_out"].rearrange("(q p) d -> p q d", p=128)[:, 16, 0:64],
              B["STAT"][:, :], inc=s_st)
        P.wait("gpsimd", s_st, s_st.n)
    elif DBG in ("biga", "s2m"):
        s_st = B["s_st"]
        P.wait("gpsimd", B["s_se"], B["s_se"].n)
        P.wait("gpsimd", B["s_ve"], B["s_ve"].n)
        P.wait("gpsimd", B["s_pe"], B["s_pe"].n)
        for j in range(NT):
            P.dma("gpsimd",
                  B["x_out"].rearrange("(q p) d -> p q d", p=128)[:, j, :],
                  B["BIGA"][:, j * 1024:(j + 1) * 1024], inc=s_st)
        P.wait("gpsimd", s_st, s_st.n)
    else:
        emit_epilogue(B)

    P.build(block)
    return nc, es




def emit_prologue(B):
    P = B["P"]
    s_pe, s_se, s_ve = B["s_pe"], B["s_se"], B["s_ve"]
    STG, IDENT, TP = B["STG"], B["IDENT"], B["TP"]

    P.dma("gpsimd", IDENT[:, :], B["iden"], stream="wconst")
    P.dma("sync", B["IDENT32"][:, :], B["iden"][0:64, 0:64], stream="wc32")
    P.op("vector", lambda e: e.memset(B["ONESC"][:, :], 1.0))
    P.op("vector", lambda e: e.memset(B["ONESR"][:, :], 1.0))
    P.op("vector", lambda e: e.memset(B["CONST"][:, :], 0.0))
    P.op("vector", lambda e: e.memset(B["CONST"][:, 0:1], EPS), inc=s_ve)

    xr = B["x_in"].rearrange("(q p) d -> p q d", p=128)
    pe_after = {}
    evac_after = {}
    sx = P.dsem("xin")
    for g in range(NT // 4):
        if g > 0:
            P.wait("gpsimd", s_pe, pe_after[g - 1])
        P.dma("gpsimd", STG[:, :].rearrange("p (j d) -> p j d", j=4),
              xr[:, g * 4:(g + 1) * 4, :], stream="xin")
        P.wait("tensor", sx, sx.n)
        if g == 0:
            P.wait("tensor", P.dsem("wconst"), 16)
        for j in range(4):
            nt = g * 4 + j
            tp = TP[nt % 2]
            if nt >= 2:
                P.wait("tensor", s_se, evac_after[nt - 2])
            for t in range(T):
                P.op("tensor", lambda e, j=j, t=t, tp=tp: e.transpose(
                    tp[:, t * 128:(t + 1) * 128],
                    STG[:, j * 1024 + t * 128:j * 1024 + (t + 1) * 128],
                    IDENT[:, :]), inc=(s_pe if t == T - 1 else None))
            P.wait("scalar", s_pe, s_pe.n)
            P.op("scalar", lambda e, nt=nt, tp=tp: e.copy(
                B["XBF"][:, :].rearrange("p (t n) -> p t n", t=T)[
                    :, :, nt * 128:(nt + 1) * 128],
                tp[:, :].rearrange("p (t n) -> p t n", t=T)), inc=s_se)
            evac_after[nt] = s_se.n
        pe_after[g] = s_pe.n


def emit_epilogue(B):
    P = B["P"]
    s_pe, s_ve, s_st = B["s_pe"], B["s_ve"], B["s_st"]
    s_se = B["s_se"]
    TP = B["TP"]
    evac_after = {}
    st_after = {}
    ep_se = s_se.n
    ep_ve = B["s_ve"].n
    for nt in range(NT):
        tp = TP[nt % 2]
        if nt < 2:
            P.wait("tensor", s_se, ep_se)
            P.wait("tensor", s_ve, ep_ve)
        if nt >= 2:
            P.wait("tensor", s_ve, evac_after[nt - 2])
        for t in range(T):
            P.op("tensor", lambda e, nt=nt, t=t, tp=tp: e.transpose(
                tp[:, t * 128:(t + 1) * 128],
                B["XBF"][:, t * N + nt * 128: t * N + (nt + 1) * 128],
                B["IDENT"][:, :]), inc=(s_pe if t == T - 1 else None))
        P.wait("vector", s_pe, s_pe.n)
        half = (nt % 2) * 1024
        if nt >= 2:
            P.wait("vector", s_st, st_after[nt - 2])
        P.op("vector", lambda e, tp=tp, half=half: e.tensor_copy(
            B["SCR32"][:, half:half + 1024], tp[:, :]), inc=s_ve)
        evac_after[nt] = s_ve.n
        P.wait("sync", s_ve, s_ve.n)
        P.dma("sync", B["x_out"].rearrange("(q p) d -> q p d", p=128)[nt, :, :],
              B["SCR32"][:, half:half + 1024], inc=s_st)
        st_after[nt] = s_st.n
    P.wait("sync", s_st, s_st.n)




def emit_ln_chunk(B, ch, produce_td, sq_off=0, sq_buf=None, scale=1.0):
    P = B["P"]
    s_pe, s_se, s_ve = B["s_pe"], B["s_se"], B["s_ve"]
    STG, PS = B["STG"], B["PS"]
    if sq_buf is None:
        sq_buf = STG
    STG = sq_buf
    pp = B["pp"]

    def xsl(td):
        return B["XBF"][:, td * N + ch * 512: td * N + (ch + 1) * 512]

    sq_ve = {}
    for td in range(T):
        bank = td % 2
        ps = pp.acquire("tensor", bank)
        pe_done = produce_td(td, ps)
        P.wait("vector", s_pe, pe_done)
        P.op("vector", lambda e, ps=ps, td=td: e.scalar_tensor_tensor(
            xsl(td), ps[:, :], scale, xsl(td), op0=ALU.mult, op1=ALU.add),
            inc=s_ve)
        pp.release(bank, "ve", s_ve.n)
        P.op("vector", lambda e, td=td: e.tensor_mul(
            STG[:, sq_off + td * 512: sq_off + (td + 1) * 512], xsl(td), xsl(td)),
            inc=s_ve)
        sq_ve[td] = s_ve.n
    pp.acquire("tensor", 2)
    pp.acquire("tensor", 3)
    for td in range(T):
        P.wait("tensor", s_ve, sq_ve[td])
        P.op("tensor", lambda e, td=td: e.matmul(
            PS[2][0:1, :], B["ONESC"][:, :], xsl(td),
            start=(td == 0), stop=(td == T - 1)),
            inc=(s_pe if td == T - 1 else None))
        P.op("tensor", lambda e, td=td: e.matmul(
            PS[3][0:1, :], B["ONESC"][:, :],
            STG[:, sq_off + td * 512: sq_off + (td + 1) * 512],
            start=(td == 0), stop=(td == T - 1)),
            inc=(s_pe if td == T - 1 else None))
    stats_pe = s_pe.n
    SS = B["SCR32"]
    P.wait("vector", s_pe, stats_pe)
    P.op("vector", lambda e: e.tensor_scalar(SS[0:1, 4096:4608], PS[2][0:1, :],
                                             1.0 / D, None, op0=ALU.mult),
         inc=s_ve)
    P.wait("vector", s_ve, s_ve.n)
    P.op("vector", lambda e: e.tensor_mul(SS[0:1, 4608:5120], SS[0:1, 4096:4608],
                                          SS[0:1, 4096:4608]), inc=s_ve)
    P.wait("vector", s_ve, s_ve.n)
    P.op("vector", lambda e: e.scalar_tensor_tensor(
        SS[0:1, 4608:5120], PS[3][0:1, :], 1.0 / D, SS[0:1, 4608:5120],
        op0=ALU.mult, op1=ALU.subtract), inc=s_ve)
    pp.release(2, "ve", s_ve.n)
    pp.release(3, "ve", s_ve.n)
    P.wait("scalar", s_ve, s_ve.n)
    P.op("scalar", lambda e: e.activation(SS[0:1, 4608:5120], SS[0:1, 4608:5120],
                                          AF.Ln, bias=B["CONST"][0:1, 0:1]),
         inc=s_se)
    P.wait("scalar", s_se, s_se.n)
    P.op("scalar", lambda e: e.activation(SS[0:1, 4608:5120], SS[0:1, 4608:5120],
                                          AF.Exp, scale=-0.5), inc=s_se)
    P.wait("vector", s_se, s_se.n)
    P.op("vector", lambda e: e.scalar_tensor_tensor(
        SS[0:1, 4096:4608], SS[0:1, 4096:4608], -1.0, SS[0:1, 4608:5120],
        op0=ALU.mult, op1=ALU.mult), inc=s_ve)
    P.op("vector", lambda e: e.tensor_copy(B["ROWS"][0:1, 0:512],
                                           SS[0:1, 4608:5120]), inc=s_ve)
    P.wait("vector", s_ve, s_ve.n)
    P.op("vector", lambda e: e.tensor_copy(B["ROWS"][0:1, 512:1024],
                                           SS[0:1, 4096:4608]), inc=s_ve)
    pp.acquire("tensor", 4)
    pp.acquire("tensor", 5)
    P.wait("tensor", s_ve, s_ve.n)
    P.op("tensor", lambda e: e.matmul(PS[4][:, :], B["ONESR"][0:1, 0:128],
                                      B["ROWS"][0:1, 0:512], start=True, stop=True))
    P.op("tensor", lambda e: e.matmul(PS[5][:, :], B["ONESR"][0:1, 0:128],
                                      B["ROWS"][0:1, 512:1024], start=True,
                                      stop=True), inc=s_pe)
    P.wait("vector", s_pe, s_pe.n)
    for td in range(T):
        P.op("vector", lambda e, td=td: e.tensor_mul(
            STG[:, sq_off + td * 512: sq_off + (td + 1) * 512], xsl(td),
            PS[4][:, :]))
        P.op("vector", lambda e, td=td: e.tensor_add(
            xsl(td), STG[:, sq_off + td * 512: sq_off + (td + 1) * 512],
            PS[5][:, :]), inc=s_ve)
    pp.release(4, "ve", s_ve.n)
    pp.release(5, "ve", s_ve.n)
    return s_ve.n




def emit_stage1(B):
    P, W = B["P"], B["W"]
    s_pe, s_se, s_ve, s_pl = (B["s_pe"], B["s_se"], B["s_ve"], B["s_pl"])
    XBF, BIGA, BIGB, STG, PS, TP = (B["XBF"], B["BIGA"], B["BIGB"], B["STG"],
                                    B["PS"], B["TP"])
    IDENT, ONESC, ONESR, STAT, MSK = (B["IDENT"], B["ONESC"], B["ONESR"],
                                      B["STAT"], B["MSK"])
    pp = B["pp"]

    QT_O = 0
    KT_O = 4096
    AT_O = 8192
    VA_O = 9728
    WK_O = [11808, 13344]
    SQ_O = 14880
    WPT_O = 4096

    P.wait("vector", B["s_se"], B["s_se"].n)
    P.op("vector", lambda e: e.memset(BIGB[64:128, QT_O:QT_O + 4096], 0.0))
    P.op("vector", lambda e: e.memset(BIGB[64:128, KT_O:KT_O + 4096], 0.0))
    P.op("vector", lambda e: e.memset(BIGB[64:128, SQ_O:SQ_O + 1024], 0.0))
    P.op("vector", lambda e: e.memset(
        BIGB[:, VA_O:VA_O + 2080].rearrange("p (k o) -> p k o", o=65)[:, :, 64:65],
        1.0), inc=s_ve)
    P.op("gpsimd", lambda e: e.iota(B["SCR32"][:, 0:512], pattern=[[1, 512]],
                                    base=0, channel_multiplier=0,
                                    allow_small_or_imprecise_dtypes=True))
    P.op("gpsimd", lambda e: e.iota(B["SCR32"][:, 512:513], pattern=[[0, 1]],
                                    base=0, channel_multiplier=1,
                                    allow_small_or_imprecise_dtypes=True), inc=s_pl)
    P.wait("vector", s_pl, s_pl.n)
    for r in range(4):
        P.op("vector", lambda e, r=r: e.tensor_scalar(
            B["SCR32"][:, 513:514], B["SCR32"][:, 512:513], float(128 * r), None,
            op0=ALU.add), inc=s_ve)
        P.wait("vector", s_ve, s_ve.n)
        P.op("vector", lambda e, r=r: e.tensor_scalar(
            MSK[:, r * 512:(r + 1) * 512], B["SCR32"][:, 0:512],
            B["SCR32"][:, 513:514], None, op0=ALU.is_ge), inc=s_ve)
    P.dma("gpsimd", B["ROWS"][0:1, 1024:2048], W["L0_attn_b_proj"],
          stream="wconst")
    s_bp = P.dsem("wconst")
    bp_ld = s_bp.n

    wkr = W["L0_attn_w_kqv"].rearrange("(h d c) k -> h d (c k)", h=H, d=DH, c=3)

    wk_lastread = [0, 0]
    qkv_lastread = 0
    at_lastread = [0, 0, 0]
    ctxn = BIGA
    init_ve = s_ve.n

    for h in range(H):
        if h == 0:
            P.wait("tensor", s_ve, init_ve)
        buf = h % 2
        wko = WK_O[buf]
        P.wait("gpsimd", s_pe, wk_lastread[buf])
        P.wait("gpsimd", s_se, s_se.n)
        P.dma("gpsimd", STG[0:64, 0:3072], wkr[h], stream="wstg")
        P.wait("tensor", P.dsem("wstg"), P.dsem("wstg").n)
        P.wait("tensor", s_se, s_se.n)
        for idx in range(24):
            t, c = divmod(idx, 3)
            tp = TP[0] if idx < 16 else TP[1]
            sl = idx if idx < 16 else idx - 16
            P.op("tensor", lambda e, t=t, c=c, tp=tp, sl=sl: e.transpose(
                tp[:, sl * 64:(sl + 1) * 64],
                STG[0:64, c * 1024 + t * 128: c * 1024 + (t + 1) * 128],
                IDENT[0:64, 0:64]), inc=(s_pe if idx in (15, 23) else None))
        P.wait("scalar", s_pe, s_pe.n)
        P.op("scalar", lambda e, wko=wko: e.copy(BIGB[:, wko:wko + 1024],
                                                 TP[0][:, :]))
        P.op("scalar", lambda e, wko=wko: e.copy(BIGB[:, wko + 1024:wko + 1536],
                                                 TP[1][:, 0:512]), inc=s_se)
        wk_ready = s_se.n

        P.op("vector", lambda e: e.memset(STAT[0:1, 0:2], -1.0e30), inc=s_ve)
        qk_ready = {}
        for ch in range(NCH):
            bank = ch % 2
            ps = pp.acquire("tensor", bank)
            P.wait("tensor", s_se, wk_ready)
            P.wait("scalar", s_pe, qkv_lastread)
            P.wait("vector", s_pe, qkv_lastread)
            for t in range(T):
                P.op("tensor", lambda e, t=t, ch=ch, ps=ps, wko=wko: e.matmul(
                    ps[:, :], BIGB[:, wko + t * 192: wko + t * 192 + 128],
                    XBF[:, t * N + ch * 512: t * N + (ch + 1) * 512],
                    start=(t == 0), stop=(t == T - 1)),
                    inc=(s_pe if t == T - 1 else None))
            mm_done = s_pe.n
            P.wait("vector", s_pe, mm_done)
            P.op("vector", lambda e, ch=ch, ps=ps: e.tensor_copy(
                BIGB[0:64, KT_O + ch * 512: KT_O + (ch + 1) * 512], ps[0:64, :]),
                inc=s_ve)
            k_ready = s_ve.n
            P.wait("scalar", s_pe, mm_done)
            P.op("scalar", lambda e, ch=ch, ps=ps: e.copy(
                BIGB[0:64, QT_O + ch * 512: QT_O + (ch + 1) * 512],
                ps[64:128, :]), inc=s_se)
            q_ready = s_se.n
            pp.release(bank, "se", s_se.n)
            qk_ready[ch] = (q_ready, k_ready)

        for kt in range(NT):
            psv = pp.acquire("tensor", 2 + (kt % 2))
            if kt == 0:
                P.wait("vector", s_pe, qkv_lastread)
            for t in range(T):
                P.op("tensor", lambda e, t=t, kt=kt, wko=wko, psv=psv: e.matmul(
                    psv[:, 0:64], XBF[:, t * N + kt * 128: t * N + (kt + 1) * 128],
                    BIGB[:, wko + t * 192 + 128: wko + t * 192 + 192],
                    start=(t == 0), stop=(t == T - 1)),
                    inc=(s_pe if t == T - 1 else None))
            P.wait("vector", s_pe, s_pe.n)
            P.op("vector", lambda e, kt=kt, psv=psv: e.tensor_copy(
                BIGB[:, VA_O + kt * 65: VA_O + kt * 65 + 64], psv[:, 0:64]),
                inc=s_ve)
            pp.release(2 + (kt % 2), "ve", s_ve.n)
        v_ready = s_ve.n

        sq_pl = {}
        items = []
        for ch in range(NCH):
            items.append(("q", qk_ready[ch][0], s_se, QT_O, ch))
            items.append(("k", qk_ready[ch][1], s_ve, KT_O, ch))
        nb_mm = {}
        for i, (which, rdy, rsem, off, ch) in enumerate(items):
            sqo = SQ_O + (i % 2) * 512
            P.wait("gpsimd", rsem, rdy)
            if i >= 2:
                P.wait("gpsimd", s_pe, nb_mm[i - 2])
            P.op("gpsimd", lambda e, off=off, ch=ch, sqo=sqo: e.tensor_tensor(
                BIGB[0:64, sqo:sqo + 512],
                BIGB[0:64, off + ch * 512: off + (ch + 1) * 512],
                BIGB[0:64, off + ch * 512: off + (ch + 1) * 512],
                op=ALU.mult), inc=s_pl)
            sq_pl[i] = s_pl.n
            psn = pp.acquire("tensor", 4 + (i % 2))
            P.wait("tensor", s_pl, sq_pl[i])
            P.op("tensor", lambda e, sqo=sqo, psn=psn: e.matmul(
                psn[0:1, :], ONESC[:, :], BIGB[:, sqo:sqo + 512],
                start=True, stop=True), inc=s_pe)
            nb_mm[i] = s_pe.n
            P.wait("vector", s_pe, nb_mm[i])
            P.op("vector", lambda e, psn=psn: e.reduce_max(
                STAT[0:1, 2:3], psn[0:1, :], axis=AX.X), inc=s_ve)
            pp.release(4 + (i % 2), "ve", s_ve.n)
            col = 0 if which == "q" else 1
            P.wait("vector", s_ve, s_ve.n)
            P.op("vector", lambda e, col=col: e.tensor_max(
                STAT[0:1, col:col + 1], STAT[0:1, col:col + 1],
                STAT[0:1, 2:3]), inc=s_ve)
        P.op("vector", lambda e: e.tensor_add(STAT[0:1, 3:4], STAT[0:1, 0:1],
                                              STAT[0:1, 1:2]), inc=s_ve)
        P.wait("vector", s_ve, s_ve.n)
        P.op("vector", lambda e: e.tensor_scalar(
            STAT[0:1, 3:4], STAT[0:1, 3:4], -1.0 / 16.0, None, op0=ALU.mult),
            inc=s_ve)
        P.wait("vector", s_ve, s_ve.n)
        P.op("vector", lambda e: e.tensor_copy(B["ROWS"][0:1, 2048:2049],
                                               STAT[0:1, 3:4]), inc=s_ve)
        pp.acquire("tensor", 4)
        P.wait("tensor", s_ve, s_ve.n)
        P.op("tensor", lambda e: e.matmul(PS[4][:, 0:1], ONESR[0:1, 0:128],
                                          B["ROWS"][0:1, 2048:2049],
                                          start=True, stop=True), inc=s_pe)
        P.wait("scalar", s_pe, s_pe.n)
        P.op("scalar", lambda e: e.copy(STAT[:, 5:6], PS[4][:, 0:1]), inc=s_se)
        pp.release(4, "se", s_se.n)
        cb_ready = s_se.n

        for kt in range(NT):
            psv = pp.acquire("tensor", 2 + (kt % 2))
            if kt == 0:
                P.wait("vector", s_pe, qkv_lastread)
            for t in range(T):
                P.op("tensor", lambda e, t=t, kt=kt, wko=wko, psv=psv: e.matmul(
                    psv[:, 0:64], XBF[:, t * N + kt * 128: t * N + (kt + 1) * 128],
                    BIGB[:, wko + t * 192 + 128: wko + t * 192 + 192],
                    start=(t == 0), stop=(t == T - 1)),
                    inc=(s_pe if t == T - 1 else None))
            P.wait("vector", s_pe, s_pe.n)
            P.op("vector", lambda e, kt=kt, psv=psv: e.tensor_copy(
                BIGB[:, VA_O + kt * 65: VA_O + kt * 65 + 64], psv[:, 0:64]),
                inc=s_ve)
            pp.release(2 + (kt % 2), "ve", s_ve.n)
        v_ready = s_ve.n

        P.wait("scalar", s_se, cb_ready)
        P.wait("tensor", s_ve, v_ready)
        P.wait("tensor", s_se, cb_ready)
        for qc in range(NCH):
            cps = [pp.acquire("tensor", 2 + qq) for qq in range(4)]
            nk = 4 * qc + 4
            at_state = {}

            def emit_ctx(kt, nk=nk, cps=cps):
                cntv, semv = at_state[kt]
                P.wait("tensor", {"se": s_se, "ve": s_ve}[semv], cntv)
                for qq in range(4):
                    atb = AT_O + (kt % 3) * 512
                    P.op("tensor", lambda e, qq=qq, kt=kt, atb=atb, cps=cps,
                         nk=nk: e.matmul(
                             cps[qq][:, 0:65],
                             BIGB[:, atb + qq * 128: atb + (qq + 1) * 128],
                             BIGB[:, VA_O + kt * 65: VA_O + (kt + 1) * 65],
                             start=(kt == 0), stop=(kt == nk - 1)),
                         inc=(s_pe if qq == 3 else None))
                at_lastread[kt % 3] = s_pe.n

            for kt in range(nk):
                sbank = kt % 2
                ps_s = pp.acquire("tensor", sbank)
                atb = AT_O + (kt % 3) * 512
                P.op("tensor", lambda e, kt=kt, qc=qc, ps_s=ps_s: e.matmul(
                    ps_s[:, :], BIGB[:, KT_O + kt * 128: KT_O + (kt + 1) * 128],
                    BIGB[:, QT_O + qc * 512: QT_O + (qc + 1) * 512],
                    start=True, stop=True), inc=s_pe)
                st_done = s_pe.n
                P.wait("scalar", s_pe, st_done)
                P.wait("scalar", s_pe, at_lastread[kt % 3])
                P.op("scalar", lambda e, ps_s=ps_s, atb=atb: e.activation(
                    BIGB[:, atb:atb + 512], ps_s[:, :], AF.Exp, scale=0.125,
                    bias=STAT[:, 5:6]), inc=s_se)
                pp.release(sbank, "se", s_se.n)
                at_state[kt] = (s_se.n, "se")
                if kt >= 4 * qc:
                    r = kt - 4 * qc
                    P.wait("vector", s_se, s_se.n)
                    P.op("vector", lambda e, atb=atb, r=r: e.tensor_mul(
                        BIGB[:, atb:atb + 512], BIGB[:, atb:atb + 512],
                        MSK[:, r * 512:(r + 1) * 512]), inc=s_ve)
                    at_state[kt] = (s_ve.n, "ve")
                if kt >= 1:
                    emit_ctx(kt - 1)
            emit_ctx(nk - 1)
            ctx_done = s_pe.n
            P.wait("vector", s_pe, ctx_done)
            for qq in range(4):
                j = qc * 4 + qq
                P.op("vector", lambda e, qq=qq, cps=cps: e.reciprocal(
                    STAT[:, 8:9], cps[qq][:, 64:65]), inc=s_ve)
                P.wait("vector", s_ve, s_ve.n)
                P.op("vector", lambda e, qq=qq, j=j, h=h, cps=cps:
                     e.tensor_scalar(
                         ctxn[:, j * 1024 + h * 64: j * 1024 + h * 64 + 64],
                         cps[qq][:, 0:64], STAT[:, 8:9], None,
                         op0=ALU.mult), inc=s_ve)
                pp.release(2 + qq, "ve", s_ve.n)
        qkv_lastread = s_pe.n
        wk_lastread[buf] = s_pe.n
    all_ctxn = s_ve.n

    wpr = W["L0_attn_w_proj"].rearrange("(q p) k -> q p k", p=128)
    first_wpt_evac = True
    for rt in range(T):
        P.wait("gpsimd", s_se, s_se.n)
        P.wait("gpsimd", s_pe, qkv_lastread)
        P.dma("gpsimd", STG[:, 0:1024], wpr[rt], stream="wstg")
        P.wait("tensor", P.dsem("wstg"), P.dsem("wstg").n)
        P.wait("tensor", s_se, s_se.n)
        for thd in range(T):
            P.op("tensor", lambda e, thd=thd: e.transpose(
                TP[0][:, thd * 128:(thd + 1) * 128],
                STG[:, thd * 128:(thd + 1) * 128], IDENT[:, :]),
                inc=(s_pe if thd == T - 1 else None))
        P.wait("scalar", s_pe, s_pe.n)
        if first_wpt_evac:
            P.wait("scalar", s_pe, qkv_lastread)
            first_wpt_evac = False
        P.op("scalar", lambda e, rt=rt: e.copy(
            BIGB[:, WPT_O:WPT_O + 8192].rearrange("p (thd c) -> p thd c", thd=T)[
                :, :, rt * 128:(rt + 1) * 128],
            TP[0][:, :].rearrange("p (thd c) -> p thd c", thd=T)), inc=s_se)
    wpt_ready = s_se.n

    for ch in range(NCH):
        P.wait("tensor", s_ve, all_ctxn)
        for thd in range(T):
            P.wait("tensor", s_se, s_se.n)
            for jj in range(4):
                j = ch * 4 + jj
                P.op("tensor", lambda e, j=j, jj=jj, thd=thd: e.transpose(
                    TP[0][:, jj * 128:(jj + 1) * 128],
                    ctxn[:, j * 1024 + thd * 128: j * 1024 + (thd + 1) * 128],
                    IDENT[:, :]), inc=(s_pe if jj == 3 else None))
            P.wait("scalar", s_pe, s_pe.n)
            if ch == 0 and thd == 0:
                P.wait("scalar", s_pe, qkv_lastread)
            else:
                P.wait("scalar", s_pe, s_pe.n)
            P.op("scalar", lambda e, thd=thd: e.copy(
                BIGB[:, QT_O + thd * 512: QT_O + (thd + 1) * 512],
                TP[0][:, 0:512]), inc=s_se)
        ctxT_ready = s_se.n

        def produce_td(td, ps, ctxT_ready=ctxT_ready):
            P.wait("tensor", s_se, ctxT_ready)
            P.wait("tensor", s_bp, bp_ld)
            for thd in range(T):
                P.op("tensor", lambda e, td=td, thd=thd, ps=ps: e.matmul(
                    ps[:, :], BIGB[:, WPT_O + thd * 1024 + td * 128:
                                   WPT_O + thd * 1024 + (td + 1) * 128],
                    BIGB[:, QT_O + thd * 512: QT_O + (thd + 1) * 512],
                    start=(thd == 0), stop=False))
            P.op("tensor", lambda e, td=td, ps=ps: e.matmul(
                ps[:, :], B["ROWS"][0:1, 1024 + td * 128:1024 + (td + 1) * 128],
                B["ONESR"][0:1, :], start=False, stop=True), inc=s_pe)
            return s_pe.n

        emit_ln_chunk(B, ch, produce_td)


def emit_stage2(B):
    P, W = B["P"], B["W"]
    s_pe, s_se, s_ve, s_pl = (B["s_pe"], B["s_se"], B["s_ve"], B["s_pl"])
    swstg = P.dsem("wstg")
    XBF, BIGA, BIGB, STG, PS, TP = (B["XBF"], B["BIGA"], B["BIGB"], B["STG"],
                                    B["PS"], B["TP"])
    IDENT, ONESC, ONESR, STAT, MSK, SCR = (B["IDENT"], B["ONESC"], B["ONESR"],
                                           B["STAT"], B["MSK"], B["SCR32"])
    pp = B["pp"]
    GW_O, MUB_O, EXP_O, ON8_O = 1024, 1088, 1600, 1728

    def sl(i, c0, c1):
        return BIGA[:, i * 4096 + c0: i * 4096 + c1]

    base_se, base_ve, base_pe = s_se.n, s_ve.n, s_pe.n
    P.wait("vector", s_se, base_se)
    P.op("vector", lambda e: e.memset(SCR[:, 0:4096], 0.0), inc=s_ve)

    P.op("gpsimd", lambda e: e.iota(SCR[0:8, 4608:4736], pattern=[[1, 128]],
                                    base=0, channel_multiplier=0,
                                    allow_small_or_imprecise_dtypes=True))
    P.op("gpsimd", lambda e: e.iota(SCR[0:8, 4736:4737], pattern=[[0, 1]],
                                    base=0, channel_multiplier=1,
                                    allow_small_or_imprecise_dtypes=True), inc=s_pl)
    P.wait("vector", s_pl, s_pl.n)
    P.op("vector", lambda e: e.tensor_scalar(
        SCR[0:8, 4737:4738], SCR[0:8, 4736:4737], 16.0, None, op0=ALU.mult),
        inc=s_ve)
    P.wait("vector", s_ve, s_ve.n)
    P.op("vector", lambda e: e.tensor_scalar(
        SCR[0:8, 4738:4739], SCR[0:8, 4737:4738], 16.0, None, op0=ALU.add),
        inc=s_ve)
    P.wait("vector", s_ve, s_ve.n)
    P.op("vector", lambda e: e.tensor_scalar(
        SCR[0:8, 4864:4992], SCR[0:8, 4608:4736], SCR[0:8, 4737:4738], None,
        op0=ALU.is_ge), inc=s_ve)
    P.wait("vector", s_ve, s_ve.n)
    P.op("vector", lambda e: e.tensor_scalar(
        SCR[0:8, 4992:5120], SCR[0:8, 4608:4736], SCR[0:8, 4738:4739], None,
        op0=ALU.is_ge), inc=s_ve)
    P.wait("vector", s_ve, s_ve.n)
    P.op("vector", lambda e: e.tensor_sub(
        SCR[0:8, 4864:4992], SCR[0:8, 4864:4992], SCR[0:8, 4992:5120]),
        inc=s_ve)
    P.wait("vector", s_ve, s_ve.n)
    P.op("vector", lambda e: e.tensor_scalar(
        MSK[0:8, EXP_O:EXP_O + 128], SCR[0:8, 4864:4992], 2.0, None,
        op0=ALU.mult), inc=s_ve)
    P.op("vector", lambda e: e.memset(MSK[0:8, ON8_O:ON8_O + 128], 1.0), inc=s_ve)

    P.wait("gpsimd", s_se, base_se)
    P.dma("gpsimd", STG[0:8, 0:1024], W["L0_ffn_gate_w"], stream="wstg")
    P.wait("tensor", swstg, swstg.n)
    P.wait("tensor", s_se, s_se.n)
    for t in range(T):
        P.op("tensor", lambda e, t=t: e.transpose(
            TP[0][:, t * 8:(t + 1) * 8], STG[0:8, t * 128:(t + 1) * 128],
            IDENT[0:8, 0:8]), inc=(s_pe if t == T - 1 else None))
    P.wait("scalar", s_pe, s_pe.n)
    P.op("scalar", lambda e: e.copy(MSK[:, GW_O:GW_O + 64], TP[0][:, 0:64]),
         inc=s_se)
    gw_ready = s_se.n

    P.wait("gpsimd", s_pe, s_pe.n)
    P.dma("gpsimd", STG[:, 0:1024], W["L0_ffn_A1"], stream="wstg")
    P.dma("gpsimd", STG[:, 2048:3072], W["L0_ffn_A3"], stream="wstg")
    P.wait("tensor", swstg, swstg.n)
    for t in range(T):
        P.op("tensor", lambda e, t=t: e.transpose(
            TP[1][:, t * 128:(t + 1) * 128],
            STG[:, t * 128:(t + 1) * 128], IDENT[:, :]),
            inc=(s_pe if t == T - 1 else None))
    P.wait("scalar", s_pe, s_pe.n)
    P.op("scalar", lambda e: e.copy(STG[:, 1024:2048], TP[1][:, :]), inc=s_se)
    P.wait("tensor", s_se, s_se.n)
    for t in range(T):
        P.op("tensor", lambda e, t=t: e.transpose(
            TP[1][:, t * 128:(t + 1) * 128],
            STG[:, 2048 + t * 128: 2048 + (t + 1) * 128], IDENT[:, :]),
            inc=(s_pe if t == T - 1 else None))
    P.wait("scalar", s_pe, s_pe.n)
    P.op("scalar", lambda e: e.copy(STG[:, 3072:4096], TP[1][:, :]), inc=s_se)
    at_ready = s_se.n

    for which, ao, slot in (("r1", 1024, 6), ("r3", 3072, 7)):
        for ch in range(NCH):
            ps = pp.acquire("tensor", ch % 2)
            P.wait("tensor", s_se, at_ready)
            for t in range(T):
                P.op("tensor", lambda e, t=t, ch=ch, ao=ao, ps=ps: e.matmul(
                    ps[:, :], STG[:, ao + t * 128: ao + (t + 1) * 128],
                    XBF[:, t * N + ch * 512: t * N + (ch + 1) * 512],
                    start=(t == 0), stop=(t == T - 1)),
                    inc=(s_pe if t == T - 1 else None))
            P.wait("scalar", s_pe, s_pe.n)
            P.op("scalar", lambda e, ch=ch, slot=slot, ps=ps: e.copy(
                sl(slot, ch * 512, (ch + 1) * 512), ps[:, :]), inc=s_se)
            pp.release(ch % 2, "se", s_se.n)
    r_ready = s_se.n

    P.wait("vector", s_se, r_ready)
    prev_tp = 0
    for nt in range(NT):
        psl = pp.acquire("tensor", 2)
        P.wait("tensor", s_se, gw_ready)
        P.wait("vector", s_pe, prev_tp)
        for t in range(T):
            P.op("tensor", lambda e, t=t, nt=nt, psl=psl: e.matmul(
                psl[:, 0:8], XBF[:, t * N + nt * 128: t * N + (nt + 1) * 128],
                MSK[:, GW_O + t * 8: GW_O + (t + 1) * 8],
                start=(t == 0), stop=(t == T - 1)),
                inc=(s_pe if t == T - 1 else None))
        lg_pe = s_pe.n
        P.wait("vector", s_pe, lg_pe)
        P.op("vector", lambda e, psl=psl: e.reduce_max(
            SCR[:, 4608:4609], psl[:, 0:8], axis=AX.X), inc=s_ve)
        P.wait("vector", s_ve, s_ve.n)
        P.op("vector", lambda e, psl=psl: e.tensor_scalar(
            STG[:, 0:8], psl[:, 0:8], SCR[:, 4608:4609], None, op0=ALU.is_ge),
            inc=s_ve)
        P.wait("vector", s_ve, s_ve.n)
        P.op("vector", lambda e, psl=psl: e.scalar_tensor_tensor(
            SCR[:, 4616:4624], STG[:, 0:8], -1.0e30, psl[:, 0:8],
            op0=ALU.mult, op1=ALU.add), inc=s_ve)
        pp.release(2, "ve", s_ve.n)
        P.wait("vector", s_ve, s_ve.n)
        P.op("vector", lambda e: e.reduce_max(
            SCR[:, 4609:4610], SCR[:, 4616:4624], axis=AX.X), inc=s_ve)
        P.wait("vector", s_ve, s_ve.n)
        P.op("vector", lambda e: e.tensor_scalar(
            STG[:, 8:16], SCR[:, 4616:4624], SCR[:, 4609:4610], None,
            op0=ALU.is_ge), inc=s_ve)
        P.op("vector", lambda e: e.tensor_sub(
            SCR[:, 4610:4611], SCR[:, 4608:4609], SCR[:, 4609:4610]), inc=s_ve)
        P.wait("scalar", s_ve, s_ve.n)
        P.op("scalar", lambda e: e.activation(SCR[:, 4611:4612],
                                              SCR[:, 4610:4611], AF.Sigmoid),
             inc=s_se)
        P.op("scalar", lambda e: e.activation(SCR[:, 4612:4613],
                                              SCR[:, 4610:4611], AF.Sigmoid,
                                              scale=-1.0), inc=s_se)
        P.wait("vector", s_se, s_se.n)
        P.op("vector", lambda e: e.tensor_scalar(
            STG[:, 16:24], STG[:, 0:8], SCR[:, 4611:4612], None, op0=ALU.mult),
            inc=s_ve)
        P.op("vector", lambda e: e.tensor_scalar(
            STG[:, 24:32], STG[:, 8:16], SCR[:, 4612:4613], None, op0=ALU.mult),
            inc=s_ve)
        sv_ready = s_ve.n
        P.wait("tensor", s_ve, sv_ready)
        P.wait("tensor", s_se, s_se.n)
        for j in range(4):
            P.op("tensor", lambda e, j=j: e.transpose(
                TP[0][0:8, j * 128:(j + 1) * 128], STG[:, j * 8:(j + 1) * 8],
                IDENT[:, :]), inc=(s_pe if j == 3 else None))
        prev_tp = s_pe.n
        P.wait("scalar", s_pe, s_pe.n)
        P.op("scalar", lambda e, nt=nt: e.copy(
            BIGB[0:8, :].rearrange("p (j c) -> p j c", j=4)[:, :, nt * 128:(nt + 1) * 128],
            TP[0][0:8, 0:512].rearrange("p (j c) -> p j c", j=4)), inc=s_se)
    sv_done = s_se.n

    for ch in range(NCH):
        ps1 = pp.acquire("tensor", 0)
        ps2 = pp.acquire("tensor", 1)
        P.wait("tensor", s_se, sv_done)
        P.op("tensor", lambda e, ch=ch, ps1=ps1: e.matmul(
            ps1[:, :], MSK[0:8, EXP_O:EXP_O + 128],
            BIGB[0:8, 0 * 4096 + ch * 512: 0 * 4096 + (ch + 1) * 512],
            start=True, stop=True), inc=s_pe)
        P.op("tensor", lambda e, ch=ch, ps2=ps2: e.matmul(
            ps2[:, :], MSK[0:8, EXP_O:EXP_O + 128],
            BIGB[0:8, 1 * 4096 + ch * 512: 1 * 4096 + (ch + 1) * 512],
            start=True, stop=True), inc=s_pe)
        mm_done = s_pe.n
        P.wait("vector", s_pe, mm_done)
        P.op("vector", lambda e, ch=ch, ps1=ps1: e.tensor_mul(
            sl(0, ch * 512, (ch + 1) * 512), sl(6, ch * 512, (ch + 1) * 512),
            ps1[:, :]))
        P.op("vector", lambda e, ch=ch, ps2=ps2: e.tensor_mul(
            sl(1, ch * 512, (ch + 1) * 512), sl(6, ch * 512, (ch + 1) * 512),
            ps2[:, :]))
        P.op("vector", lambda e, ch=ch, ps1=ps1: e.tensor_mul(
            sl(2, ch * 512, (ch + 1) * 512), sl(7, ch * 512, (ch + 1) * 512),
            ps1[:, :]))
        P.op("vector", lambda e, ch=ch, ps2=ps2: e.tensor_mul(
            sl(3, ch * 512, (ch + 1) * 512), sl(7, ch * 512, (ch + 1) * 512),
            ps2[:, :]), inc=s_ve)
        pp.release(0, "ve", s_ve.n)
        pp.release(1, "ve", s_ve.n)
    P.wait("vector", s_ve, s_ve.n)
    P.op("vector", lambda e: e.tensor_sub(sl(1, 0, 4096), sl(1, 0, 4096),
                                          sl(0, 0, 4096)))
    P.op("vector", lambda e: e.tensor_sub(sl(3, 0, 4096), sl(3, 0, 4096),
                                          sl(2, 0, 4096)), inc=s_ve)
    for ch in range(NCH):
        ps1 = pp.acquire("tensor", 0)
        ps2 = pp.acquire("tensor", 1)
        for slot, svj in ((4, 2), (5, 3)):
            ps = ps1 if slot == 4 else ps2
            P.op("tensor", lambda e, ch=ch, svj=svj, ps=ps: e.matmul(
                ps[:, :], MSK[0:8, ON8_O:ON8_O + 128],
                BIGB[0:8, svj * 4096 + ch * 512: svj * 4096 + (ch + 1) * 512],
                start=True, stop=True), inc=s_pe)
        P.wait("scalar", s_pe, s_pe.n)
        P.op("scalar", lambda e, ch=ch, ps1=ps1: e.copy(
            sl(4, ch * 512, (ch + 1) * 512), ps1[:, :]))
        P.op("scalar", lambda e, ch=ch, ps2=ps2: e.copy(
            sl(5, ch * 512, (ch + 1) * 512), ps2[:, :]), inc=s_se)
        pp.release(0, "se", s_se.n)
        pp.release(1, "se", s_se.n)
    rs_all_ve = s_ve.n
    for ch in range(NCH):
        ps1 = pp.acquire("tensor", 0)
        ps2 = pp.acquire("tensor", 1)
        P.op("tensor", lambda e, ch=ch, ps1=ps1: e.matmul(
            ps1[:, :], MSK[0:8, EXP_O:EXP_O + 128],
            BIGB[0:8, 0 * 4096 + ch * 512: 0 * 4096 + (ch + 1) * 512],
            start=True, stop=True), inc=s_pe)
        P.op("tensor", lambda e, ch=ch, ps2=ps2: e.matmul(
            ps2[:, :], MSK[0:8, EXP_O:EXP_O + 128],
            BIGB[0:8, 1 * 4096 + ch * 512: 1 * 4096 + (ch + 1) * 512],
            start=True, stop=True), inc=s_pe)
        P.wait("scalar", s_pe, s_pe.n)
        P.wait("scalar", s_ve, rs_all_ve)
        P.op("scalar", lambda e, ch=ch, ps1=ps1: e.copy(
            sl(6, ch * 512, (ch + 1) * 512), ps1[:, :]))
        P.op("scalar", lambda e, ch=ch, ps2=ps2: e.copy(
            sl(7, ch * 512, (ch + 1) * 512), ps2[:, :]), inc=s_se)
        pp.release(0, "se", s_se.n)
        pp.release(1, "se", s_se.n)
    masks_ready = s_se.n
    if DBG == "s2m":
        return

    wgr = W["L0_ffn_w_gate"].rearrange("(f p) k -> p f k", p=128)
    wur = W["L0_ffn_w_up"].rearrange("(f p) k -> p f k", p=128)
    b1g = W["L0_ffn_B1"].rearrange("(e grp g p) r -> p grp g e r", e=E,
                                   grp=NCH, p=128)
    b3g = W["L0_ffn_B3"].rearrange("(e grp g p) r -> p grp g e r", e=E,
                                   grp=NCH, p=128)
    a2r = W["L0_ffn_A2"]
    S1_O, S2_O, TM_O, T1_O, T2_O, P1_O, P2_O = (8192, 8704, 9216, 9728, 10240,
                                                10752, 14848)
    DS_O = 14336
    B1Q_O, B3Q_O, A2Q_O, NATB_O = 11264, 11776, 12288, 12800
    p_spill = {}
    pbuf_last = [0, 0]
    GF = 4
    for grp in range(NCH):
        P.wait("gpsimd", s_pe, s_pe.n)
        P.dma("gpsimd", STG[:, :].rearrange("p (f k) -> p f k", f=GF),
              wgr[:, grp * GF:(grp + 1) * GF, :], stream="wstg")
        P.wait("tensor", swstg, swstg.n)
        P.wait("tensor", s_pe, s_pe.n)
        P.wait("tensor", s_se, s_se.n)
        evg = {}
        for ftl in range(GF):
            tp = TP[ftl % 2]
            if ftl >= 2:
                P.wait("tensor", s_se, evg[ftl - 2])
            for t in range(T):
                P.op("tensor", lambda e, ftl=ftl, t=t, tp=tp: e.transpose(
                    tp[:, t * 128:(t + 1) * 128],
                    STG[:, ftl * 1024 + t * 128: ftl * 1024 + (t + 1) * 128],
                    IDENT[:, :]), inc=(s_pe if t == T - 1 else None))
            P.wait("scalar", s_pe, s_pe.n)
            P.op("scalar", lambda e, ftl=ftl, tp=tp: e.copy(
                BIGB[:, ftl * 1024:(ftl + 1) * 1024], tp[:, :]), inc=s_se)
            evg[ftl] = s_se.n
        P.wait("gpsimd", s_pe, s_pe.n)
        P.dma("gpsimd", STG[:, :].rearrange("p (f k) -> p f k", f=GF),
              wur[:, grp * GF:(grp + 1) * GF, :], stream="wstg")
        P.wait("tensor", swstg, swstg.n)
        P.wait("tensor", s_se, s_se.n)
        evg = {}
        for ftl in range(GF):
            tp = TP[ftl % 2]
            if ftl >= 2:
                P.wait("tensor", s_se, evg[ftl - 2])
            for t in range(T):
                P.op("tensor", lambda e, ftl=ftl, t=t, tp=tp: e.transpose(
                    tp[:, t * 128:(t + 1) * 128],
                    STG[:, ftl * 1024 + t * 128: ftl * 1024 + (t + 1) * 128],
                    IDENT[:, :]), inc=(s_pe if t == T - 1 else None))
            P.wait("scalar", s_pe, s_pe.n)
            P.op("scalar", lambda e, ftl=ftl, tp=tp: e.copy(
                BIGB[:, 4096 + ftl * 1024: 4096 + (ftl + 1) * 1024], tp[:, :]),
                inc=s_se)
            evg[ftl] = s_se.n
        for ftl in range(GF):
            P.dma("gpsimd", BIGB[:, NATB_O + ftl * 128: NATB_O + (ftl + 1) * 128]
                  .rearrange("p (e r) -> p e r", e=E),
                  b1g[:, grp, ftl, :, :], stream="b1s%d" % ftl)
            P.dma("gpsimd", BIGB[:, NATB_O + 512 + ftl * 128: NATB_O + 512 + (ftl + 1) * 128]
                  .rearrange("p (e r) -> p e r", e=E),
                  b3g[:, grp, ftl, :, :], stream="b3s%d" % ftl)
        P.dma("gpsimd", BIGB[:, NATB_O + 1024: NATB_O + 1536],
              a2r[:, grp * 512:(grp + 1) * 512], stream="a2s")
        for ftl in range(GF):
            P.wait("tensor", P.dsem("b1s%d" % ftl), P.dsem("b1s%d" % ftl).n)
            P.wait("tensor", P.dsem("b3s%d" % ftl), P.dsem("b3s%d" % ftl).n)
        P.wait("tensor", P.dsem("a2s"), P.dsem("a2s").n)
        P.wait("tensor", s_se, s_se.n)
        for ftl in range(GF):
            P.op("tensor", lambda e, ftl=ftl: e.transpose(
                TP[0][:, ftl * 128:(ftl + 1) * 128],
                BIGB[:, NATB_O + ftl * 128: NATB_O + (ftl + 1) * 128],
                IDENT[:, :]))
            P.op("tensor", lambda e, ftl=ftl: e.transpose(
                TP[1][:, ftl * 128:(ftl + 1) * 128],
                BIGB[:, NATB_O + 512 + ftl * 128: NATB_O + 512 + (ftl + 1) * 128],
                IDENT[:, :]), inc=(s_pe if ftl == GF - 1 else None))
        P.wait("scalar", s_pe, s_pe.n)
        P.op("scalar", lambda e: e.copy(BIGB[:, B1Q_O:B1Q_O + 512],
                                        TP[0][:, 0:512]))
        P.op("scalar", lambda e: e.copy(BIGB[:, B3Q_O:B3Q_O + 512],
                                        TP[1][:, 0:512]), inc=s_se)
        P.wait("tensor", s_se, s_se.n)
        for ftl in range(GF):
            P.op("tensor", lambda e, ftl=ftl: e.transpose(
                TP[0][:, ftl * 128:(ftl + 1) * 128],
                BIGB[:, NATB_O + 1024 + ftl * 128: NATB_O + 1024 + (ftl + 1) * 128],
                IDENT[:, :]), inc=(s_pe if ftl == GF - 1 else None))
        P.wait("scalar", s_pe, s_pe.n)
        P.op("scalar", lambda e: e.copy(BIGB[:, A2Q_O:A2Q_O + 512],
                                        TP[0][:, 0:512]), inc=s_se)
        grp_w_ready = s_se.n

        for ch in range(NCH):
            psu1 = pp.acquire("tensor", 4)
            psu2 = pp.acquire("tensor", 5)
            pend_u = [None]
            u_done = [0]

            def emit_u(ftl, t2_ve):
                P.wait("tensor", s_ve, t2_ve)
                P.op("tensor", lambda e, ftl=ftl: e.matmul(
                    psu1[:, :], BIGB[:, A2Q_O + ftl * 128: A2Q_O + (ftl + 1) * 128],
                    BIGB[:, T1_O:T1_O + 512], start=(ftl == 0),
                    stop=(ftl == GF - 1)))
                P.op("tensor", lambda e, ftl=ftl: e.matmul(
                    psu2[:, :], BIGB[:, A2Q_O + ftl * 128: A2Q_O + (ftl + 1) * 128],
                    BIGB[:, T2_O:T2_O + 512], start=(ftl == 0),
                    stop=(ftl == GF - 1)), inc=s_pe)
                u_done[0] = s_pe.n

            for ftl in range(GF):
                psa = pp.acquire("tensor", ftl % 2)
                psb = pp.acquire("tensor", 2 + (ftl % 2))
                psd = pp.acquire("tensor", 3 - (ftl % 2))
                P.wait("tensor", s_se, grp_w_ready)
                if grp == 0 and ch == 0 and ftl == 0:
                    P.wait("tensor", s_se, masks_ready)
                    P.wait("tensor", s_ve, s_ve.n)
                for t in range(T):
                    P.op("tensor", lambda e, t=t, ftl=ftl, ch=ch, psa=psa: e.matmul(
                        psa[:, :], BIGB[:, ftl * 1024 + t * 128: ftl * 1024 + (t + 1) * 128],
                        XBF[:, t * N + ch * 512: t * N + (ch + 1) * 512],
                        start=(t == 0), stop=False))
                P.op("tensor", lambda e, ftl=ftl, ch=ch, psa=psa: e.matmul(
                    psa[:, :], BIGB[:, B1Q_O + ftl * 128: B1Q_O + (ftl + 1) * 128],
                    sl(0, ch * 512, (ch + 1) * 512), start=False, stop=True),
                    inc=s_pe)
                a1_pe = s_pe.n
                P.op("tensor", lambda e, ftl=ftl, ch=ch, psd=psd: e.matmul(
                    psd[:, :], BIGB[:, B1Q_O + ftl * 128: B1Q_O + (ftl + 1) * 128],
                    sl(1, ch * 512, (ch + 1) * 512), start=True, stop=True),
                    inc=s_pe)
                d1_pe = s_pe.n
                if pend_u[0] is not None:
                    emit_u(*pend_u[0])
                    pend_u[0] = None
                for t in range(T):
                    P.op("tensor", lambda e, t=t, ftl=ftl, ch=ch, psb=psb: e.matmul(
                        psb[:, :], BIGB[:, 4096 + ftl * 1024 + t * 128: 4096 + ftl * 1024 + (t + 1) * 128],
                        XBF[:, t * N + ch * 512: t * N + (ch + 1) * 512],
                        start=(t == 0), stop=False))
                P.op("tensor", lambda e, ftl=ftl, ch=ch, psb=psb: e.matmul(
                    psb[:, :], BIGB[:, B3Q_O + ftl * 128: B3Q_O + (ftl + 1) * 128],
                    sl(2, ch * 512, (ch + 1) * 512), start=False, stop=True),
                    inc=s_pe)
                b1_pe = s_pe.n
                P.wait("scalar", s_pe, a1_pe)
                P.op("scalar", lambda e, psa=psa: e.activation(
                    BIGB[:, S1_O:S1_O + 512], psa[:, :], AF.Silu), inc=s_se)
                s1_se = s_se.n
                P.wait("scalar", s_pe, d1_pe)
                P.op("scalar", lambda e, psd=psd: e.copy(
                    BIGB[:, DS_O:DS_O + 512], psd[:, :]), inc=s_se)
                ds1_se = s_se.n
                P.wait("vector", s_se, ds1_se)
                P.op("vector", lambda e, psa=psa: e.tensor_add(
                    SCR[:, 4096:4608], psa[:, :], BIGB[:, DS_O:DS_O + 512]),
                    inc=s_ve)
                a2_ve = s_ve.n
                pp.release(ftl % 2, "ve", a2_ve)
                P.wait("scalar", s_ve, a2_ve)
                P.op("scalar", lambda e: e.activation(
                    BIGB[:, S2_O:S2_O + 512], SCR[:, 4096:4608], AF.Silu),
                    inc=s_se)
                s2_se = s_se.n
                P.wait("tensor", s_se, ds1_se)
                P.op("tensor", lambda e, ftl=ftl, ch=ch, psd=psd: e.matmul(
                    psd[:, :], BIGB[:, B3Q_O + ftl * 128: B3Q_O + (ftl + 1) * 128],
                    sl(3, ch * 512, (ch + 1) * 512), start=True, stop=True),
                    inc=s_pe)
                d3_pe = s_pe.n
                P.wait("scalar", s_pe, d3_pe)
                P.wait("scalar", s_ve, a2_ve)
                P.op("scalar", lambda e, psd=psd: e.copy(
                    BIGB[:, DS_O:DS_O + 512], psd[:, :]), inc=s_se)
                ds3_se = s_se.n
                pp.release(3 - (ftl % 2), "se", ds3_se)
                P.wait("vector", s_pe, b1_pe)
                P.wait("vector", s_se, ds3_se)
                P.op("vector", lambda e, psb=psb: e.tensor_add(
                    SCR[:, 4608:5120], psb[:, :], BIGB[:, DS_O:DS_O + 512]),
                    inc=s_ve)
                P.wait("vector", s_se, s1_se)
                P.op("vector", lambda e, psb=psb: e.tensor_mul(
                    BIGB[:, TM_O:TM_O + 512], BIGB[:, S1_O:S1_O + 512],
                    psb[:, :]), inc=s_ve)
                pp.release(2 + (ftl % 2), "ve", s_ve.n)
                P.wait("vector", s_pe, u_done[0])
                P.op("vector", lambda e, ch=ch: e.tensor_mul(
                    BIGB[:, T1_O:T1_O + 512], BIGB[:, TM_O:TM_O + 512],
                    sl(4, ch * 512, (ch + 1) * 512)), inc=s_ve)
                t1_ve = s_ve.n
                P.wait("vector", s_se, s2_se)
                P.op("vector", lambda e: e.tensor_mul(
                    BIGB[:, TM_O:TM_O + 512], BIGB[:, S2_O:S2_O + 512],
                    SCR[:, 4608:5120]), inc=s_ve)
                P.op("vector", lambda e, ch=ch: e.tensor_mul(
                    BIGB[:, T2_O:T2_O + 512], BIGB[:, TM_O:TM_O + 512],
                    sl(5, ch * 512, (ch + 1) * 512)), inc=s_ve)
                t2_ve = s_ve.n
                ft = grp * GF + ftl
                pb = P1_O if (ft % 2 == 0) else P2_O
                ssp = P.dsem("sp%d" % (ft % 2))
                P.wait("vector", ssp, pbuf_last[ft % 2])
                P.op("vector", lambda e, pb=pb: e.tensor_add(
                    BIGB[:, pb:pb + 512], BIGB[:, T1_O:T1_O + 512],
                    BIGB[:, T2_O:T2_O + 512]), inc=s_ve)
                p_ve = s_ve.n
                pend_u[0] = (ftl, t2_ve)
                P.wait("sync", s_ve, p_ve)
                P.dma("sync", W["pT_scr"][ft, ch, :, :], BIGB[:, pb:pb + 512],
                      stream="sp%d" % (ft % 2))
                pbuf_last[ft % 2] = ssp.n
            emit_u(*pend_u[0])
            u_pe = s_pe.n
            P.wait("vector", s_pe, u_pe)
            P.op("vector", lambda e, ch=ch, psu1=psu1: e.tensor_mul(
                SCR[:, 4608:5120], psu1[:, :], sl(6, ch * 512, (ch + 1) * 512)),
                inc=s_ve)
            P.op("vector", lambda e, ch=ch: e.tensor_add(
                SCR[:, ch * 512:(ch + 1) * 512], SCR[:, ch * 512:(ch + 1) * 512],
                SCR[:, 4608:5120]), inc=s_ve)
            P.op("vector", lambda e, ch=ch, psu2=psu2: e.tensor_mul(
                SCR[:, 4608:5120], psu2[:, :], sl(7, ch * 512, (ch + 1) * 512)),
                inc=s_ve)
            P.op("vector", lambda e, ch=ch: e.tensor_add(
                SCR[:, ch * 512:(ch + 1) * 512], SCR[:, ch * 512:(ch + 1) * 512],
                SCR[:, 4608:5120]), inc=s_ve)
            pp.release(4, "ve", s_ve.n)
            pp.release(5, "ve", s_ve.n)
    sweep_pe = s_pe.n

    b2rr = W["L0_ffn_B2"].rearrange("(e q p) r -> p q e r", e=E, p=128)
    P.wait("gpsimd", s_pe, sweep_pe)
    P.wait("gpsimd", s_ve, s_ve.n)
    for td in range(T):
        P.dma("gpsimd", STG[:, td * 128:(td + 1) * 128]
              .rearrange("p (e r) -> p e r", e=E), b2rr[:, td, :, :],
              stream="wstg")
    P.wait("tensor", swstg, swstg.n)
    P.wait("tensor", s_se, s_se.n)
    for td in range(T):
        P.op("tensor", lambda e, td=td: e.transpose(
            TP[0][:, td * 128:(td + 1) * 128], STG[:, td * 128:(td + 1) * 128],
            IDENT[:, :]), inc=(s_pe if td == T - 1 else None))
    P.wait("scalar", s_pe, s_pe.n)
    P.op("scalar", lambda e: e.copy(MSK[:, 0:1024], TP[0][:, :]), inc=s_se)
    wdr = W["L0_ffn_w_down"].rearrange("(q p) k -> p q k", p=128)
    for td in range(T):
        P.wait("gpsimd", s_pe, s_pe.n)
        P.wait("gpsimd", s_se, s_se.n)
        P.dma("gpsimd", STG[:, :], wdr[:, td, :], stream="wstg")
        P.wait("tensor", swstg, swstg.n)
        P.wait("tensor", s_se, s_se.n)
        evg = {}
        for fg in range(4):
            tp = TP[fg % 2]
            if fg >= 2:
                P.wait("tensor", s_se, evg[fg - 2])
            for i in range(8):
                P.op("tensor", lambda e, fg=fg, i=i, tp=tp: e.transpose(
                    tp[:, i * 128:(i + 1) * 128],
                    STG[:, (fg * 8 + i) * 128:(fg * 8 + i + 1) * 128],
                    IDENT[:, :]), inc=(s_pe if i == 7 else None))
            P.wait("scalar", s_pe, s_pe.n)
            P.op("scalar", lambda e, td=td, fg=fg, tp=tp: e.copy(
                BIGA[:, :].rearrange("p (f c) -> p f c", f=32)[
                    :, fg * 8:(fg + 1) * 8, td * 128:(td + 1) * 128],
                tp[:, :].rearrange("p (i c) -> p i c", i=8)), inc=s_se)
            evg[fg] = s_se.n
    wdt_ready = s_se.n

    for ch in range(NCH):
        P.wait("sync", P.dsem("sp0"), P.dsem("sp0").n)
        P.wait("sync", P.dsem("sp1"), P.dsem("sp1").n)
        P.wait("sync", s_pe, s_pe.n)
        P.dma("sync", BIGB[:, 0:16384].rearrange("p (f n) -> p f n", f=32),
              W["pT_scr"][:, ch, :, :].rearrange("f p n -> p f n"),
              stream="pchunk")
        p_ld = P.dsem("pchunk").n
        P.wait("vector", s_ve, s_ve.n)
        P.op("vector", lambda e, ch=ch: e.tensor_copy(
            MSK[:, MUB_O:MUB_O + 512], SCR[:, ch * 512:(ch + 1) * 512]), inc=s_ve)
        mub_ve = s_ve.n

        def produce_td(td, ps, p_ld=p_ld, mub_ve=mub_ve):
            P.wait("tensor", P.dsem("pchunk"), p_ld)
            P.wait("tensor", s_se, wdt_ready)
            P.wait("tensor", s_ve, mub_ve)
            for ft in range(32):
                P.op("tensor", lambda e, td=td, ft=ft, ps=ps: e.matmul(
                    ps[:, :], BIGA[:, ft * 1024 + td * 128: ft * 1024 + (td + 1) * 128],
                    BIGB[:, ft * 512:(ft + 1) * 512],
                    start=(ft == 0), stop=False))
            P.op("tensor", lambda e, td=td, ps=ps: e.matmul(
                ps[:, :], MSK[:, td * 128:(td + 1) * 128],
                MSK[:, MUB_O:MUB_O + 512], start=False, stop=True), inc=s_pe)
            return s_pe.n

        emit_ln_chunk(B, ch, produce_td, sq_off=0)


def emit_stage3(B):
    P, W = B["P"], B["W"]
    s_pe, s_se, s_ve, s_pl = B["s_pe"], B["s_se"], B["s_ve"], B["s_pl"]
    XBF, BIGA, BIGB, STG, PS, TP = (B["XBF"], B["BIGA"], B["BIGB"], B["STG"],
                                    B["PS"], B["TP"])
    IDENT, ONESC, ONESR, STAT, MSK, SCR = (B["IDENT"], B["ONESC"], B["ONESR"],
                                           B["STAT"], B["MSK"], B["SCR32"])
    pp = B["pp"]
    KVS = B["KVS"]
    V_O, CS_O, QL_O = 0, 4096, [8192, 12288]
    INV2PI = 1.0 / (2.0 * math.pi)
    TWOPI = 2.0 * math.pi

    P.dma("gpsimd", B["THSB"][:, :], W["L1_attn_theta"], stream="wconst")
    P.dma("gpsimd", B["BQN"][0:32, 0:128],
          W["L1_attn_bqkvu"].rearrange("o (q p) -> (o q) p", p=128),
          stream="wconst")
    P.dma("gpsimd", B["BQNB"][0:1, :], W["L1_attn_bqkvu"][0:1, 2048:3072],
          stream="wconst")
    P.op("gpsimd", lambda e: e.iota(SCR[0:64, 0:4096], pattern=[[1, 4096]],
                                    base=0, channel_multiplier=0,
                                    allow_small_or_imprecise_dtypes=True),
         inc=s_pl)
    P.op("gpsimd", lambda e: e.iota(SCR[64:128, 0:4096], pattern=[[1, 4096]],
                                    base=0, channel_multiplier=0,
                                    allow_small_or_imprecise_dtypes=True),
         inc=s_pl)
    swc = P.dsem("wconst")
    th_ld = swc.n
    P.wait("tensor", swc, th_ld)
    P.wait("tensor", P.dsem("wc32"), P.dsem("wc32").n)
    P.wait("tensor", s_se, s_se.n)
    ps_a = pp.acquire("tensor", 2)
    ps_b = pp.acquire("tensor", 3)
    P.op("tensor", lambda e, ps_a=ps_a: e.transpose(
        ps_a[0:64, 0:16], B["THSB"][:, :], B["IDENT32"][0:16, 0:16]))
    P.op("tensor", lambda e, ps_b=ps_b: e.transpose(
        ps_b[:, 0:32], B["BQN"][0:32, :], B["IDENT32"][0:32, 0:32]), inc=s_pe)
    P.wait("vector", s_pe, s_pe.n)
    P.op("vector", lambda e, ps_a=ps_a: e.tensor_scalar(
        B["THT2"][0:64, 0:16], ps_a[0:64, 0:16], INV2PI, None, op0=ALU.mult),
        inc=s_ve)
    tht_ve = s_ve.n
    pp.release(2, "ve", s_ve.n)
    P.wait("scalar", s_pe, s_pe.n)
    P.op("scalar", lambda e, ps_b=ps_b: e.copy(B["BQT"][:, 0:32],
                                               ps_b[:, 0:32]), inc=s_se)
    bqt_se = s_se.n
    pp.release(3, "se", s_se.n)
    P.wait("gpsimd", s_ve, tht_ve)
    P.dma("gpsimd", B["THT2"][64:128, 0:16], B["THT2"][0:64, 0:16],
          stream="thtd")
    tht_d = P.dsem("thtd").n

    wq = W["L1_attn_Wqkvu"]
    pair_lastread = [0, 0]
    lrpe_done = [0, 0]
    prev_sin_se = 0

    for hp in range(H // 2):
        P.wait("gpsimd", s_pe, s_pe.n)
        P.wait("gpsimd", s_se, s_se.n)
        P.dma("gpsimd", STG[:, 0:1024],
              wq[2048 + hp * 128: 2048 + (hp + 1) * 128, :], stream="wstg")
        P.wait("tensor", P.dsem("wstg"), P.dsem("wstg").n)
        P.wait("tensor", s_se, s_se.n)
        for t in range(T):
            P.op("tensor", lambda e, t=t: e.transpose(
                TP[0][:, t * 128:(t + 1) * 128], STG[:, t * 128:(t + 1) * 128],
                IDENT[:, :]), inc=(s_pe if t == T - 1 else None))
        P.wait("scalar", s_pe, s_pe.n)
        P.op("scalar", lambda e: e.copy(MSK[:, 1024:2048], TP[0][:, :]),
             inc=s_se)
        wvt_se = s_se.n
        for nt in range(NT):
            psv = pp.acquire("tensor", 2 + (nt % 2))
            P.wait("tensor", s_se, wvt_se)
            P.wait("tensor", swc, th_ld)
            if hp > 0 and nt == 0:
                P.wait("tensor", s_pe, pair_lastread[0])
                P.wait("tensor", s_ve, s_ve.n)
            for t in range(T):
                P.op("tensor", lambda e, t=t, nt=nt, psv=psv: e.matmul(
                    psv[:, 0:128],
                    XBF[:, t * N + nt * 128: t * N + (nt + 1) * 128],
                    MSK[:, 1024 + t * 128: 1024 + (t + 1) * 128],
                    start=(t == 0), stop=False))
            P.op("tensor", lambda e, hp=hp, psv=psv: e.matmul(
                psv[:, 0:128], ONESR[0:1, 0:128],
                B["BQNB"][0:1, hp * 128:(hp + 1) * 128],
                start=False, stop=True), inc=s_pe)
            P.wait("vector", s_pe, s_pe.n)
            P.op("vector", lambda e, nt=nt, psv=psv: e.tensor_copy(
                BIGB[:, V_O + nt * 128: V_O + (nt + 1) * 128], psv[:, 0:128]),
                inc=s_ve)
            pp.release(2 + (nt % 2), "ve", s_ve.n)
        v_ready = s_ve.n

        for hh in range(2):
            h = hp * 2 + hh
            qlo = QL_O[hh]
            P.wait("gpsimd", s_pe, s_pe.n)
            P.wait("gpsimd", s_se, s_se.n)
            P.dma("gpsimd", STG[0:64, 0:1024], wq[h * 64:(h + 1) * 64, :],
                  stream="wstg")
            P.dma("gpsimd", STG[0:64, 1024:2048],
                  wq[1024 + h * 64: 1024 + (h + 1) * 64, :], stream="wstg")
            P.wait("tensor", P.dsem("wstg"), P.dsem("wstg").n)
            P.wait("tensor", s_se, s_se.n)
            for t in range(T):
                P.op("tensor", lambda e, t=t: e.transpose(
                    TP[0][:, t * 64:(t + 1) * 64],
                    STG[0:64, t * 128:(t + 1) * 128], IDENT[0:64, 0:64]))
                P.op("tensor", lambda e, t=t: e.transpose(
                    TP[1][:, t * 64:(t + 1) * 64],
                    STG[0:64, 1024 + t * 128: 1024 + (t + 1) * 128],
                    IDENT[0:64, 0:64]), inc=(s_pe if t == T - 1 else None))
            P.wait("scalar", s_pe, s_pe.n)
            for half in range(2):
                P.op("scalar", lambda e, half=half: e.copy(
                    MSK[:, 0:1024].rearrange("p (t c) -> p t c", t=T)[
                        :, :, half * 64:(half + 1) * 64],
                    TP[0][:, 0:512].rearrange("p (t c) -> p t c", t=T)))
                P.op("scalar", lambda e, half=half: e.copy(
                    MSK[:, 1024:2048].rearrange("p (t c) -> p t c", t=T)[
                        :, :, half * 64:(half + 1) * 64],
                    TP[1][:, 0:512].rearrange("p (t c) -> p t c", t=T)),
                    inc=(s_se if half == 1 else None))
            wqk_se = s_se.n
            P.wait("vector", s_se, bqt_se)
            for half in range(2):
                P.op("vector", lambda e, h=h, half=half: e.tensor_copy(
                    B["BIASC"][half * 64:(half + 1) * 64, 0:1],
                    B["BQT"][(h % 2) * 64:(h % 2) * 64 + 64,
                             h // 2: h // 2 + 1]), inc=s_ve)
                P.op("vector", lambda e, h=h, half=half: e.tensor_copy(
                    B["BIASC"][half * 64:(half + 1) * 64, 1:2],
                    B["BQT"][(h % 2) * 64:(h % 2) * 64 + 64,
                             8 + h // 2: 8 + h // 2 + 1]), inc=s_ve)
            bias_ve = s_ve.n

            P.wait("vector", s_ve, tht_ve)
            P.wait("vector", P.dsem("thtd"), tht_d)
            P.wait("vector", s_ve, lrpe_done[0])
            P.wait("vector", s_ve, lrpe_done[1])
            P.wait("vector", s_pl, s_pl.n)
            for cc in range(8):
                P.wait("vector", s_se, prev_sin_se)
                P.op("vector", lambda e, cc=cc, h=h: e.tensor_scalar(
                    SCR[:, 4096:4608], SCR[:, cc * 512:(cc + 1) * 512],
                    B["THT2"][:, h:h + 1], None, op0=ALU.mult), inc=s_ve)
                P.op("vector", lambda e: e.tensor_scalar(
                    SCR[0:64, 4096:4608], SCR[0:64, 4096:4608], 0.25, None,
                    op0=ALU.add), inc=s_ve)
                P.op("vector", lambda e: e.tensor_copy(B["INT2"][:, :],
                                                       SCR[:, 4096:4608]),
                     inc=s_ve)
                P.op("vector", lambda e: e.tensor_copy(SCR[:, 4608:5120],
                                                       B["INT2"][:, :]),
                     inc=s_ve)
                P.op("vector", lambda e: e.tensor_sub(
                    SCR[:, 4096:4608], SCR[:, 4096:4608], SCR[:, 4608:5120]),
                    inc=s_ve)
                P.op("vector", lambda e: e.tensor_scalar(
                    SCR[:, 4608:5120], SCR[:, 4096:4608], 0.5, None,
                    op0=ALU.is_ge), inc=s_ve)
                P.op("vector", lambda e: e.scalar_tensor_tensor(
                    SCR[:, 4096:4608], SCR[:, 4608:5120], -1.0,
                    SCR[:, 4096:4608], op0=ALU.mult, op1=ALU.add), inc=s_ve)
                P.wait("scalar", s_ve, s_ve.n)
                P.wait("scalar", s_ve, lrpe_done[hh])
                P.op("scalar", lambda e, cc=cc: e.activation(
                    BIGB[:, CS_O + cc * 512: CS_O + (cc + 1) * 512],
                    SCR[:, 4096:4608], AF.Sin, scale=TWOPI), inc=s_se)
                prev_sin_se = s_se.n
            cs_se = s_se.n

            for ch in range(NCH):
                psq = pp.acquire("tensor", 2 + (ch % 2))
                psk = pp.acquire("tensor", 4 + (ch % 2))
                P.wait("tensor", s_se, wqk_se)
                if ch == 0:
                    P.wait("tensor", s_pe, pair_lastread[hh])
                for t in range(T):
                    P.op("tensor", lambda e, t=t, ch=ch, psq=psq: e.matmul(
                        psq[:, :], MSK[:, t * 128:(t + 1) * 128],
                        XBF[:, t * N + ch * 512: t * N + (ch + 1) * 512],
                        start=(t == 0), stop=(t == T - 1)),
                        inc=(s_pe if t == T - 1 else None))
                q_pe = s_pe.n
                for t in range(T):
                    P.op("tensor", lambda e, t=t, ch=ch, psk=psk: e.matmul(
                        psk[:, :], MSK[:, 1024 + t * 128: 1024 + (t + 1) * 128],
                        XBF[:, t * N + ch * 512: t * N + (ch + 1) * 512],
                        start=(t == 0), stop=(t == T - 1)),
                        inc=(s_pe if t == T - 1 else None))
                k_pe = s_pe.n
                P.wait("scalar", s_pe, q_pe)
                P.wait("scalar", s_ve, bias_ve)
                P.op("scalar", lambda e, ch=ch, qlo=qlo, psq=psq: e.activation(
                    BIGB[:, qlo + ch * 512: qlo + (ch + 1) * 512], psq[:, :],
                    AF.Silu, bias=B["BIASC"][:, 0:1]), inc=s_se)
                pp.release(2 + (ch % 2), "se", s_se.n)
                P.wait("scalar", s_pe, k_pe)
                P.op("scalar", lambda e, ch=ch, psk=psk: e.activation(
                    STG[:, ch * 512:(ch + 1) * 512], psk[:, :],
                    AF.Silu, bias=B["BIASC"][:, 1:2]), inc=s_se)
                pp.release(4 + (ch % 2), "se", s_se.n)
                qk_se = s_se.n
                P.wait("vector", s_se, qk_se)
                P.wait("vector", s_se, cs_se)
                P.op("vector", lambda e, ch=ch, qlo=qlo: e.tensor_mul(
                    BIGB[:, qlo + ch * 512: qlo + (ch + 1) * 512],
                    BIGB[:, qlo + ch * 512: qlo + (ch + 1) * 512],
                    BIGB[:, CS_O + ch * 512: CS_O + (ch + 1) * 512]), inc=s_ve)
                P.op("vector", lambda e, ch=ch: e.tensor_mul(
                    STG[:, ch * 512:(ch + 1) * 512],
                    STG[:, ch * 512:(ch + 1) * 512],
                    BIGB[:, CS_O + ch * 512: CS_O + (ch + 1) * 512]), inc=s_ve)
            lrpe_done[hh] = s_ve.n

            kvbank = 4 + hh
            pskv = pp.acquire("tensor", kvbank)
            P.wait("tensor", s_ve, lrpe_done[hh])
            P.wait("tensor", s_ve, v_ready)
            kl_evac = {}
            kv_mm = {}
            tr_pe = {}

            def emit_tr(nt):
                tpb = TP[nt % 2]
                if nt >= 2:
                    P.wait("tensor", s_se, kl_evac[nt - 2])
                P.op("tensor", lambda e, nt=nt, tpb=tpb: e.transpose(
                    tpb[:, 0:128], STG[:, nt * 128:(nt + 1) * 128],
                    IDENT[:, :]), inc=s_pe)
                tr_pe[nt] = s_pe.n
                P.wait("scalar", s_pe, tr_pe[nt])
                if nt >= 2:
                    P.wait("scalar", s_pe, kv_mm[nt - 2])
                P.op("scalar", lambda e, nt=nt, tpb=tpb: e.copy(
                    B["KLS"][:, (nt % 2) * 128:(nt % 2) * 128 + 128],
                    tpb[:, 0:128]), inc=s_se)
                kl_evac[nt] = s_se.n

            emit_tr(0)
            for nt in range(NT):
                if nt + 1 < NT:
                    emit_tr(nt + 1)
                P.wait("tensor", s_se, kl_evac[nt])
                P.op("tensor", lambda e, nt=nt, hh=hh, pskv=pskv: e.matmul(
                    pskv[:, 0:64],
                    B["KLS"][:, (nt % 2) * 128:(nt % 2) * 128 + 128],
                    BIGB[:, V_O + nt * 128 + hh * 64:
                         V_O + nt * 128 + hh * 64 + 64],
                    start=(nt == 0), stop=(nt == NT - 1)), inc=s_pe)
                kv_mm[nt] = s_pe.n
            P.wait("scalar", s_pe, s_pe.n)
            P.op("scalar", lambda e, hh=hh, pskv=pskv: e.copy(
                KVS[:, hh * 64:(hh + 1) * 64], pskv[:, 0:64]), inc=s_se)
            pp.release(kvbank, "se", s_se.n)
        kvs_se = s_se.n

        ctx_ev = {}
        for nt in range(NT):
            psc = pp.acquire("tensor", 2 + (nt % 2))
            P.wait("tensor", s_se, kvs_se)
            for hh in range(2):
                P.op("tensor", lambda e, nt=nt, hh=hh, psc=psc: e.matmul(
                    psc[:, hh * 64:(hh + 1) * 64],
                    BIGB[:, QL_O[hh] + nt * 128: QL_O[hh] + (nt + 1) * 128],
                    KVS[:, hh * 64:(hh + 1) * 64], start=True, stop=True),
                    inc=(s_pe if hh == 1 else None))
            P.wait("scalar", s_pe, s_pe.n)
            P.op("scalar", lambda e, nt=nt, hp=hp, psc=psc: e.copy(
                BIGA[:, nt * 1024 + hp * 128: nt * 1024 + (hp + 1) * 128],
                psc[:, 0:128]), inc=s_se)
            pp.release(2 + (nt % 2), "se", s_se.n)
        pair_lastread[0] = s_pe.n
        pair_lastread[1] = s_pe.n
    ctx_all_se = s_se.n

    wo = W["L1_attn_Wout"].rearrange("(q p) k -> q p k", p=128)
    P.wait("gpsimd", s_pe, s_pe.n)
    P.wait("gpsimd", s_se, s_se.n)
    for which, base in (("wout", 0), ("wu", 8192)):
        for rt in range(T):
            P.wait("gpsimd", s_se, s_se.n)
            P.wait("gpsimd", s_pe, s_pe.n)
            if which == "wout":
                P.dma("gpsimd", STG[:, 0:1024], wo[rt], stream="wstg")
            else:
                P.dma("gpsimd", STG[:, 0:1024],
                      wq[3072 + rt * 128: 3072 + (rt + 1) * 128, :],
                      stream="wstg")
            P.wait("tensor", P.dsem("wstg"), P.dsem("wstg").n)
            P.wait("tensor", s_se, s_se.n)
            for thd in range(T):
                P.op("tensor", lambda e, thd=thd: e.transpose(
                    TP[0][:, thd * 128:(thd + 1) * 128],
                    STG[:, thd * 128:(thd + 1) * 128], IDENT[:, :]),
                    inc=(s_pe if thd == T - 1 else None))
            P.wait("scalar", s_pe, s_pe.n)
            P.op("scalar", lambda e, rt=rt, base=base: e.copy(
                BIGB[:, base:base + 8192].rearrange(
                    "p (thd c) -> p thd c", thd=T)[:, :, rt * 128:(rt + 1) * 128],
                TP[0][:, :].rearrange("p (thd c) -> p thd c", thd=T)),
                inc=s_se)
    wouts_se = s_se.n
    for wname, dst in (("L1_attn_ln_w", 0), ("L1_attn_ln_b", 1024)):
        P.wait("gpsimd", s_ve, s_ve.n)
        P.dma("gpsimd", B["ROWS"][0:1, 2560:3584], W[wname], stream="wconst")
        P.wait("tensor", P.dsem("wconst"), P.dsem("wconst").n)
        ps1 = pp.acquire("tensor", 2)
        P.op("tensor", lambda e, ps1=ps1: e.matmul(
            ps1[:, :], ONESR[0:1, 0:128], B["ROWS"][0:1, 2560:3072],
            start=True, stop=True))
        ps2 = pp.acquire("tensor", 3)
        P.op("tensor", lambda e, ps2=ps2: e.matmul(
            ps2[:, :], ONESR[0:1, 0:128], B["ROWS"][0:1, 3072:3584],
            start=True, stop=True), inc=s_pe)
        P.wait("vector", s_pe, s_pe.n)
        P.op("vector", lambda e, dst=dst, ps1=ps1: e.tensor_copy(
            SCR[:, dst:dst + 512], ps1[:, :]), inc=s_ve)
        P.op("vector", lambda e, dst=dst, ps2=ps2: e.tensor_copy(
            SCR[:, dst + 512:dst + 1024], ps2[:, :]), inc=s_ve)
        pp.release(2, "ve", s_ve.n)
        pp.release(3, "ve", s_ve.n)
    reps_ve = s_ve.n
    P.wait("gpsimd", s_ve, reps_ve)
    P.dma("gpsimd", B["ROWS"][0:1, 1024:2048], W["L1_attn_bout"], stream="wconst")
    P.dma("gpsimd", B["ROWS"][0:1, 2560:3584],
          W["L1_attn_bqkvu"][0:1, 3072:4096], stream="wconst")
    bias_ld = P.dsem("wconst").n

    for ch in range(NCH):
        sum_ve = {}
        sq_se = {}
        for jj in range(4):
            nt = ch * 4 + jj
            P.wait("vector", s_se, ctx_all_se)
            P.op("vector", lambda e, nt=nt, jj=jj: e.reduce_sum(
                STAT[:, 24 + jj:25 + jj], BIGA[:, nt * 1024:(nt + 1) * 1024],
                axis=AX.X), inc=s_ve)
            sum_ve[jj] = s_ve.n
            P.wait("scalar", s_se, ctx_all_se)
            P.op("scalar", lambda e, nt=nt, jj=jj: e.activation(
                SCR[:, 3072:4096], BIGA[:, nt * 1024:(nt + 1) * 1024],
                AF.Square, accum_out=STAT[:, 28 + jj:29 + jj]), inc=s_se)
            sq_se[jj] = s_se.n
        P.wait("vector", s_ve, sum_ve[3])
        P.wait("vector", s_se, sq_se[3])
        P.op("vector", lambda e: e.tensor_scalar(
            STAT[:, 32:36], STAT[:, 24:28], 1.0 / D, None, op0=ALU.mult),
            inc=s_ve)
        P.wait("vector", s_ve, s_ve.n)
        P.op("vector", lambda e: e.tensor_mul(
            STAT[:, 36:40], STAT[:, 32:36], STAT[:, 32:36]), inc=s_ve)
        P.wait("vector", s_ve, s_ve.n)
        P.op("vector", lambda e: e.scalar_tensor_tensor(
            STAT[:, 36:40], STAT[:, 28:32], 1.0 / D, STAT[:, 36:40],
            op0=ALU.mult, op1=ALU.subtract), inc=s_ve)
        P.wait("scalar", s_ve, s_ve.n)
        P.op("scalar", lambda e: e.activation(
            STAT[:, 36:40], STAT[:, 36:40], AF.Ln, bias=B["CONST"][:, 0:1]),
            inc=s_se)
        P.wait("scalar", s_se, s_se.n)
        P.op("scalar", lambda e: e.activation(
            STAT[:, 36:40], STAT[:, 36:40], AF.Exp, scale=-0.5), inc=s_se)
        P.wait("vector", s_se, s_se.n)
        P.op("vector", lambda e: e.tensor_mul(
            STAT[:, 32:36], STAT[:, 32:36], STAT[:, 36:40]), inc=s_ve)
        P.wait("vector", s_ve, s_ve.n)
        P.op("vector", lambda e: e.tensor_scalar(
            STAT[:, 32:36], STAT[:, 32:36], -1.0, None, op0=ALU.mult), inc=s_ve)
        stats_ve = s_ve.n
        P.wait("vector", s_ve, stats_ve)

        for jj in range(4):
            nt = ch * 4 + jj
            psu = [pp.acquire("tensor", 2 + i) for i in range(2)]
            P.wait("tensor", s_se, wouts_se)
            P.wait("tensor", P.dsem("wconst"), bias_ld)
            for i in range(2):
                for t in range(T):
                    P.op("tensor", lambda e, i=i, t=t, nt=nt, psu=psu: e.matmul(
                        psu[i][:, :],
                        XBF[:, t * N + nt * 128: t * N + (nt + 1) * 128],
                        BIGB[:, 8192 + t * 1024 + i * 512:
                             8192 + t * 1024 + (i + 1) * 512],
                        start=(t == 0), stop=False))
                P.op("tensor", lambda e, i=i, psu=psu: e.matmul(
                    psu[i][:, :], ONESR[0:1, 0:128],
                    B["ROWS"][0:1, 2560 + i * 512:2560 + (i + 1) * 512],
                    start=False, stop=True),
                    inc=(s_pe if i == 1 else None))
            u_pe = s_pe.n
            P.op("vector", lambda e, nt=nt, jj=jj: e.tensor_scalar(
                SCR[:, 2048:3072], BIGA[:, nt * 1024:(nt + 1) * 1024],
                STAT[:, 36 + jj:37 + jj], STAT[:, 32 + jj:33 + jj],
                op0=ALU.mult, op1=ALU.add), inc=s_ve)
            P.wait("vector", s_ve, s_ve.n)
            P.op("vector", lambda e: e.tensor_mul(
                SCR[:, 2048:3072], SCR[:, 2048:3072], SCR[:, 0:1024]),
                inc=s_ve)
            P.wait("vector", s_ve, s_ve.n)
            P.op("vector", lambda e: e.tensor_add(
                SCR[:, 2048:3072], SCR[:, 2048:3072], SCR[:, 1024:2048]),
                inc=s_ve)
            P.wait("vector", s_pe, u_pe)
            P.wait("vector", s_ve, s_ve.n)
            for i in range(2):
                P.op("vector", lambda e, i=i, psu=psu: e.tensor_mul(
                    MSK[:, i * 512:(i + 1) * 512],
                    SCR[:, 2048 + i * 512:2048 + (i + 1) * 512], psu[i][:, :]),
                    inc=s_ve)
            nlu_ve = s_ve.n
            pp.release(2, "ve", nlu_ve)
            pp.release(3, "ve", nlu_ve)
            tpn = TP[jj % 2]
            P.wait("tensor", s_ve, nlu_ve)
            P.wait("tensor", s_se, s_se.n)
            for thd in range(T):
                P.op("tensor", lambda e, thd=thd, tpn=tpn: e.transpose(
                    tpn[:, thd * 128:(thd + 1) * 128],
                    MSK[:, thd * 128:(thd + 1) * 128], IDENT[:, :]),
                    inc=(s_pe if thd == T - 1 else None))
            P.wait("scalar", s_pe, s_pe.n)
            P.op("scalar", lambda e, jj=jj, tpn=tpn: e.copy(
                STG[:, :].rearrange("p (thd j c) -> p thd j c", thd=T, j=4)[
                    :, :, jj, :],
                tpn[:, :].rearrange("p (thd c) -> p thd c", thd=T)),
                inc=s_se)
        nlut_se = s_se.n

        def produce_td(td, ps, nlut_se=nlut_se):        nlut_se = s_se.n

        def produce_td(td, ps, nlut_se=nlut_se):
            P.wait("tensor", s_se, nlut_se)
            for thd in range(T):
                P.op("tensor", lambda e, td=td, thd=thd, ps=ps: e.matmul(
                    ps[:, :],
                    BIGB[:, thd * 1024 + td * 128: thd * 1024 + (td + 1) * 128],
                    STG[:, thd * 512:(thd + 1) * 512],
                    start=(thd == 0), stop=False))
            P.op("tensor", lambda e, td=td, ps=ps: e.matmul(
                ps[:, :], B["ROWS"][0:1, 1024 + td * 128:1024 + (td + 1) * 128],
                ONESR[0:1, :], start=False, stop=True), inc=s_pe)
            return s_pe.n

        emit_ln_chunk(B, ch, produce_td, sq_off=ch * 4 * 1024, sq_buf=BIGA)


def emit_stage4(B):
    P, W = B["P"], B["W"]
    s_pe, s_se, s_ve, s_pl = B["s_pe"], B["s_se"], B["s_ve"], B["s_pl"]
    XBF, BIGA, BIGB, STG, PS, TP = (B["XBF"], B["BIGA"], B["BIGB"], B["STG"],
                                    B["PS"], B["TP"])
    IDENT, ONESC, ONESR, MSK, SCR = (B["IDENT"], B["ONESC"], B["ONESR"],
                                     B["MSK"], B["SCR32"])
    pp = B["pp"]
    NF = F1 // 128
    Q = 1.0 / 127.0
    Q2 = Q * Q

    wgr = W["L1_ffn_w_gate"].rearrange("(f p) k -> p f k", p=128)
    wur = W["L1_ffn_w_up"].rearrange("(f p) k -> p f k", p=128)
    wdr = W["L1_ffn_w_down"].rearrange("(q p) k -> p q k", p=128)

    hbuf_last = [0, 0]
    prev_h_ve = 0
    for ft in range(NF):
        P.wait("gpsimd", s_ve, s_ve.n)
        P.dma("gpsimd", SCR[:, 0:1024], wgr[:, ft, :], stream="wstg")
        P.dma("gpsimd", SCR[:, 1024:2048], wur[:, ft, :], stream="wstg")
        P.wait("vector", P.dsem("wstg"), P.dsem("wstg").n)
        P.wait("vector", s_pe, s_pe.n)
        P.op("vector", lambda e: e.tensor_copy(STG[:, 0:1024], SCR[:, 0:1024]),
             inc=s_ve)
        P.op("vector", lambda e: e.tensor_copy(STG[:, 1024:2048],
                                               SCR[:, 1024:2048]), inc=s_ve)
        wnat_ve = s_ve.n
        P.wait("tensor", s_ve, wnat_ve)
        P.wait("tensor", s_se, s_se.n)
        for t in range(T):
            P.op("tensor", lambda e, t=t: e.transpose(
                TP[0][:, t * 128:(t + 1) * 128], STG[:, t * 128:(t + 1) * 128],
                IDENT[:, :]))
            P.op("tensor", lambda e, t=t: e.transpose(
                TP[1][:, t * 128:(t + 1) * 128],
                STG[:, 1024 + t * 128: 1024 + (t + 1) * 128],
                IDENT[:, :]), inc=(s_pe if t == T - 1 else None))
        P.wait("scalar", s_pe, s_pe.n)
        P.op("scalar", lambda e: e.copy(MSK[:, 0:1024], TP[0][:, :]))
        P.op("scalar", lambda e: e.copy(MSK[:, 1024:2048], TP[1][:, :]),
             inc=s_se)
        wt_se = s_se.n

        for ch in range(NCH):
            psg = pp.acquire("tensor", ch % 2)
            psu = pp.acquire("tensor", 2 + (ch % 2))
            P.wait("tensor", s_se, wt_se)
            for t in range(T):
                P.op("tensor", lambda e, t=t, ch=ch, psg=psg: e.matmul(
                    psg[:, :], MSK[:, t * 128:(t + 1) * 128],
                    XBF[:, t * N + ch * 512: t * N + (ch + 1) * 512],
                    start=(t == 0), stop=(t == T - 1)),
                    inc=(s_pe if t == T - 1 else None))
            g_pe = s_pe.n
            for t in range(T):
                P.op("tensor", lambda e, t=t, ch=ch, psu=psu: e.matmul(
                    psu[:, :], MSK[:, 1024 + t * 128: 1024 + (t + 1) * 128],
                    XBF[:, t * N + ch * 512: t * N + (ch + 1) * 512],
                    start=(t == 0), stop=(t == T - 1)),
                    inc=(s_pe if t == T - 1 else None))
            u_pe = s_pe.n
            P.wait("scalar", s_pe, g_pe)
            P.wait("scalar", s_ve, prev_h_ve)
            P.op("scalar", lambda e, psg=psg: e.activation(
                STG[:, 3072:3584], psg[:, :], AF.Silu, scale=Q), inc=s_se)
            sg_se = s_se.n
            pp.release(ch % 2, "se", sg_se)
            hb = 2048 + (ft % 2) * 512
            shs = P.dsem("hs%d" % (ft % 2))
            P.wait("vector", shs, hbuf_last[ft % 2])
            P.wait("vector", s_se, sg_se)
            P.wait("vector", s_pe, u_pe)
            P.op("vector", lambda e, hb=hb, psu=psu: e.tensor_mul(
                STG[:, hb:hb + 512], STG[:, 3072:3584], psu[:, :]), inc=s_ve)
            h_ve = s_ve.n
            prev_h_ve = h_ve
            pp.release(2 + (ch % 2), "ve", h_ve)
            P.wait("sync", s_ve, h_ve)
            P.dma("sync", W["h_scr"][ft, ch, :, :], STG[:, hb:hb + 512],
                  stream="hs%d" % (ft % 2))
            hbuf_last[ft % 2] = shs.n
    sweep_pe = s_pe.n

    def wdt_slice(ft, td):
        if td < 5:
            return BIGB[:, td * 3072 + ft * 128: td * 3072 + (ft + 1) * 128]
        return BIGA[:, 12288 + (td - 5) * 3072 + ft * 128:
                    12288 + (td - 5) * 3072 + (ft + 1) * 128]

    P.wait("gpsimd", s_pe, s_pe.n)
    for td in range(T):
        P.wait("gpsimd", s_ve, s_ve.n)
        P.dma("gpsimd", SCR[:, 0:3072], wdr[:, td, :], stream="wstg")
        P.wait("vector", P.dsem("wstg"), P.dsem("wstg").n)
        P.wait("vector", s_pe, s_pe.n)
        if td == 0:
            P.wait("vector", P.dsem("hs0"), P.dsem("hs0").n)
            P.wait("vector", P.dsem("hs1"), P.dsem("hs1").n)
        P.op("vector", lambda e: e.tensor_copy(STG[:, 0:3072], SCR[:, 0:3072]),
             inc=s_ve)
        P.wait("tensor", s_ve, s_ve.n)
        evg = {}
        for fg in range(3):
            tp = TP[fg % 2]
            if fg >= 2:
                P.wait("tensor", s_se, evg[fg - 2])
            else:
                P.wait("tensor", s_se, s_se.n)
            for i in range(8):
                P.op("tensor", lambda e, fg=fg, i=i, tp=tp: e.transpose(
                    tp[:, i * 128:(i + 1) * 128],
                    STG[:, (fg * 8 + i) * 128:(fg * 8 + i + 1) * 128],
                    IDENT[:, :]), inc=(s_pe if i == 7 else None))
            P.wait("scalar", s_pe, s_pe.n)
            for i in range(8):
                P.op("scalar", lambda e, td=td, fg=fg, i=i, tp=tp: e.copy(
                    wdt_slice(fg * 8 + i, td), tp[:, i * 128:(i + 1) * 128]),
                    inc=(s_se if i == 7 else None))
            evg[fg] = s_se.n
    wdt_se = s_se.n

    for ch in range(NCH):
        P.wait("sync", P.dsem("hs0"), P.dsem("hs0").n)
        P.wait("sync", P.dsem("hs1"), P.dsem("hs1").n)
        P.wait("sync", s_pe, s_pe.n)
        P.dma("sync", BIGA[:, 0:12288].rearrange("p (f n) -> p f n", f=NF),
              W["h_scr"][:, ch, :, :].rearrange("f p n -> p f n"),
              stream="pchunk")
        h_ld = P.dsem("pchunk").n

        def produce_td(td, ps, h_ld=h_ld):
            P.wait("tensor", P.dsem("pchunk"), h_ld)
            P.wait("tensor", s_se, wdt_se)
            for ft in range(NF):
                P.op("tensor", lambda e, td=td, ft=ft, ps=ps: e.matmul(
                    ps[:, :], wdt_slice(ft, td),
                    BIGA[:, ft * 512:(ft + 1) * 512],
                    start=(ft == 0), stop=(ft == NF - 1)), inc=s_pe)
            return s_pe.n

        emit_ln_chunk(B, ch, produce_td, sq_off=0, scale=Q2)




_IDEN = np.eye(128, dtype=np.float32)


def _weight_names(grp):
    names = []
    if "s1" in grp:
        names += ["L0_attn_w_kqv", "L0_attn_w_proj", "L0_attn_b_proj"]
    if "s2" in grp:
        names += ["L0_ffn_gate_w", "L0_ffn_w_gate", "L0_ffn_w_up", "L0_ffn_w_down",
                  "L0_ffn_A1", "L0_ffn_B1", "L0_ffn_A3", "L0_ffn_B3",
                  "L0_ffn_A2", "L0_ffn_B2"]
    if "s3" in grp:
        names += ["L1_attn_Wqkvu", "L1_attn_bqkvu", "L1_attn_Wout", "L1_attn_bout",
                  "L1_attn_theta", "L1_attn_ln_w", "L1_attn_ln_b"]
    if "s4" in grp:
        names += ["L1_ffn_w_gate", "L1_ffn_w_up", "L1_ffn_w_down"]
    return names


def _prep(name, arr):
    a = np.asarray(arr)
    if name in ("L0_attn_b_proj", "L1_attn_bqkvu", "L1_attn_bout",
                "L1_attn_ln_w", "L1_attn_ln_b"):
        a = a.reshape(1, -1)
    elif name == "L1_attn_theta":
        a = a.reshape(H, DH)
    elif name in ("L0_ffn_A1", "L0_ffn_A3"):
        a = a.reshape(E * R, D)
    elif name in ("L0_ffn_B1", "L0_ffn_B3"):
        a = a.reshape(E * F0, R)
    elif name == "L0_ffn_A2":
        a = a.reshape(E * R, F0)
    elif name == "L0_ffn_B2":
        a = a.reshape(E * D, R)
    return np.ascontiguousarray(a)


def _run_stages(stage_groups, inputs):
    x = np.ascontiguousarray(inputs["L0_attn_x"].astype(np.float32, copy=False))
    nb = x.shape[0]
    cur = [np.ascontiguousarray(x[b]) for b in range(nb)]
    for grp in stage_groups:
        nc, es = build_program(grp)
        in_maps = []
        for b in range(nb):
            m = {"x_in": cur[b], "iden": _IDEN}
            for name in _weight_names(grp):
                m[name] = _prep(name, inputs[name])
            in_maps.append(m)
        res = run_bass_kernel_spmd(nc, in_maps, core_ids=list(range(nb)))
        cur = [np.asarray(res.results[b]["x_out"]) for b in range(nb)]
        es.close()
    return np.stack(cur, axis=0)


def kernel(**inputs):
    return _run_stages([["s1", "s2", "s3", "s4"]], inputs)


# version 05
# speedup vs baseline: 1.4250x; 1.0267x over previous
import math
from contextlib import ExitStack

import numpy as np
import concourse.bass as bass
import concourse.mybir as mybir
from concourse.bass_utils import run_bass_kernel_spmd

F32 = mybir.dt.float32
BF16 = mybir.dt.bfloat16
I32 = mybir.dt.int32
AF = mybir.ActivationFunctionType
ALU = mybir.AluOpType
AX = mybir.AxisListType

N = 4096
D = 1024
T = D // 128
NCH = N // 512
NT = N // 128
H = 16
DH = 64
F0 = 4096
F1 = 3072
E = 8
R = 16
EPS = 1e-5

ENGINES = ("tensor", "scalar", "vector", "gpsimd", "sync")
DBG = None


class Sem:
    __slots__ = ("h", "n", "name")

    def __init__(self, h, name):
        self.h = h
        self.n = 0
        self.name = name


class Prog:
    CHAINED = ("scalar", "vector", "gpsimd")

    def __init__(self, nc, es):
        self.nc = nc
        self.es = es
        self.q = {e: [] for e in ENGINES}
        self.sems = {}
        self.chain = {e: self.sem("chain_" + e) for e in self.CHAINED}

    def sem(self, name):
        if name not in self.sems:
            h = self.es.enter_context(self.nc.semaphore(name))
            self.sems[name] = Sem(h, name)
        return self.sems[name]

    def op(self, eng, fn, inc=None, inc_k=1, chain=True):
        cs = self.chain.get(eng) if chain else None
        if cs is not None:
            assert inc is None or inc is cs, \
                f"chained op on {eng} may only inc its chain sem"
            if cs.n > 0:
                self.q[eng].append(lambda e, s=cs, v=cs.n: e.wait_ge(s.h, v))
            cs.n += 1
            self.q[eng].append(lambda e, fn=fn, c=cs, v=cs.n - 1:
                               fn(e).then_inc(c.h, 1))
            return cs.n
        if inc is not None:
            inc.n += inc_k
            self.q[eng].append(
                lambda e, fn=fn, s=inc, k=inc_k: fn(e).then_inc(s.h, k))
            return inc.n
        self.q[eng].append(lambda e, fn=fn: fn(e))
        return None

    def wait(self, eng, sem, val=None):
        v = sem.n if val is None else val
        if v <= 0:
            return
        self.q[eng].append(lambda e, s=sem, v=v: e.wait_ge(s.h, v))

    def dma(self, eng, out, in_, inc=None, stream=None, serial=True, **kw):
        s = inc if inc is not None else self.sem("dma_" + stream)
        if serial and s.n > 0:
            self.q[eng].append(lambda e, ss=s, v=s.n: e.wait_ge(ss.h, v))
        return self.op(eng, lambda e: e.dma_start(out, in_, **kw), inc=s,
                       inc_k=16, chain=False)

    def dsem(self, stream):
        return self.sem("dma_" + stream)

    def barrier(self, tag):
        s = self.sem("bar")
        for eng in ENGINES:
            self.op(eng, lambda e: e.nop(), inc=s, inc_k=1, chain=False)
        tgt = s.n
        for eng in ENGINES:
            self.wait(eng, s, tgt)

    def build(self, block):
        for eng in ENGINES:
            items = self.q[eng]

            def section(e, items=items):
                for fn in items:
                    fn(e)

            getattr(block, eng)(section)


class PsumPool:

    def __init__(self, B):
        self.B = B
        self.free = {i: (0, "pe") for i in range(6)}

    def acquire(self, eng, i):
        cnt, sem = self.free[i]
        self.B["P"].wait(eng, self.B["s_" + sem], cnt)
        return self.B["PS"][i]

    def release(self, i, sem_name, cnt):
        self.free[i] = (cnt, sem_name)




def build_program(stages):
    nc = bass.Bass("TRN2", target_bir_lowering=False)
    es = ExitStack()
    P = Prog(nc, es)

    x_in = nc.dram_tensor("x_in", [N, D], F32, kind="ExternalInput").ap()
    x_out = nc.dram_tensor("x_out", [N, D], F32, kind="ExternalOutput").ap()
    iden_h = nc.dram_tensor("iden", [128, 128], F32, kind="ExternalInput").ap()

    W = {}

    def win(name, shape, dtype=F32):
        W[name] = nc.dram_tensor(name, shape, dtype, kind="ExternalInput").ap()

    if "s1" in stages:
        win("L0_attn_w_kqv", [3 * D, D])
        win("L0_attn_w_proj", [D, D])
        win("L0_attn_b_proj", [1, D])
    if "s2" in stages:
        win("L0_ffn_gate_w", [E, D])
        win("L0_ffn_w_gate", [F0, D])
        win("L0_ffn_w_up", [F0, D])
        win("L0_ffn_w_down", [D, F0])
        win("L0_ffn_A1", [E * R, D])
        win("L0_ffn_B1", [E * F0, R])
        win("L0_ffn_A3", [E * R, D])
        win("L0_ffn_B3", [E * F0, R])
        win("L0_ffn_A2", [E * R, F0])
        win("L0_ffn_B2", [E * D, R])
        W["pT_scr"] = nc.dram_tensor("pT_scr", [F0 // 128, NCH, 128, 512], BF16,
                                     kind="ExternalOutput" if DBG == "s2p"
                                     else "Internal").ap()
    if "s3" in stages:
        win("L1_attn_Wqkvu", [4 * D, D])
        win("L1_attn_bqkvu", [1, 4 * D])
        win("L1_attn_Wout", [D, D])
        win("L1_attn_bout", [1, D])
        win("L1_attn_theta", [H, DH])
        win("L1_attn_ln_w", [1, D])
        win("L1_attn_ln_b", [1, D])
    if "s4" in stages:
        win("L1_ffn_w_gate", [F1, D], I32)
        win("L1_ffn_w_up", [F1, D], I32)
        win("L1_ffn_w_down", [D, F1], I32)
        W["h_scr"] = nc.dram_tensor("h_scr", [F1 // 128, NCH, 128, 512], BF16,
                                    kind="Internal").ap()

    sbt = lambda name, shape, dt: es.enter_context(nc.sbuf_tensor(name, shape, dt))
    B = {}
    B["XBF"] = sbt("XBF", [128, T * N], BF16)
    B["BIGA"] = sbt("BIGA", [128, 32768], BF16)
    B["BIGB"] = sbt("BIGB", [128, 16384], BF16)
    B["SCR32"] = sbt("SCR32", [128, 5120], F32)
    B["IDENT"] = sbt("IDENT", [128, 128], BF16)
    B["IDENT32"] = sbt("IDENT32", [64, 64], F32)
    B["ONESC"] = sbt("ONESC", [128, 1], BF16)
    B["ONESR"] = sbt("ONESR", [1, 512], BF16)
    B["CONST"] = sbt("CONST", [128, 8], F32)
    B["STAT"] = sbt("STAT", [128, 64], F32)
    B["STG"] = sbt("STG", [128, 4096], BF16)
    B["MSK"] = sbt("MSK", [128, 2048], BF16)
    B["ROWS"] = sbt("ROWS", [1, 3584], BF16)
    B["KVS"] = sbt("KVS", [128, 128], BF16)
    B["KLS"] = sbt("KLS", [128, 256], BF16)
    B["THSB"] = sbt("THSB", [16, 64], F32)
    B["THT2"] = sbt("THT2", [128, 16], F32)
    B["BQN"] = sbt("BQN", [32, 128], F32)
    B["BQNB"] = sbt("BQNB", [1, 1024], BF16)
    B["BQT"] = sbt("BQT", [128, 32], F32)
    B["BIASC"] = sbt("BIASC", [128, 2], F32)
    B["INT2"] = sbt("INT2", [128, 512], I32)

    B["PS"] = [es.enter_context(nc.psum_tensor(f"PS{i}", [128, 512], F32))
               for i in range(6)]
    B["TP"] = [es.enter_context(nc.psum_tensor(f"TP{i}", [128, 1024], BF16))
               for i in range(2)]

    block = es.enter_context(nc.Block())
    for s in ("s_ld", "s_pe", "s_st", "s_sp"):
        B[s] = P.sem(s)
    B["s_se"] = P.chain["scalar"]
    B["s_ve"] = P.chain["vector"]
    B["s_pl"] = P.chain["gpsimd"]
    B.update(nc=nc, P=P, W=W, x_in=x_in, x_out=x_out, iden=iden_h)
    B["pp"] = PsumPool(B)

    emit_prologue(B)
    for st in stages:
        {"s1": emit_stage1, "s2": emit_stage2, "s3": emit_stage3,
         "s4": emit_stage4}[st](B)
        P.barrier(st)
    if DBG in ("biga", "s2m") and False:
        pass
    elif DBG == "bigb":
        s_st = B["s_st"]
        for j in range(16):
            P.dma("gpsimd",
                  B["x_out"].rearrange("(q p) d -> p q d", p=128)[:, j, :],
                  B["BIGB"][:, j * 1024:(j + 1) * 1024], inc=s_st)
        P.dma("gpsimd",
              B["x_out"].rearrange("(q p) d -> p q d", p=128)[:, 16, 0:64],
              B["STAT"][:, :], inc=s_st)
        P.wait("gpsimd", s_st, s_st.n)
    elif DBG in ("biga", "s2m"):
        s_st = B["s_st"]
        P.wait("gpsimd", B["s_se"], B["s_se"].n)
        P.wait("gpsimd", B["s_ve"], B["s_ve"].n)
        P.wait("gpsimd", B["s_pe"], B["s_pe"].n)
        for j in range(NT):
            P.dma("gpsimd",
                  B["x_out"].rearrange("(q p) d -> p q d", p=128)[:, j, :],
                  B["BIGA"][:, j * 1024:(j + 1) * 1024], inc=s_st)
        P.wait("gpsimd", s_st, s_st.n)
    else:
        emit_epilogue(B)

    P.build(block)
    return nc, es




def emit_prologue(B):
    P = B["P"]
    s_pe, s_se, s_ve = B["s_pe"], B["s_se"], B["s_ve"]
    STG, IDENT, TP = B["STG"], B["IDENT"], B["TP"]

    P.dma("gpsimd", IDENT[:, :], B["iden"], stream="wconst")
    P.dma("sync", B["IDENT32"][:, :], B["iden"][0:64, 0:64], stream="wc32")
    P.op("vector", lambda e: e.memset(B["ONESC"][:, :], 1.0))
    P.op("vector", lambda e: e.memset(B["ONESR"][:, :], 1.0))
    P.op("vector", lambda e: e.memset(B["CONST"][:, :], 0.0))
    P.op("vector", lambda e: e.memset(B["CONST"][:, 0:1], EPS), inc=s_ve)

    xr = B["x_in"].rearrange("(q p) d -> p q d", p=128)
    pe_after = {}
    evac_after = {}
    sx = P.dsem("xin")
    for g in range(NT // 4):
        if g > 0:
            P.wait("gpsimd", s_pe, pe_after[g - 1])
        P.dma("gpsimd", STG[:, :].rearrange("p (j d) -> p j d", j=4),
              xr[:, g * 4:(g + 1) * 4, :], stream="xin")
        P.wait("tensor", sx, sx.n)
        if g == 0:
            P.wait("tensor", P.dsem("wconst"), 16)
        for j in range(4):
            nt = g * 4 + j
            tp = TP[nt % 2]
            if nt >= 2:
                P.wait("tensor", s_se, evac_after[nt - 2])
            for t in range(T):
                P.op("tensor", lambda e, j=j, t=t, tp=tp: e.transpose(
                    tp[:, t * 128:(t + 1) * 128],
                    STG[:, j * 1024 + t * 128:j * 1024 + (t + 1) * 128],
                    IDENT[:, :]), inc=(s_pe if t == T - 1 else None))
            P.wait("scalar", s_pe, s_pe.n)
            P.op("scalar", lambda e, nt=nt, tp=tp: e.copy(
                B["XBF"][:, :].rearrange("p (t n) -> p t n", t=T)[
                    :, :, nt * 128:(nt + 1) * 128],
                tp[:, :].rearrange("p (t n) -> p t n", t=T)), inc=s_se)
            evac_after[nt] = s_se.n
        pe_after[g] = s_pe.n


def emit_epilogue(B):
    P = B["P"]
    s_pe, s_ve, s_st = B["s_pe"], B["s_ve"], B["s_st"]
    s_se = B["s_se"]
    TP = B["TP"]
    evac_after = {}
    st_after = {}
    ep_se = s_se.n
    ep_ve = B["s_ve"].n
    for nt in range(NT):
        tp = TP[nt % 2]
        if nt < 2:
            P.wait("tensor", s_se, ep_se)
            P.wait("tensor", s_ve, ep_ve)
        if nt >= 2:
            P.wait("tensor", s_ve, evac_after[nt - 2])
        for t in range(T):
            P.op("tensor", lambda e, nt=nt, t=t, tp=tp: e.transpose(
                tp[:, t * 128:(t + 1) * 128],
                B["XBF"][:, t * N + nt * 128: t * N + (nt + 1) * 128],
                B["IDENT"][:, :]), inc=(s_pe if t == T - 1 else None))
        P.wait("vector", s_pe, s_pe.n)
        half = (nt % 2) * 1024
        if nt >= 2:
            P.wait("vector", s_st, st_after[nt - 2])
        P.op("vector", lambda e, tp=tp, half=half: e.tensor_copy(
            B["SCR32"][:, half:half + 1024], tp[:, :]), inc=s_ve)
        evac_after[nt] = s_ve.n
        P.wait("sync", s_ve, s_ve.n)
        P.dma("sync", B["x_out"].rearrange("(q p) d -> q p d", p=128)[nt, :, :],
              B["SCR32"][:, half:half + 1024], inc=s_st)
        st_after[nt] = s_st.n
    P.wait("sync", s_st, s_st.n)




def emit_ln_chunk(B, ch, produce_td, sq_off=0, sq_buf=None, scale=1.0):
    P = B["P"]
    s_pe, s_se, s_ve = B["s_pe"], B["s_se"], B["s_ve"]
    STG, PS = B["STG"], B["PS"]
    if sq_buf is None:
        sq_buf = STG
    STG = sq_buf
    pp = B["pp"]

    def xsl(td):
        return B["XBF"][:, td * N + ch * 512: td * N + (ch + 1) * 512]

    sq_ve = {}
    for td in range(T):
        bank = td % 2
        ps = pp.acquire("tensor", bank)
        pe_done = produce_td(td, ps)
        P.wait("vector", s_pe, pe_done)
        P.op("vector", lambda e, ps=ps, td=td: e.scalar_tensor_tensor(
            xsl(td), ps[:, :], scale, xsl(td), op0=ALU.mult, op1=ALU.add),
            inc=s_ve)
        pp.release(bank, "ve", s_ve.n)
        P.op("vector", lambda e, td=td: e.tensor_mul(
            STG[:, sq_off + td * 512: sq_off + (td + 1) * 512], xsl(td), xsl(td)),
            inc=s_ve)
        sq_ve[td] = s_ve.n
    pp.acquire("tensor", 2)
    pp.acquire("tensor", 3)
    for td in range(T):
        P.wait("tensor", s_ve, sq_ve[td])
        P.op("tensor", lambda e, td=td: e.matmul(
            PS[2][0:1, :], B["ONESC"][:, :], xsl(td),
            start=(td == 0), stop=(td == T - 1)),
            inc=(s_pe if td == T - 1 else None))
        P.op("tensor", lambda e, td=td: e.matmul(
            PS[3][0:1, :], B["ONESC"][:, :],
            STG[:, sq_off + td * 512: sq_off + (td + 1) * 512],
            start=(td == 0), stop=(td == T - 1)),
            inc=(s_pe if td == T - 1 else None))
    stats_pe = s_pe.n
    SS = B["SCR32"]
    P.wait("vector", s_pe, stats_pe)
    P.op("vector", lambda e: e.tensor_scalar(SS[0:1, 4096:4608], PS[2][0:1, :],
                                             1.0 / D, None, op0=ALU.mult),
         inc=s_ve)
    P.wait("vector", s_ve, s_ve.n)
    P.op("vector", lambda e: e.tensor_mul(SS[0:1, 4608:5120], SS[0:1, 4096:4608],
                                          SS[0:1, 4096:4608]), inc=s_ve)
    P.wait("vector", s_ve, s_ve.n)
    P.op("vector", lambda e: e.scalar_tensor_tensor(
        SS[0:1, 4608:5120], PS[3][0:1, :], 1.0 / D, SS[0:1, 4608:5120],
        op0=ALU.mult, op1=ALU.subtract), inc=s_ve)
    pp.release(2, "ve", s_ve.n)
    pp.release(3, "ve", s_ve.n)
    P.wait("scalar", s_ve, s_ve.n)
    P.op("scalar", lambda e: e.activation(SS[0:1, 4608:5120], SS[0:1, 4608:5120],
                                          AF.Ln, bias=B["CONST"][0:1, 0:1]),
         inc=s_se)
    P.wait("scalar", s_se, s_se.n)
    P.op("scalar", lambda e: e.activation(SS[0:1, 4608:5120], SS[0:1, 4608:5120],
                                          AF.Exp, scale=-0.5), inc=s_se)
    P.wait("vector", s_se, s_se.n)
    P.op("vector", lambda e: e.scalar_tensor_tensor(
        SS[0:1, 4096:4608], SS[0:1, 4096:4608], -1.0, SS[0:1, 4608:5120],
        op0=ALU.mult, op1=ALU.mult), inc=s_ve)
    P.op("vector", lambda e: e.tensor_copy(B["ROWS"][0:1, 0:512],
                                           SS[0:1, 4608:5120]), inc=s_ve)
    P.wait("vector", s_ve, s_ve.n)
    P.op("vector", lambda e: e.tensor_copy(B["ROWS"][0:1, 512:1024],
                                           SS[0:1, 4096:4608]), inc=s_ve)
    pp.acquire("tensor", 4)
    pp.acquire("tensor", 5)
    P.wait("tensor", s_ve, s_ve.n)
    P.op("tensor", lambda e: e.matmul(PS[4][:, :], B["ONESR"][0:1, 0:128],
                                      B["ROWS"][0:1, 0:512], start=True, stop=True))
    P.op("tensor", lambda e: e.matmul(PS[5][:, :], B["ONESR"][0:1, 0:128],
                                      B["ROWS"][0:1, 512:1024], start=True,
                                      stop=True), inc=s_pe)
    P.wait("vector", s_pe, s_pe.n)
    for td in range(T):
        P.op("vector", lambda e, td=td: e.tensor_mul(
            STG[:, sq_off + td * 512: sq_off + (td + 1) * 512], xsl(td),
            PS[4][:, :]))
        P.op("vector", lambda e, td=td: e.tensor_add(
            xsl(td), STG[:, sq_off + td * 512: sq_off + (td + 1) * 512],
            PS[5][:, :]), inc=s_ve)
    pp.release(4, "ve", s_ve.n)
    pp.release(5, "ve", s_ve.n)
    return s_ve.n




def emit_stage1(B):
    P, W = B["P"], B["W"]
    s_pe, s_se, s_ve, s_pl = (B["s_pe"], B["s_se"], B["s_ve"], B["s_pl"])
    XBF, BIGA, BIGB, STG, PS, TP = (B["XBF"], B["BIGA"], B["BIGB"], B["STG"],
                                    B["PS"], B["TP"])
    IDENT, ONESC, ONESR, STAT, MSK = (B["IDENT"], B["ONESC"], B["ONESR"],
                                      B["STAT"], B["MSK"])
    pp = B["pp"]

    QT_O = 0
    KT_O = 4096
    AT_O = 8192
    VA_O = 9728
    WK_O = [11808, 13344]
    SQ_O = 14880
    WPT_O = 4096

    P.wait("vector", B["s_se"], B["s_se"].n)
    P.op("vector", lambda e: e.memset(BIGB[64:128, QT_O:QT_O + 4096], 0.0))
    P.op("vector", lambda e: e.memset(BIGB[64:128, KT_O:KT_O + 4096], 0.0))
    P.op("vector", lambda e: e.memset(BIGB[64:128, SQ_O:SQ_O + 1024], 0.0))
    P.op("vector", lambda e: e.memset(
        BIGB[:, VA_O:VA_O + 2080].rearrange("p (k o) -> p k o", o=65)[:, :, 64:65],
        1.0), inc=s_ve)
    P.op("gpsimd", lambda e: e.iota(B["SCR32"][:, 0:512], pattern=[[1, 512]],
                                    base=0, channel_multiplier=0,
                                    allow_small_or_imprecise_dtypes=True))
    P.op("gpsimd", lambda e: e.iota(B["SCR32"][:, 512:513], pattern=[[0, 1]],
                                    base=0, channel_multiplier=1,
                                    allow_small_or_imprecise_dtypes=True), inc=s_pl)
    P.wait("vector", s_pl, s_pl.n)
    for r in range(4):
        P.op("vector", lambda e, r=r: e.tensor_scalar(
            B["SCR32"][:, 513:514], B["SCR32"][:, 512:513], float(128 * r), None,
            op0=ALU.add), inc=s_ve)
        P.wait("vector", s_ve, s_ve.n)
        P.op("vector", lambda e, r=r: e.tensor_scalar(
            MSK[:, r * 512:(r + 1) * 512], B["SCR32"][:, 0:512],
            B["SCR32"][:, 513:514], None, op0=ALU.is_ge), inc=s_ve)
    P.dma("gpsimd", B["ROWS"][0:1, 1024:2048], W["L0_attn_b_proj"],
          stream="wconst")
    s_bp = P.dsem("wconst")
    bp_ld = s_bp.n

    wkr = W["L0_attn_w_kqv"].rearrange("(h d c) k -> h d (c k)", h=H, d=DH, c=3)

    wk_lastread = [0, 0]
    qkv_lastread = 0
    at_lastread = [0, 0, 0]
    ctxn = BIGA
    init_ve = s_ve.n

    for h in range(H):
        if h == 0:
            P.wait("tensor", s_ve, init_ve)
        buf = h % 2
        wko = WK_O[buf]
        P.wait("gpsimd", s_pe, wk_lastread[buf])
        P.wait("gpsimd", s_se, s_se.n)
        P.dma("gpsimd", STG[0:64, 0:3072], wkr[h], stream="wstg")
        P.wait("tensor", P.dsem("wstg"), P.dsem("wstg").n)
        P.wait("tensor", s_se, s_se.n)
        for idx in range(24):
            t, c = divmod(idx, 3)
            tp = TP[0] if idx < 16 else TP[1]
            sl = idx if idx < 16 else idx - 16
            P.op("tensor", lambda e, t=t, c=c, tp=tp, sl=sl: e.transpose(
                tp[:, sl * 64:(sl + 1) * 64],
                STG[0:64, c * 1024 + t * 128: c * 1024 + (t + 1) * 128],
                IDENT[0:64, 0:64]), inc=(s_pe if idx in (15, 23) else None))
        P.wait("scalar", s_pe, s_pe.n)
        P.op("scalar", lambda e, wko=wko: e.copy(BIGB[:, wko:wko + 1024],
                                                 TP[0][:, :]))
        P.op("scalar", lambda e, wko=wko: e.copy(BIGB[:, wko + 1024:wko + 1536],
                                                 TP[1][:, 0:512]), inc=s_se)
        wk_ready = s_se.n

        P.op("vector", lambda e: e.memset(STAT[0:1, 0:2], -1.0e30), inc=s_ve)
        qk_ready = {}
        for ch in range(NCH):
            bank = ch % 2
            ps = pp.acquire("tensor", bank)
            P.wait("tensor", s_se, wk_ready)
            P.wait("scalar", s_pe, qkv_lastread)
            P.wait("vector", s_pe, qkv_lastread)
            for t in range(T):
                P.op("tensor", lambda e, t=t, ch=ch, ps=ps, wko=wko: e.matmul(
                    ps[:, :], BIGB[:, wko + t * 192: wko + t * 192 + 128],
                    XBF[:, t * N + ch * 512: t * N + (ch + 1) * 512],
                    start=(t == 0), stop=(t == T - 1)),
                    inc=(s_pe if t == T - 1 else None))
            mm_done = s_pe.n
            P.wait("vector", s_pe, mm_done)
            P.op("vector", lambda e, ch=ch, ps=ps: e.tensor_copy(
                BIGB[0:64, KT_O + ch * 512: KT_O + (ch + 1) * 512], ps[0:64, :]),
                inc=s_ve)
            k_ready = s_ve.n
            P.wait("scalar", s_pe, mm_done)
            P.op("scalar", lambda e, ch=ch, ps=ps: e.copy(
                BIGB[0:64, QT_O + ch * 512: QT_O + (ch + 1) * 512],
                ps[64:128, :]), inc=s_se)
            q_ready = s_se.n
            pp.release(bank, "se", s_se.n)
            qk_ready[ch] = (q_ready, k_ready)

        sq_pl = {}
        items = []
        for ch in range(NCH):
            items.append(("q", qk_ready[ch][0], s_se, QT_O, ch))
            items.append(("k", qk_ready[ch][1], s_ve, KT_O, ch))
        nb_mm = {}
        for i, (which, rdy, rsem, off, ch) in enumerate(items):
            sqo = SQ_O + (i % 2) * 512
            P.wait("vector", rsem, rdy)
            if i >= 2:
                P.wait("vector", s_pe, nb_mm[i - 2])
            P.op("vector", lambda e, off=off, ch=ch, sqo=sqo: e.tensor_tensor(
                BIGB[0:64, sqo:sqo + 512],
                BIGB[0:64, off + ch * 512: off + (ch + 1) * 512],
                BIGB[0:64, off + ch * 512: off + (ch + 1) * 512],
                op=ALU.mult), inc=s_ve)
            sq_pl[i] = s_ve.n
            psn = pp.acquire("tensor", 4 + (i % 2))
            P.wait("tensor", s_ve, sq_pl[i])
            P.op("tensor", lambda e, sqo=sqo, psn=psn: e.matmul(
                psn[0:1, :], ONESC[:, :], BIGB[:, sqo:sqo + 512],
                start=True, stop=True), inc=s_pe)
            nb_mm[i] = s_pe.n
            P.wait("vector", s_pe, nb_mm[i])
            P.op("vector", lambda e, psn=psn: e.reduce_max(
                STAT[0:1, 2:3], psn[0:1, :], axis=AX.X), inc=s_ve)
            pp.release(4 + (i % 2), "ve", s_ve.n)
            col = 0 if which == "q" else 1
            P.wait("vector", s_ve, s_ve.n)
            P.op("vector", lambda e, col=col: e.tensor_max(
                STAT[0:1, col:col + 1], STAT[0:1, col:col + 1],
                STAT[0:1, 2:3]), inc=s_ve)
        P.op("vector", lambda e: e.tensor_add(STAT[0:1, 3:4], STAT[0:1, 0:1],
                                              STAT[0:1, 1:2]), inc=s_ve)
        P.wait("vector", s_ve, s_ve.n)
        P.op("vector", lambda e: e.tensor_scalar(
            STAT[0:1, 3:4], STAT[0:1, 3:4], -1.0 / 16.0, None, op0=ALU.mult),
            inc=s_ve)
        P.wait("vector", s_ve, s_ve.n)
        P.op("vector", lambda e: e.tensor_copy(B["ROWS"][0:1, 2048:2049],
                                               STAT[0:1, 3:4]), inc=s_ve)
        pp.acquire("tensor", 4)
        P.wait("tensor", s_ve, s_ve.n)
        P.op("tensor", lambda e: e.matmul(PS[4][:, 0:1], ONESR[0:1, 0:128],
                                          B["ROWS"][0:1, 2048:2049],
                                          start=True, stop=True), inc=s_pe)
        P.wait("scalar", s_pe, s_pe.n)
        P.op("scalar", lambda e: e.copy(STAT[:, 5:6], PS[4][:, 0:1]), inc=s_se)
        pp.release(4, "se", s_se.n)
        cb_ready = s_se.n

        for kt in range(NT):
            psv = pp.acquire("tensor", 2 + (kt % 2))
            if kt == 0:
                P.wait("vector", s_pe, qkv_lastread)
            for t in range(T):
                P.op("tensor", lambda e, t=t, kt=kt, wko=wko, psv=psv: e.matmul(
                    psv[:, 0:64], XBF[:, t * N + kt * 128: t * N + (kt + 1) * 128],
                    BIGB[:, wko + t * 192 + 128: wko + t * 192 + 192],
                    start=(t == 0), stop=(t == T - 1)),
                    inc=(s_pe if t == T - 1 else None))
            P.wait("vector", s_pe, s_pe.n)
            P.op("vector", lambda e, kt=kt, psv=psv: e.tensor_copy(
                BIGB[:, VA_O + kt * 65: VA_O + kt * 65 + 64], psv[:, 0:64]),
                inc=s_ve)
            pp.release(2 + (kt % 2), "ve", s_ve.n)
        v_ready = s_ve.n

        P.wait("scalar", s_se, cb_ready)
        P.wait("tensor", s_ve, v_ready)
        P.wait("tensor", s_se, cb_ready)
        for qc in range(NCH):
            cps = [pp.acquire("tensor", 2 + qq) for qq in range(4)]
            nk = 4 * qc + 4
            at_state = {}

            def emit_ctx(kt, nk=nk, cps=cps):
                cntv, semv = at_state[kt]
                P.wait("tensor", {"se": s_se, "ve": s_ve}[semv], cntv)
                for qq in range(4):
                    atb = AT_O + (kt % 3) * 512
                    P.op("tensor", lambda e, qq=qq, kt=kt, atb=atb, cps=cps,
                         nk=nk: e.matmul(
                             cps[qq][:, 0:65],
                             BIGB[:, atb + qq * 128: atb + (qq + 1) * 128],
                             BIGB[:, VA_O + kt * 65: VA_O + (kt + 1) * 65],
                             start=(kt == 0), stop=(kt == nk - 1)),
                         inc=(s_pe if qq == 3 else None))
                at_lastread[kt % 3] = s_pe.n

            for kt in range(nk):
                sbank = kt % 2
                ps_s = pp.acquire("tensor", sbank)
                atb = AT_O + (kt % 3) * 512
                P.op("tensor", lambda e, kt=kt, qc=qc, ps_s=ps_s: e.matmul(
                    ps_s[:, :], BIGB[:, KT_O + kt * 128: KT_O + (kt + 1) * 128],
                    BIGB[:, QT_O + qc * 512: QT_O + (qc + 1) * 512],
                    start=True, stop=True), inc=s_pe)
                st_done = s_pe.n
                P.wait("scalar", s_pe, st_done)
                P.wait("scalar", s_pe, at_lastread[kt % 3])
                P.op("scalar", lambda e, ps_s=ps_s, atb=atb: e.activation(
                    BIGB[:, atb:atb + 512], ps_s[:, :], AF.Exp, scale=0.125,
                    bias=STAT[:, 5:6]), inc=s_se)
                pp.release(sbank, "se", s_se.n)
                at_state[kt] = (s_se.n, "se")
                if kt >= 4 * qc:
                    r = kt - 4 * qc
                    P.wait("vector", s_se, s_se.n)
                    P.op("vector", lambda e, atb=atb, r=r: e.tensor_mul(
                        BIGB[:, atb:atb + 512], BIGB[:, atb:atb + 512],
                        MSK[:, r * 512:(r + 1) * 512]), inc=s_ve)
                    at_state[kt] = (s_ve.n, "ve")
                if kt >= 1:
                    emit_ctx(kt - 1)
            emit_ctx(nk - 1)
            ctx_done = s_pe.n
            P.wait("vector", s_pe, ctx_done)
            for qq in range(4):
                j = qc * 4 + qq
                P.op("vector", lambda e, qq=qq, cps=cps: e.reciprocal(
                    STAT[:, 8:9], cps[qq][:, 64:65]), inc=s_ve)
                P.wait("vector", s_ve, s_ve.n)
                P.op("vector", lambda e, qq=qq, j=j, h=h, cps=cps:
                     e.tensor_scalar(
                         ctxn[:, j * 1024 + h * 64: j * 1024 + h * 64 + 64],
                         cps[qq][:, 0:64], STAT[:, 8:9], None,
                         op0=ALU.mult), inc=s_ve)
                pp.release(2 + qq, "ve", s_ve.n)
        qkv_lastread = s_pe.n
        wk_lastread[buf] = s_pe.n
    all_ctxn = s_ve.n

    wpr = W["L0_attn_w_proj"].rearrange("(q p) k -> q p k", p=128)
    first_wpt_evac = True
    for rt in range(T):
        P.wait("gpsimd", s_se, s_se.n)
        P.wait("gpsimd", s_pe, qkv_lastread)
        P.dma("gpsimd", STG[:, 0:1024], wpr[rt], stream="wstg")
        P.wait("tensor", P.dsem("wstg"), P.dsem("wstg").n)
        P.wait("tensor", s_se, s_se.n)
        for thd in range(T):
            P.op("tensor", lambda e, thd=thd: e.transpose(
                TP[0][:, thd * 128:(thd + 1) * 128],
                STG[:, thd * 128:(thd + 1) * 128], IDENT[:, :]),
                inc=(s_pe if thd == T - 1 else None))
        P.wait("scalar", s_pe, s_pe.n)
        if first_wpt_evac:
            P.wait("scalar", s_pe, qkv_lastread)
            first_wpt_evac = False
        P.op("scalar", lambda e, rt=rt: e.copy(
            BIGB[:, WPT_O:WPT_O + 8192].rearrange("p (thd c) -> p thd c", thd=T)[
                :, :, rt * 128:(rt + 1) * 128],
            TP[0][:, :].rearrange("p (thd c) -> p thd c", thd=T)), inc=s_se)
    wpt_ready = s_se.n

    for ch in range(NCH):
        P.wait("tensor", s_ve, all_ctxn)
        for thd in range(T):
            P.wait("tensor", s_se, s_se.n)
            for jj in range(4):
                j = ch * 4 + jj
                P.op("tensor", lambda e, j=j, jj=jj, thd=thd: e.transpose(
                    TP[0][:, jj * 128:(jj + 1) * 128],
                    ctxn[:, j * 1024 + thd * 128: j * 1024 + (thd + 1) * 128],
                    IDENT[:, :]), inc=(s_pe if jj == 3 else None))
            P.wait("scalar", s_pe, s_pe.n)
            if ch == 0 and thd == 0:
                P.wait("scalar", s_pe, qkv_lastread)
            else:
                P.wait("scalar", s_pe, s_pe.n)
            P.op("scalar", lambda e, thd=thd: e.copy(
                BIGB[:, QT_O + thd * 512: QT_O + (thd + 1) * 512],
                TP[0][:, 0:512]), inc=s_se)
        ctxT_ready = s_se.n

        def produce_td(td, ps, ctxT_ready=ctxT_ready):
            P.wait("tensor", s_se, ctxT_ready)
            P.wait("tensor", s_bp, bp_ld)
            for thd in range(T):
                P.op("tensor", lambda e, td=td, thd=thd, ps=ps: e.matmul(
                    ps[:, :], BIGB[:, WPT_O + thd * 1024 + td * 128:
                                   WPT_O + thd * 1024 + (td + 1) * 128],
                    BIGB[:, QT_O + thd * 512: QT_O + (thd + 1) * 512],
                    start=(thd == 0), stop=False))
            P.op("tensor", lambda e, td=td, ps=ps: e.matmul(
                ps[:, :], B["ROWS"][0:1, 1024 + td * 128:1024 + (td + 1) * 128],
                B["ONESR"][0:1, :], start=False, stop=True), inc=s_pe)
            return s_pe.n

        emit_ln_chunk(B, ch, produce_td)


def emit_stage2(B):
    P, W = B["P"], B["W"]
    s_pe, s_se, s_ve, s_pl = (B["s_pe"], B["s_se"], B["s_ve"], B["s_pl"])
    swstg = P.dsem("wstg")
    XBF, BIGA, BIGB, STG, PS, TP = (B["XBF"], B["BIGA"], B["BIGB"], B["STG"],
                                    B["PS"], B["TP"])
    IDENT, ONESC, ONESR, STAT, MSK, SCR = (B["IDENT"], B["ONESC"], B["ONESR"],
                                           B["STAT"], B["MSK"], B["SCR32"])
    pp = B["pp"]
    GW_O, MUB_O, EXP_O, ON8_O = 1024, 1088, 1600, 1728

    def sl(i, c0, c1):
        return BIGA[:, i * 4096 + c0: i * 4096 + c1]

    base_se, base_ve, base_pe = s_se.n, s_ve.n, s_pe.n
    P.wait("vector", s_se, base_se)
    P.op("vector", lambda e: e.memset(SCR[:, 0:4096], 0.0), inc=s_ve)

    P.op("gpsimd", lambda e: e.iota(SCR[0:8, 4608:4736], pattern=[[1, 128]],
                                    base=0, channel_multiplier=0,
                                    allow_small_or_imprecise_dtypes=True))
    P.op("gpsimd", lambda e: e.iota(SCR[0:8, 4736:4737], pattern=[[0, 1]],
                                    base=0, channel_multiplier=1,
                                    allow_small_or_imprecise_dtypes=True), inc=s_pl)
    P.wait("vector", s_pl, s_pl.n)
    P.op("vector", lambda e: e.tensor_scalar(
        SCR[0:8, 4737:4738], SCR[0:8, 4736:4737], 16.0, None, op0=ALU.mult),
        inc=s_ve)
    P.wait("vector", s_ve, s_ve.n)
    P.op("vector", lambda e: e.tensor_scalar(
        SCR[0:8, 4738:4739], SCR[0:8, 4737:4738], 16.0, None, op0=ALU.add),
        inc=s_ve)
    P.wait("vector", s_ve, s_ve.n)
    P.op("vector", lambda e: e.tensor_scalar(
        SCR[0:8, 4864:4992], SCR[0:8, 4608:4736], SCR[0:8, 4737:4738], None,
        op0=ALU.is_ge), inc=s_ve)
    P.wait("vector", s_ve, s_ve.n)
    P.op("vector", lambda e: e.tensor_scalar(
        SCR[0:8, 4992:5120], SCR[0:8, 4608:4736], SCR[0:8, 4738:4739], None,
        op0=ALU.is_ge), inc=s_ve)
    P.wait("vector", s_ve, s_ve.n)
    P.op("vector", lambda e: e.tensor_sub(
        SCR[0:8, 4864:4992], SCR[0:8, 4864:4992], SCR[0:8, 4992:5120]),
        inc=s_ve)
    P.wait("vector", s_ve, s_ve.n)
    P.op("vector", lambda e: e.tensor_scalar(
        MSK[0:8, EXP_O:EXP_O + 128], SCR[0:8, 4864:4992], 2.0, None,
        op0=ALU.mult), inc=s_ve)
    P.op("vector", lambda e: e.memset(MSK[0:8, ON8_O:ON8_O + 128], 1.0), inc=s_ve)

    P.wait("gpsimd", s_se, base_se)
    P.dma("gpsimd", STG[0:8, 0:1024], W["L0_ffn_gate_w"], stream="wstg")
    P.wait("tensor", swstg, swstg.n)
    P.wait("tensor", s_se, s_se.n)
    for t in range(T):
        P.op("tensor", lambda e, t=t: e.transpose(
            TP[0][:, t * 8:(t + 1) * 8], STG[0:8, t * 128:(t + 1) * 128],
            IDENT[0:8, 0:8]), inc=(s_pe if t == T - 1 else None))
    P.wait("scalar", s_pe, s_pe.n)
    P.op("scalar", lambda e: e.copy(MSK[:, GW_O:GW_O + 64], TP[0][:, 0:64]),
         inc=s_se)
    gw_ready = s_se.n

    P.wait("gpsimd", s_pe, s_pe.n)
    P.dma("gpsimd", STG[:, 0:1024], W["L0_ffn_A1"], stream="wstg")
    P.dma("gpsimd", STG[:, 2048:3072], W["L0_ffn_A3"], stream="wstg")
    P.wait("tensor", swstg, swstg.n)
    for t in range(T):
        P.op("tensor", lambda e, t=t: e.transpose(
            TP[1][:, t * 128:(t + 1) * 128],
            STG[:, t * 128:(t + 1) * 128], IDENT[:, :]),
            inc=(s_pe if t == T - 1 else None))
    P.wait("scalar", s_pe, s_pe.n)
    P.op("scalar", lambda e: e.copy(STG[:, 1024:2048], TP[1][:, :]), inc=s_se)
    P.wait("tensor", s_se, s_se.n)
    for t in range(T):
        P.op("tensor", lambda e, t=t: e.transpose(
            TP[1][:, t * 128:(t + 1) * 128],
            STG[:, 2048 + t * 128: 2048 + (t + 1) * 128], IDENT[:, :]),
            inc=(s_pe if t == T - 1 else None))
    P.wait("scalar", s_pe, s_pe.n)
    P.op("scalar", lambda e: e.copy(STG[:, 3072:4096], TP[1][:, :]), inc=s_se)
    at_ready = s_se.n

    for which, ao, slot in (("r1", 1024, 6), ("r3", 3072, 7)):
        for ch in range(NCH):
            ps = pp.acquire("tensor", ch % 2)
            P.wait("tensor", s_se, at_ready)
            for t in range(T):
                P.op("tensor", lambda e, t=t, ch=ch, ao=ao, ps=ps: e.matmul(
                    ps[:, :], STG[:, ao + t * 128: ao + (t + 1) * 128],
                    XBF[:, t * N + ch * 512: t * N + (ch + 1) * 512],
                    start=(t == 0), stop=(t == T - 1)),
                    inc=(s_pe if t == T - 1 else None))
            P.wait("scalar", s_pe, s_pe.n)
            P.op("scalar", lambda e, ch=ch, slot=slot, ps=ps: e.copy(
                sl(slot, ch * 512, (ch + 1) * 512), ps[:, :]), inc=s_se)
            pp.release(ch % 2, "se", s_se.n)
    r_ready = s_se.n

    P.wait("vector", s_se, r_ready)
    prev_tp = 0
    for nt in range(NT):
        psl = pp.acquire("tensor", 2)
        P.wait("tensor", s_se, gw_ready)
        P.wait("vector", s_pe, prev_tp)
        for t in range(T):
            P.op("tensor", lambda e, t=t, nt=nt, psl=psl: e.matmul(
                psl[:, 0:8], XBF[:, t * N + nt * 128: t * N + (nt + 1) * 128],
                MSK[:, GW_O + t * 8: GW_O + (t + 1) * 8],
                start=(t == 0), stop=(t == T - 1)),
                inc=(s_pe if t == T - 1 else None))
        lg_pe = s_pe.n
        P.wait("vector", s_pe, lg_pe)
        P.op("vector", lambda e, psl=psl: e.reduce_max(
            SCR[:, 4608:4609], psl[:, 0:8], axis=AX.X), inc=s_ve)
        P.wait("vector", s_ve, s_ve.n)
        P.op("vector", lambda e, psl=psl: e.tensor_scalar(
            STG[:, 0:8], psl[:, 0:8], SCR[:, 4608:4609], None, op0=ALU.is_ge),
            inc=s_ve)
        P.wait("vector", s_ve, s_ve.n)
        P.op("vector", lambda e, psl=psl: e.scalar_tensor_tensor(
            SCR[:, 4616:4624], STG[:, 0:8], -1.0e30, psl[:, 0:8],
            op0=ALU.mult, op1=ALU.add), inc=s_ve)
        pp.release(2, "ve", s_ve.n)
        P.wait("vector", s_ve, s_ve.n)
        P.op("vector", lambda e: e.reduce_max(
            SCR[:, 4609:4610], SCR[:, 4616:4624], axis=AX.X), inc=s_ve)
        P.wait("vector", s_ve, s_ve.n)
        P.op("vector", lambda e: e.tensor_scalar(
            STG[:, 8:16], SCR[:, 4616:4624], SCR[:, 4609:4610], None,
            op0=ALU.is_ge), inc=s_ve)
        P.op("vector", lambda e: e.tensor_sub(
            SCR[:, 4610:4611], SCR[:, 4608:4609], SCR[:, 4609:4610]), inc=s_ve)
        P.wait("scalar", s_ve, s_ve.n)
        P.op("scalar", lambda e: e.activation(SCR[:, 4611:4612],
                                              SCR[:, 4610:4611], AF.Sigmoid),
             inc=s_se)
        P.op("scalar", lambda e: e.activation(SCR[:, 4612:4613],
                                              SCR[:, 4610:4611], AF.Sigmoid,
                                              scale=-1.0), inc=s_se)
        P.wait("vector", s_se, s_se.n)
        P.op("vector", lambda e: e.tensor_scalar(
            STG[:, 16:24], STG[:, 0:8], SCR[:, 4611:4612], None, op0=ALU.mult),
            inc=s_ve)
        P.op("vector", lambda e: e.tensor_scalar(
            STG[:, 24:32], STG[:, 8:16], SCR[:, 4612:4613], None, op0=ALU.mult),
            inc=s_ve)
        sv_ready = s_ve.n
        P.wait("tensor", s_ve, sv_ready)
        P.wait("tensor", s_se, s_se.n)
        for j in range(4):
            P.op("tensor", lambda e, j=j: e.transpose(
                TP[0][0:8, j * 128:(j + 1) * 128], STG[:, j * 8:(j + 1) * 8],
                IDENT[:, :]), inc=(s_pe if j == 3 else None))
        prev_tp = s_pe.n
        P.wait("scalar", s_pe, s_pe.n)
        P.op("scalar", lambda e, nt=nt: e.copy(
            BIGB[0:8, :].rearrange("p (j c) -> p j c", j=4)[:, :, nt * 128:(nt + 1) * 128],
            TP[0][0:8, 0:512].rearrange("p (j c) -> p j c", j=4)), inc=s_se)
    sv_done = s_se.n

    for ch in range(NCH):
        ps1 = pp.acquire("tensor", 0)
        ps2 = pp.acquire("tensor", 1)
        P.wait("tensor", s_se, sv_done)
        P.op("tensor", lambda e, ch=ch, ps1=ps1: e.matmul(
            ps1[:, :], MSK[0:8, EXP_O:EXP_O + 128],
            BIGB[0:8, 0 * 4096 + ch * 512: 0 * 4096 + (ch + 1) * 512],
            start=True, stop=True), inc=s_pe)
        P.op("tensor", lambda e, ch=ch, ps2=ps2: e.matmul(
            ps2[:, :], MSK[0:8, EXP_O:EXP_O + 128],
            BIGB[0:8, 1 * 4096 + ch * 512: 1 * 4096 + (ch + 1) * 512],
            start=True, stop=True), inc=s_pe)
        mm_done = s_pe.n
        P.wait("vector", s_pe, mm_done)
        P.op("vector", lambda e, ch=ch, ps1=ps1: e.tensor_mul(
            sl(0, ch * 512, (ch + 1) * 512), sl(6, ch * 512, (ch + 1) * 512),
            ps1[:, :]))
        P.op("vector", lambda e, ch=ch, ps2=ps2: e.tensor_mul(
            sl(1, ch * 512, (ch + 1) * 512), sl(6, ch * 512, (ch + 1) * 512),
            ps2[:, :]))
        P.op("vector", lambda e, ch=ch, ps1=ps1: e.tensor_mul(
            sl(2, ch * 512, (ch + 1) * 512), sl(7, ch * 512, (ch + 1) * 512),
            ps1[:, :]))
        P.op("vector", lambda e, ch=ch, ps2=ps2: e.tensor_mul(
            sl(3, ch * 512, (ch + 1) * 512), sl(7, ch * 512, (ch + 1) * 512),
            ps2[:, :]), inc=s_ve)
        pp.release(0, "ve", s_ve.n)
        pp.release(1, "ve", s_ve.n)
    P.wait("vector", s_ve, s_ve.n)
    P.op("vector", lambda e: e.tensor_sub(sl(1, 0, 4096), sl(1, 0, 4096),
                                          sl(0, 0, 4096)))
    P.op("vector", lambda e: e.tensor_sub(sl(3, 0, 4096), sl(3, 0, 4096),
                                          sl(2, 0, 4096)), inc=s_ve)
    for ch in range(NCH):
        ps1 = pp.acquire("tensor", 0)
        ps2 = pp.acquire("tensor", 1)
        for slot, svj in ((4, 2), (5, 3)):
            ps = ps1 if slot == 4 else ps2
            P.op("tensor", lambda e, ch=ch, svj=svj, ps=ps: e.matmul(
                ps[:, :], MSK[0:8, ON8_O:ON8_O + 128],
                BIGB[0:8, svj * 4096 + ch * 512: svj * 4096 + (ch + 1) * 512],
                start=True, stop=True), inc=s_pe)
        P.wait("scalar", s_pe, s_pe.n)
        P.op("scalar", lambda e, ch=ch, ps1=ps1: e.copy(
            sl(4, ch * 512, (ch + 1) * 512), ps1[:, :]))
        P.op("scalar", lambda e, ch=ch, ps2=ps2: e.copy(
            sl(5, ch * 512, (ch + 1) * 512), ps2[:, :]), inc=s_se)
        pp.release(0, "se", s_se.n)
        pp.release(1, "se", s_se.n)
    rs_all_ve = s_ve.n
    for ch in range(NCH):
        ps1 = pp.acquire("tensor", 0)
        ps2 = pp.acquire("tensor", 1)
        P.op("tensor", lambda e, ch=ch, ps1=ps1: e.matmul(
            ps1[:, :], MSK[0:8, EXP_O:EXP_O + 128],
            BIGB[0:8, 0 * 4096 + ch * 512: 0 * 4096 + (ch + 1) * 512],
            start=True, stop=True), inc=s_pe)
        P.op("tensor", lambda e, ch=ch, ps2=ps2: e.matmul(
            ps2[:, :], MSK[0:8, EXP_O:EXP_O + 128],
            BIGB[0:8, 1 * 4096 + ch * 512: 1 * 4096 + (ch + 1) * 512],
            start=True, stop=True), inc=s_pe)
        P.wait("scalar", s_pe, s_pe.n)
        P.wait("scalar", s_ve, rs_all_ve)
        P.op("scalar", lambda e, ch=ch, ps1=ps1: e.copy(
            sl(6, ch * 512, (ch + 1) * 512), ps1[:, :]))
        P.op("scalar", lambda e, ch=ch, ps2=ps2: e.copy(
            sl(7, ch * 512, (ch + 1) * 512), ps2[:, :]), inc=s_se)
        pp.release(0, "se", s_se.n)
        pp.release(1, "se", s_se.n)
    masks_ready = s_se.n
    if DBG == "s2m":
        return

    wgr = W["L0_ffn_w_gate"].rearrange("(f p) k -> p f k", p=128)
    wur = W["L0_ffn_w_up"].rearrange("(f p) k -> p f k", p=128)
    b1g = W["L0_ffn_B1"].rearrange("(e grp g p) r -> p grp g e r", e=E,
                                   grp=NCH, p=128)
    b3g = W["L0_ffn_B3"].rearrange("(e grp g p) r -> p grp g e r", e=E,
                                   grp=NCH, p=128)
    a2r = W["L0_ffn_A2"]
    S1_O, S2_O, TM_O, T1_O, T2_O, P1_O, P2_O = (8192, 8704, 9216, 9728, 10240,
                                                10752, 14848)
    DS_O = 14336
    B1Q_O, B3Q_O, A2Q_O, NATB_O = 11264, 11776, 12288, 12800
    p_spill = {}
    pbuf_last = [0, 0]
    prev_tm_ve = 0
    GF = 4
    for grp in range(NCH):
        P.wait("gpsimd", s_pe, s_pe.n)
        P.dma("gpsimd", STG[:, :].rearrange("p (f k) -> p f k", f=GF),
              wgr[:, grp * GF:(grp + 1) * GF, :], stream="wstg")
        P.wait("tensor", swstg, swstg.n)
        P.wait("tensor", s_pe, s_pe.n)
        P.wait("tensor", s_se, s_se.n)
        evg = {}
        for ftl in range(GF):
            tp = TP[ftl % 2]
            if ftl >= 2:
                P.wait("tensor", s_se, evg[ftl - 2])
            for t in range(T):
                P.op("tensor", lambda e, ftl=ftl, t=t, tp=tp: e.transpose(
                    tp[:, t * 128:(t + 1) * 128],
                    STG[:, ftl * 1024 + t * 128: ftl * 1024 + (t + 1) * 128],
                    IDENT[:, :]), inc=(s_pe if t == T - 1 else None))
            P.wait("scalar", s_pe, s_pe.n)
            P.op("scalar", lambda e, ftl=ftl, tp=tp: e.copy(
                BIGB[:, ftl * 1024:(ftl + 1) * 1024], tp[:, :]), inc=s_se)
            evg[ftl] = s_se.n
        P.wait("gpsimd", s_pe, s_pe.n)
        P.dma("gpsimd", STG[:, :].rearrange("p (f k) -> p f k", f=GF),
              wur[:, grp * GF:(grp + 1) * GF, :], stream="wstg")
        P.wait("tensor", swstg, swstg.n)
        P.wait("tensor", s_se, s_se.n)
        evg = {}
        for ftl in range(GF):
            tp = TP[ftl % 2]
            if ftl >= 2:
                P.wait("tensor", s_se, evg[ftl - 2])
            for t in range(T):
                P.op("tensor", lambda e, ftl=ftl, t=t, tp=tp: e.transpose(
                    tp[:, t * 128:(t + 1) * 128],
                    STG[:, ftl * 1024 + t * 128: ftl * 1024 + (t + 1) * 128],
                    IDENT[:, :]), inc=(s_pe if t == T - 1 else None))
            P.wait("scalar", s_pe, s_pe.n)
            P.op("scalar", lambda e, ftl=ftl, tp=tp: e.copy(
                BIGB[:, 4096 + ftl * 1024: 4096 + (ftl + 1) * 1024], tp[:, :]),
                inc=s_se)
            evg[ftl] = s_se.n
        for ftl in range(GF):
            P.dma("gpsimd", BIGB[:, NATB_O + ftl * 128: NATB_O + (ftl + 1) * 128]
                  .rearrange("p (e r) -> p e r", e=E),
                  b1g[:, grp, ftl, :, :], stream="b1s%d" % ftl)
            P.dma("gpsimd", BIGB[:, NATB_O + 512 + ftl * 128: NATB_O + 512 + (ftl + 1) * 128]
                  .rearrange("p (e r) -> p e r", e=E),
                  b3g[:, grp, ftl, :, :], stream="b3s%d" % ftl)
        P.dma("gpsimd", BIGB[:, NATB_O + 1024: NATB_O + 1536],
              a2r[:, grp * 512:(grp + 1) * 512], stream="a2s")
        for ftl in range(GF):
            P.wait("tensor", P.dsem("b1s%d" % ftl), P.dsem("b1s%d" % ftl).n)
            P.wait("tensor", P.dsem("b3s%d" % ftl), P.dsem("b3s%d" % ftl).n)
        P.wait("tensor", P.dsem("a2s"), P.dsem("a2s").n)
        P.wait("tensor", s_se, s_se.n)
        for ftl in range(GF):
            P.op("tensor", lambda e, ftl=ftl: e.transpose(
                TP[0][:, ftl * 128:(ftl + 1) * 128],
                BIGB[:, NATB_O + ftl * 128: NATB_O + (ftl + 1) * 128],
                IDENT[:, :]))
            P.op("tensor", lambda e, ftl=ftl: e.transpose(
                TP[1][:, ftl * 128:(ftl + 1) * 128],
                BIGB[:, NATB_O + 512 + ftl * 128: NATB_O + 512 + (ftl + 1) * 128],
                IDENT[:, :]), inc=(s_pe if ftl == GF - 1 else None))
        P.wait("scalar", s_pe, s_pe.n)
        P.op("scalar", lambda e: e.copy(BIGB[:, B1Q_O:B1Q_O + 512],
                                        TP[0][:, 0:512]))
        P.op("scalar", lambda e: e.copy(BIGB[:, B3Q_O:B3Q_O + 512],
                                        TP[1][:, 0:512]), inc=s_se)
        P.wait("tensor", s_se, s_se.n)
        for ftl in range(GF):
            P.op("tensor", lambda e, ftl=ftl: e.transpose(
                TP[0][:, ftl * 128:(ftl + 1) * 128],
                BIGB[:, NATB_O + 1024 + ftl * 128: NATB_O + 1024 + (ftl + 1) * 128],
                IDENT[:, :]), inc=(s_pe if ftl == GF - 1 else None))
        P.wait("scalar", s_pe, s_pe.n)
        P.op("scalar", lambda e: e.copy(BIGB[:, A2Q_O:A2Q_O + 512],
                                        TP[0][:, 0:512]), inc=s_se)
        grp_w_ready = s_se.n

        for ch in range(NCH):
            psu = [None, None]
            pend_u = [None]
            u_done = [0]

            def emit_u(ftl, t2_ve):
                P.wait("tensor", s_ve, t2_ve)
                P.op("tensor", lambda e, ftl=ftl: e.matmul(
                    psu[0][:, :], BIGB[:, A2Q_O + ftl * 128: A2Q_O + (ftl + 1) * 128],
                    BIGB[:, T1_O:T1_O + 512], start=(ftl == 0),
                    stop=(ftl == GF - 1)))
                P.op("tensor", lambda e, ftl=ftl: e.matmul(
                    psu[1][:, :], BIGB[:, A2Q_O + ftl * 128: A2Q_O + (ftl + 1) * 128],
                    BIGB[:, T2_O:T2_O + 512], start=(ftl == 0),
                    stop=(ftl == GF - 1)), inc=s_pe)
                u_done[0] = s_pe.n

            for ftl in range(GF):
                psa = pp.acquire("tensor", ftl % 2)
                P.wait("tensor", s_se, grp_w_ready)
                if grp == 0 and ch == 0 and ftl == 0:
                    P.wait("tensor", s_se, masks_ready)
                    P.wait("tensor", s_ve, s_ve.n)
                for t in range(T):
                    P.op("tensor", lambda e, t=t, ftl=ftl, ch=ch, psa=psa: e.matmul(
                        psa[:, :], BIGB[:, ftl * 1024 + t * 128: ftl * 1024 + (t + 1) * 128],
                        XBF[:, t * N + ch * 512: t * N + (ch + 1) * 512],
                        start=(t == 0), stop=False))
                P.op("tensor", lambda e, ftl=ftl, ch=ch, psa=psa: e.matmul(
                    psa[:, :], BIGB[:, B1Q_O + ftl * 128: B1Q_O + (ftl + 1) * 128],
                    sl(0, ch * 512, (ch + 1) * 512), start=False, stop=True),
                    inc=s_pe)
                a1_pe = s_pe.n
                psd = pp.acquire("tensor", 3 - (ftl % 2))
                P.op("tensor", lambda e, ftl=ftl, ch=ch, psd=psd: e.matmul(
                    psd[:, :], BIGB[:, B1Q_O + ftl * 128: B1Q_O + (ftl + 1) * 128],
                    sl(1, ch * 512, (ch + 1) * 512), start=True, stop=True),
                    inc=s_pe)
                d1_pe = s_pe.n
                if ftl == 0:
                    psu[0] = pp.acquire("tensor", 4)
                    psu[1] = pp.acquire("tensor", 5)
                if pend_u[0] is not None:
                    emit_u(*pend_u[0])
                    pend_u[0] = None
                psb = pp.acquire("tensor", 2 + (ftl % 2))
                for t in range(T):
                    P.op("tensor", lambda e, t=t, ftl=ftl, ch=ch, psb=psb: e.matmul(
                        psb[:, :], BIGB[:, 4096 + ftl * 1024 + t * 128: 4096 + ftl * 1024 + (t + 1) * 128],
                        XBF[:, t * N + ch * 512: t * N + (ch + 1) * 512],
                        start=(t == 0), stop=False))
                P.op("tensor", lambda e, ftl=ftl, ch=ch, psb=psb: e.matmul(
                    psb[:, :], BIGB[:, B3Q_O + ftl * 128: B3Q_O + (ftl + 1) * 128],
                    sl(2, ch * 512, (ch + 1) * 512), start=False, stop=True),
                    inc=s_pe)
                b1_pe = s_pe.n
                P.wait("scalar", s_pe, a1_pe)
                P.wait("scalar", s_ve, prev_tm_ve)
                P.op("scalar", lambda e, psa=psa: e.activation(
                    BIGB[:, S1_O:S1_O + 512], psa[:, :], AF.Silu), inc=s_se)
                s1_se = s_se.n
                P.wait("scalar", s_pe, d1_pe)
                P.op("scalar", lambda e, psd=psd: e.copy(
                    BIGB[:, DS_O:DS_O + 512], psd[:, :]), inc=s_se)
                ds1_se = s_se.n
                P.wait("vector", s_se, ds1_se)
                P.op("vector", lambda e, psa=psa: e.tensor_add(
                    SCR[:, 4096:4608], psa[:, :], BIGB[:, DS_O:DS_O + 512]),
                    inc=s_ve)
                a2_ve = s_ve.n
                pp.release(ftl % 2, "ve", a2_ve)
                P.wait("scalar", s_ve, a2_ve)
                P.op("scalar", lambda e: e.activation(
                    BIGB[:, S2_O:S2_O + 512], SCR[:, 4096:4608], AF.Silu),
                    inc=s_se)
                s2_se = s_se.n
                P.wait("tensor", s_se, ds1_se)
                P.op("tensor", lambda e, ftl=ftl, ch=ch, psd=psd: e.matmul(
                    psd[:, :], BIGB[:, B3Q_O + ftl * 128: B3Q_O + (ftl + 1) * 128],
                    sl(3, ch * 512, (ch + 1) * 512), start=True, stop=True),
                    inc=s_pe)
                d3_pe = s_pe.n
                P.wait("scalar", s_pe, d3_pe)
                P.wait("scalar", s_ve, a2_ve)
                P.op("scalar", lambda e, psd=psd: e.copy(
                    BIGB[:, DS_O:DS_O + 512], psd[:, :]), inc=s_se)
                ds3_se = s_se.n
                pp.release(3 - (ftl % 2), "se", ds3_se)
                P.wait("vector", s_pe, b1_pe)
                P.wait("vector", s_se, ds3_se)
                P.op("vector", lambda e, psb=psb: e.tensor_add(
                    SCR[:, 4608:5120], psb[:, :], BIGB[:, DS_O:DS_O + 512]),
                    inc=s_ve)
                P.wait("vector", s_se, s1_se)
                P.op("vector", lambda e, psb=psb: e.tensor_mul(
                    BIGB[:, TM_O:TM_O + 512], BIGB[:, S1_O:S1_O + 512],
                    psb[:, :]), inc=s_ve)
                pp.release(2 + (ftl % 2), "ve", s_ve.n)
                prev_tm_ve = s_ve.n
                P.wait("vector", s_pe, u_done[0])
                P.op("vector", lambda e, ch=ch: e.tensor_mul(
                    BIGB[:, T1_O:T1_O + 512], BIGB[:, TM_O:TM_O + 512],
                    sl(4, ch * 512, (ch + 1) * 512)), inc=s_ve)
                t1_ve = s_ve.n
                P.wait("vector", s_se, s2_se)
                P.op("vector", lambda e: e.tensor_mul(
                    BIGB[:, TM_O:TM_O + 512], BIGB[:, S2_O:S2_O + 512],
                    SCR[:, 4608:5120]), inc=s_ve)
                P.op("vector", lambda e, ch=ch: e.tensor_mul(
                    BIGB[:, T2_O:T2_O + 512], BIGB[:, TM_O:TM_O + 512],
                    sl(5, ch * 512, (ch + 1) * 512)), inc=s_ve)
                t2_ve = s_ve.n
                ft = grp * GF + ftl
                pb = P1_O if (ft % 2 == 0) else P2_O
                ssp = P.dsem("sp%d" % (ft % 2))
                P.wait("vector", ssp, pbuf_last[ft % 2])
                P.op("vector", lambda e, pb=pb: e.tensor_add(
                    BIGB[:, pb:pb + 512], BIGB[:, T1_O:T1_O + 512],
                    BIGB[:, T2_O:T2_O + 512]), inc=s_ve)
                p_ve = s_ve.n
                pend_u[0] = (ftl, t2_ve)
                P.wait("sync", s_ve, p_ve)
                P.dma("sync", W["pT_scr"][ft, ch, :, :], BIGB[:, pb:pb + 512],
                      stream="sp%d" % (ft % 2))
                pbuf_last[ft % 2] = ssp.n
            emit_u(*pend_u[0])
            u_pe = s_pe.n
            P.wait("vector", s_pe, u_pe)
            P.op("vector", lambda e, ch=ch, psu1=psu[0]: e.tensor_mul(
                SCR[:, 4608:5120], psu1[:, :], sl(6, ch * 512, (ch + 1) * 512)),
                inc=s_ve)
            P.op("vector", lambda e, ch=ch: e.tensor_add(
                SCR[:, ch * 512:(ch + 1) * 512], SCR[:, ch * 512:(ch + 1) * 512],
                SCR[:, 4608:5120]), inc=s_ve)
            P.op("vector", lambda e, ch=ch, psu2=psu[1]: e.tensor_mul(
                SCR[:, 4608:5120], psu2[:, :], sl(7, ch * 512, (ch + 1) * 512)),
                inc=s_ve)
            P.op("vector", lambda e, ch=ch: e.tensor_add(
                SCR[:, ch * 512:(ch + 1) * 512], SCR[:, ch * 512:(ch + 1) * 512],
                SCR[:, 4608:5120]), inc=s_ve)
            pp.release(4, "ve", s_ve.n)
            pp.release(5, "ve", s_ve.n)
    sweep_pe = s_pe.n

    b2rr = W["L0_ffn_B2"].rearrange("(e q p) r -> p q e r", e=E, p=128)
    P.wait("gpsimd", s_pe, sweep_pe)
    P.wait("gpsimd", s_ve, s_ve.n)
    for td in range(T):
        P.dma("gpsimd", STG[:, td * 128:(td + 1) * 128]
              .rearrange("p (e r) -> p e r", e=E), b2rr[:, td, :, :],
              stream="wstg")
    P.wait("tensor", swstg, swstg.n)
    P.wait("tensor", s_se, s_se.n)
    for td in range(T):
        P.op("tensor", lambda e, td=td: e.transpose(
            TP[0][:, td * 128:(td + 1) * 128], STG[:, td * 128:(td + 1) * 128],
            IDENT[:, :]), inc=(s_pe if td == T - 1 else None))
    P.wait("scalar", s_pe, s_pe.n)
    P.op("scalar", lambda e: e.copy(MSK[:, 0:1024], TP[0][:, :]), inc=s_se)
    wdr = W["L0_ffn_w_down"].rearrange("(q p) k -> p q k", p=128)
    for td in range(T):
        P.wait("gpsimd", s_pe, s_pe.n)
        P.wait("gpsimd", s_se, s_se.n)
        P.dma("gpsimd", STG[:, :], wdr[:, td, :], stream="wstg")
        P.wait("tensor", swstg, swstg.n)
        P.wait("tensor", s_se, s_se.n)
        evg = {}
        for fg in range(4):
            tp = TP[fg % 2]
            if fg >= 2:
                P.wait("tensor", s_se, evg[fg - 2])
            for i in range(8):
                P.op("tensor", lambda e, fg=fg, i=i, tp=tp: e.transpose(
                    tp[:, i * 128:(i + 1) * 128],
                    STG[:, (fg * 8 + i) * 128:(fg * 8 + i + 1) * 128],
                    IDENT[:, :]), inc=(s_pe if i == 7 else None))
            P.wait("scalar", s_pe, s_pe.n)
            P.op("scalar", lambda e, td=td, fg=fg, tp=tp: e.copy(
                BIGA[:, :].rearrange("p (f c) -> p f c", f=32)[
                    :, fg * 8:(fg + 1) * 8, td * 128:(td + 1) * 128],
                tp[:, :].rearrange("p (i c) -> p i c", i=8)), inc=s_se)
            evg[fg] = s_se.n
    wdt_ready = s_se.n

    for ch in range(NCH):
        P.wait("sync", P.dsem("sp0"), P.dsem("sp0").n)
        P.wait("sync", P.dsem("sp1"), P.dsem("sp1").n)
        P.wait("sync", s_pe, s_pe.n)
        P.dma("sync", BIGB[:, 0:16384].rearrange("p (f n) -> p f n", f=32),
              W["pT_scr"][:, ch, :, :].rearrange("f p n -> p f n"),
              stream="pchunk")
        p_ld = P.dsem("pchunk").n
        P.wait("vector", s_ve, s_ve.n)
        P.op("vector", lambda e, ch=ch: e.tensor_copy(
            MSK[:, MUB_O:MUB_O + 512], SCR[:, ch * 512:(ch + 1) * 512]), inc=s_ve)
        mub_ve = s_ve.n

        def produce_td(td, ps, p_ld=p_ld, mub_ve=mub_ve):
            P.wait("tensor", P.dsem("pchunk"), p_ld)
            P.wait("tensor", s_se, wdt_ready)
            P.wait("tensor", s_ve, mub_ve)
            for ft in range(32):
                P.op("tensor", lambda e, td=td, ft=ft, ps=ps: e.matmul(
                    ps[:, :], BIGA[:, ft * 1024 + td * 128: ft * 1024 + (td + 1) * 128],
                    BIGB[:, ft * 512:(ft + 1) * 512],
                    start=(ft == 0), stop=False))
            P.op("tensor", lambda e, td=td, ps=ps: e.matmul(
                ps[:, :], MSK[:, td * 128:(td + 1) * 128],
                MSK[:, MUB_O:MUB_O + 512], start=False, stop=True), inc=s_pe)
            return s_pe.n

        emit_ln_chunk(B, ch, produce_td, sq_off=0)


def emit_stage3(B):
    P, W = B["P"], B["W"]
    s_pe, s_se, s_ve, s_pl = B["s_pe"], B["s_se"], B["s_ve"], B["s_pl"]
    XBF, BIGA, BIGB, STG, PS, TP = (B["XBF"], B["BIGA"], B["BIGB"], B["STG"],
                                    B["PS"], B["TP"])
    IDENT, ONESC, ONESR, STAT, MSK, SCR = (B["IDENT"], B["ONESC"], B["ONESR"],
                                           B["STAT"], B["MSK"], B["SCR32"])
    pp = B["pp"]
    KVS = B["KVS"]
    V_O, CS_O, QL_O = 0, 4096, [8192, 12288]
    INV2PI = 1.0 / (2.0 * math.pi)
    TWOPI = 2.0 * math.pi

    P.dma("gpsimd", B["THSB"][:, :], W["L1_attn_theta"], stream="wconst")
    P.dma("gpsimd", B["BQN"][0:32, 0:128],
          W["L1_attn_bqkvu"].rearrange("o (q p) -> (o q) p", p=128),
          stream="wconst")
    P.dma("gpsimd", B["BQNB"][0:1, :], W["L1_attn_bqkvu"][0:1, 2048:3072],
          stream="wconst")
    P.op("gpsimd", lambda e: e.iota(SCR[0:64, 0:4096], pattern=[[1, 4096]],
                                    base=0, channel_multiplier=0,
                                    allow_small_or_imprecise_dtypes=True),
         inc=s_pl)
    P.op("gpsimd", lambda e: e.iota(SCR[64:128, 0:4096], pattern=[[1, 4096]],
                                    base=0, channel_multiplier=0,
                                    allow_small_or_imprecise_dtypes=True),
         inc=s_pl)
    swc = P.dsem("wconst")
    th_ld = swc.n
    P.wait("tensor", swc, th_ld)
    P.wait("tensor", P.dsem("wc32"), P.dsem("wc32").n)
    P.wait("tensor", s_se, s_se.n)
    ps_a = pp.acquire("tensor", 2)
    ps_b = pp.acquire("tensor", 3)
    P.op("tensor", lambda e, ps_a=ps_a: e.transpose(
        ps_a[0:64, 0:16], B["THSB"][:, :], B["IDENT32"][0:16, 0:16]))
    P.op("tensor", lambda e, ps_b=ps_b: e.transpose(
        ps_b[:, 0:32], B["BQN"][0:32, :], B["IDENT32"][0:32, 0:32]), inc=s_pe)
    P.wait("vector", s_pe, s_pe.n)
    P.op("vector", lambda e, ps_a=ps_a: e.tensor_scalar(
        B["THT2"][0:64, 0:16], ps_a[0:64, 0:16], INV2PI, None, op0=ALU.mult),
        inc=s_ve)
    tht_ve = s_ve.n
    pp.release(2, "ve", s_ve.n)
    P.wait("scalar", s_pe, s_pe.n)
    P.op("scalar", lambda e, ps_b=ps_b: e.copy(B["BQT"][:, 0:32],
                                               ps_b[:, 0:32]), inc=s_se)
    bqt_se = s_se.n
    pp.release(3, "se", s_se.n)
    P.wait("gpsimd", s_ve, tht_ve)
    P.dma("gpsimd", B["THT2"][64:128, 0:16], B["THT2"][0:64, 0:16],
          stream="thtd")
    tht_d = P.dsem("thtd").n

    wq = W["L1_attn_Wqkvu"]
    pair_lastread = [0, 0]
    lrpe_done = [0, 0]
    prev_sin_se = 0

    for hp in range(H // 2):
        P.wait("gpsimd", s_pe, s_pe.n)
        P.wait("gpsimd", s_se, s_se.n)
        P.dma("gpsimd", STG[:, 0:1024],
              wq[2048 + hp * 128: 2048 + (hp + 1) * 128, :], stream="wstg")
        P.wait("tensor", P.dsem("wstg"), P.dsem("wstg").n)
        P.wait("tensor", s_se, s_se.n)
        for t in range(T):
            P.op("tensor", lambda e, t=t: e.transpose(
                TP[0][:, t * 128:(t + 1) * 128], STG[:, t * 128:(t + 1) * 128],
                IDENT[:, :]), inc=(s_pe if t == T - 1 else None))
        P.wait("scalar", s_pe, s_pe.n)
        P.op("scalar", lambda e: e.copy(MSK[:, 1024:2048], TP[0][:, :]),
             inc=s_se)
        wvt_se = s_se.n
        for nt in range(NT):
            psv = pp.acquire("tensor", 2 + (nt % 2))
            P.wait("tensor", s_se, wvt_se)
            P.wait("tensor", swc, th_ld)
            if hp > 0 and nt == 0:
                P.wait("tensor", s_pe, pair_lastread[0])
                P.wait("tensor", s_ve, s_ve.n)
            for t in range(T):
                P.op("tensor", lambda e, t=t, nt=nt, psv=psv: e.matmul(
                    psv[:, 0:128],
                    XBF[:, t * N + nt * 128: t * N + (nt + 1) * 128],
                    MSK[:, 1024 + t * 128: 1024 + (t + 1) * 128],
                    start=(t == 0), stop=False))
            P.op("tensor", lambda e, hp=hp, psv=psv: e.matmul(
                psv[:, 0:128], ONESR[0:1, 0:128],
                B["BQNB"][0:1, hp * 128:(hp + 1) * 128],
                start=False, stop=True), inc=s_pe)
            P.wait("vector", s_pe, s_pe.n)
            P.op("vector", lambda e, nt=nt, psv=psv: e.tensor_copy(
                BIGB[:, V_O + nt * 128: V_O + (nt + 1) * 128], psv[:, 0:128]),
                inc=s_ve)
            pp.release(2 + (nt % 2), "ve", s_ve.n)
        v_ready = s_ve.n

        for hh in range(2):
            h = hp * 2 + hh
            qlo = QL_O[hh]
            P.wait("gpsimd", s_pe, s_pe.n)
            P.wait("gpsimd", s_se, s_se.n)
            P.dma("gpsimd", STG[0:64, 0:1024], wq[h * 64:(h + 1) * 64, :],
                  stream="wstg")
            P.dma("gpsimd", STG[0:64, 1024:2048],
                  wq[1024 + h * 64: 1024 + (h + 1) * 64, :], stream="wstg")
            P.wait("tensor", P.dsem("wstg"), P.dsem("wstg").n)
            P.wait("tensor", s_se, s_se.n)
            for t in range(T):
                P.op("tensor", lambda e, t=t: e.transpose(
                    TP[0][:, t * 64:(t + 1) * 64],
                    STG[0:64, t * 128:(t + 1) * 128], IDENT[0:64, 0:64]))
                P.op("tensor", lambda e, t=t: e.transpose(
                    TP[1][:, t * 64:(t + 1) * 64],
                    STG[0:64, 1024 + t * 128: 1024 + (t + 1) * 128],
                    IDENT[0:64, 0:64]), inc=(s_pe if t == T - 1 else None))
            P.wait("scalar", s_pe, s_pe.n)
            for half in range(2):
                P.op("scalar", lambda e, half=half: e.copy(
                    MSK[:, 0:1024].rearrange("p (t c) -> p t c", t=T)[
                        :, :, half * 64:(half + 1) * 64],
                    TP[0][:, 0:512].rearrange("p (t c) -> p t c", t=T)))
                P.op("scalar", lambda e, half=half: e.copy(
                    MSK[:, 1024:2048].rearrange("p (t c) -> p t c", t=T)[
                        :, :, half * 64:(half + 1) * 64],
                    TP[1][:, 0:512].rearrange("p (t c) -> p t c", t=T)),
                    inc=(s_se if half == 1 else None))
            wqk_se = s_se.n
            P.wait("vector", s_se, bqt_se)
            for half in range(2):
                P.op("vector", lambda e, h=h, half=half: e.tensor_copy(
                    B["BIASC"][half * 64:(half + 1) * 64, 0:1],
                    B["BQT"][(h % 2) * 64:(h % 2) * 64 + 64,
                             h // 2: h // 2 + 1]), inc=s_ve)
                P.op("vector", lambda e, h=h, half=half: e.tensor_copy(
                    B["BIASC"][half * 64:(half + 1) * 64, 1:2],
                    B["BQT"][(h % 2) * 64:(h % 2) * 64 + 64,
                             8 + h // 2: 8 + h // 2 + 1]), inc=s_ve)
            bias_ve = s_ve.n

            P.wait("vector", s_ve, tht_ve)
            P.wait("vector", P.dsem("thtd"), tht_d)
            P.wait("vector", s_ve, lrpe_done[0])
            P.wait("vector", s_ve, lrpe_done[1])
            P.wait("vector", s_pl, s_pl.n)
            for cc in range(8):
                P.wait("vector", s_se, prev_sin_se)
                P.op("vector", lambda e, cc=cc, h=h: e.tensor_scalar(
                    SCR[:, 4096:4608], SCR[:, cc * 512:(cc + 1) * 512],
                    B["THT2"][:, h:h + 1], None, op0=ALU.mult), inc=s_ve)
                P.op("vector", lambda e: e.tensor_scalar(
                    SCR[0:64, 4096:4608], SCR[0:64, 4096:4608], 0.25, None,
                    op0=ALU.add), inc=s_ve)
                P.op("vector", lambda e: e.tensor_copy(B["INT2"][:, :],
                                                       SCR[:, 4096:4608]),
                     inc=s_ve)
                P.op("vector", lambda e: e.tensor_copy(SCR[:, 4608:5120],
                                                       B["INT2"][:, :]),
                     inc=s_ve)
                P.op("vector", lambda e: e.tensor_sub(
                    SCR[:, 4096:4608], SCR[:, 4096:4608], SCR[:, 4608:5120]),
                    inc=s_ve)
                P.op("vector", lambda e: e.tensor_scalar(
                    SCR[:, 4608:5120], SCR[:, 4096:4608], 0.5, None,
                    op0=ALU.is_ge), inc=s_ve)
                P.op("vector", lambda e: e.scalar_tensor_tensor(
                    SCR[:, 4096:4608], SCR[:, 4608:5120], -1.0,
                    SCR[:, 4096:4608], op0=ALU.mult, op1=ALU.add), inc=s_ve)
                P.wait("scalar", s_ve, s_ve.n)
                P.wait("scalar", s_ve, lrpe_done[hh])
                P.op("scalar", lambda e, cc=cc: e.activation(
                    BIGB[:, CS_O + cc * 512: CS_O + (cc + 1) * 512],
                    SCR[:, 4096:4608], AF.Sin, scale=TWOPI), inc=s_se)
                prev_sin_se = s_se.n
            cs_se = s_se.n

            for ch in range(NCH):
                psq = pp.acquire("tensor", 2 + (ch % 2))
                psk = pp.acquire("tensor", 4 + (ch % 2))
                P.wait("tensor", s_se, wqk_se)
                if ch == 0:
                    P.wait("tensor", s_pe, pair_lastread[hh])
                for t in range(T):
                    P.op("tensor", lambda e, t=t, ch=ch, psq=psq: e.matmul(
                        psq[:, :], MSK[:, t * 128:(t + 1) * 128],
                        XBF[:, t * N + ch * 512: t * N + (ch + 1) * 512],
                        start=(t == 0), stop=(t == T - 1)),
                        inc=(s_pe if t == T - 1 else None))
                q_pe = s_pe.n
                for t in range(T):
                    P.op("tensor", lambda e, t=t, ch=ch, psk=psk: e.matmul(
                        psk[:, :], MSK[:, 1024 + t * 128: 1024 + (t + 1) * 128],
                        XBF[:, t * N + ch * 512: t * N + (ch + 1) * 512],
                        start=(t == 0), stop=(t == T - 1)),
                        inc=(s_pe if t == T - 1 else None))
                k_pe = s_pe.n
                P.wait("scalar", s_pe, q_pe)
                P.wait("scalar", s_ve, bias_ve)
                P.op("scalar", lambda e, ch=ch, qlo=qlo, psq=psq: e.activation(
                    BIGB[:, qlo + ch * 512: qlo + (ch + 1) * 512], psq[:, :],
                    AF.Silu, bias=B["BIASC"][:, 0:1]), inc=s_se)
                pp.release(2 + (ch % 2), "se", s_se.n)
                P.wait("scalar", s_pe, k_pe)
                P.op("scalar", lambda e, ch=ch, psk=psk: e.activation(
                    STG[:, ch * 512:(ch + 1) * 512], psk[:, :],
                    AF.Silu, bias=B["BIASC"][:, 1:2]), inc=s_se)
                pp.release(4 + (ch % 2), "se", s_se.n)
                qk_se = s_se.n
                P.wait("vector", s_se, qk_se)
                P.wait("vector", s_se, cs_se)
                P.op("vector", lambda e, ch=ch, qlo=qlo: e.tensor_mul(
                    BIGB[:, qlo + ch * 512: qlo + (ch + 1) * 512],
                    BIGB[:, qlo + ch * 512: qlo + (ch + 1) * 512],
                    BIGB[:, CS_O + ch * 512: CS_O + (ch + 1) * 512]), inc=s_ve)
                P.op("vector", lambda e, ch=ch: e.tensor_mul(
                    STG[:, ch * 512:(ch + 1) * 512],
                    STG[:, ch * 512:(ch + 1) * 512],
                    BIGB[:, CS_O + ch * 512: CS_O + (ch + 1) * 512]), inc=s_ve)
            lrpe_done[hh] = s_ve.n

            kvbank = 4 + hh
            pskv = pp.acquire("tensor", kvbank)
            P.wait("tensor", s_ve, lrpe_done[hh])
            P.wait("tensor", s_ve, v_ready)
            kl_evac = {}
            kv_mm = {}
            tr_pe = {}

            def emit_tr(nt):
                tpb = TP[nt % 2]
                if nt >= 2:
                    P.wait("tensor", s_se, kl_evac[nt - 2])
                P.op("tensor", lambda e, nt=nt, tpb=tpb: e.transpose(
                    tpb[:, 0:128], STG[:, nt * 128:(nt + 1) * 128],
                    IDENT[:, :]), inc=s_pe)
                tr_pe[nt] = s_pe.n
                P.wait("scalar", s_pe, tr_pe[nt])
                if nt >= 2:
                    P.wait("scalar", s_pe, kv_mm[nt - 2])
                P.op("scalar", lambda e, nt=nt, tpb=tpb: e.copy(
                    B["KLS"][:, (nt % 2) * 128:(nt % 2) * 128 + 128],
                    tpb[:, 0:128]), inc=s_se)
                kl_evac[nt] = s_se.n

            emit_tr(0)
            for nt in range(NT):
                if nt + 1 < NT:
                    emit_tr(nt + 1)
                P.wait("tensor", s_se, kl_evac[nt])
                P.op("tensor", lambda e, nt=nt, hh=hh, pskv=pskv: e.matmul(
                    pskv[:, 0:64],
                    B["KLS"][:, (nt % 2) * 128:(nt % 2) * 128 + 128],
                    BIGB[:, V_O + nt * 128 + hh * 64:
                         V_O + nt * 128 + hh * 64 + 64],
                    start=(nt == 0), stop=(nt == NT - 1)), inc=s_pe)
                kv_mm[nt] = s_pe.n
            P.wait("scalar", s_pe, s_pe.n)
            P.op("scalar", lambda e, hh=hh, pskv=pskv: e.copy(
                KVS[:, hh * 64:(hh + 1) * 64], pskv[:, 0:64]), inc=s_se)
            pp.release(kvbank, "se", s_se.n)
        kvs_se = s_se.n

        ctx_ev = {}
        for nt in range(NT):
            psc = pp.acquire("tensor", 2 + (nt % 2))
            P.wait("tensor", s_se, kvs_se)
            for hh in range(2):
                P.op("tensor", lambda e, nt=nt, hh=hh, psc=psc: e.matmul(
                    psc[:, hh * 64:(hh + 1) * 64],
                    BIGB[:, QL_O[hh] + nt * 128: QL_O[hh] + (nt + 1) * 128],
                    KVS[:, hh * 64:(hh + 1) * 64], start=True, stop=True),
                    inc=(s_pe if hh == 1 else None))
            P.wait("scalar", s_pe, s_pe.n)
            P.op("scalar", lambda e, nt=nt, hp=hp, psc=psc: e.copy(
                BIGA[:, nt * 1024 + hp * 128: nt * 1024 + (hp + 1) * 128],
                psc[:, 0:128]), inc=s_se)
            pp.release(2 + (nt % 2), "se", s_se.n)
        pair_lastread[0] = s_pe.n
        pair_lastread[1] = s_pe.n
    ctx_all_se = s_se.n

    wo = W["L1_attn_Wout"].rearrange("(q p) k -> q p k", p=128)
    P.wait("gpsimd", s_pe, s_pe.n)
    P.wait("gpsimd", s_se, s_se.n)
    for which, base in (("wout", 0), ("wu", 8192)):
        for rt in range(T):
            P.wait("gpsimd", s_se, s_se.n)
            P.wait("gpsimd", s_pe, s_pe.n)
            if which == "wout":
                P.dma("gpsimd", STG[:, 0:1024], wo[rt], stream="wstg")
            else:
                P.dma("gpsimd", STG[:, 0:1024],
                      wq[3072 + rt * 128: 3072 + (rt + 1) * 128, :],
                      stream="wstg")
            P.wait("tensor", P.dsem("wstg"), P.dsem("wstg").n)
            P.wait("tensor", s_se, s_se.n)
            for thd in range(T):
                P.op("tensor", lambda e, thd=thd: e.transpose(
                    TP[0][:, thd * 128:(thd + 1) * 128],
                    STG[:, thd * 128:(thd + 1) * 128], IDENT[:, :]),
                    inc=(s_pe if thd == T - 1 else None))
            P.wait("scalar", s_pe, s_pe.n)
            P.op("scalar", lambda e, rt=rt, base=base: e.copy(
                BIGB[:, base:base + 8192].rearrange(
                    "p (thd c) -> p thd c", thd=T)[:, :, rt * 128:(rt + 1) * 128],
                TP[0][:, :].rearrange("p (thd c) -> p thd c", thd=T)),
                inc=s_se)
    wouts_se = s_se.n
    for wname, dst in (("L1_attn_ln_w", 0), ("L1_attn_ln_b", 1024)):
        P.wait("gpsimd", s_ve, s_ve.n)
        P.dma("gpsimd", B["ROWS"][0:1, 2560:3584], W[wname], stream="wconst")
        P.wait("tensor", P.dsem("wconst"), P.dsem("wconst").n)
        ps1 = pp.acquire("tensor", 2)
        P.op("tensor", lambda e, ps1=ps1: e.matmul(
            ps1[:, :], ONESR[0:1, 0:128], B["ROWS"][0:1, 2560:3072],
            start=True, stop=True))
        ps2 = pp.acquire("tensor", 3)
        P.op("tensor", lambda e, ps2=ps2: e.matmul(
            ps2[:, :], ONESR[0:1, 0:128], B["ROWS"][0:1, 3072:3584],
            start=True, stop=True), inc=s_pe)
        P.wait("vector", s_pe, s_pe.n)
        P.op("vector", lambda e, dst=dst, ps1=ps1: e.tensor_copy(
            SCR[:, dst:dst + 512], ps1[:, :]), inc=s_ve)
        P.op("vector", lambda e, dst=dst, ps2=ps2: e.tensor_copy(
            SCR[:, dst + 512:dst + 1024], ps2[:, :]), inc=s_ve)
        pp.release(2, "ve", s_ve.n)
        pp.release(3, "ve", s_ve.n)
    reps_ve = s_ve.n
    P.wait("gpsimd", s_ve, reps_ve)
    P.dma("gpsimd", B["ROWS"][0:1, 1024:2048], W["L1_attn_bout"], stream="wconst")
    P.dma("gpsimd", B["ROWS"][0:1, 2560:3584],
          W["L1_attn_bqkvu"][0:1, 3072:4096], stream="wconst")
    bias_ld = P.dsem("wconst").n

    for ch in range(NCH):
        sum_ve = {}
        sq_se = {}
        for jj in range(4):
            nt = ch * 4 + jj
            P.wait("vector", s_se, ctx_all_se)
            P.op("vector", lambda e, nt=nt, jj=jj: e.reduce_sum(
                STAT[:, 24 + jj:25 + jj], BIGA[:, nt * 1024:(nt + 1) * 1024],
                axis=AX.X), inc=s_ve)
            sum_ve[jj] = s_ve.n
            P.wait("scalar", s_se, ctx_all_se)
            P.op("scalar", lambda e, nt=nt, jj=jj: e.activation(
                SCR[:, 3072:4096], BIGA[:, nt * 1024:(nt + 1) * 1024],
                AF.Square, accum_out=STAT[:, 28 + jj:29 + jj]), inc=s_se)
            sq_se[jj] = s_se.n
        P.wait("vector", s_ve, sum_ve[3])
        P.wait("vector", s_se, sq_se[3])
        P.op("vector", lambda e: e.tensor_scalar(
            STAT[:, 32:36], STAT[:, 24:28], 1.0 / D, None, op0=ALU.mult),
            inc=s_ve)
        P.wait("vector", s_ve, s_ve.n)
        P.op("vector", lambda e: e.tensor_mul(
            STAT[:, 36:40], STAT[:, 32:36], STAT[:, 32:36]), inc=s_ve)
        P.wait("vector", s_ve, s_ve.n)
        P.op("vector", lambda e: e.scalar_tensor_tensor(
            STAT[:, 36:40], STAT[:, 28:32], 1.0 / D, STAT[:, 36:40],
            op0=ALU.mult, op1=ALU.subtract), inc=s_ve)
        P.wait("scalar", s_ve, s_ve.n)
        P.op("scalar", lambda e: e.activation(
            STAT[:, 36:40], STAT[:, 36:40], AF.Ln, bias=B["CONST"][:, 0:1]),
            inc=s_se)
        P.wait("scalar", s_se, s_se.n)
        P.op("scalar", lambda e: e.activation(
            STAT[:, 36:40], STAT[:, 36:40], AF.Exp, scale=-0.5), inc=s_se)
        P.wait("vector", s_se, s_se.n)
        P.op("vector", lambda e: e.tensor_mul(
            STAT[:, 32:36], STAT[:, 32:36], STAT[:, 36:40]), inc=s_ve)
        P.wait("vector", s_ve, s_ve.n)
        P.op("vector", lambda e: e.tensor_scalar(
            STAT[:, 32:36], STAT[:, 32:36], -1.0, None, op0=ALU.mult), inc=s_ve)
        stats_ve = s_ve.n
        P.wait("vector", s_ve, stats_ve)

        for jj in range(4):
            nt = ch * 4 + jj
            psu = [pp.acquire("tensor", 2 + i) for i in range(2)]
            P.wait("tensor", s_se, wouts_se)
            P.wait("tensor", P.dsem("wconst"), bias_ld)
            for i in range(2):
                for t in range(T):
                    P.op("tensor", lambda e, i=i, t=t, nt=nt, psu=psu: e.matmul(
                        psu[i][:, :],
                        XBF[:, t * N + nt * 128: t * N + (nt + 1) * 128],
                        BIGB[:, 8192 + t * 1024 + i * 512:
                             8192 + t * 1024 + (i + 1) * 512],
                        start=(t == 0), stop=False))
                P.op("tensor", lambda e, i=i, psu=psu: e.matmul(
                    psu[i][:, :], ONESR[0:1, 0:128],
                    B["ROWS"][0:1, 2560 + i * 512:2560 + (i + 1) * 512],
                    start=False, stop=True),
                    inc=(s_pe if i == 1 else None))
            u_pe = s_pe.n
            P.op("vector", lambda e, nt=nt, jj=jj: e.tensor_scalar(
                SCR[:, 2048:3072], BIGA[:, nt * 1024:(nt + 1) * 1024],
                STAT[:, 36 + jj:37 + jj], STAT[:, 32 + jj:33 + jj],
                op0=ALU.mult, op1=ALU.add), inc=s_ve)
            P.wait("vector", s_ve, s_ve.n)
            P.op("vector", lambda e: e.tensor_mul(
                SCR[:, 2048:3072], SCR[:, 2048:3072], SCR[:, 0:1024]),
                inc=s_ve)
            P.wait("vector", s_ve, s_ve.n)
            P.op("vector", lambda e: e.tensor_add(
                SCR[:, 2048:3072], SCR[:, 2048:3072], SCR[:, 1024:2048]),
                inc=s_ve)
            P.wait("vector", s_pe, u_pe)
            P.wait("vector", s_ve, s_ve.n)
            for i in range(2):
                P.op("vector", lambda e, i=i, psu=psu: e.tensor_mul(
                    MSK[:, i * 512:(i + 1) * 512],
                    SCR[:, 2048 + i * 512:2048 + (i + 1) * 512], psu[i][:, :]),
                    inc=s_ve)
            nlu_ve = s_ve.n
            pp.release(2, "ve", nlu_ve)
            pp.release(3, "ve", nlu_ve)
            tpn = TP[jj % 2]
            P.wait("tensor", s_ve, nlu_ve)
            P.wait("tensor", s_se, s_se.n)
            for thd in range(T):
                P.op("tensor", lambda e, thd=thd, tpn=tpn: e.transpose(
                    tpn[:, thd * 128:(thd + 1) * 128],
                    MSK[:, thd * 128:(thd + 1) * 128], IDENT[:, :]),
                    inc=(s_pe if thd == T - 1 else None))
            P.wait("scalar", s_pe, s_pe.n)
            P.op("scalar", lambda e, jj=jj, tpn=tpn: e.copy(
                STG[:, :].rearrange("p (thd j c) -> p thd j c", thd=T, j=4)[
                    :, :, jj, :],
                tpn[:, :].rearrange("p (thd c) -> p thd c", thd=T)),
                inc=s_se)
        nlut_se = s_se.n

        def produce_td(td, ps, nlut_se=nlut_se):        nlut_se = s_se.n

        def produce_td(td, ps, nlut_se=nlut_se):
            P.wait("tensor", s_se, nlut_se)
            for thd in range(T):
                P.op("tensor", lambda e, td=td, thd=thd, ps=ps: e.matmul(
                    ps[:, :],
                    BIGB[:, thd * 1024 + td * 128: thd * 1024 + (td + 1) * 128],
                    STG[:, thd * 512:(thd + 1) * 512],
                    start=(thd == 0), stop=False))
            P.op("tensor", lambda e, td=td, ps=ps: e.matmul(
                ps[:, :], B["ROWS"][0:1, 1024 + td * 128:1024 + (td + 1) * 128],
                ONESR[0:1, :], start=False, stop=True), inc=s_pe)
            return s_pe.n

        emit_ln_chunk(B, ch, produce_td, sq_off=ch * 4 * 1024, sq_buf=BIGA)


def emit_stage4(B):
    P, W = B["P"], B["W"]
    s_pe, s_se, s_ve, s_pl = B["s_pe"], B["s_se"], B["s_ve"], B["s_pl"]
    XBF, BIGA, BIGB, STG, PS, TP = (B["XBF"], B["BIGA"], B["BIGB"], B["STG"],
                                    B["PS"], B["TP"])
    IDENT, ONESC, ONESR, MSK, SCR = (B["IDENT"], B["ONESC"], B["ONESR"],
                                     B["MSK"], B["SCR32"])
    pp = B["pp"]
    NF = F1 // 128
    Q = 1.0 / 127.0
    Q2 = Q * Q

    wgr = W["L1_ffn_w_gate"].rearrange("(f p) k -> p f k", p=128)
    wur = W["L1_ffn_w_up"].rearrange("(f p) k -> p f k", p=128)
    wdr = W["L1_ffn_w_down"].rearrange("(q p) k -> p q k", p=128)

    hbuf_last = [0, 0]
    prev_h_ve = 0
    for ft in range(NF):
        P.wait("gpsimd", s_ve, s_ve.n)
        P.dma("gpsimd", SCR[:, 0:1024], wgr[:, ft, :], stream="wstg")
        P.dma("gpsimd", SCR[:, 1024:2048], wur[:, ft, :], stream="wstg")
        P.wait("vector", P.dsem("wstg"), P.dsem("wstg").n)
        P.wait("vector", s_pe, s_pe.n)
        P.op("vector", lambda e: e.tensor_copy(STG[:, 0:1024], SCR[:, 0:1024]),
             inc=s_ve)
        P.op("vector", lambda e: e.tensor_copy(STG[:, 1024:2048],
                                               SCR[:, 1024:2048]), inc=s_ve)
        wnat_ve = s_ve.n
        P.wait("tensor", s_ve, wnat_ve)
        P.wait("tensor", s_se, s_se.n)
        for t in range(T):
            P.op("tensor", lambda e, t=t: e.transpose(
                TP[0][:, t * 128:(t + 1) * 128], STG[:, t * 128:(t + 1) * 128],
                IDENT[:, :]))
            P.op("tensor", lambda e, t=t: e.transpose(
                TP[1][:, t * 128:(t + 1) * 128],
                STG[:, 1024 + t * 128: 1024 + (t + 1) * 128],
                IDENT[:, :]), inc=(s_pe if t == T - 1 else None))
        P.wait("scalar", s_pe, s_pe.n)
        P.op("scalar", lambda e: e.copy(MSK[:, 0:1024], TP[0][:, :]))
        P.op("scalar", lambda e: e.copy(MSK[:, 1024:2048], TP[1][:, :]),
             inc=s_se)
        wt_se = s_se.n

        for ch in range(NCH):
            psg = pp.acquire("tensor", ch % 2)
            psu = pp.acquire("tensor", 2 + (ch % 2))
            P.wait("tensor", s_se, wt_se)
            for t in range(T):
                P.op("tensor", lambda e, t=t, ch=ch, psg=psg: e.matmul(
                    psg[:, :], MSK[:, t * 128:(t + 1) * 128],
                    XBF[:, t * N + ch * 512: t * N + (ch + 1) * 512],
                    start=(t == 0), stop=(t == T - 1)),
                    inc=(s_pe if t == T - 1 else None))
            g_pe = s_pe.n
            for t in range(T):
                P.op("tensor", lambda e, t=t, ch=ch, psu=psu: e.matmul(
                    psu[:, :], MSK[:, 1024 + t * 128: 1024 + (t + 1) * 128],
                    XBF[:, t * N + ch * 512: t * N + (ch + 1) * 512],
                    start=(t == 0), stop=(t == T - 1)),
                    inc=(s_pe if t == T - 1 else None))
            u_pe = s_pe.n
            P.wait("scalar", s_pe, g_pe)
            P.wait("scalar", s_ve, prev_h_ve)
            P.op("scalar", lambda e, psg=psg: e.activation(
                STG[:, 3072:3584], psg[:, :], AF.Silu, scale=Q), inc=s_se)
            sg_se = s_se.n
            pp.release(ch % 2, "se", sg_se)
            hb = 2048 + (ft % 2) * 512
            shs = P.dsem("hs%d" % (ft % 2))
            P.wait("vector", shs, hbuf_last[ft % 2])
            P.wait("vector", s_se, sg_se)
            P.wait("vector", s_pe, u_pe)
            P.op("vector", lambda e, hb=hb, psu=psu: e.tensor_mul(
                STG[:, hb:hb + 512], STG[:, 3072:3584], psu[:, :]), inc=s_ve)
            h_ve = s_ve.n
            prev_h_ve = h_ve
            pp.release(2 + (ch % 2), "ve", h_ve)
            P.wait("sync", s_ve, h_ve)
            P.dma("sync", W["h_scr"][ft, ch, :, :], STG[:, hb:hb + 512],
                  stream="hs%d" % (ft % 2))
            hbuf_last[ft % 2] = shs.n
    sweep_pe = s_pe.n

    def wdt_slice(ft, td):
        if td < 5:
            return BIGB[:, td * 3072 + ft * 128: td * 3072 + (ft + 1) * 128]
        return BIGA[:, 12288 + (td - 5) * 3072 + ft * 128:
                    12288 + (td - 5) * 3072 + (ft + 1) * 128]

    P.wait("gpsimd", s_pe, s_pe.n)
    for td in range(T):
        P.wait("gpsimd", s_ve, s_ve.n)
        P.dma("gpsimd", SCR[:, 0:3072], wdr[:, td, :], stream="wstg")
        P.wait("vector", P.dsem("wstg"), P.dsem("wstg").n)
        P.wait("vector", s_pe, s_pe.n)
        if td == 0:
            P.wait("vector", P.dsem("hs0"), P.dsem("hs0").n)
            P.wait("vector", P.dsem("hs1"), P.dsem("hs1").n)
        P.op("vector", lambda e: e.tensor_copy(STG[:, 0:3072], SCR[:, 0:3072]),
             inc=s_ve)
        P.wait("tensor", s_ve, s_ve.n)
        evg = {}
        for fg in range(3):
            tp = TP[fg % 2]
            if fg >= 2:
                P.wait("tensor", s_se, evg[fg - 2])
            else:
                P.wait("tensor", s_se, s_se.n)
            for i in range(8):
                P.op("tensor", lambda e, fg=fg, i=i, tp=tp: e.transpose(
                    tp[:, i * 128:(i + 1) * 128],
                    STG[:, (fg * 8 + i) * 128:(fg * 8 + i + 1) * 128],
                    IDENT[:, :]), inc=(s_pe if i == 7 else None))
            P.wait("scalar", s_pe, s_pe.n)
            for i in range(8):
                P.op("scalar", lambda e, td=td, fg=fg, i=i, tp=tp: e.copy(
                    wdt_slice(fg * 8 + i, td), tp[:, i * 128:(i + 1) * 128]),
                    inc=(s_se if i == 7 else None))
            evg[fg] = s_se.n
    wdt_se = s_se.n

    for ch in range(NCH):
        P.wait("sync", P.dsem("hs0"), P.dsem("hs0").n)
        P.wait("sync", P.dsem("hs1"), P.dsem("hs1").n)
        P.wait("sync", s_pe, s_pe.n)
        P.dma("sync", BIGA[:, 0:12288].rearrange("p (f n) -> p f n", f=NF),
              W["h_scr"][:, ch, :, :].rearrange("f p n -> p f n"),
              stream="pchunk")
        h_ld = P.dsem("pchunk").n

        def produce_td(td, ps, h_ld=h_ld):
            P.wait("tensor", P.dsem("pchunk"), h_ld)
            P.wait("tensor", s_se, wdt_se)
            for ft in range(NF):
                P.op("tensor", lambda e, td=td, ft=ft, ps=ps: e.matmul(
                    ps[:, :], wdt_slice(ft, td),
                    BIGA[:, ft * 512:(ft + 1) * 512],
                    start=(ft == 0), stop=(ft == NF - 1)), inc=s_pe)
            return s_pe.n

        emit_ln_chunk(B, ch, produce_td, sq_off=0, scale=Q2)




_IDEN = np.eye(128, dtype=np.float32)


def _weight_names(grp):
    names = []
    if "s1" in grp:
        names += ["L0_attn_w_kqv", "L0_attn_w_proj", "L0_attn_b_proj"]
    if "s2" in grp:
        names += ["L0_ffn_gate_w", "L0_ffn_w_gate", "L0_ffn_w_up", "L0_ffn_w_down",
                  "L0_ffn_A1", "L0_ffn_B1", "L0_ffn_A3", "L0_ffn_B3",
                  "L0_ffn_A2", "L0_ffn_B2"]
    if "s3" in grp:
        names += ["L1_attn_Wqkvu", "L1_attn_bqkvu", "L1_attn_Wout", "L1_attn_bout",
                  "L1_attn_theta", "L1_attn_ln_w", "L1_attn_ln_b"]
    if "s4" in grp:
        names += ["L1_ffn_w_gate", "L1_ffn_w_up", "L1_ffn_w_down"]
    return names


def _prep(name, arr):
    a = np.asarray(arr)
    if name in ("L0_attn_b_proj", "L1_attn_bqkvu", "L1_attn_bout",
                "L1_attn_ln_w", "L1_attn_ln_b"):
        a = a.reshape(1, -1)
    elif name == "L1_attn_theta":
        a = a.reshape(H, DH)
    elif name in ("L0_ffn_A1", "L0_ffn_A3"):
        a = a.reshape(E * R, D)
    elif name in ("L0_ffn_B1", "L0_ffn_B3"):
        a = a.reshape(E * F0, R)
    elif name == "L0_ffn_A2":
        a = a.reshape(E * R, F0)
    elif name == "L0_ffn_B2":
        a = a.reshape(E * D, R)
    return np.ascontiguousarray(a)


def _run_stages(stage_groups, inputs):
    x = np.ascontiguousarray(inputs["L0_attn_x"].astype(np.float32, copy=False))
    nb = x.shape[0]
    cur = [np.ascontiguousarray(x[b]) for b in range(nb)]
    for grp in stage_groups:
        nc, es = build_program(grp)
        in_maps = []
        for b in range(nb):
            m = {"x_in": cur[b], "iden": _IDEN}
            for name in _weight_names(grp):
                m[name] = _prep(name, inputs[name])
            in_maps.append(m)
        res = run_bass_kernel_spmd(nc, in_maps, core_ids=list(range(nb)))
        cur = [np.asarray(res.results[b]["x_out"]) for b in range(nb)]
        es.close()
    return np.stack(cur, axis=0)


def kernel(**inputs):
    return _run_stages([["s1", "s2", "s3", "s4"]], inputs)
